# version 06
# baseline (speedup 1.0000x reference)
import numpy as np
import ml_dtypes
from contextlib import ExitStack

import concourse.bass as bass
import concourse.mybir as mybir
from concourse.bass_utils import run_bass_kernel_spmd

F32 = mybir.dt.float32
BF16 = mybir.dt.bfloat16
I32 = mybir.dt.int32
AF = mybir.ActivationFunctionType
ALU = mybir.AluOpType
AX = mybir.AxisListType.X
BF16NP = ml_dtypes.bfloat16

B, T, D = 4, 4096, 1024
DI = 2048
DS = 16
TLOC = 2048
NCORES = 8
TC = 512


def f32c(x):
    return np.ascontiguousarray(x, dtype=np.float32)


def cols128(v):
    return f32c(np.asarray(v, np.float32).reshape(-1, 128).T)


def ceil_div(a, b):
    return (a + b - 1) // b


class Cnt(dict):
    def bump(self, k, n=1):
        self[k] = self.get(k, 0) + n
        return self[k]

    def g(self, k):
        return self.get(k, 0)


class PhaseList:
    def __init__(self):
        self.phases = []

    def add(self, p):
        self.phases.append(p)
        return p

    def emit_all(self, block):
        for engname in ('gpsimd', 'sync', 'tensor', 'scalar', 'vector'):
            def mk(en):
                def run(eng):
                    for p in self.phases:
                        p.emit(en, eng)
                return run
            getattr(block, engname)(mk(engname))


class RawPhase:
    def __init__(self, emitters):
        self.emitters = emitters

    def emit(self, engname, eng):
        f = self.emitters.get(engname)
        if f is not None:
            f(eng)


MM_WCFG = {}


class MMPhase:
    def __init__(self, S, C, wT_dram, K, N, cols, act_fn, wpanel, psums,
                 evac_fn, evac_engines=('scalar',), c_off=0, part_base=0,
                 extra_tensor_waits=(), wq='gpsimd'):
        self.wq = wq
        self.wc_n = None
        nm = getattr(wT_dram, 'name', None)
        if nm in MM_WCFG:
            self.wq = 'gpsimd'
            self.wc_n = MM_WCFG[nm]
        self.S, self.C = S, C
        self.wT, self.K, self.N = wT_dram, K, N
        self.act_fn, self.wpanel, self.psums = act_fn, wpanel, psums
        self.evac_fn = evac_fn
        self.evac_engines = evac_engines
        self.pb = part_base
        self.kt = ceil_div(K, 128)
        self.supers = [(n0, min(512, N - n0)) for n0 in range(0, N, 512)]
        self.cgs = [(c0 + c_off, min(512, cols - c0))
                    for c0 in range(0, cols, 512)]
        self.groups = []
        for si, (n0, nw) in enumerate(self.supers):
            for nt in range(ceil_div(nw, 128)):
                nr = min(128, nw - nt * 128)
                for (c0, cw) in self.cgs:
                    self.groups.append((si, n0 + nt * 128, nr, c0, cw))
        self.extra_tw = list(extra_tensor_waits)
        self.base_wl = [C.g('wl0'), C.g('wl1')]
        self.base_mm = C.g('mm')
        self.base_ev = C.g('ev')
        ns = len(self.supers)
        C.bump('wl0', 16 * self.kt * ((ns + 1) // 2))
        C.bump('wl1', 16 * self.kt * (ns // 2))
        C.bump('mm', len(self.groups))
        C.bump('ev', len(self.groups))

    def ktile_rows(self, k):
        return min(128, self.K - k * 128)

    def mm_thru_super(self, si):
        return self.base_mm + sum(1 for g in self.groups if g[0] <= si)

    def emit(self, engname, eng):
        if engname == self.wq:
            self._emit_gpsimd(eng)
        if engname == 'tensor':
            self._emit_tensor(eng)
        if engname in self.evac_engines:
            self._emit_evac(engname, eng)

    def _emit_gpsimd(self, g):
        S = self.S
        if self.wc_n is not None:
            g.wait_ge(S['wc'], self.wc_n)
        g.wait_ge(S['mm'], self.base_mm)
        for si, (n0, nw) in enumerate(self.supers):
            if si >= 2:
                g.wait_ge(S['mm'], self.mm_thru_super(si - 2))
            for k in range(self.kt):
                kr = self.ktile_rows(k)
                g.dma_start(
                    out=self.wpanel[si % 2][self.pb:self.pb + kr,
                                            k * 512:k * 512 + nw],
                    in_=self.wT[k * 128:k * 128 + kr, n0:n0 + nw],
                ).then_inc(S['wl0' if si % 2 == 0 else 'wl1'], 16)

    def _emit_tensor(self, t):
        S = self.S
        t.wait_ge(S['ev'], self.base_ev)
        for sem, val in self.extra_tw:
            t.wait_ge(S[sem], val)
        nps = len(self.psums)
        cur_si = -1
        for gi, (si, n_abs, nr, c0, cw) in enumerate(self.groups):
            if si != cur_si:
                par = si % 2
                t.wait_ge(S['wl0' if par == 0 else 'wl1'],
                          self.base_wl[par] + 16 * self.kt * (si // 2 + 1))
                cur_si = si
            ps = self.psums[gi % nps]
            if gi >= nps:
                t.wait_ge(S['ev'], self.base_ev + gi - nps + 1)
            panel = self.wpanel[si % 2]
            n_loc = n_abs - self.supers[si][0]
            for k in range(self.kt):
                kr = self.ktile_rows(k)
                ins = t.matmul(
                    ps[0:nr, 0:cw],
                    panel[self.pb:self.pb + kr,
                          k * 512 + n_loc:k * 512 + n_loc + nr],
                    self.act_fn(k)[:, c0:c0 + cw],
                    start=(k == 0), stop=(k == self.kt - 1),
                )
                if k == self.kt - 1:
                    ins.then_inc(S['mm'], 1)

    def _emit_evac(self, engname, eng):
        S = self.S
        nps = len(self.psums)
        for gi, (si, n_abs, nr, c0, cw) in enumerate(self.groups):
            eng.wait_ge(S['mm'], self.base_mm + gi + 1)
            self.evac_fn(engname, eng, self.psums[gi % nps],
                         n_abs, nr, c0, cw, gi)


class LNPhase:
    def __init__(self, S, C, s_bf_fn, s_fn, out_fn, cw, aux, pre=None):
        self.S, self.C = S, C
        self.s_bf_fn, self.s_fn, self.out_fn = s_bf_fn, s_fn, out_fn
        self.cw, self.aux = cw, aux
        self.pre = pre or {}
        self.base_sq = C.g('sq'); C.bump('sq', 8)
        self.base_lnm = C.g('lnm'); C.bump('lnm', 1)
        self.base_lnv = C.g('lnv'); C.bump('lnv', 1)
        self.base_lns = C.g('lns'); C.bump('lns', 1)
        self.base_lnb = C.g('lnb'); C.bump('lnb', 2)
        self.base_lne = C.g('lne'); C.bump('lne', 8)
        self.base_squ = C.g('squ'); C.bump('squ', 8)

    def emit(self, engname, eng):
        S, A, cw = self.S, self.aux, self.cw
        for sem, val in self.pre.get(engname, ()):
            eng.wait_ge(S[sem], val)
        if engname == 'scalar':
            for k in range(8):
                if k >= 2:
                    eng.wait_ge(S['squ'], self.base_squ + k - 1)
                eng.activation(A['sq_bf'][k % 2][:, 0:cw],
                               self.s_bf_fn(k)[:, 0:cw],
                               AF.Square).then_inc(S['sq'], 1)
            eng.wait_ge(S['lnv'], self.base_lnv + 1)
            eng.activation(A['row_t'][0:1, 0:cw], A['row_v'][0:1, 0:cw],
                           AF.Ln, bias=A['eps'][0:1, 0:1])
            eng.activation(A['row_r'][0:1, 0:cw], A['row_t'][0:1, 0:cw],
                           AF.Exp, bias=0.0, scale=A['neghalf'][0:1, 0:1]
                           ).then_inc(S['lns'], 1)
        elif engname == 'tensor':
            eng.wait_ge(S['lnv'], self.base_lnv)
            eng.wait_ge(S['lne'], self.base_lne)
            for k in range(8):
                ins = eng.matmul(A['ps_m'][0:1, 0:cw], A['ones128_bf'][:, 0:1],
                                 self.s_bf_fn(k)[:, 0:cw],
                                 start=(k == 0), stop=(k == 7))
                if k == 7:
                    ins.then_inc(S['lnm'], 1)
            for k in range(8):
                eng.wait_ge(S['sq'], self.base_sq + k + 1)
                ins = eng.matmul(A['ps_q'][0:1, 0:cw], A['ones128_bf'][:, 0:1],
                                 A['sq_bf'][k % 2][:, 0:cw],
                                 start=(k == 0), stop=(k == 7))
                ins.then_inc(S['squ'], 1)
            eng.wait_ge(S['lns'], self.base_lns + 1)
            eng.matmul(A['ps_mb'][:, 0:cw], A['ones1_f32'][0:1, :],
                       A['row_m'][0:1, 0:cw], start=True, stop=True
                       ).then_inc(S['lnb'], 1)
            eng.matmul(A['ps_rb'][:, 0:cw], A['ones1_f32'][0:1, :],
                       A['row_r'][0:1, 0:cw], start=True, stop=True
                       ).then_inc(S['lnb'], 1)
        elif engname == 'vector':
            eng.wait_ge(S['lnm'], self.base_lnm + 1)
            eng.wait_ge(S['squ'], self.base_squ + 8)
            eng.tensor_scalar_mul(A['row_m'][0:1, 0:cw], A['ps_m'][0:1, 0:cw],
                                  1.0 / 1024.0)
            eng.tensor_scalar_mul(A['row_v'][0:1, 0:cw], A['ps_q'][0:1, 0:cw],
                                  1.0 / 1024.0)
            eng.tensor_mul(A['row_t'][0:1, 0:cw], A['row_m'][0:1, 0:cw],
                           A['row_m'][0:1, 0:cw])
            eng.tensor_sub(A['row_v'][0:1, 0:cw], A['row_v'][0:1, 0:cw],
                           A['row_t'][0:1, 0:cw]).then_inc(S['lnv'], 1)
            eng.wait_ge(S['lnb'], self.base_lnb + 2)
            for k in range(8):
                eng.tensor_sub(A['nrm'][k % 2][:, 0:cw], self.s_fn(k)[:, 0:cw],
                               A['ps_mb'][:, 0:cw])
                eng.tensor_mul(self.out_fn(k)[:, 0:cw],
                               A['nrm'][k % 2][:, 0:cw],
                               A['ps_rb'][:, 0:cw]).then_inc(S['lne'], 1)


A1_SEMS = ('wl0', 'wl1', 'mm', 'ev', 'xl', 'cl', 'cv', 'xs', 'dv',
           'sb', 'sbr', 'sbc', 'st', 'sm0', 'sm1')


def _emit_A1(nc, e, S, C, PL, io):
    xw = io['xw']; w_in = io['w_in']; wx_main = io['wx_main']
    wx_small = io['wx_small']; w_dt = io['w_dt']; convw_d = io['convw']
    convb_d = io['convb']; bdt_d = io['bdt']; selb_d = io['selb']
    G_o = io['G']; P_o = io['P']; M1_o = io['M1']
    if True:
        x_bf = [e(nc.sbuf_tensor(f"xbf{k}", [128, 516], BF16)) for k in range(8)]
        xin = [e(nc.sbuf_tensor(f"xin{j}", [128, 516], F32)) for j in range(16)]
        sres = [e(nc.sbuf_tensor(f"sres{j}", [128, 516], F32)) for j in range(16)]
        xc32 = [e(nc.sbuf_tensor(f"xc32{j}", [128, 512], F32)) for j in range(16)]
        xcbf = [e(nc.sbuf_tensor(f"xcbf{j}", [128, 512], BF16)) for j in range(16)]
        wpanel = [e(nc.sbuf_tensor(f"wp{i}", [128, 16 * 512], BF16)) for i in range(2)]
        small32 = e(nc.sbuf_tensor("small32", [32, 512], F32))
        small32_bf = e(nc.sbuf_tensor("small32bf", [32, 512], BF16))
        sb_row = e(nc.sbuf_tensor("sbrow", [1, 512], F32))
        ctap = [e(nc.sbuf_tensor(f"ctap{i}", [128, 512], F32)) for i in range(2)]
        m1buf = [e(nc.sbuf_tensor(f"m1b{i}", [128, 512], F32)) for i in range(2)]
        tA = [e(nc.sbuf_tensor(f"tA{i}", [128, 512], F32)) for i in range(2)]
        tB = [e(nc.sbuf_tensor(f"tB{i}", [128, 512], F32)) for i in range(2)]
        dbuf = [e(nc.sbuf_tensor(f"dbuf{i}", [128, 512], F32)) for i in range(2)]
        ctbuf = [e(nc.sbuf_tensor(f"ctb{i}", [128, 512], F32)) for i in range(2)]
        gbuf = [e(nc.sbuf_tensor(f"gbuf{i}", [128, 512], F32)) for i in range(2)]
        pbuf = [e(nc.sbuf_tensor(f"pbuf{i}", [128, 512], F32)) for i in range(2)]
        gcarry = e(nc.sbuf_tensor("gcarry", [128, 16], F32))
        pcarry = e(nc.sbuf_tensor("pcarry", [128, 16], F32))
        zeros512 = e(nc.sbuf_tensor("zeros512", [128, 512], F32))
        convw_t = e(nc.sbuf_tensor("convwt", [128, 64], F32))
        convb_t = e(nc.sbuf_tensor("convbt", [128, 16], F32))
        bdt_t = e(nc.sbuf_tensor("bdtt", [128, 16], F32))
        selb_bf = e(nc.sbuf_tensor("selbbf", [32, 1], BF16))
        ones1 = e(nc.sbuf_tensor("ones1", [1, 128], F32))
        psA = [e(nc.psum_tensor(f"psA{i}", [128, 512], F32)) for i in range(2)]
        ps_sb = e(nc.psum_tensor("pssb", [1, 512], F32))
        ps_sbc = e(nc.psum_tensor("pssbc", [128, 512], F32))

        def consts_g(g):
            g.dma_start(out=convw_t[:, :], in_=convw_d).then_inc(S['cl'], 16)
            g.dma_start(out=convb_t[:, :], in_=convb_d).then_inc(S['cl'], 16)
            g.dma_start(out=bdt_t[:, :], in_=bdt_d).then_inc(S['cl'], 16)
            g.dma_start(out=selb_bf[:, :], in_=selb_d).then_inc(S['cl'], 16)

        def consts_v(v):
            v.memset(ones1[0:1, :], 1.0)
            v.memset(zeros512[:, :], 0.0)
            v.memset(gcarry[:, :], 0.0)
            v.memset(pcarry[:, :], 1.0)
            v.wait_ge(S['cl'], 64)

        def consts_s(s):
            s.wait_ge(S['cl'], 64)

        def consts_t(t):
            t.wait_ge(S['cl'], 64)
        C.bump('cl', 64)
        PL.add(RawPhase({'gpsimd': consts_g, 'vector': consts_v,
                         'scalar': consts_s, 'tensor': consts_t}))

        for ci in range(4):
            c0d = ci * 512

            prev_mm = C.g('mm')

            def loadx(g, ci=ci, prev_mm=prev_mm):
                if ci > 0:
                    g.wait_ge(S['mm'], prev_mm)
                for k in range(8):
                    g.dma_start(out=x_bf[k][:, :],
                                in_=xw[k * 128:(k + 1) * 128,
                                       ci * 512:ci * 512 + 516]
                                ).then_inc(S['xl'], 16)
            C.bump('xl', 128)
            xl_after = C.g('xl')
            PL.add(RawPhase({'gpsimd': loadx}))

            st_mm1 = {'conv_j': 0, 'silu_j': 0}
            base_cv = C.g('cv')
            base_xs = C.g('xs')

            def mk_mm1_evac(base_cv=base_cv, base_xs=base_xs, st=st_mm1,
                            ph_holder=[]):
                def evac(engname, eng, ps, n_abs, nr, c0, cw, gi):
                    ph = ph_holder[0]
                    if n_abs < DI:
                        j = n_abs // 128
                        if engname == 'scalar':
                            eng.activation(xin[j][:, c0:c0 + cw], ps[0:nr, 0:cw],
                                           AF.Copy).then_inc(S['ev'], 1)
                            if c0 == 512:
                                if j >= 1:
                                    eng.wait_ge(S['cv'], base_cv + j)
                                    eng.activation(xc32[j - 1][:, :],
                                                   ctap[(j - 1) % 2][:, :],
                                                   AF.Silu,
                                                   bias=convb_t[:, j - 1:j])
                                    eng.activation(xcbf[j - 1][:, :],
                                                   xc32[j - 1][:, :],
                                                   AF.Copy).then_inc(S['xs'], 1)
                        elif engname == 'vector':
                            if c0 == 512:
                                j = n_abs // 128
                                eng.wait_ge(S['ev'], ph.base_ev + gi + 1)
                                if j >= 2:
                                    eng.wait_ge(S['xs'], base_xs + j - 1)
                                eng.tensor_scalar_mul(
                                    ctap[j % 2][:, :], xin[j][:, 1:513],
                                    convw_t[:, 4 * j:4 * j + 1])
                                for k in range(1, 4):
                                    ins = eng.scalar_tensor_tensor(
                                        ctap[j % 2][:, :],
                                        xin[j][:, 1 + k:513 + k],
                                        convw_t[:, 4 * j + k:4 * j + k + 1],
                                        ctap[j % 2][:, :], ALU.mult, ALU.add)
                                ins.then_inc(S['cv'], 1)
                    else:
                        jr = (n_abs - DI) // 128
                        if engname == 'scalar':
                            eng.activation(sres[jr][:, c0:c0 + cw],
                                           ps[0:nr, 0:cw],
                                           AF.Silu).then_inc(S['ev'], 1)
                            if gi == 32:
                                eng.wait_ge(S['cv'], base_cv + 16)
                                eng.activation(xc32[15][:, :], ctap[15 % 2][:, :],
                                               AF.Silu, bias=convb_t[:, 15:16])
                                eng.activation(xcbf[15][:, :], xc32[15][:, :],
                                               AF.Copy).then_inc(S['xs'], 1)
                return evac, ph_holder

            evac1, holder1 = mk_mm1_evac()
            mm1 = MMPhase(S, C, w_in, 1024, 4096, 516, lambda k: x_bf[k][:, :],
                          wpanel, psA, evac1, evac_engines=('scalar', 'vector'),
                          extra_tensor_waits=[('xl', xl_after)])
            holder1.append(mm1)
            C.bump('cv', 16)
            C.bump('xs', 16)
            PL.add(mm1)

            base_st2 = C.g('st')

            base_sm = [C.g('sm0'), C.g('sm1')]

            def mk_mm2_evac(base_sm=base_sm, c0d=c0d, ph_holder=[]):
                def evac(engname, eng, ps, n_abs, nr, c0, cw, gi):
                    j = n_abs // 128
                    if engname == 'vector':
                        par = gi % 2
                        eng.wait_ge(S['sm0' if par == 0 else 'sm1'],
                                    base_sm[par] + 16 * (gi // 2))
                        eng.tensor_mul(m1buf[gi % 2][:, 0:cw], ps[0:nr, 0:cw],
                                       sres[j][:, 4:4 + cw]).then_inc(S['ev'], 1)
                    elif engname == 'sync':
                        eng.wait_ge(S['ev'], ph_holder[0].base_ev + gi + 1)
                        eng.dma_start(out=M1_o[j * 128:(j + 1) * 128,
                                               c0d:c0d + cw],
                                      in_=m1buf[gi % 2][:, 0:cw]
                                      ).then_inc(S['sm0' if gi % 2 == 0 else 'sm1'], 16)
                return evac, ph_holder

            evac2, holder2 = mk_mm2_evac()
            mm2 = MMPhase(S, C, wx_main, DI, DI, 512,
                          lambda k: xcbf[k][:, :], wpanel, psA, evac2,
                          evac_engines=('vector', 'sync'),
                          extra_tensor_waits=[('xs', C.g('xs'))])
            holder2.append(mm2)
            C.bump('sm0', 16 * 8)
            C.bump('sm1', 16 * 8)
            PL.add(mm2)

            def evac3(engname, eng, ps, n_abs, nr, c0, cw, gi):
                eng.activation(small32[0:32, 0:cw], ps[0:32, 0:cw], AF.Copy)
                eng.activation(small32_bf[0:32, 0:cw], ps[0:32, 0:cw],
                               AF.Copy).then_inc(S['ev'], 1)

            mm3 = MMPhase(S, C, wx_small, DI, 32, 512,
                          lambda k: xcbf[k][:, :], wpanel, psA, evac3,
                          evac_engines=('scalar',))
            PL.add(mm3)

            base_dv = C.g('dv')
            sb_n = C.g('sb')

            def sb_t(t, after_ev=C.g('ev'), base_dv=base_dv, sb_n=sb_n):
                t.wait_ge(S['ev'], after_ev)
                t.wait_ge(S['dv'], base_dv)
                t.matmul(ps_sb[0:1, 0:512], selb_bf[0:32, 0:1],
                         small32_bf[0:32, 0:512], start=True, stop=True
                         ).then_inc(S['sb'], 1)
                t.wait_ge(S['sbr'], sb_n + 1)
                t.matmul(ps_sbc[:, 0:512], ones1[0:1, 0:128],
                         sb_row[0:1, 0:512], start=True, stop=True
                         ).then_inc(S['sbc'], 1)

            def sb_s(s, sb_n=sb_n):
                s.wait_ge(S['sb'], sb_n + 1)
                s.activation(sb_row[0:1, 0:512], ps_sb[0:1, 0:512],
                             AF.Copy).then_inc(S['sbr'], 1)
            C.bump('sb', 1)
            C.bump('sbr', 1)
            C.bump('sbc', 1)
            PL.add(RawPhase({'tensor': sb_t, 'scalar': sb_s}))

            base_dv4 = C.g('dv')
            base_st4 = C.g('st')
            sbc_n = C.g('sbc')

            def mk_mm4_evac(base_dv4=base_dv4, base_st4=base_st4, c0d=c0d,
                            sbc_n=sbc_n, ph_holder=[]):
                def evac(engname, eng, ps, n_abs, nr, c0, cw, gi):
                    j = gi
                    if engname == 'scalar':
                        eng.wait_ge(S['dv'], base_dv4 + max(0, j - 1))
                        eng.activation(tA[j % 2][:, 0:cw], ps[0:nr, 0:cw],
                                       AF.Abs, bias=bdt_t[:, j:j + 1])
                        eng.activation(tB[j % 2][:, 0:cw], tA[j % 2][:, 0:cw],
                                       AF.Exp, bias=0.0, scale=-1.0)
                        eng.activation(tA[j % 2][:, 0:cw], tB[j % 2][:, 0:cw],
                                       AF.Ln, bias=1.0)
                        eng.activation(tB[j % 2][:, 0:cw], ps[0:nr, 0:cw],
                                       AF.Relu, bias=bdt_t[:, j:j + 1]
                                       ).then_inc(S['ev'], 1)
                    elif engname == 'vector':
                        eng.wait_ge(S['ev'], ph_holder[0].base_ev + gi + 1)
                        eng.wait_ge(S['sbc'], sbc_n)
                        eng.wait_ge(S['st'], base_st4 + 32 * max(0, j - 1))
                        eng.tensor_add(dbuf[j % 2][:, 0:cw], tA[j % 2][:, 0:cw],
                                       tB[j % 2][:, 0:cw])
                        eng.tensor_mul(ctbuf[j % 2][:, 0:cw], xc32[j][:, 0:cw],
                                       ps_sbc[:, 0:cw])
                        eng.tensor_tensor_scan(
                            gbuf[j % 2][:, 0:cw], dbuf[j % 2][:, 0:cw],
                            ctbuf[j % 2][:, 0:cw], gcarry[:, j:j + 1],
                            ALU.mult, ALU.add)
                        eng.tensor_tensor_scan(
                            pbuf[j % 2][:, 0:cw], dbuf[j % 2][:, 0:cw],
                            zeros512[:, 0:cw], pcarry[:, j:j + 1],
                            ALU.mult, ALU.add)
                        eng.tensor_copy(gcarry[:, j:j + 1],
                                        gbuf[j % 2][:, cw - 1:cw])
                        eng.tensor_copy(pcarry[:, j:j + 1],
                                        pbuf[j % 2][:, cw - 1:cw]
                                        ).then_inc(S['dv'], 1)
                    elif engname == 'sync':
                        eng.wait_ge(S['dv'], base_dv4 + j + 1)
                        eng.dma_start(out=G_o[j * 128:(j + 1) * 128,
                                              c0d:c0d + cw],
                                      in_=gbuf[j % 2][:, 0:cw]
                                      ).then_inc(S['st'], 16)
                        eng.dma_start(out=P_o[j * 128:(j + 1) * 128,
                                              c0d:c0d + cw],
                                      in_=pbuf[j % 2][:, 0:cw]
                                      ).then_inc(S['st'], 16)
                return evac, ph_holder

            evac4, holder4 = mk_mm4_evac()
            mm4 = MMPhase(S, C, w_dt, DS, DI, 512,
                          lambda k: small32_bf[0:16, :], wpanel, psA, evac4,
                          evac_engines=('scalar', 'vector', 'sync'))
            holder4.append(mm4)
            C.bump('dv', 16)
            C.bump('st', 32 * 16)
            PL.add(mm4)

        st_total = C.g('st')
        sm0_t, sm1_t = C.g('sm0'), C.g('sm1')

        def fin(s):
            s.wait_ge(S['st'], st_total)
            s.wait_ge(S['sm0'], sm0_t)
            s.wait_ge(S['sm1'], sm1_t)
        PL.add(RawPhase({'sync': fin}))
        io['_a1'] = dict(gcarry=gcarry)


def build_A1():
    nc = bass.Bass("TRN2", target_bir_lowering=False)
    io = {}
    for nm, shp, knd in (('xw', [D, TLOC + 4], 'ExternalInput'),
                         ('w_in', [D, 2 * DI], 'ExternalInput'),
                         ('wx_main', [DI, DI], 'ExternalInput'),
                         ('wx_small', [DI, 32], 'ExternalInput'),
                         ('w_dt', [DS, DI], 'ExternalInput'),
                         ('convw', [128, 64], 'ExternalInput'),
                         ('convb', [128, 16], 'ExternalInput'),
                         ('bdt', [128, 16], 'ExternalInput'),
                         ('selb', [32, 1], 'ExternalInput'),
                         ('G', [DI, TLOC], 'ExternalOutput'),
                         ('P', [DI, TLOC], 'ExternalOutput'),
                         ('M1', [DI, TLOC], 'ExternalOutput')):
        io[nm] = nc.dram_tensor(nm, shp, F32, kind=knd).ap()
    S = {}
    C = Cnt()
    PL = PhaseList()
    with ExitStack() as cs:
        e = cs.enter_context
        for sname in A1_SEMS:
            S[sname] = e(nc.semaphore(sname))
        _emit_A1(nc, e, S, C, PL, io)
        block = e(nc.Block())
        PL.emit_all(block)
    return nc


def host_prep_A1(inp):
    x = np.asarray(inp["L0_attn_x"], np.float32)
    w_in = f32c(inp["L0_attn_W_in"])
    wx = np.asarray(inp["L0_attn_W_x"], np.float32)
    wx_main = f32c(wx[:, 32:2080])
    wx_small = f32c(np.concatenate([wx[:, 0:16], wx[:, 16:32]], axis=1))
    w_dt = f32c(inp["L0_attn_W_dt"])
    convw = f32c(np.asarray(inp["L0_attn_conv_w"], np.float32)
                 .reshape(DI, 4).reshape(16, 128, 4)
                 .transpose(1, 0, 2).reshape(128, 64))
    convb = cols128(inp["L0_attn_conv_b"])
    bdt = cols128(inp["L0_attn_b_dt"])
    selb = np.zeros((32, 1), np.float32)
    selb[16:32, 0] = 1.0
    maps = []
    for c in range(NCORES):
        b, h = c // 2, c % 2
        t0 = h * TLOC
        xw = np.zeros((D, TLOC + 4), np.float32)
        lo = max(0, t0 - 4)
        xw[:, 4 - (t0 - lo):] = x[b, lo:t0 + TLOC, :].T
        maps.append(dict(xw=f32c(xw), w_in=w_in, wx_main=wx_main,
                         wx_small=wx_small, w_dt=w_dt, convw=convw,
                         convb=convb, bdt=bdt, selb=selb))
    return maps


A2_SEMS = ('wl0', 'wl1', 'mm', 'ev', 'xl0', 'xl1', 'xf', 'cl',
           'yv', 'sv', 'st', 'sq', 'squ', 'lnm', 'lnv', 'lns', 'lnb', 'lne',
           'cc1')


def _emit_A2(nc, e, S, C, PL, io):
    G_i = io['G']; P_i = io['P']; M1_i = io['M1']
    xT_d = io['xT']; wout_d = io['wout']; ln1_o = io['ln1']
    ccol_d = io.get('ccol')
    debug = io.get('debug', False)
    if debug:
        y2_dbg = io['y2dbg']; s_dbg = io['sdbg']; var_dbg = io['vardbg']
    if True:
        grot = [e(nc.sbuf_tensor(f"gr{i}", [128, 512], F32)) for i in range(2)]
        prot = [e(nc.sbuf_tensor(f"pr{i}", [128, 512], F32)) for i in range(2)]
        mrot = [e(nc.sbuf_tensor(f"mr{i}", [128, 512], F32)) for i in range(2)]
        y2t = [e(nc.sbuf_tensor(f"y2t{i}", [128, 512], F32)) for i in range(2)]
        y2bf = [e(nc.sbuf_tensor(f"y2bf{j}", [128, 512], BF16)) for j in range(16)]
        x32 = [e(nc.sbuf_tensor(f"x32{k}", [128, 512], F32)) for k in range(8)]
        s32 = [e(nc.sbuf_tensor(f"s32{k}", [128, 512], F32)) for k in range(8)]
        sbf = [e(nc.sbuf_tensor(f"sbf{k}", [128, 512], BF16)) for k in range(8)]
        lnout = [e(nc.sbuf_tensor(f"lno{k}", [128, 512], F32)) for k in range(8)]
        wpanel = [e(nc.sbuf_tensor(f"wp{i}", [128, 16 * 512], BF16)) for i in range(2)]
        ccol_t = e(nc.sbuf_tensor("ccolt", [128, 16], F32))
        cf_t = e(nc.sbuf_tensor("cft", [128, 1], F32))
        aux = dict(
            ones128_bf=e(nc.sbuf_tensor("o128", [128, 1], BF16)),
            ones1_f32=e(nc.sbuf_tensor("o1f", [1, 128], F32)),
            sq_bf=[e(nc.sbuf_tensor(f"sqbf{i}", [128, 512], BF16)) for i in range(2)],
            row_m=e(nc.sbuf_tensor("row_m", [1, 512], F32)),
            row_v=e(nc.sbuf_tensor("row_v", [1, 512], F32)),
            row_t=e(nc.sbuf_tensor("row_t", [1, 512], F32)),
            row_r=e(nc.sbuf_tensor("row_r", [1, 512], F32)),
            eps=e(nc.sbuf_tensor("cst_eps", [1, 1], F32)),
            neghalf=e(nc.sbuf_tensor("cst_nh", [1, 1], F32)),
            nrm=[e(nc.sbuf_tensor(f"nrm{i}", [128, 512], F32)) for i in range(2)],
            ps_m=e(nc.psum_tensor("psm", [1, 512], F32)),
            ps_q=e(nc.psum_tensor("psq", [1, 512], F32)),
            ps_mb=e(nc.psum_tensor("psmb", [128, 512], F32)),
            ps_rb=e(nc.psum_tensor("psrb", [128, 512], F32)),
        )
        psA = [e(nc.psum_tensor(f"psA{i}", [128, 512], F32)) for i in range(2)]

        base_cl = C.g('cl')
        if ccol_d is not None:
            def consts_g(g, base_cl=base_cl):
                g.dma_start(out=ccol_t[:, :], in_=ccol_d).then_inc(S['cl'], 16)

            def consts_v(v, base_cl=base_cl):
                v.memset(aux['ones1_f32'][0:1, :], 1.0)
                v.memset(aux['ones128_bf'][:, 0:1], 1.0)
                v.memset(aux['eps'][0:1, 0:1], 1e-5)
                v.memset(aux['neghalf'][0:1, 0:1], -0.5)
                v.wait_ge(S['cl'], base_cl + 16)
            C.bump('cl', 16)
            PL.add(RawPhase({'gpsimd': consts_g, 'vector': consts_v}))
        else:
            gout = io['gend_out']; cflag_d = io['cflag']
            cc1_n = io['cc1_n']

            def consts_g(g, base_cl=base_cl):
                g.dma_start(out=cf_t[:, :], in_=cflag_d).then_inc(S['cl'], 16)
                g.wait_ge(S['cc1'], cc1_n)
                g.dma_start(out=ccol_t[:, :], in_=gout[0:128, :]
                            ).then_inc(S['cl'], 16)

            def consts_v(v, base_cl=base_cl):
                v.memset(aux['ones1_f32'][0:1, :], 1.0)
                v.memset(aux['ones128_bf'][:, 0:1], 1.0)
                v.memset(aux['eps'][0:1, 0:1], 1e-5)
                v.memset(aux['neghalf'][0:1, 0:1], -0.5)
                v.wait_ge(S['cl'], base_cl + 32)
                v.tensor_scalar_mul(ccol_t[:, :], ccol_t[:, :], cf_t[:, 0:1]
                                    ).then_inc(S['cl'], 1)
            C.bump('cl', 33)
            PL.add(RawPhase({'gpsimd': consts_g, 'vector': consts_v}))

        for ci in range(4):
            c0d = ci * 512

            base_xl = [C.g('xl0'), C.g('xl1')]
            base_yv = C.g('yv')
            prev_mm = C.g('mm')
            prev_sv = C.g('sv')

            def loads(s, base_yv=base_yv, c0d=c0d, ci=ci, prev_mm=prev_mm,
                      prev_sv=prev_sv):
                if ci > 0:
                    s.wait_ge(S['sv'], prev_sv)
                for j in range(16):
                    if j >= 2:
                        s.wait_ge(S['yv'], base_yv + j - 1)
                    sem = S['xl0' if j % 2 == 0 else 'xl1']
                    s.dma_start(out=grot[j % 2][:, :],
                                in_=G_i[j * 128:(j + 1) * 128, c0d:c0d + 512]
                                ).then_inc(sem, 16)
                    s.dma_start(out=prot[j % 2][:, :],
                                in_=P_i[j * 128:(j + 1) * 128, c0d:c0d + 512]
                                ).then_inc(sem, 16)
                    s.dma_start(out=mrot[j % 2][:, :],
                                in_=M1_i[j * 128:(j + 1) * 128, c0d:c0d + 512]
                                ).then_inc(sem, 16)
                for k in range(8):
                    s.dma_start(out=x32[k][:, :],
                                in_=xT_d[k * 128:(k + 1) * 128, c0d:c0d + 512]
                                ).then_inc(S['xf'], 16)
            C.bump('xl0', 48 * 8)
            C.bump('xl1', 48 * 8)
            C.bump('xf', 16 * 8)
            PL.add(RawPhase({'sync': loads}))

            def y2ph(v, base_xl=base_xl, base_yv=base_yv, prev_mm=prev_mm,
                     ci=ci, cl_now=C.g('cl')):
                if ci == 0:
                    v.wait_ge(S['cl'], cl_now)
                if ci > 0:
                    v.wait_ge(S['mm'], prev_mm)
                for j in range(16):
                    par = j % 2
                    v.wait_ge(S['xl0' if par == 0 else 'xl1'],
                              base_xl[par] + 48 * (j // 2 + 1))
                    v.scalar_tensor_tensor(
                        y2t[j % 2][:, :], prot[j % 2][:, :],
                        ccol_t[:, j:j + 1], grot[j % 2][:, :],
                        ALU.mult, ALU.add)
                    v.tensor_mul(y2bf[j][:, :], y2t[j % 2][:, :],
                                 mrot[j % 2][:, :]).then_inc(S['yv'], 1)
            C.bump('yv', 16)
            PL.add(RawPhase({'vector': y2ph}))

            xl_all = C.g('xf')

            def mk_evac5(ph_holder=[], xl_all=xl_all):
                def evac(engname, eng, ps, n_abs, nr, c0, cw, gi):
                    k = n_abs // 128
                    if engname == 'vector':
                        if gi == 0:
                            eng.wait_ge(S['xf'], xl_all)
                        eng.tensor_add(s32[k][:, 0:cw], ps[0:nr, 0:cw],
                                       x32[k][:, 0:cw]).then_inc(S['ev'], 1)
                        eng.tensor_copy(sbf[k][:, 0:cw], s32[k][:, 0:cw]
                                        ).then_inc(S['sv'], 1)
                return evac, ph_holder

            evac5, holder5 = mk_evac5()
            mm5 = MMPhase(S, C, wout_d, DI, D, 512,
                          lambda k: y2bf[k][:, :], wpanel, psA, evac5,
                          evac_engines=('vector',),
                          extra_tensor_waits=[('yv', C.g('yv'))])
            holder5.append(mm5)
            C.bump('sv', 8)
            PL.add(mm5)

            base_st = C.g('st')
            ln = LNPhase(S, C, lambda k: sbf[k], lambda k: s32[k],
                         lambda k: lnout[k], 512, aux,
                         pre={'tensor': [('sv', C.g('sv'))],
                              'scalar': [('sv', C.g('sv'))],
                              'vector': [('st', base_st)]})
            PL.add(ln)

            def store(s, base=ln.base_lne, c0d=c0d):
                for k in range(8):
                    s.wait_ge(S['lne'], base + k + 1)
                    s.dma_start(out=ln1_o[k * 128:(k + 1) * 128, c0d:c0d + 512],
                                in_=lnout[k][:, :]).then_inc(S['st'], 16)
            C.bump('st', 16 * 8)
            PL.add(RawPhase({'sync': store}))

            if debug:
                def dstore(s, c0d=c0d, lne_end=C.g('lne')):
                    s.wait_ge(S['lne'], lne_end)
                    for j in range(16):
                        s.dma_start(out=y2_dbg[j * 128:(j + 1) * 128,
                                               c0d:c0d + 512],
                                    in_=y2bf[j][:, :]).then_inc(S['st'], 16)
                    for k in range(8):
                        s.dma_start(out=s_dbg[k * 128:(k + 1) * 128,
                                              c0d:c0d + 512],
                                    in_=s32[k][:, :]).then_inc(S['st'], 16)
                    s.dma_start(out=var_dbg[0:1, c0d:c0d + 512],
                                in_=aux['row_m'][0:1, :]).then_inc(S['st'], 16)
                    s.dma_start(out=var_dbg[1:2, c0d:c0d + 512],
                                in_=aux['row_v'][0:1, :]).then_inc(S['st'], 16)
                    s.dma_start(out=var_dbg[2:3, c0d:c0d + 512],
                                in_=aux['row_r'][0:1, :]).then_inc(S['st'], 16)
                C.bump('st', 16 * 27)
                PL.add(RawPhase({'sync': dstore}))

        st_total = C.g('st')
        PL.add(RawPhase({'sync': lambda s: s.wait_ge(S['st'], st_total)}))
        io['_a2'] = dict(lnout=lnout)


def build_A2(debug=False):
    nc = bass.Bass("TRN2", target_bir_lowering=False)
    io = {'debug': debug}
    for nm, shp, knd in (('G', [DI, TLOC], 'ExternalInput'),
                         ('P', [DI, TLOC], 'ExternalInput'),
                         ('M1', [DI, TLOC], 'ExternalInput'),
                         ('ccol', [128, 16], 'ExternalInput'),
                         ('xT', [D, TLOC], 'ExternalInput'),
                         ('wout', [DI, D], 'ExternalInput'),
                         ('ln1', [D, TLOC], 'ExternalOutput')):
        io[nm] = nc.dram_tensor(nm, shp, F32, kind=knd).ap()
    if debug:
        io['y2dbg'] = nc.dram_tensor("y2dbg", [DI, TLOC], BF16, kind="ExternalOutput").ap()
        io['sdbg'] = nc.dram_tensor("sdbg", [D, TLOC], F32, kind="ExternalOutput").ap()
        io['vardbg'] = nc.dram_tensor("vardbg", [4, TLOC], F32, kind="ExternalOutput").ap()
    S = {}
    C = Cnt()
    PL = PhaseList()
    with ExitStack() as cs:
        e = cs.enter_context
        for sname in A2_SEMS:
            S[sname] = e(nc.semaphore(sname))
        _emit_A2(nc, e, S, C, PL, io)
        block = e(nc.Block())
        PL.emit_all(block)
    return nc


def host_prep_A2(inp, a1_results):
    x = np.asarray(inp["L0_attn_x"], np.float32)
    wout = f32c(inp["L0_attn_W_out"])
    maps = []
    for c in range(NCORES):
        b, h = c // 2, c % 2
        t0 = h * TLOC
        if h == 0:
            ccol = np.zeros((128, 16), np.float32)
        else:
            ccol = cols128(a1_results[c - 1]["G"][:, -1])
        maps.append(dict(G=a1_results[c]["G"], P=a1_results[c]["P"],
                         M1=a1_results[c]["M1"], ccol=ccol,
                         xT=f32c(x[b, t0:t0 + TLOC, :].T), wout=wout))
    return maps


B_SEMS = ('wl0', 'wl1', 'mm', 'ev', 'xl', 'cl', 'lp', 'sv', 'st',
          'sq', 'squ', 'lnm', 'lnv', 'lns', 'lnb', 'lne', 'cc2', 'tp')


def _emit_B(nc, e, S, C, PL, io):
    ln1_i = io['ln1']; mur_d = io['mur']; muk_d = io['muk']
    wrT_d = io['wrT']; wkT_d = io['wkT']; wvT_d = io['wvT']
    ln2_o = io['ln2']
    pcol_d = io.get('pcol')
    if True:
        x32 = [e(nc.sbuf_tensor(f"x32{k}", [128, 512], F32)) for k in range(8)]
        xp = [e(nc.sbuf_tensor(f"xp{k}", [128, 512], F32)) for k in range(8)]
        dt_t = [e(nc.sbuf_tensor(f"dt{i}", [128, 512], F32)) for i in range(2)]
        xr32 = [e(nc.sbuf_tensor(f"xr32{i}", [128, 512], F32)) for i in range(2)]
        xrbf = [e(nc.sbuf_tensor(f"xrbf{k}", [128, 512], BF16)) for k in range(8)]
        xkbf = [e(nc.sbuf_tensor(f"xkbf{k}", [128, 512], BF16)) for k in range(8)]
        r32 = [e(nc.sbuf_tensor(f"r32{k}", [128, 512], F32)) for k in range(8)]
        kabf = [e(nc.sbuf_tensor(f"kabf{k}", [128, 512], BF16)) for k in range(8)]
        t32 = [e(nc.sbuf_tensor(f"t32{i}", [128, 512], F32)) for i in range(2)]
        t32v = [e(nc.sbuf_tensor(f"t32v{i}", [128, 512], F32)) for i in range(2)]
        s32 = [e(nc.sbuf_tensor(f"s32{k}", [128, 512], F32)) for k in range(8)]
        sbf = [e(nc.sbuf_tensor(f"sbf{k}", [128, 512], BF16)) for k in range(8)]
        lnout = [e(nc.sbuf_tensor(f"lno{k}", [128, 512], F32)) for k in range(8)]
        wpanel = [e(nc.sbuf_tensor(f"wp{i}", [128, 8 * 512], BF16)) for i in range(2)]
        pcol_t = e(nc.sbuf_tensor("pcolt", [128, 8], F32))
        cf_t = e(nc.sbuf_tensor("cft", [128, 1], F32))
        mur_t = e(nc.sbuf_tensor("murt", [128, 8], F32))
        muk_t = e(nc.sbuf_tensor("mukt", [128, 8], F32))
        aux = dict(
            ones128_bf=e(nc.sbuf_tensor("o128", [128, 1], BF16)),
            ones1_f32=e(nc.sbuf_tensor("o1f", [1, 128], F32)),
            sq_bf=[e(nc.sbuf_tensor(f"sqbf{i}", [128, 512], BF16)) for i in range(2)],
            row_m=e(nc.sbuf_tensor("row_m", [1, 512], F32)),
            row_v=e(nc.sbuf_tensor("row_v", [1, 512], F32)),
            row_t=e(nc.sbuf_tensor("row_t", [1, 512], F32)),
            row_r=e(nc.sbuf_tensor("row_r", [1, 512], F32)),
            eps=e(nc.sbuf_tensor("cst_eps", [1, 1], F32)),
            neghalf=e(nc.sbuf_tensor("cst_nh", [1, 1], F32)),
            nrm=[e(nc.sbuf_tensor(f"nrm{i}", [128, 512], F32)) for i in range(2)],
            ps_m=e(nc.psum_tensor("psm", [1, 512], F32)),
            ps_q=e(nc.psum_tensor("psq", [1, 512], F32)),
            ps_mb=e(nc.psum_tensor("psmb", [128, 512], F32)),
            ps_rb=e(nc.psum_tensor("psrb", [128, 512], F32)),
        )
        psA = [e(nc.psum_tensor(f"psA{i}", [128, 512], F32)) for i in range(2)]

        base_cl = C.g('cl')
        if pcol_d is not None:
            def consts_g(g, base_cl=base_cl):
                g.dma_start(out=pcol_t[:, :], in_=pcol_d).then_inc(S['cl'], 16)
                g.dma_start(out=mur_t[:, :], in_=mur_d).then_inc(S['cl'], 16)
                g.dma_start(out=muk_t[:, :], in_=muk_d).then_inc(S['cl'], 16)

            def consts_v(v, base_cl=base_cl):
                v.memset(aux['ones1_f32'][0:1, :], 1.0)
                v.memset(aux['ones128_bf'][:, 0:1], 1.0)
                v.memset(aux['eps'][0:1, 0:1], 1e-5)
                v.memset(aux['neghalf'][0:1, 0:1], -0.5)
                v.wait_ge(S['cl'], base_cl + 48)
            C.bump('cl', 48)
            PL.add(RawPhase({'gpsimd': consts_g, 'vector': consts_v}))
        else:
            lout = io['l1col_out']; cflag_d = io['cflag']
            cc2_n = io['cc2_n']

            def consts_g(g, base_cl=base_cl):
                g.dma_start(out=cf_t[:, :], in_=cflag_d).then_inc(S['cl'], 16)
                g.dma_start(out=mur_t[:, :], in_=mur_d).then_inc(S['cl'], 16)
                g.dma_start(out=muk_t[:, :], in_=muk_d).then_inc(S['cl'], 16)
                g.wait_ge(S['cc2'], cc2_n)
                g.dma_start(out=pcol_t[:, :], in_=lout[0:128, :]
                            ).then_inc(S['cl'], 16)

            def consts_v(v, base_cl=base_cl):
                v.memset(aux['ones1_f32'][0:1, :], 1.0)
                v.memset(aux['ones128_bf'][:, 0:1], 1.0)
                v.memset(aux['eps'][0:1, 0:1], 1e-5)
                v.memset(aux['neghalf'][0:1, 0:1], -0.5)
                v.wait_ge(S['cl'], base_cl + 64)
                v.tensor_scalar_mul(pcol_t[:, :], pcol_t[:, :], cf_t[:, 0:1]
                                    ).then_inc(S['cl'], 1)
            C.bump('cl', 65)
            PL.add(RawPhase({'gpsimd': consts_g, 'vector': consts_v}))

        for ci in range(4):
            c0d = ci * 512

            base_xl = C.g('xl')
            prev_mm = C.g('mm')

            prev_sv = C.g('sv')
            prev_lne = C.g('lne')

            def loads(s, c0d=c0d, ci=ci, prev_mm=prev_mm, prev_sv=prev_sv,
                      prev_lne=prev_lne):
                if ci > 0:
                    s.wait_ge(S['mm'], prev_mm)
                    s.wait_ge(S['sv'], prev_sv)
                    s.wait_ge(S['lne'], prev_lne)
                for k in range(8):
                    s.dma_start(out=x32[k][:, :],
                                in_=ln1_i[k * 128:(k + 1) * 128, c0d:c0d + 512]
                                ).then_inc(S['xl'], 16)
            C.bump('xl', 16 * 8)
            PL.add(RawPhase({'sync': loads}))

            def lerp(v, base_xl=base_xl, ci=ci, xl_all=C.g('xl'),
                     cl_now=C.g('cl')):
                if ci == 0:
                    v.wait_ge(S['cl'], cl_now)
                v.wait_ge(S['xl'], xl_all)
                for k in range(8):
                    v.tensor_copy(xp[k][:, 1:512], x32[k][:, 0:511])
                    v.tensor_copy(xp[k][:, 0:1], pcol_t[:, k:k + 1])
                    v.tensor_copy(pcol_t[:, k:k + 1], x32[k][:, 511:512])
                    v.tensor_sub(dt_t[k % 2][:, :], x32[k][:, :], xp[k][:, :])
                    v.scalar_tensor_tensor(xr32[k % 2][:, :], dt_t[k % 2][:, :],
                                           mur_t[:, k:k + 1], xp[k][:, :],
                                           ALU.mult, ALU.add)
                    v.tensor_copy(xrbf[k][:, :], xr32[k % 2][:, :])
                    v.scalar_tensor_tensor(xr32[k % 2][:, :], dt_t[k % 2][:, :],
                                           muk_t[:, k:k + 1], xp[k][:, :],
                                           ALU.mult, ALU.add)
                    v.tensor_copy(xkbf[k][:, :], xr32[k % 2][:, :]
                                  ).then_inc(S['lp'], 1)
            C.bump('lp', 8)
            PL.add(RawPhase({'vector': lerp}))

            def evac_r(engname, eng, ps, n_abs, nr, c0, cw, gi):
                eng.activation(r32[n_abs // 128][:, 0:cw], ps[0:nr, 0:cw],
                               AF.Sigmoid).then_inc(S['ev'], 1)

            mmr = MMPhase(S, C, wrT_d, D, D, 512, lambda k: xrbf[k][:, :],
                          wpanel, psA, evac_r, evac_engines=('scalar',),
                          extra_tensor_waits=[('lp', C.g('lp'))])
            PL.add(mmr)

            def evac_k(engname, eng, ps, n_abs, nr, c0, cw, gi):
                k = n_abs // 128
                eng.activation(t32[gi % 2][:, 0:cw], ps[0:nr, 0:cw], AF.Relu)
                eng.activation(kabf[k][:, 0:cw], t32[gi % 2][:, 0:cw],
                               AF.Square).then_inc(S['ev'], 1)

            mmk = MMPhase(S, C, wkT_d, D, D, 512, lambda k: xkbf[k][:, :],
                          wpanel, psA, evac_k, evac_engines=('scalar',))
            PL.add(mmk)

            def evac_v(engname, eng, ps, n_abs, nr, c0, cw, gi):
                k = n_abs // 128
                eng.tensor_mul(t32v[gi % 2][:, 0:cw], ps[0:nr, 0:cw],
                               r32[k][:, 0:cw]).then_inc(S['ev'], 1)
                eng.tensor_add(s32[k][:, 0:cw], t32v[gi % 2][:, 0:cw],
                               x32[k][:, 0:cw])
                eng.tensor_copy(sbf[k][:, 0:cw], s32[k][:, 0:cw]
                                ).then_inc(S['sv'], 1)

            mmv = MMPhase(S, C, wvT_d, D, D, 512, lambda k: kabf[k][:, :],
                          wpanel, psA, evac_v, evac_engines=('vector',))
            C.bump('sv', 8)
            PL.add(mmv)

            base_st = C.g('st')
            ln = LNPhase(S, C, lambda k: sbf[k], lambda k: s32[k],
                         lambda k: lnout[k], 512, aux,
                         pre={'tensor': [('sv', C.g('sv'))],
                              'scalar': [('sv', C.g('sv'))],
                              'vector': [('st', base_st)]})
            PL.add(ln)

            def store(s, base=ln.base_lne, c0d=c0d):
                for k in range(8):
                    s.wait_ge(S['lne'], base + k + 1)
                    s.dma_start(out=ln2_o[k * 128:(k + 1) * 128, c0d:c0d + 512],
                                in_=lnout[k][:, :]).then_inc(S['st'], 16)
            C.bump('st', 16 * 8)
            PL.add(RawPhase({'sync': store}))

        st_total = C.g('st')
        PL.add(RawPhase({'sync': lambda s: s.wait_ge(S['st'], st_total)}))
        io['_b'] = dict(lnout=lnout)


def build_B():
    nc = bass.Bass("TRN2", target_bir_lowering=False)
    io = {}
    for nm, shp, knd in (('ln1', [D, TLOC], 'ExternalInput'),
                         ('pcol', [128, 8], 'ExternalInput'),
                         ('mur', [128, 8], 'ExternalInput'),
                         ('muk', [128, 8], 'ExternalInput'),
                         ('wrT', [D, D], 'ExternalInput'),
                         ('wkT', [D, D], 'ExternalInput'),
                         ('wvT', [D, D], 'ExternalInput'),
                         ('ln2', [D, TLOC], 'ExternalOutput')):
        io[nm] = nc.dram_tensor(nm, shp, F32, kind=knd).ap()
    S = {}
    C = Cnt()
    PL = PhaseList()
    with ExitStack() as cs:
        e = cs.enter_context
        for sname in B_SEMS:
            S[sname] = e(nc.semaphore(sname))
        _emit_B(nc, e, S, C, PL, io)
        block = e(nc.Block())
        PL.emit_all(block)
    return nc


def host_prep_B(inp, a2_results):
    wrT = f32c(np.asarray(inp["L0_ffn_W_r"], np.float32).T)
    wkT = f32c(np.asarray(inp["L0_ffn_W_k"], np.float32).T)
    wvT = f32c(np.asarray(inp["L0_ffn_W_v"], np.float32).T)
    mur = cols128(inp["L0_ffn_mu_r"])
    muk = cols128(inp["L0_ffn_mu_k"])
    maps = []
    for c in range(NCORES):
        h = c % 2
        if h == 0:
            pcol = np.zeros((128, 8), np.float32)
        else:
            pcol = cols128(a2_results[c - 1]["ln1"][:, -1])
        maps.append(dict(ln1=a2_results[c]["ln1"], pcol=pcol, mur=mur,
                         muk=muk, wrT=wrT, wkT=wkT, wvT=wvT))
    return maps


C_SEMS = ('wl0', 'wl1', 'mm', 'ev', 'xl', 'cl', 'cv', 'xs',
          'dv', 'st', 'sd0', 'sd1', 'su0', 'su1', 'sz0', 'sz1', 'cc3', 'gx',
          'gv')


def _emit_C(nc, e, S, C, PL, io):
    xT_d = io.get('xT')
    winT_d = io['winT']; wxT_d = io['wxT']; wdtT_d = io['wdtT']
    convw_d = io['convw']; convb_d = io['convb']; bdt_d = io['bdt']
    u_o = io['u']; sz_o = io['sz']; delta_o = io['delta']; bc_o = io['bc']
    if True:
        x_bf = [e(nc.sbuf_tensor(f"xbf{k}", [128, 516], BF16)) for k in range(8)]
        upre = [e(nc.sbuf_tensor(f"up{j}", [128, 516], F32)) for j in range(16)]
        zbuf = [e(nc.sbuf_tensor(f"zb{i}", [128, 516], F32)) for i in range(2)]
        u32 = [e(nc.sbuf_tensor(f"u32{i}", [128, 512], F32)) for i in range(2)]
        u_bf = [e(nc.sbuf_tensor(f"ubf{j}", [128, 512], BF16)) for j in range(16)]
        ctap = [e(nc.sbuf_tensor(f"ctap{i}", [128, 512], F32)) for i in range(2)]
        small96 = e(nc.sbuf_tensor("s96", [128, 512], F32))
        small96_bf = e(nc.sbuf_tensor("s96bf", [128, 512], BF16))
        tA = [e(nc.sbuf_tensor(f"tA{i}", [128, 512], F32)) for i in range(2)]
        tB = [e(nc.sbuf_tensor(f"tB{i}", [128, 512], F32)) for i in range(2)]
        dbuf = [e(nc.sbuf_tensor(f"dbuf{i}", [128, 512], F32)) for i in range(2)]
        wpanel = [e(nc.sbuf_tensor(f"wp{i}", [128, 16 * 512], BF16)) for i in range(2)]
        convw_t = e(nc.sbuf_tensor("convwt", [128, 64], F32))
        convb_t = e(nc.sbuf_tensor("convbt", [128, 16], F32))
        bdt_t = e(nc.sbuf_tensor("bdtt", [128, 16], F32))
        psA = [e(nc.psum_tensor(f"psA{i}", [128, 512], F32)) for i in range(2)]
        fused_x = io.get('xT') is None
        if fused_x:
            tokt = [e(nc.sbuf_tensor(f"tok{i}", [128, 1024], F32))
                    for i in range(5)]
            idxc_t = e(nc.sbuf_tensor("idxct", [128, 36], I32))
            ident_t = e(nc.sbuf_tensor("identt", [128, 128], F32))
            ps_tr = [e(nc.psum_tensor(f"pstr{i}", [128, 128], F32))
                     for i in range(2)]

        base_cl0 = C.g('cl')

        def consts_g(g, base_cl0=base_cl0):
            g.dma_start(out=convw_t[:, :], in_=convw_d).then_inc(S['cl'], 16)
            g.dma_start(out=convb_t[:, :], in_=convb_d).then_inc(S['cl'], 16)
            g.dma_start(out=bdt_t[:, :], in_=bdt_d).then_inc(S['cl'], 16)
            if fused_x:
                g.dma_start(out=idxc_t[:, 0:33], in_=io['idxC']
                            ).then_inc(S['cl'], 16)
                g.dma_start(out=ident_t[:, :], in_=io['ident']
                            ).then_inc(S['cl'], 16)

        n_cl = 80 if fused_x else 48

        def consts_w(eng, base_cl0=base_cl0, n_cl=n_cl):
            eng.wait_ge(S['cl'], base_cl0 + n_cl)
        C.bump('cl', n_cl)
        io['_c_cl'] = C.g('cl')
        PL.add(RawPhase({'gpsimd': consts_g, 'vector': consts_w,
                         'scalar': consts_w, 'tensor': consts_w}))

        for ci in range(8):
            c0d = ci * 512

            prev_mm = C.g('mm')
            if xT_d is not None:
                def loadx(g, ci=ci, prev_mm=prev_mm):
                    if ci > 0:
                        g.wait_ge(S['mm'], prev_mm)
                    for k in range(8):
                        g.dma_start(out=x_bf[k][:, :],
                                    in_=xT_d[k * 128:(k + 1) * 128,
                                             ci * 512:ci * 512 + 516]
                                    ).then_inc(S['xl'], 16)
                C.bump('xl', 128)
                xl_after = C.g('xl')
                PL.add(RawPhase({'gpsimd': loadx}))
            else:
                src = io['ln2_all']; cc3_n = io['cc3_n']
                base_gx = C.g('gx')
                base_gv = C.g('gv')
                base_xl = C.g('xl')

                def loadg(g, ci=ci, base_gx=base_gx):
                    if ci == 0:
                        g.wait_ge(S['cc3'], cc3_n)
                        g.wait_ge(S['cl'], io['_c_cl'])
                    if ci > 0:
                        g.wait_ge(S['gx'], base_gx)
                    for gI in range(4):
                        g.indirect_dma_start(
                            out=tokt[gI][:, :], out_offset=None, in_=src,
                            in_offset=bass.IndirectOffsetOnAxis(
                                idxc_t[:, 4 * ci + gI:4 * ci + gI + 1], 0),
                        ).then_inc(S['xl'], 16)
                    g.indirect_dma_start(
                        out=tokt[4][0:4, :], out_offset=None, in_=src,
                        in_offset=bass.IndirectOffsetOnAxis(
                            idxc_t[0:4, 4 * ci + 4:4 * ci + 5], 0),
                    ).then_inc(S['xl'], 16)

                def trt(t, ci=ci, base_xl=base_xl, base_gv=base_gv,
                        base_gx=base_gx):
                    t.wait_ge(S['xl'], base_xl + 80)
                    i = 0
                    for gI in range(5):
                        for k in range(8):
                            if i >= 2:
                                t.wait_ge(S['gv'], base_gv + i - 1)
                            if gI < 4:
                                t.transpose(ps_tr[i % 2][:, 0:128],
                                            tokt[gI][:, k * 128:(k + 1) * 128],
                                            ident_t[:, :]).then_inc(S['gx'], 1)
                            else:
                                t.transpose(ps_tr[i % 2][:, 0:4],
                                            tokt[4][0:4, k * 128:(k + 1) * 128],
                                            ident_t[0:4, 0:4]).then_inc(S['gx'], 1)
                            i += 1

                def tre(sc, ci=ci, base_gx=base_gx, base_gv=base_gv):
                    i = 0
                    for gI in range(5):
                        for k in range(8):
                            sc.wait_ge(S['gx'], base_gx + i + 1)
                            if gI < 4:
                                sc.activation(
                                    x_bf[k][:, gI * 128:(gI + 1) * 128],
                                    ps_tr[i % 2][:, 0:128],
                                    AF.Copy).then_inc(S['gv'], 1)
                            else:
                                sc.activation(x_bf[k][:, 512:516],
                                              ps_tr[i % 2][:, 0:4],
                                              AF.Copy).then_inc(S['gv'], 1)
                            i += 1
                C.bump('xl', 80)
                C.bump('gx', 40)
                C.bump('gv', 40)
                xl_after = C.g('xl')
                PL.add(RawPhase({'gpsimd': loadg, 'tensor': trt,
                                 'scalar': tre}))

            base_cv = C.g('cv')
            base_xs = C.g('xs')
            base_su = [C.g('su0'), C.g('su1')]
            base_sz = [C.g('sz0'), C.g('sz1')]

            def mk_evwin(base_cv=base_cv, base_xs=base_xs, base_su=base_su,
                         base_sz=base_sz, c0d=c0d, ph_holder=[]):
                def evac(engname, eng, ps, n_abs, nr, c0, cw, gi):
                    ph = ph_holder[0]
                    if n_abs < DI:
                        j = n_abs // 128
                        if engname == 'scalar':
                            eng.activation(upre[j][:, c0:c0 + cw],
                                           ps[0:nr, 0:cw],
                                           AF.Copy).then_inc(S['ev'], 1)
                            if c0 == 512:
                                if j >= 1:
                                    jj = j - 1
                                    eng.wait_ge(S['cv'], base_cv + j)
                                    par = jj % 2
                                    eng.wait_ge(S['su0' if par == 0 else 'su1'],
                                                base_su[par] + 16 * (jj // 2))
                                    eng.activation(u32[jj % 2][:, :],
                                                   ctap[jj % 2][:, :], AF.Silu,
                                                   bias=convb_t[:, jj:jj + 1])
                                    eng.activation(u_bf[jj][:, :],
                                                   u32[jj % 2][:, :],
                                                   AF.Copy).then_inc(S['xs'], 1)
                        elif engname == 'vector':
                            if c0 == 512:
                                j = n_abs // 128
                                eng.wait_ge(S['ev'], ph.base_ev + gi + 1)
                                if j >= 2:
                                    eng.wait_ge(S['xs'], base_xs + j - 1)
                                eng.tensor_scalar_mul(
                                    ctap[j % 2][:, :], upre[j][:, 1:513],
                                    convw_t[:, 4 * j:4 * j + 1])
                                for k in range(1, 4):
                                    ins = eng.scalar_tensor_tensor(
                                        ctap[j % 2][:, :],
                                        upre[j][:, 1 + k:513 + k],
                                        convw_t[:, 4 * j + k:4 * j + k + 1],
                                        ctap[j % 2][:, :], ALU.mult, ALU.add)
                                ins.then_inc(S['cv'], 1)
                        elif engname == 'sync':
                            if c0 == 512:
                                j = n_abs // 128
                                if j >= 1:
                                    jj = j - 1
                                    eng.wait_ge(S['xs'], base_xs + jj + 1)
                                    eng.dma_start(
                                        out=u_o[jj * 128:(jj + 1) * 128,
                                                c0d:c0d + 512],
                                        in_=u32[jj % 2][:, :]
                                        ).then_inc(S['su0' if jj % 2 == 0
                                                   else 'su1'], 16)
                    else:
                        jr = (n_abs - DI) // 128
                        if engname == 'scalar':
                            if gi == 32:
                                eng.wait_ge(S['cv'], base_cv + 16)
                                eng.wait_ge(S['su1'], base_su[1] + 16 * 7)
                                eng.activation(u32[15 % 2][:, :],
                                               ctap[15 % 2][:, :], AF.Silu,
                                               bias=convb_t[:, 15:16])
                                eng.activation(u_bf[15][:, :],
                                               u32[15 % 2][:, :],
                                               AF.Copy).then_inc(S['xs'], 1)
                            if gi % 2 == 0:
                                par = jr % 2
                                eng.wait_ge(S['sz0' if par == 0 else 'sz1'],
                                            base_sz[par] + 16 * (jr // 2))
                            eng.activation(zbuf[jr % 2][:, c0:c0 + cw],
                                           ps[0:nr, 0:cw],
                                           AF.Silu).then_inc(S['ev'], 1)
                        elif engname == 'sync':
                            if c0 == 512:
                                eng.wait_ge(S['ev'], ph.base_ev + gi + 1)
                                eng.dma_start(
                                    out=sz_o[jr * 128:(jr + 1) * 128,
                                             c0d:c0d + 512],
                                    in_=zbuf[jr % 2][:, 4:516]
                                    ).then_inc(S['sz0' if jr % 2 == 0
                                               else 'sz1'], 16)
                            if gi == 63:
                                eng.wait_ge(S['xs'], base_xs + 16)
                                eng.dma_start(
                                    out=u_o[15 * 128:16 * 128, c0d:c0d + 512],
                                    in_=u32[15 % 2][:, :]
                                    ).then_inc(S['su1'], 16)
                return evac, ph_holder

            evwin, hwin = mk_evwin()
            win_tw = [('xl', xl_after)]
            if fused_x:
                win_tw.append(('gv', C.g('gv')))
            mmwin = MMPhase(S, C, winT_d, D, 2 * DI, 516,
                            lambda k: x_bf[k][:, :], wpanel, psA, evwin,
                            evac_engines=('scalar', 'vector', 'sync'),
                            extra_tensor_waits=win_tw)
            hwin.append(mmwin)
            C.bump('cv', 16)
            C.bump('xs', 16)
            C.bump('su0', 16 * 8)
            C.bump('su1', 16 * 8)
            C.bump('sz0', 16 * 8)
            C.bump('sz1', 16 * 8)
            PL.add(mmwin)

            def evwx(engname, eng, ps, n_abs, nr, c0, cw, gi):
                eng.activation(small96[0:128, 0:cw], ps[0:128, 0:cw], AF.Copy)
                eng.activation(small96_bf[0:128, 0:cw], ps[0:128, 0:cw],
                               AF.Copy).then_inc(S['ev'], 1)

            mmwx = MMPhase(S, C, wxT_d, DI, 128, 512,
                           lambda k: u_bf[k][:, :], wpanel, psA, evwx,
                           evac_engines=('scalar',),
                           extra_tensor_waits=[('xs', C.g('xs'))])
            PL.add(mmwx)

            ev_afterwx = C.g('ev')

            def store_bc(s, ev_afterwx=ev_afterwx, c0d=c0d):
                s.wait_ge(S['ev'], ev_afterwx)
                s.dma_start(out=bc_o[0:32, c0d:c0d + 512],
                            in_=small96[0:32, :]).then_inc(S['st'], 16)
            C.bump('st', 16)
            PL.add(RawPhase({'sync': store_bc}))

            base_dv4 = C.g('dv')
            base_sd = [C.g('sd0'), C.g('sd1')]

            def mk_evdt(base_dv4=base_dv4, base_sd=base_sd, c0d=c0d,
                        ph_holder=[]):
                def evac(engname, eng, ps, n_abs, nr, c0, cw, gi):
                    j = gi
                    if engname == 'scalar':
                        eng.wait_ge(S['dv'], base_dv4 + max(0, j - 1))
                        eng.activation(tA[j % 2][:, 0:cw], ps[0:nr, 0:cw],
                                       AF.Abs, bias=bdt_t[:, j:j + 1])
                        eng.activation(tB[j % 2][:, 0:cw], tA[j % 2][:, 0:cw],
                                       AF.Exp, bias=0.0, scale=-1.0)
                        eng.activation(tA[j % 2][:, 0:cw], tB[j % 2][:, 0:cw],
                                       AF.Ln, bias=1.0)
                        eng.activation(tB[j % 2][:, 0:cw], ps[0:nr, 0:cw],
                                       AF.Relu, bias=bdt_t[:, j:j + 1]
                                       ).then_inc(S['ev'], 1)
                    elif engname == 'vector':
                        eng.wait_ge(S['ev'], ph_holder[0].base_ev + gi + 1)
                        par = j % 2
                        eng.wait_ge(S['sd0' if par == 0 else 'sd1'],
                                    base_sd[par] + 16 * (j // 2))
                        eng.tensor_add(dbuf[j % 2][:, 0:cw], tA[j % 2][:, 0:cw],
                                       tB[j % 2][:, 0:cw]).then_inc(S['dv'], 1)
                    elif engname == 'sync':
                        eng.wait_ge(S['dv'], base_dv4 + j + 1)
                        eng.dma_start(out=delta_o[j * 128:(j + 1) * 128,
                                                  c0d:c0d + 512],
                                      in_=dbuf[j % 2][:, 0:cw]
                                      ).then_inc(S['sd0' if j % 2 == 0
                                                 else 'sd1'], 16)
                return evac, ph_holder

            evdt, hdt = mk_evdt()
            mmdt = MMPhase(S, C, wdtT_d, 64, DI, 512,
                           lambda k: small96_bf[64:128, :], wpanel, psA, evdt,
                           evac_engines=('scalar', 'vector', 'sync'),
                           part_base=64)
            hdt.append(mmdt)
            C.bump('dv', 16)
            C.bump('sd0', 16 * 8)
            C.bump('sd1', 16 * 8)
            PL.add(mmdt)

        def fin(s, tot={k: C.g(k) for k in ('st', 'sd0', 'sd1', 'su0', 'su1',
                                            'sz0', 'sz1')}):
            for k, v in tot.items():
                s.wait_ge(S[k], v)
        PL.add(RawPhase({'sync': fin}))


def build_C():
    nc = bass.Bass("TRN2", target_bir_lowering=False)
    io = {}
    for nm, shp, knd in (('xT', [D, T + 4], 'ExternalInput'),
                         ('winT', [D, 2 * DI], 'ExternalInput'),
                         ('wxT', [DI, 128], 'ExternalInput'),
                         ('wdtT', [64, DI], 'ExternalInput'),
                         ('convw', [128, 64], 'ExternalInput'),
                         ('convb', [128, 16], 'ExternalInput'),
                         ('bdt', [128, 16], 'ExternalInput'),
                         ('u', [DI, T], 'ExternalOutput'),
                         ('sz', [DI, T], 'ExternalOutput'),
                         ('delta', [DI, T], 'ExternalOutput'),
                         ('bc', [32, T], 'ExternalOutput')):
        io[nm] = nc.dram_tensor(nm, shp, F32, kind=knd).ap()
    S = {}
    C = Cnt()
    PL = PhaseList()
    with ExitStack() as cs:
        e = cs.enter_context
        for sname in C_SEMS:
            S[sname] = e(nc.semaphore(sname))
        _emit_C(nc, e, S, C, PL, io)
        block = e(nc.Block())
        PL.emit_all(block)
    return nc


def host_prep_C(inp, ln2_by_core):
    maps = []
    for c in range(NCORES):
        b, dirb = c // 2, c % 2
        sfx = 'b' if dirb else 'f'
        full = np.concatenate([ln2_by_core[2 * b], ln2_by_core[2 * b + 1]],
                              axis=1)
        if dirb:
            full = full[:, ::-1]
        xT = np.zeros((D, T + 4), np.float32)
        xT[:, 4:] = full
        wx = np.asarray(inp[f"L1_attn_wx_{sfx}"], np.float32)
        wxT = np.zeros((DI, 128), np.float32)
        wxT[:, 0:32] = wx[64:96].T
        wxT[:, 64:128] = wx[0:64].T
        wxT = f32c(wxT)
        maps.append(dict(
            xT=f32c(xT),
            winT=f32c(np.asarray(inp[f"L1_attn_win_{sfx}"], np.float32).T),
            wxT=wxT,
            wdtT=f32c(np.asarray(inp[f"L1_attn_wdt_{sfx}"], np.float32).T),
            convw=f32c(np.asarray(inp[f"L1_attn_wconv_{sfx}"], np.float32)
                       .reshape(DI, 4).reshape(16, 128, 4)
                       .transpose(1, 0, 2).reshape(128, 64)),
            convb=cols128(inp[f"L1_attn_bconv_{sfx}"]),
            bdt=cols128(inp[f"L1_attn_bdt_{sfx}"]),
        ))
    return maps


D_SEMS = ('wl0', 'wl1', 'mm', 'ev', 'cl', 'bl', 'bm', 'bce',
          'ld0', 'ld1', 'da', 'hv', 'pv', 'ym', 'sc', 'so0', 'so1')


def _emit_D(nc, e, S, C, PL, io):
    u_i = io['u']; sz_i = io['sz']; delta_i = io['delta']; bc_i = io['bc']
    alogc_d = io['alogc']; dpc_d = io['dpc']; woutT_d = io['woutT']
    oh32_d = io['oh32']; i128_d = io['i128']; od_o = io['od']
    io['_d_sc0'] = C.g('sc'); io['_d_da0'] = C.g('da')
    io['_d_pv0'] = C.g('pv'); io['_d_hv0'] = C.g('hv')
    io['_d_ym0'] = C.g('ym')
    if True:
        Bm_w = e(nc.sbuf_tensor("Bmw", [128, 16 * 512], BF16))
        C_w = e(nc.sbuf_tensor("Cw", [128, 16 * 512], F32))
        bc_bf = e(nc.sbuf_tensor("bcbf", [32, 512], BF16))
        da_b = [e(nc.sbuf_tensor(f"dab{i}", [128, 512], F32)) for i in range(2)]
        db_b = [e(nc.sbuf_tensor(f"dbb{i}", [128, 512], BF16)) for i in range(2)]
        hb_b = [e(nc.sbuf_tensor(f"hbb{i}", [128, 512], F32)) for i in range(2)]
        ch_b = [e(nc.sbuf_tensor(f"chb{i}", [128, 512], BF16)) for i in range(2)]
        dl = [e(nc.sbuf_tensor(f"dl{i}", [128, 512], F32)) for i in range(2)]
        ul = [e(nc.sbuf_tensor(f"ul{i}", [128, 512], F32)) for i in range(2)]
        zl = [e(nc.sbuf_tensor(f"zl{i}", [128, 512], F32)) for i in range(2)]
        duj = [e(nc.sbuf_tensor(f"duj{i}", [128, 512], BF16)) for i in range(2)]
        dujf = [e(nc.sbuf_tensor(f"dujf{i}", [128, 512], F32)) for i in range(2)]
        tp = [e(nc.sbuf_tensor(f"tp{i}", [128, 512], F32)) for i in range(2)]
        ybf = [e(nc.sbuf_tensor(f"ybf{j}", [128, 512], BF16)) for j in range(16)]
        ob = [e(nc.sbuf_tensor(f"ob{i}", [128, 512], F32)) for i in range(2)]
        hcarry = e(nc.sbuf_tensor("hcarry", [128, 256], F32))
        nA = e(nc.sbuf_tensor("nA", [128, 256], F32))
        atmp = e(nc.sbuf_tensor("atmp", [128, 256], F32))
        dpc_t = e(nc.sbuf_tensor("dpct", [128, 16], F32))
        oh_bf = e(nc.sbuf_tensor("ohbf", [32, 32 * 128], BF16))
        i_bf = e(nc.sbuf_tensor("ibf", [128, 128], BF16))
        wpanel = [e(nc.sbuf_tensor(f"wp{i}", [128, 16 * 512], BF16)) for i in range(2)]
        ps_bc = [e(nc.psum_tensor(f"psbc{i}", [128, 512], F32)) for i in range(2)]
        ps_y = [e(nc.psum_tensor(f"psy{i}", [128, 512], F32)) for i in range(2)]
        psA = [e(nc.psum_tensor(f"psA{i}", [128, 512], F32)) for i in range(2)]

        bcl = C.g('cl')

        def consts_g(g, bcl=bcl):
            g.dma_start(out=atmp[:, :], in_=alogc_d).then_inc(S['cl'], 16)
            g.dma_start(out=dpc_t[:, :], in_=dpc_d).then_inc(S['cl'], 16)
            g.dma_start(out=oh_bf[:, :], in_=oh32_d).then_inc(S['cl'], 16)
            g.dma_start(out=i_bf[:, :], in_=i128_d).then_inc(S['cl'], 16)

        def consts_s(s, bcl=bcl):
            s.wait_ge(S['cl'], bcl + 64)
            s.activation(nA[:, :], atmp[:, :], AF.Exp).then_inc(S['cl'], 1)

        def consts_v(v, bcl=bcl):
            v.memset(hcarry[:, :], 0.0)
            v.wait_ge(S['cl'], bcl + 65)
            v.tensor_scalar_mul(nA[:, :], nA[:, :], -1.0).then_inc(S['cl'], 1)

        def consts_t(t, bcl=bcl):
            t.wait_ge(S['cl'], bcl + 66)
        C.bump('cl', 66)
        PL.add(RawPhase({'gpsimd': consts_g, 'scalar': consts_s,
                         'vector': consts_v, 'tensor': consts_t}))

        for ci in range(8):
            c0d = ci * 512
            jj0 = 16 * ci
            gg0 = 256 * ci

            def bcload(g, c0d=c0d, bm_prev=C.g('bm')):
                g.wait_ge(S['bm'], bm_prev)
                g.dma_start(out=bc_bf[:, :], in_=bc_i[0:32, c0d:c0d + 512]
                            ).then_inc(S['bl'], 16)
            C.bump('bl', 16)
            bl_now = C.g('bl')
            PL.add(RawPhase({'gpsimd': bcload}))

            base_bm = C.g('bm')
            base_bce = C.g('bce')

            def bct(t, base_bm=base_bm, base_bce=base_bce, bl_now=bl_now):
                t.wait_ge(S['bl'], bl_now)
                t.wait_ge(S['bce'], base_bce)
                for n in range(32):
                    if n >= 2:
                        t.wait_ge(S['bce'], base_bce + n - 1)
                    t.matmul(ps_bc[n % 2][:, :],
                             oh_bf[0:32, n * 128:(n + 1) * 128],
                             bc_bf[0:32, :], start=True, stop=True
                             ).then_inc(S['bm'], 1)

            def bcs(s, base_bm=base_bm, ci=ci, hv_prev=C.g('hv'),
                    pv_prev=C.g('pv')):
                s.wait_ge(S['hv'], hv_prev)
                s.wait_ge(S['pv'], pv_prev)
                for n in range(32):
                    s.wait_ge(S['bm'], base_bm + n + 1)
                    tgt = (Bm_w[:, n * 512:(n + 1) * 512] if n < 16 else
                           C_w[:, (n - 16) * 512:(n - 15) * 512])
                    s.activation(tgt, ps_bc[n % 2][:, :], AF.Copy
                                 ).then_inc(S['bce'], 1)
            C.bump('bm', 32)
            C.bump('bce', 32)
            PL.add(RawPhase({'tensor': bct, 'scalar': bcs}))
            bce_now = C.g('bce')

            base_ld = [C.g('ld0'), C.g('ld1')]

            def loads(s, base_sc=C.g('sc') - 16 * ci, c0d=c0d, jj0=jj0,
                      sc0=io['_d_sc0']):
                for j in range(16):
                    if jj0 + j >= 2:
                        s.wait_ge(S['sc'], sc0 + jj0 + j - 1)
                    sem = S['ld0' if j % 2 == 0 else 'ld1']
                    s.dma_start(out=dl[j % 2][:, :],
                                in_=delta_i[j * 128:(j + 1) * 128,
                                            c0d:c0d + 512]).then_inc(sem, 16)
                    s.dma_start(out=ul[j % 2][:, :],
                                in_=u_i[j * 128:(j + 1) * 128,
                                        c0d:c0d + 512]).then_inc(sem, 16)
                    s.dma_start(out=zl[j % 2][:, :],
                                in_=sz_i[j * 128:(j + 1) * 128,
                                         c0d:c0d + 512]).then_inc(sem, 16)
            C.bump('ld0', 48 * 8)
            C.bump('ld1', 48 * 8)
            PL.add(RawPhase({'sync': loads}))

            def scal(s, jj0=jj0, gg0=gg0, base_ld=base_ld, bce_now=bce_now,
                     cl_now=C.g('cl'), hv0=io['_d_hv0']):
                s.wait_ge(S['cl'], cl_now)
                for j in range(16):
                    par = j % 2
                    s.wait_ge(S['ld0' if par == 0 else 'ld1'],
                              base_ld[par] + 48 * (j // 2 + 1))
                    for n in range(16):
                        g = gg0 + j * 16 + n
                        s.wait_ge(S['hv'], hv0 + max(0, g - 1))
                        s.activation(da_b[n % 2][:, :], dl[j % 2][:, :],
                                     AF.Exp, bias=0.0,
                                     scale=nA[:, j * 16 + n:j * 16 + n + 1]
                                     ).then_inc(S['da'], 1)
            C.bump('da', 256)
            PL.add(RawPhase({'scalar': scal}))

            def vec(v, jj0=jj0, gg0=gg0, base_ld=base_ld, bce_now=bce_now,
                    sc0=io['_d_sc0'], da0=io['_d_da0'], pv0=io['_d_pv0'],
                    hv0=io['_d_hv0'], ym0=io['_d_ym0']):
                for j in range(16):
                    par = j % 2
                    v.wait_ge(S['ld0' if par == 0 else 'ld1'],
                              base_ld[par] + 48 * (j // 2 + 1))
                    if j == 0:
                        v.wait_ge(S['bce'], bce_now)
                    v.tensor_mul(duj[j % 2][:, :], dl[j % 2][:, :],
                                 ul[j % 2][:, :])
                    for n in range(16):
                        g = gg0 + j * 16 + n
                        cidx = j * 16 + n
                        v.wait_ge(S['da'], da0 + g + 1)
                        v.wait_ge(S['pv'], pv0 + max(0, g - 1))
                        v.tensor_mul(db_b[n % 2][:, :], duj[j % 2][:, :],
                                     Bm_w[:, n * 512:(n + 1) * 512])
                        v.tensor_tensor_scan(
                            hb_b[n % 2][:, :], da_b[n % 2][:, :],
                            db_b[n % 2][:, :], hcarry[:, cidx:cidx + 1],
                            ALU.mult, ALU.add)
                        v.tensor_copy(hcarry[:, cidx:cidx + 1],
                                      hb_b[n % 2][:, 511:512]
                                      ).then_inc(S['hv'], 1)
                    v.wait_ge(S['ym'], ym0 + gg0 + j * 16 + 16)
                    v.scalar_tensor_tensor(tp[j % 2][:, :], ul[j % 2][:, :],
                                           dpc_t[:, j:j + 1], ps_y[j % 2][:, :],
                                           ALU.mult, ALU.add)
                    v.tensor_mul(ybf[j][:, :], tp[j % 2][:, :],
                                 zl[j % 2][:, :]).then_inc(S['sc'], 1)
            C.bump('hv', 256)
            C.bump('sc', 16)
            PL.add(RawPhase({'vector': vec}))

            def pool(p, gg0=gg0, hv0=io['_d_hv0'], ym0=io['_d_ym0']):
                for j in range(16):
                    for n in range(16):
                        g = gg0 + j * 16 + n
                        p.wait_ge(S['hv'], hv0 + g + 1)
                        p.wait_ge(S['ym'], ym0 + max(0, g - 1))
                        p.tensor_mul(ch_b[n % 2][:, :], hb_b[n % 2][:, :],
                                     C_w[:, n * 512:(n + 1) * 512]
                                     ).then_inc(S['pv'], 1)
            C.bump('pv', 256)
            PL.add(RawPhase({'gpsimd': pool}))

            def tens(t, jj0=jj0, gg0=gg0, sc0=io['_d_sc0'],
                     pv0=io['_d_pv0']):
                for j in range(16):
                    if jj0 + j >= 2:
                        t.wait_ge(S['sc'], sc0 + jj0 + j - 1)
                    for n in range(16):
                        g = gg0 + j * 16 + n
                        t.wait_ge(S['pv'], pv0 + g + 1)
                        t.matmul(ps_y[j % 2][:, :], i_bf[:, :],
                                 ch_b[n % 2][0:128, :],
                                 start=(n == 0), stop=(n == 15)
                                 ).then_inc(S['ym'], 1)
            C.bump('ym', 256)
            PL.add(RawPhase({'tensor': tens}))

            base_so = [C.g('so0'), C.g('so1')]

            def mk_evo(base_so=base_so, c0d=c0d, ph_holder=[]):
                def evac(engname, eng, ps, n_abs, nr, c0, cw, gi):
                    k = n_abs // 128
                    if engname == 'scalar':
                        par = gi % 2
                        eng.wait_ge(S['so0' if par == 0 else 'so1'],
                                    base_so[par] + 16 * (gi // 2))
                        eng.activation(ob[gi % 2][:, 0:cw], ps[0:nr, 0:cw],
                                       AF.Copy).then_inc(S['ev'], 1)
                    elif engname == 'sync':
                        eng.wait_ge(S['ev'], ph_holder[0].base_ev + gi + 1)
                        eng.dma_start(out=od_o[k * 128:(k + 1) * 128,
                                               c0d:c0d + 512],
                                      in_=ob[gi % 2][:, 0:cw]
                                      ).then_inc(S['so0' if gi % 2 == 0
                                                 else 'so1'], 16)
                return evac, ph_holder

            evo, ho = mk_evo()
            mmo = MMPhase(S, C, woutT_d, DI, D, 512,
                          lambda k: ybf[k][:, :], wpanel, psA, evo,
                          evac_engines=('scalar', 'sync'),
                          extra_tensor_waits=[('sc', C.g('sc'))])
            ho.append(mmo)
            C.bump('so0', 16 * 4)
            C.bump('so1', 16 * 4)
            PL.add(mmo)

        def fin(s, t0=C.g('so0'), t1=C.g('so1')):
            s.wait_ge(S['so0'], t0)
            s.wait_ge(S['so1'], t1)
        PL.add(RawPhase({'sync': fin}))


def build_D():
    nc = bass.Bass("TRN2", target_bir_lowering=False)
    io = {}
    for nm, shp, knd in (('u', [DI, T], 'ExternalInput'),
                         ('sz', [DI, T], 'ExternalInput'),
                         ('delta', [DI, T], 'ExternalInput'),
                         ('bc', [32, T], 'ExternalInput'),
                         ('alogc', [128, 256], 'ExternalInput'),
                         ('dpc', [128, 16], 'ExternalInput'),
                         ('woutT', [DI, D], 'ExternalInput'),
                         ('oh32', [32, 32 * 128], 'ExternalInput'),
                         ('i128', [128, 128], 'ExternalInput'),
                         ('od', [D, T], 'ExternalOutput')):
        io[nm] = nc.dram_tensor(nm, shp, F32, kind=knd).ap()
    S = {}
    C = Cnt()
    PL = PhaseList()
    with ExitStack() as cs:
        e = cs.enter_context
        for sname in D_SEMS:
            S[sname] = e(nc.semaphore(sname))
        _emit_D(nc, e, S, C, PL, io)
        block = e(nc.Block())
        PL.emit_all(block)
    return nc


def host_prep_D(inp, c_results):
    eye = np.eye(128, dtype=np.float32)
    oh = np.zeros((32, 32 * 128), np.float32)
    for n in range(32):
        oh[n, n * 128:(n + 1) * 128] = 1.0
    maps = []
    for c in range(NCORES):
        dirb = c % 2
        sfx = 'b' if dirb else 'f'
        alog = np.asarray(inp[f"L1_attn_Alog_{sfx}"], np.float32)
        alogc = f32c(alog.reshape(16, 128, 16).transpose(1, 0, 2)
                     .reshape(128, 256))
        maps.append(dict(
            u=c_results[c]["u"], sz=c_results[c]["sz"],
            delta=c_results[c]["delta"], bc=c_results[c]["bc"],
            alogc=alogc, dpc=cols128(inp[f"L1_attn_Dp_{sfx}"]),
            woutT=f32c(np.asarray(inp[f"L1_attn_wout_{sfx}"], np.float32).T),
            oh32=oh, i128=eye,
        ))
    return maps


E_SEMS = ('wl0', 'wl1', 'mm', 'ev', 'xl', 'xf', 'cl', 'sv', 'st',
          'sq', 'squ', 'lnm', 'lnv', 'lns', 'lnb', 'lne', 'cc4', 'gx', 'gv')


def _emit_E(nc, e, S, C, PL, io):
    ln2_i = io['ln2']; wprojT_d = io['wprojT']; bprojc_d = io['bprojc']
    ln3_o = io['ln3']
    fwd_i = io.get('fwdh'); bwd_i = io.get('bwdh')
    if True:
        catbf = [e(nc.sbuf_tensor(f"cat{k}", [128, 512], BF16)) for k in range(16)]
        x32 = [e(nc.sbuf_tensor(f"x32{k}", [128, 512], F32)) for k in range(8)]
        t32 = [e(nc.sbuf_tensor(f"t32{i}", [128, 512], F32)) for i in range(2)]
        s32 = [e(nc.sbuf_tensor(f"s32{k}", [128, 512], F32)) for k in range(8)]
        sbf = [e(nc.sbuf_tensor(f"sbf{k}", [128, 512], BF16)) for k in range(8)]
        lnout = [e(nc.sbuf_tensor(f"lno{k}", [128, 512], F32)) for k in range(8)]
        wpanel = [e(nc.sbuf_tensor(f"wp{i}", [128, 16 * 512], BF16)) for i in range(2)]
        bprojc_t = e(nc.sbuf_tensor("bpc", [128, 8], F32))
        if io.get('fwdh') is None:
            tokt = [e(nc.sbuf_tensor(f"tok{i}", [128, 1024], F32))
                    for i in range(4)]
            idxef_t = e(nc.sbuf_tensor("idxef", [128, 16], I32))
            idxeb_t = e(nc.sbuf_tensor("idxeb", [128, 16], I32))
            ident_t = e(nc.sbuf_tensor("identt", [128, 128], F32))
            ps_tr = [e(nc.psum_tensor(f"pstr{i}", [128, 128], F32))
                     for i in range(2)]
        aux = dict(
            ones128_bf=e(nc.sbuf_tensor("o128", [128, 1], BF16)),
            ones1_f32=e(nc.sbuf_tensor("o1f", [1, 128], F32)),
            sq_bf=[e(nc.sbuf_tensor(f"sqbf{i}", [128, 512], BF16)) for i in range(2)],
            row_m=e(nc.sbuf_tensor("row_m", [1, 512], F32)),
            row_v=e(nc.sbuf_tensor("row_v", [1, 512], F32)),
            row_t=e(nc.sbuf_tensor("row_t", [1, 512], F32)),
            row_r=e(nc.sbuf_tensor("row_r", [1, 512], F32)),
            eps=e(nc.sbuf_tensor("cst_eps", [1, 1], F32)),
            neghalf=e(nc.sbuf_tensor("cst_nh", [1, 1], F32)),
            nrm=[e(nc.sbuf_tensor(f"nrm{i}", [128, 512], F32)) for i in range(2)],
            ps_m=e(nc.psum_tensor("psm", [1, 512], F32)),
            ps_q=e(nc.psum_tensor("psq", [1, 512], F32)),
            ps_mb=e(nc.psum_tensor("psmb", [128, 512], F32)),
            ps_rb=e(nc.psum_tensor("psrb", [128, 512], F32)),
        )
        psA = [e(nc.psum_tensor(f"psA{i}", [128, 512], F32)) for i in range(2)]

        bcl = C.g('cl')
        fused_x = fwd_i is None
        n_cl = 64 if fused_x else 16

        def consts_g(g, bcl=bcl):
            g.dma_start(out=bprojc_t[:, :], in_=bprojc_d).then_inc(S['cl'], 16)
            if fused_x:
                g.dma_start(out=idxef_t[:, :], in_=io['idxEf']
                            ).then_inc(S['cl'], 16)
                g.dma_start(out=idxeb_t[:, :], in_=io['idxEb']
                            ).then_inc(S['cl'], 16)
                g.dma_start(out=ident_t[:, :], in_=io['ident']
                            ).then_inc(S['cl'], 16)

        def consts_v(v, bcl=bcl, n_cl=n_cl):
            v.memset(aux['ones1_f32'][0:1, :], 1.0)
            v.memset(aux['ones128_bf'][:, 0:1], 1.0)
            v.memset(aux['eps'][0:1, 0:1], 1e-5)
            v.memset(aux['neghalf'][0:1, 0:1], -0.5)
            v.wait_ge(S['cl'], bcl + n_cl)

        def consts_s(s, bcl=bcl, n_cl=n_cl):
            s.wait_ge(S['cl'], bcl + n_cl)
        C.bump('cl', n_cl)
        io['_e_cl'] = C.g('cl')
        PL.add(RawPhase({'gpsimd': consts_g, 'vector': consts_v,
                         'scalar': consts_s}))

        for ci in range(4):
            c0d = ci * 512
            prev_mm = C.g('mm')
            prev_sv = C.g('sv')

            if fwd_i is not None:
                def loads_g(g, c0d=c0d, ci=ci, prev_mm=prev_mm):
                    if ci > 0:
                        g.wait_ge(S['mm'], prev_mm)
                    for k in range(8):
                        g.dma_start(out=catbf[k][:, :],
                                    in_=fwd_i[k * 128:(k + 1) * 128,
                                              c0d:c0d + 512]
                                    ).then_inc(S['xl'], 16)
                    for k in range(8):
                        g.dma_start(out=catbf[8 + k][:, :],
                                    in_=bwd_i[k * 128:(k + 1) * 128,
                                              c0d:c0d + 512]
                                    ).then_inc(S['xl'], 16)
                C.bump('xl', 16 * 16)
                gv_now = None
            else:
                src = io['odall']
                base_gx = C.g('gx')
                base_gv = C.g('gv')
                base_xl = C.g('xl')

                def loads_g(g, ci=ci, base_gx=base_gx, cc4_n=io['cc4_n']):
                    if ci == 0:
                        g.wait_ge(S['cc4'], cc4_n)
                        g.wait_ge(S['cl'], io['_e_cl'])
                    if ci > 0:
                        g.wait_ge(S['gx'], base_gx)
                    for gI in range(4):
                        g.indirect_dma_start(
                            out=tokt[gI][:, :], out_offset=None, in_=src,
                            in_offset=bass.IndirectOffsetOnAxis(
                                idxef_t[:, 4 * ci + gI:4 * ci + gI + 1], 0),
                        ).then_inc(S['xl'], 16)
                    g.wait_ge(S['gx'], base_gx + 32)
                    for gI in range(4):
                        g.indirect_dma_start(
                            out=tokt[gI][:, :], out_offset=None, in_=src,
                            in_offset=bass.IndirectOffsetOnAxis(
                                idxeb_t[:, 4 * ci + gI:4 * ci + gI + 1], 0),
                        ).then_inc(S['xl'], 16)

                def trt(t, base_xl=base_xl, base_gv=base_gv, base_gx=base_gx):
                    i = 0
                    for half in range(2):
                        t.wait_ge(S['xl'], base_xl + 64 * (half + 1))
                        for gI in range(4):
                            for k in range(8):
                                if i >= 2:
                                    t.wait_ge(S['gv'], base_gv + i - 1)
                                t.transpose(ps_tr[i % 2][:, 0:128],
                                            tokt[gI][:, k * 128:(k + 1) * 128],
                                            ident_t[:, :]).then_inc(S['gx'], 1)
                                i += 1

                def tre(sc, base_gx=base_gx, base_gv=base_gv):
                    i = 0
                    for half in range(2):
                        for gI in range(4):
                            for k in range(8):
                                sc.wait_ge(S['gx'], base_gx + i + 1)
                                sc.activation(
                                    catbf[half * 8 + k][:, gI * 128:(gI + 1) * 128],
                                    ps_tr[i % 2][:, 0:128],
                                    AF.Copy).then_inc(S['gv'], 1)
                                i += 1
                C.bump('xl', 16 * 8)
                C.bump('gx', 64)
                C.bump('gv', 64)
                gv_now = C.g('gv')

            def loads_s(s, c0d=c0d, ci=ci, prev_sv=prev_sv):
                if ci > 0:
                    s.wait_ge(S['sv'], prev_sv)
                for k in range(8):
                    s.dma_start(out=x32[k][:, :],
                                in_=ln2_i[k * 128:(k + 1) * 128, c0d:c0d + 512]
                                ).then_inc(S['xf'], 16)
            C.bump('xf', 16 * 8)
            xl_now = C.g('xl')
            xf_now = C.g('xf')
            if fwd_i is not None:
                PL.add(RawPhase({'gpsimd': loads_g, 'sync': loads_s}))
            else:
                PL.add(RawPhase({'gpsimd': loads_g, 'tensor': trt,
                                 'scalar': tre, 'sync': loads_s}))

            def mk_evp(xf_now=xf_now, ph_holder=[]):
                def evac(engname, eng, ps, n_abs, nr, c0, cw, gi):
                    k = n_abs // 128
                    if engname == 'vector':
                        if gi == 0:
                            eng.wait_ge(S['xf'], xf_now)
                        eng.scalar_tensor_tensor(
                            t32[gi % 2][:, 0:cw], ps[0:nr, 0:cw],
                            bprojc_t[:, k:k + 1], x32[k][:, 0:cw],
                            ALU.add, ALU.add).then_inc(S['ev'], 1)
                        eng.tensor_copy(s32[k][:, 0:cw], t32[gi % 2][:, 0:cw])
                        eng.tensor_copy(sbf[k][:, 0:cw], t32[gi % 2][:, 0:cw]
                                        ).then_inc(S['sv'], 1)
                return evac, ph_holder

            evp, hp = mk_evp()
            e_tw = [('xl', xl_now)]
            if fwd_i is None:
                e_tw.append(('gv', gv_now))
            mmp = MMPhase(S, C, wprojT_d, DI, D, 512,
                          lambda k: catbf[k][:, :], wpanel, psA, evp,
                          evac_engines=('vector',),
                          extra_tensor_waits=e_tw)
            hp.append(mmp)
            C.bump('sv', 8)
            PL.add(mmp)

            base_st = C.g('st')
            ln = LNPhase(S, C, lambda k: sbf[k], lambda k: s32[k],
                         lambda k: lnout[k], 512, aux,
                         pre={'tensor': [('sv', C.g('sv'))],
                              'scalar': [('sv', C.g('sv'))],
                              'vector': [('st', base_st)]})
            PL.add(ln)

            def store(s, base=ln.base_lne, c0d=c0d):
                for k in range(8):
                    s.wait_ge(S['lne'], base + k + 1)
                    s.dma_start(out=ln3_o[k * 128:(k + 1) * 128, c0d:c0d + 512],
                                in_=lnout[k][:, :]).then_inc(S['st'], 16)
            C.bump('st', 16 * 8)
            PL.add(RawPhase({'sync': store}))

        st_total = C.g('st')
        PL.add(RawPhase({'sync': lambda s: s.wait_ge(S['st'], st_total)}))


def build_E():
    nc = bass.Bass("TRN2", target_bir_lowering=False)
    io = {}
    for nm, shp, knd in (('fwdh', [D, TLOC], 'ExternalInput'),
                         ('bwdh', [D, TLOC], 'ExternalInput'),
                         ('ln2', [D, TLOC], 'ExternalInput'),
                         ('wprojT', [DI, D], 'ExternalInput'),
                         ('bprojc', [128, 8], 'ExternalInput'),
                         ('ln3', [D, TLOC], 'ExternalOutput')):
        io[nm] = nc.dram_tensor(nm, shp, F32, kind=knd).ap()
    S = {}
    C = Cnt()
    PL = PhaseList()
    with ExitStack() as cs:
        e = cs.enter_context
        for sname in E_SEMS:
            S[sname] = e(nc.semaphore(sname))
        _emit_E(nc, e, S, C, PL, io)
        block = e(nc.Block())
        PL.emit_all(block)
    return nc


def host_prep_E(inp, d_results, ln2_by_core):
    wprojT = f32c(np.asarray(inp["L1_attn_wproj"], np.float32).T)
    bprojc = cols128(inp["L1_attn_bproj"])
    maps = []
    for c in range(NCORES):
        b, h = c // 2, c % 2
        fwd_full = d_results[2 * b]["od"]
        bwd_full = d_results[2 * b + 1]["od"][:, ::-1]
        maps.append(dict(
            fwdh=f32c(fwd_full[:, h * TLOC:(h + 1) * TLOC]),
            bwdh=f32c(bwd_full[:, h * TLOC:(h + 1) * TLOC]),
            ln2=ln2_by_core[c], wprojT=wprojT, bprojc=bprojc))
    return maps


F_SEMS = ('wl0', 'wl1', 'mm', 'ev', 'xl', 'xf', 'cl', 'sv', 'st',
          'sh0', 'sh1', 'sq', 'squ', 'lnm', 'lnv', 'lns', 'lnb', 'lne')


def _emit_F(nc, e, S, C, PL, io):
    ln3_i = io['ln3']; w1T_d = io['w1T']; w2T_d = io['w2T']
    h1T_d = io['h1T']; h2T_d = io['h2T']; w3T_d = io['w3T']
    s1c_d = io['s1c']; s2c_d = io['s2c']; sh1c_d = io['sh1c']
    sh2c_d = io['sh2c']; s3c_d = io['s3c']; y_o = io['y']
    h0_d = io['h0d']; h1_d = io['h1d']; h2_d = io['h2d']
    if True:
        a32 = [e(nc.sbuf_tensor(f"a32{j}", [128, 512], F32)) for j in range(32)]
        hbf = [a32[j][:, :].bitcast(BF16)[:, 0:512] for j in range(32)]
        xbf = [e(nc.sbuf_tensor(f"xbf{k}", [128, 512], BF16)) for k in range(8)]
        hobuf = [e(nc.sbuf_tensor(f"hob{i}", [128, 512], BF16)) for i in range(2)]
        x32 = [e(nc.sbuf_tensor(f"x32{k}", [128, 512], F32)) for k in range(8)]
        s32 = [e(nc.sbuf_tensor(f"s32{k}", [128, 512], F32)) for k in range(8)]
        sbf = [e(nc.sbuf_tensor(f"sbf{k}", [128, 512], BF16)) for k in range(8)]
        wpanel = [e(nc.sbuf_tensor(f"wp{i}", [128, 32 * 512], BF16)) for i in range(2)]
        s1c = e(nc.sbuf_tensor("s1ct", [128, 32], F32))
        s2c = e(nc.sbuf_tensor("s2ct", [128, 32], F32))
        sh1c = e(nc.sbuf_tensor("sh1ct", [128, 32], F32))
        sh2c = e(nc.sbuf_tensor("sh2ct", [128, 32], F32))
        s3c = e(nc.sbuf_tensor("s3ct", [128, 8], F32))
        aux = dict(
            ones128_bf=e(nc.sbuf_tensor("o128", [128, 1], BF16)),
            ones1_f32=e(nc.sbuf_tensor("o1f", [1, 128], F32)),
            sq_bf=[e(nc.sbuf_tensor(f"sqbf{i}", [128, 512], BF16)) for i in range(2)],
            row_m=e(nc.sbuf_tensor("row_m", [1, 512], F32)),
            row_v=e(nc.sbuf_tensor("row_v", [1, 512], F32)),
            row_t=e(nc.sbuf_tensor("row_t", [1, 512], F32)),
            row_r=e(nc.sbuf_tensor("row_r", [1, 512], F32)),
            eps=e(nc.sbuf_tensor("cst_eps", [1, 1], F32)),
            neghalf=e(nc.sbuf_tensor("cst_nh", [1, 1], F32)),
            nrm=[e(nc.sbuf_tensor(f"nrm{i}", [128, 512], F32)) for i in range(2)],
            ps_m=e(nc.psum_tensor("psm", [1, 512], F32)),
            ps_q=e(nc.psum_tensor("psq", [1, 512], F32)),
            ps_mb=e(nc.psum_tensor("psmb", [128, 512], F32)),
            ps_rb=e(nc.psum_tensor("psrb", [128, 512], F32)),
        )
        psA = [e(nc.psum_tensor(f"psA{i}", [128, 512], F32)) for i in range(2)]

        bcl = C.g('cl')

        def consts_g(g, bcl=bcl):
            for t, d_ in ((s1c, s1c_d), (s2c, s2c_d), (sh1c, sh1c_d),
                          (sh2c, sh2c_d), (s3c, s3c_d)):
                g.dma_start(out=t[:, :], in_=d_).then_inc(S['cl'], 16)

        def consts_v(v, bcl=bcl):
            v.memset(aux['ones1_f32'][0:1, :], 1.0)
            v.memset(aux['ones128_bf'][:, 0:1], 1.0)
            v.memset(aux['eps'][0:1, 0:1], 1e-5)
            v.memset(aux['neghalf'][0:1, 0:1], -0.5)
            v.wait_ge(S['cl'], bcl + 80)

        def consts_s(s, bcl=bcl):
            s.wait_ge(S['cl'], bcl + 80)
        C.bump('cl', 80)
        PL.add(RawPhase({'gpsimd': consts_g, 'vector': consts_v,
                         'scalar': consts_s}))

        for ci in range(4):
            c0d = ci * 512
            prev_mm = C.g('mm')

            def loadx(g, c0d=c0d, ci=ci, prev_mm=prev_mm):
                if ci > 0:
                    g.wait_ge(S['mm'], prev_mm)
                for k in range(8):
                    g.dma_start(out=xbf[k][:, :],
                                in_=ln3_i[k * 128:(k + 1) * 128, c0d:c0d + 512]
                                ).then_inc(S['xl'], 16)
            C.bump('xl', 128)
            xl_now = C.g('xl')
            PL.add(RawPhase({'gpsimd': loadx}))

            def ev1(engname, eng, ps, n_abs, nr, c0, cw, gi):
                j = n_abs // 128
                eng.activation(a32[j][:, 0:cw], ps[0:nr, 0:cw], AF.Silu,
                               bias=0.0, scale=s1c[:, j:j + 1]
                               ).then_inc(S['ev'], 1)

            mm1 = MMPhase(S, C, w1T_d, D, 4096, 512, lambda k: xbf[k][:, :],
                          wpanel, psA, ev1, evac_engines=('scalar',),
                          extra_tensor_waits=[('xl', xl_now)])
            PL.add(mm1)

            base_sh = [C.g('sh0'), C.g('sh1')]

            def mk_ev2(base_sh=base_sh, c0d=c0d, ph_holder=[]):
                def evac(engname, eng, ps, n_abs, nr, c0, cw, gi):
                    j = n_abs // 128
                    if engname == 'vector':
                        par = gi % 2
                        eng.wait_ge(S['sh0' if par == 0 else 'sh1'],
                                    base_sh[par] + 16 * (gi // 2))
                        eng.scalar_tensor_tensor(
                            hobuf[gi % 2][:, 0:cw], ps[0:nr, 0:cw],
                            s2c[:, j:j + 1], a32[j][:, 0:cw],
                            ALU.mult, ALU.mult).then_inc(S['ev'], 1)
                    elif engname == 'sync':
                        eng.wait_ge(S['ev'], ph_holder[0].base_ev + gi + 1)
                        eng.dma_start(out=h0_d[j * 128:(j + 1) * 128,
                                               c0d:c0d + cw],
                                      in_=hobuf[gi % 2][:, 0:cw]
                                      ).then_inc(S['sh0' if gi % 2 == 0
                                                 else 'sh1'], 16)
                return evac, ph_holder

            ev2, h2h = mk_ev2()
            mm2 = MMPhase(S, C, w2T_d, D, 4096, 512, lambda k: xbf[k][:, :],
                          wpanel, psA, ev2, evac_engines=('vector', 'sync'))
            h2h.append(mm2)
            C.bump('sh0', 16 * 16)
            C.bump('sh1', 16 * 16)
            PL.add(mm2)

        for (src_d, dst_d, wT, scol) in ((h0_d, h1_d, h1T_d, sh1c),
                                         (h1_d, h2_d, h2T_d, sh2c)):
            for ci in range(4):
                c0d = ci * 512
                prev_mm = C.g('mm')

                def loadh(s, c0d=c0d, prev_mm=prev_mm, src_d=src_d,
                          prev_ev=C.g('ev')):
                    s.wait_ge(S['mm'], prev_mm)
                    s.wait_ge(S['ev'], prev_ev)
                    for j in range(32):
                        s.dma_start(out=hbf[j],
                                    in_=src_d[j * 128:(j + 1) * 128,
                                              c0d:c0d + 512]
                                    ).then_inc(S['xf'], 16)
                C.bump('xf', 16 * 32)
                xf_now = C.g('xf')
                PL.add(RawPhase({'sync': loadh}))

                base_sh = [C.g('sh0'), C.g('sh1')]

                def mk_evh(base_sh=base_sh, c0d=c0d, dst_d=dst_d, scol=scol,
                           ph_holder=[]):
                    def evac(engname, eng, ps, n_abs, nr, c0, cw, gi):
                        j = n_abs // 128
                        if engname == 'scalar':
                            par = gi % 2
                            eng.wait_ge(S['sh0' if par == 0 else 'sh1'],
                                        base_sh[par] + 16 * (gi // 2))
                            eng.activation(hobuf[gi % 2][:, 0:cw],
                                           ps[0:nr, 0:cw], AF.Gelu,
                                           bias=0.0, scale=scol[:, j:j + 1]
                                           ).then_inc(S['ev'], 1)
                        elif engname == 'sync':
                            eng.wait_ge(S['ev'],
                                        ph_holder[0].base_ev + gi + 1)
                            eng.dma_start(out=dst_d[j * 128:(j + 1) * 128,
                                                    c0d:c0d + cw],
                                          in_=hobuf[gi % 2][:, 0:cw]
                                          ).then_inc(S['sh0' if gi % 2 == 0
                                                     else 'sh1'], 16)
                    return evac, ph_holder

                evh, hh = mk_evh()
                mmh = MMPhase(S, C, wT, 4096, 4096, 512, lambda k: hbf[k],
                              wpanel, psA, evh, evac_engines=('scalar', 'sync'),
                              extra_tensor_waits=[('xf', xf_now)])
                hh.append(mmh)
                C.bump('sh0', 16 * 16)
                C.bump('sh1', 16 * 16)
                PL.add(mmh)

        for ci in range(4):
            c0d = ci * 512
            prev_mm = C.g('mm')
            prev_sv = C.g('sv')

            def loadh2(s, c0d=c0d, prev_mm=prev_mm, prev_sv=prev_sv, ci=ci,
                       prev_ev=C.g('ev')):
                s.wait_ge(S['mm'], prev_mm)
                s.wait_ge(S['ev'], prev_ev)
                if ci > 0:
                    s.wait_ge(S['sv'], prev_sv)
                for j in range(32):
                    s.dma_start(out=hbf[j],
                                in_=h2_d[j * 128:(j + 1) * 128, c0d:c0d + 512]
                                ).then_inc(S['xf'], 16)
                for k in range(8):
                    s.dma_start(out=x32[k][:, :],
                                in_=ln3_i[k * 128:(k + 1) * 128, c0d:c0d + 512]
                                ).then_inc(S['xf'], 16)
            C.bump('xf', 16 * 40)
            xf_now = C.g('xf')
            PL.add(RawPhase({'sync': loadh2}))

            def mk_ev4(xf_now=xf_now, ph_holder=[]):
                def evac(engname, eng, ps, n_abs, nr, c0, cw, gi,
                         st_prev=C.g('st')):
                    k = n_abs // 128
                    if engname == 'vector':
                        if gi == 0:
                            eng.wait_ge(S['xf'], xf_now)
                            eng.wait_ge(S['st'], st_prev)
                        eng.scalar_tensor_tensor(
                            s32[k][:, 0:cw], ps[0:nr, 0:cw], s3c[:, k:k + 1],
                            x32[k][:, 0:cw], ALU.mult, ALU.add
                            ).then_inc(S['ev'], 1)
                        eng.tensor_copy(sbf[k][:, 0:cw], s32[k][:, 0:cw]
                                        ).then_inc(S['sv'], 1)
                return evac, ph_holder

            ev4, h4 = mk_ev4()
            mm4 = MMPhase(S, C, w3T_d, 4096, D, 512, lambda k: hbf[k],
                          wpanel, psA, ev4, evac_engines=('vector',),
                          extra_tensor_waits=[('xf', xf_now)])
            h4.append(mm4)
            C.bump('sv', 8)
            PL.add(mm4)

            base_st = C.g('st')
            ln = LNPhase(S, C, lambda k: sbf[k], lambda k: s32[k],
                         lambda k: s32[k], 512, aux,
                         pre={'tensor': [('sv', C.g('sv'))],
                              'scalar': [('sv', C.g('sv'))],
                              'vector': [('st', base_st)]})
            PL.add(ln)

            def store(s, base=ln.base_lne, c0d=c0d):
                for k in range(8):
                    s.wait_ge(S['lne'], base + k + 1)
                    s.dma_start(out=y_o[k * 128:(k + 1) * 128, c0d:c0d + 512],
                                in_=s32[k][:, :]).then_inc(S['st'], 16)
            C.bump('st', 16 * 8)
            PL.add(RawPhase({'sync': store}))

        st_total = C.g('st')
        PL.add(RawPhase({'sync': lambda s: s.wait_ge(S['st'], st_total)}))


def build_F():
    nc = bass.Bass("TRN2", target_bir_lowering=False)
    io = {}
    for nm, shp, knd in (('ln3', [D, TLOC], 'ExternalInput'),
                         ('w1T', [D, 4096], 'ExternalInput'),
                         ('w2T', [D, 4096], 'ExternalInput'),
                         ('h1T', [4096, 4096], 'ExternalInput'),
                         ('h2T', [4096, 4096], 'ExternalInput'),
                         ('w3T', [4096, D], 'ExternalInput'),
                         ('s1c', [128, 32], 'ExternalInput'),
                         ('s2c', [128, 32], 'ExternalInput'),
                         ('sh1c', [128, 32], 'ExternalInput'),
                         ('sh2c', [128, 32], 'ExternalInput'),
                         ('s3c', [128, 8], 'ExternalInput'),
                         ('y', [D, TLOC], 'ExternalOutput')):
        io[nm] = nc.dram_tensor(nm, shp, F32, kind=knd).ap()
    io['h0d'] = nc.dram_tensor("h0d", [4096, TLOC], BF16).ap()
    io['h1d'] = nc.dram_tensor("h1d", [4096, TLOC], BF16).ap()
    io['h2d'] = nc.dram_tensor("h2d", [4096, TLOC], BF16).ap()
    S = {}
    C = Cnt()
    PL = PhaseList()
    with ExitStack() as cs:
        e = cs.enter_context
        for sname in F_SEMS:
            S[sname] = e(nc.semaphore(sname))
        _emit_F(nc, e, S, C, PL, io)
        block = e(nc.Block())
        PL.emit_all(block)
    return nc


def host_prep_F(inp, e_results):
    def colsN(v):
        return f32c(np.asarray(v, np.float32).reshape(-1, 128).T)
    m = dict(
        w1T=f32c(np.asarray(inp["L1_ffn_w1_w"], np.float32).T),
        w2T=f32c(np.asarray(inp["L1_ffn_w2_w"], np.float32).T),
        h1T=f32c(np.asarray(inp["L1_ffn_h1_w"], np.float32).T),
        h2T=f32c(np.asarray(inp["L1_ffn_h2_w"], np.float32).T),
        w3T=f32c(np.asarray(inp["L1_ffn_w3_w"], np.float32).T),
        s1c=colsN(inp["L1_ffn_w1_s"]), s2c=colsN(inp["L1_ffn_w2_s"]),
        sh1c=colsN(inp["L1_ffn_h1_s"]), sh2c=colsN(inp["L1_ffn_h2_s"]),
        s3c=colsN(inp["L1_ffn_w3_s"]),
    )
    return [dict(m, ln3=e_results[c]["ln3"]) for c in range(NCORES)]


def kernel_unfused(**inputs):
    cores = list(range(NCORES))
    r1 = run_bass_kernel_spmd(build_A1(), host_prep_A1(inputs), core_ids=cores).results
    r2 = run_bass_kernel_spmd(build_A2(), host_prep_A2(inputs, r1), core_ids=cores).results
    r3 = run_bass_kernel_spmd(build_B(), host_prep_B(inputs, r2), core_ids=cores).results
    ln2_by_core = [np.asarray(r3[c]["ln2"], np.float32) for c in range(NCORES)]
    rC = run_bass_kernel_spmd(build_C(), host_prep_C(inputs, ln2_by_core), core_ids=cores).results
    rD = run_bass_kernel_spmd(build_D(), host_prep_D(inputs, rC), core_ids=cores).results
    rE = run_bass_kernel_spmd(build_E(), host_prep_E(inputs, rD, ln2_by_core), core_ids=cores).results
    rF = run_bass_kernel_spmd(build_F(), host_prep_F(inputs, rE), core_ids=cores).results
    out = np.empty((B, T, D), np.float32)
    for c in range(NCORES):
        b, h = c // 2, c % 2
        out[b, h * TLOC:(h + 1) * TLOC, :] = rF[c]["y"].T
    return out


def kernel(**inputs):
    return kernel_fused(**inputs)


def _emit_fm_to_tm(nc, e, S, C, PL, src, dst, ident_ap, nrows, ncols,
                   pre_sync=()):
    kt = nrows // 128
    tbn = ncols // 128
    lt = [e(nc.sbuf_tensor(f"tml{C.g('tmuid')}_{k}", [128, 128], F32))
          for k in range(kt)]
    tmrow = [e(nc.sbuf_tensor(f"tmr{C.g('tmuid')}_{i}", [128, nrows], F32))
             for i in range(2)]
    idt = e(nc.sbuf_tensor(f"tmi{C.g('tmuid')}", [128, 128], F32))
    ps_tr = [e(nc.psum_tensor(f"tmp{C.g('tmuid')}_{i}", [128, 128], F32))
             for i in range(2)]
    C.bump('tmuid', 1)
    b_tma = C.g('tma'); b_tmb = C.g('tmb'); b_tmc = C.g('tmc')
    b_tms = C.g('tms'); b_tmi = C.g('tmi')

    def loads(s):
        for sem, val in pre_sync:
            s.wait_ge(S[sem], val)
        s.dma_start(out=idt[:, :], in_=ident_ap).then_inc(S['tmi'], 16)
        for tb in range(tbn):
            if tb > 0:
                s.wait_ge(S['tmb'], b_tmb + kt * tb)
            for k in range(kt):
                s.dma_start(out=lt[k][:, :],
                            in_=src[k * 128:(k + 1) * 128,
                                    tb * 128:(tb + 1) * 128]
                            ).then_inc(S['tma'], 16)

    def trans(t):
        t.wait_ge(S['tmi'], b_tmi + 16)
        i = 0
        for tb in range(tbn):
            t.wait_ge(S['tma'], b_tma + 16 * kt * (tb + 1))
            for k in range(kt):
                if i >= 2:
                    t.wait_ge(S['tmc'], b_tmc + i - 1)
                t.transpose(ps_tr[i % 2][:, :], lt[k][:, :], idt[:, :]
                            ).then_inc(S['tmb'], 1)
                i += 1

    def evac(sc):
        i = 0
        for tb in range(tbn):
            for k in range(kt):
                sc.wait_ge(S['tmb'], b_tmb + i + 1)
                if k == 0 and tb >= 2:
                    sc.wait_ge(S['tms'], b_tms + 16 * (tb - 1))
                sc.activation(tmrow[tb % 2][:, k * 128:(k + 1) * 128],
                              ps_tr[i % 2][:, :], AF.Copy
                              ).then_inc(S['tmc'], 1)
                i += 1

    def stores(s):
        for tb in range(tbn):
            s.wait_ge(S['tmc'], b_tmc + kt * (tb + 1))
            s.dma_start(out=dst[tb * 128:(tb + 1) * 128, :],
                        in_=tmrow[tb % 2][:, :]).then_inc(S['tms'], 16)
    C.bump('tmi', 16)
    C.bump('tma', 16 * kt * tbn)
    C.bump('tmb', kt * tbn)
    C.bump('tmc', kt * tbn)
    C.bump('tms', 16 * tbn)
    PL.add(RawPhase({'sync': loads, 'tensor': trans, 'scalar': evac,
                     'gpsimd': stores}))
    return C.g('tms')


ALL_SEMS = sorted(set(A1_SEMS) | set(A2_SEMS) | set(B_SEMS) | set(C_SEMS) |
                  set(D_SEMS) | set(E_SEMS) | set(F_SEMS) |
                  {'cc1', 'cc2', 'cc3', 'cc4', 'tm',
                   'tma', 'tmb', 'tmc', 'tms', 'tmi', 'wc'})


def build_fused():
    nc = bass.Bass("TRN2", target_bir_lowering=False)
    IN, INT = 'ExternalInput', None

    def dt(nm, shp, kind=IN, dtype=F32):
        if kind is None:
            return nc.dram_tensor(nm, shp, dtype).ap()
        return nc.dram_tensor(nm, shp, dtype, kind=kind).ap()

    dr = {}
    for nm, shp in (('xw', [D, TLOC + 4]), ('w_in', [D, 2 * DI]),
                    ('wx_main', [DI, DI]), ('wx_small', [DI, 32]),
                    ('w_dt', [DS, DI]), ('convw', [128, 64]),
                    ('convb', [128, 16]), ('bdt', [128, 16]),
                    ('selb', [32, 1]), ('xT', [D, TLOC]), ('wout', [DI, D]),
                    ('cflag', [128, 1]), ('mur', [128, 8]), ('muk', [128, 8]),
                    ('wrT', [D, D]), ('wkT', [D, D]), ('wvT', [D, D]),
                    ('winT', [D, 2 * DI]), ('wxT', [DI, 128]),
                    ('wdtT', [64, DI]), ('convwc', [128, 64]),
                    ('convbc', [128, 16]), ('bdtc', [128, 16]),
                    ('ident', [128, 128]), ('alogc', [128, 256]),
                    ('dpc', [128, 16]), ('woutT', [DI, D]),
                    ('oh32', [32, 32 * 128]), ('wprojT', [DI, D]),
                    ('bprojc', [128, 8]), ('w1T', [D, 4096]),
                    ('w2T', [D, 4096]), ('h1T', [4096, 4096]),
                    ('h2T', [4096, 4096]), ('w3T', [4096, D]),
                    ('s1c', [128, 32]), ('s2c', [128, 32]),
                    ('sh1c', [128, 32]), ('sh2c', [128, 32]),
                    ('s3c', [128, 8])):
        dr[nm] = dt(nm, shp)
    for nm, shp in (('idxC', [128, 33]), ('idxEf', [128, 16]),
                    ('idxEb', [128, 16])):
        dr[nm] = dt(nm, shp, IN, I32)
    y_o = dt('y', [D, TLOC], 'ExternalOutput')
    for nm, shp in (('G', [DI, TLOC]), ('P', [DI, TLOC]), ('M1', [DI, TLOC]),
                    ('ln1', [D, TLOC]), ('ln2', [D, TLOC]),
                    ('ln3', [D, TLOC]), ('u', [DI, T]), ('sz', [DI, T]),
                    ('delta', [DI, T]), ('bc', [32, T]), ('od', [D, T]),
                    ('gend_in', [128, 16]), ('gend_out', [256, 16]),
                    ('l1col_in', [128, 8]), ('l1col_out', [256, 8]),
                    ('ln2tm', [TLOC, D]), ('ln2_all', [T + 1, D]),
                    ('odtm', [T, D]), ('odall', [2 * T, D])):
        dr[nm] = dt(nm, shp, INT)
    for nm in ('h0d', 'h1d', 'h2d'):
        dr[nm] = dt(nm, [4096, TLOC], INT, BF16)
    CAST_ORDER = ('w_in', 'wx_main', 'wx_small', 'w_dt', 'wout',
                  'wrT', 'wkT', 'wvT', 'winT', 'wxT', 'wdtT', 'woutT',
                  'wprojT', 'w1T', 'w2T', 'w3T', 'h1T', 'h2T')
    wshapes = dict(w_in=[D, 2 * DI], wx_main=[DI, DI], wx_small=[DI, 32],
                   w_dt=[DS, DI], wout=[DI, D], wrT=[D, D], wkT=[D, D],
                   wvT=[D, D], winT=[D, 2 * DI], wxT=[DI, 128],
                   wdtT=[64, DI], woutT=[DI, D], wprojT=[DI, D],
                   w1T=[D, 4096], w2T=[D, 4096], w3T=[4096, D],
                   h1T=[4096, 4096], h2T=[4096, 4096])
    for nm in CAST_ORDER:
        dr[nm + '_b'] = dt(nm + '_b', wshapes[nm], INT, BF16)
    wc_counts = {}
    cnt = 0
    for nm in CAST_ORDER:
        cnt += 16 * ceil_div(wshapes[nm][0], 128)
        wc_counts[nm + '_b'] = cnt
    MM_WCFG.clear()
    MM_WCFG.update(wc_counts)

    def wsel(nm):
        return dr[nm + '_b'] if nm in CAST_ORDER else dr[nm]

    S = {}
    C = Cnt()
    with ExitStack() as cs:
        e = cs.enter_context
        for sname in ALL_SEMS:
            S[sname] = e(nc.semaphore(sname))

        stage_ctr = [0]

        def stage(emit_fn, io):
            sid = stage_ctr[0]
            stage_ctr[0] += 1
            orig_sb = bass.Bass.sbuf_tensor
            orig_ps = bass.Bass.psum_tensor

            def sb(name, *a, **kw):
                return orig_sb(nc, f"st{sid}_{name}", *a, **kw)

            def ps(name, *a, **kw):
                return orig_ps(nc, f"st{sid}_{name}", *a, **kw)
            nc.sbuf_tensor = sb
            nc.psum_tensor = ps
            try:
                with ExitStack() as scs:
                    PL = PhaseList()
                    emit_fn(nc, scs.enter_context, S, C, PL, io)
                    blk = scs.enter_context(nc.Block())
                    PL.emit_all(blk)
            finally:
                del nc.sbuf_tensor
                del nc.psum_tensor
            return io

        def emit_a1(nc_, e_, S_, C_, PL_, io_):
            def casts(g):
                for nm in CAST_ORDER:
                    rows = wshapes[nm][0]
                    for r in range(0, rows, 128):
                        rr = min(128, rows - r)
                        g.dma_start(out=dr[nm + '_b'][r:r + rr, :],
                                    in_=dr[nm][r:r + rr, :]
                                    ).then_inc(S_['wc'], 16)
            C_.bump('wc', sum(16 * ceil_div(wshapes[nm][0], 128)
                              for nm in CAST_ORDER))
            PL_.add(RawPhase({'gpsimd': casts}))
            _emit_A1(nc_, e_, S_, C_, PL_, io_)
            gcarry = io_['_a1']['gcarry']
            dv_tot = C_.g('dv')
            st_b = C_.g('st')

            def ag_sync(s):
                s.wait_ge(S_['dv'], dv_tot)
                s.dma_start(out=dr['gend_in'], in_=gcarry[:, :]
                            ).then_inc(S_['st'], 16)

            def ag_g(g):
                g.wait_ge(S_['st'], st_b + 16)
                g.collective_compute(
                    "AllGather", ALU.bypass,
                    replica_groups=[[0, 1], [2, 3], [4, 5], [6, 7]],
                    ins=[dr['gend_in'].opt()],
                    outs=[dr['gend_out'].opt()],
                ).then_inc(S_['cc1'], 1)
            C_.bump('st', 16)
            C_.bump('cc1', 1)
            PL_.add(RawPhase({'sync': ag_sync, 'gpsimd': ag_g}))

        io1 = {k: dr[k] for k in ('xw', 'convw', 'convb', 'bdt', 'selb',
                                  'G', 'P', 'M1')}
        for k in ('w_in', 'wx_main', 'wx_small', 'w_dt'):
            io1[k] = wsel(k)
        stage(emit_a1, io1)

        def emit_a2(nc_, e_, S_, C_, PL_, io_):
            _emit_A2(nc_, e_, S_, C_, PL_, io_)
            lnout = io_['_a2']['lnout']
            lne_tot = C_.g('lne')
            tm_b = C_.g('tm')
            st_b = C_.g('st')
            l1col_sb = e_(nc.sbuf_tensor("l1colsb", [128, 8], F32))

            def col_v(v):
                v.wait_ge(S_['lne'], lne_tot)
                for k in range(8):
                    ins = v.tensor_copy(l1col_sb[:, k:k + 1],
                                        lnout[k][:, 511:512])
                ins.then_inc(S_['tm'], 1)

            def col_sync(s):
                s.wait_ge(S_['tm'], tm_b + 1)
                s.dma_start(out=dr['l1col_in'], in_=l1col_sb[:, :]
                            ).then_inc(S_['st'], 16)

            def col_g(g):
                g.wait_ge(S_['st'], st_b + 16)
                g.collective_compute(
                    "AllGather", ALU.bypass,
                    replica_groups=[[0, 1], [2, 3], [4, 5], [6, 7]],
                    ins=[dr['l1col_in'].opt()],
                    outs=[dr['l1col_out'].opt()],
                ).then_inc(S_['cc2'], 1)
            C_.bump('tm', 1)
            C_.bump('st', 16)
            C_.bump('cc2', 1)
            PL_.add(RawPhase({'vector': col_v, 'sync': col_sync,
                              'gpsimd': col_g}))

        io2 = {k: dr[k] for k in ('G', 'P', 'M1', 'xT', 'ln1',
                                  'gend_out', 'cflag')}
        io2['wout'] = wsel('wout')
        io2['ccol'] = None
        io2['cc1_n'] = 1
        stage(emit_a2, io2)

        io3 = {k: dr[k] for k in ('ln1', 'mur', 'muk',
                                  'ln2', 'l1col_out', 'cflag')}
        for k in ('wrT', 'wkT', 'wvT'):
            io3[k] = wsel(k)
        io3['pcol'] = None
        io3['cc2_n'] = 1
        stage(_emit_B, io3)

        def emit_btm(nc_, e_, S_, C_, PL_, io_):
            tms_tot = _emit_fm_to_tm(nc_, e_, S_, C_, PL_, dr['ln2'],
                                     dr['ln2tm'], dr['ident'], D, TLOC)
            zr = e_(nc.sbuf_tensor("zrow_b", [1, D], F32))
            st_b = C_.g('st')

            def zrow_v(v):
                v.memset(zr[0:1, :], 0.0).then_inc(S_['tm'], 1)

            def zrow_s(s, tm_n=C_.g('tm') + 1):
                s.wait_ge(S_['tm'], tm_n)
                s.dma_start(out=dr['ln2_all'][T:T + 1, :], in_=zr[0:1, :]
                            ).then_inc(S_['st'], 16)

            def ag_g(g, tms_tot=tms_tot, st_n=st_b + 16):
                g.wait_ge(S_['tms'], tms_tot)
                g.wait_ge(S_['st'], st_n)
                g.collective_compute(
                    "AllGather", ALU.bypass,
                    replica_groups=[[0, 1], [2, 3], [4, 5], [6, 7]],
                    ins=[dr['ln2tm'].opt()],
                    outs=[dr['ln2_all'][0:T, :].opt()],
                ).then_inc(S_['cc3'], 1)
            C_.bump('tm', 1)
            C_.bump('st', 16)
            C_.bump('cc3', 1)
            PL_.add(RawPhase({'vector': zrow_v, 'sync': zrow_s,
                              'gpsimd': ag_g}))

        stage(emit_btm, {})

        io4 = dict(xT=None, winT=wsel('winT'), wxT=wsel('wxT'),
                   wdtT=wsel('wdtT'),
                   convw=dr['convwc'], convb=dr['convbc'], bdt=dr['bdtc'],
                   u=dr['u'], sz=dr['sz'], delta=dr['delta'], bc=dr['bc'],
                   ln2_all=dr['ln2_all'], idxC=dr['idxC'], ident=dr['ident'],
                   cc3_n=1)
        stage(_emit_C, io4)

        io5 = {k: dr[k] for k in ('u', 'sz', 'delta', 'bc', 'alogc', 'dpc',
                                  'oh32', 'od')}
        io5['woutT'] = wsel('woutT')
        io5['i128'] = dr['ident']
        stage(_emit_D, io5)

        def emit_dtm(nc_, e_, S_, C_, PL_, io_):
            tms_tot = _emit_fm_to_tm(nc_, e_, S_, C_, PL_, dr['od'],
                                     dr['odtm'], dr['ident'], D, T)

            def ag_g(g, tms_tot=tms_tot):
                g.wait_ge(S_['tms'], tms_tot)
                g.collective_compute(
                    "AllGather", ALU.bypass,
                    replica_groups=[[0, 1], [2, 3], [4, 5], [6, 7]],
                    ins=[dr['odtm'].opt()],
                    outs=[dr['odall'].opt()],
                ).then_inc(S_['cc4'], 1)
            C_.bump('cc4', 1)
            PL_.add(RawPhase({'gpsimd': ag_g}))

        stage(emit_dtm, {})

        io6 = dict(fwdh=None, bwdh=None, ln2=dr['ln2'],
                   wprojT=wsel('wprojT'),
                   bprojc=dr['bprojc'], ln3=dr['ln3'], odall=dr['odall'],
                   idxEf=dr['idxEf'], idxEb=dr['idxEb'], ident=dr['ident'],
                   cc4_n=1)
        stage(_emit_E, io6)

        io7 = {k: dr[k] for k in ('ln3', 's1c', 's2c', 'sh1c', 'sh2c', 's3c',
                                  'h0d', 'h1d', 'h2d')}
        for k in ('w1T', 'w2T', 'h1T', 'h2T', 'w3T'):
            io7[k] = wsel(k)
        io7['y'] = y_o
        stage(_emit_F, io7)
        MM_WCFG.clear()
    return nc


def host_prep_fused(inp):
    m1 = host_prep_A1(inp)
    x = np.asarray(inp["L0_attn_x"], np.float32)
    wout = f32c(inp["L0_attn_W_out"])
    wrT = f32c(np.asarray(inp["L0_ffn_W_r"], np.float32).T)
    wkT = f32c(np.asarray(inp["L0_ffn_W_k"], np.float32).T)
    wvT = f32c(np.asarray(inp["L0_ffn_W_v"], np.float32).T)
    mur = cols128(inp["L0_ffn_mu_r"])
    muk = cols128(inp["L0_ffn_mu_k"])
    wprojT = f32c(np.asarray(inp["L1_attn_wproj"], np.float32).T)
    bprojc = cols128(inp["L1_attn_bproj"])
    eye = np.eye(128, dtype=np.float32)
    oh = np.zeros((32, 32 * 128), np.float32)
    for n in range(32):
        oh[n, n * 128:(n + 1) * 128] = 1.0

    def colsN(v):
        return f32c(np.asarray(v, np.float32).reshape(-1, 128).T)
    fshared = dict(
        w1T=f32c(np.asarray(inp["L1_ffn_w1_w"], np.float32).T),
        w2T=f32c(np.asarray(inp["L1_ffn_w2_w"], np.float32).T),
        h1T=f32c(np.asarray(inp["L1_ffn_h1_w"], np.float32).T),
        h2T=f32c(np.asarray(inp["L1_ffn_h2_w"], np.float32).T),
        w3T=f32c(np.asarray(inp["L1_ffn_w3_w"], np.float32).T),
        s1c=colsN(inp["L1_ffn_w1_s"]), s2c=colsN(inp["L1_ffn_w2_s"]),
        sh1c=colsN(inp["L1_ffn_h1_s"]), sh2c=colsN(inp["L1_ffn_h2_s"]),
        s3c=colsN(inp["L1_ffn_w3_s"]),
    )
    maps = []
    for c in range(NCORES):
        b, h = c // 2, c % 2
        dirb = c % 2
        sfx = 'b' if dirb else 'f'
        t0 = h * TLOC
        idxC = np.empty((128, 33), np.int32)
        for j in range(33):
            pos = j * 128 + np.arange(128)
            tau = pos - 4
            if dirb == 0:
                row = np.where((tau >= 0) & (tau < T), tau, T)
            else:
                row = np.where((tau >= 0) & (tau < T), (T - 1) - tau, T)
            idxC[:, j] = row
        idxEf = np.empty((128, 16), np.int32)
        idxEb = np.empty((128, 16), np.int32)
        for j in range(16):
            t_ = t0 + j * 128 + np.arange(128)
            idxEf[:, j] = t_
            idxEb[:, j] = T + (T - 1) - t_
        alog = np.asarray(inp[f"L1_attn_Alog_{sfx}"], np.float32)
        m = dict(m1[c])
        m.update(dict(
            xT=f32c(x[b, t0:t0 + TLOC, :].T), wout=wout,
            cflag=np.full((128, 1), float(h), np.float32),
            mur=mur, muk=muk, wrT=wrT, wkT=wkT, wvT=wvT,
            winT=f32c(np.asarray(inp[f"L1_attn_win_{sfx}"], np.float32).T),
            wdtT=f32c(np.asarray(inp[f"L1_attn_wdt_{sfx}"], np.float32).T),
            convwc=f32c(np.asarray(inp[f"L1_attn_wconv_{sfx}"], np.float32)
                        .reshape(DI, 4).reshape(16, 128, 4)
                        .transpose(1, 0, 2).reshape(128, 64)),
            convbc=cols128(inp[f"L1_attn_bconv_{sfx}"]),
            bdtc=cols128(inp[f"L1_attn_bdt_{sfx}"]),
            ident=eye, alogc=f32c(alog.reshape(16, 128, 16)
                                  .transpose(1, 0, 2).reshape(128, 256)),
            dpc=cols128(inp[f"L1_attn_Dp_{sfx}"]),
            woutT=f32c(np.asarray(inp[f"L1_attn_wout_{sfx}"], np.float32).T),
            oh32=oh, wprojT=wprojT, bprojc=bprojc,
            idxC=idxC, idxEf=f32c(idxEf).astype(np.int32),
            idxEb=f32c(idxEb).astype(np.int32),
        ))
        m['idxC'] = np.ascontiguousarray(idxC)
        m['idxEf'] = np.ascontiguousarray(idxEf)
        m['idxEb'] = np.ascontiguousarray(idxEb)
        wxf = np.asarray(inp[f"L1_attn_wx_{sfx}"], np.float32)
        wxT = np.zeros((DI, 128), np.float32)
        wxT[:, 0:32] = wxf[64:96].T
        wxT[:, 64:128] = wxf[0:64].T
        m['wxT'] = f32c(wxT)
        m.update(fshared)
        maps.append(m)
    return maps


def kernel_fused(**inputs):
    nc = build_fused()
    maps = host_prep_fused(inputs)
    res = run_bass_kernel_spmd(nc, maps, core_ids=list(range(NCORES)))
    out = np.empty((B, T, D), np.float32)
    for c in range(NCORES):
        b, h = c // 2, c % 2
        out[b, h * TLOC:(h + 1) * TLOC, :] = res.results[c]["y"].T
    return out


# version 08 (final)
# speedup vs baseline: 1.1224x; 1.1224x over previous
import numpy as np
import ml_dtypes
from contextlib import ExitStack

import concourse.bass as bass
import concourse.mybir as mybir
from concourse.bass_utils import run_bass_kernel_spmd

F32 = mybir.dt.float32
BF16 = mybir.dt.bfloat16
I32 = mybir.dt.int32
AF = mybir.ActivationFunctionType
ALU = mybir.AluOpType
AX = mybir.AxisListType.X
BF16NP = ml_dtypes.bfloat16

B, T, D = 4, 4096, 1024
DI = 2048
DS = 16
TLOC = 2048
NCORES = 8
TC = 512


def f32c(x):
    return np.ascontiguousarray(x, dtype=np.float32)


def cols128(v):
    return f32c(np.asarray(v, np.float32).reshape(-1, 128).T)


def ceil_div(a, b):
    return (a + b - 1) // b


class Cnt(dict):
    def bump(self, k, n=1):
        self[k] = self.get(k, 0) + n
        return self[k]

    def g(self, k):
        return self.get(k, 0)


class PhaseList:
    def __init__(self):
        self.phases = []

    def add(self, p):
        self.phases.append(p)
        return p

    def emit_all(self, block):
        for engname in ('gpsimd', 'sync', 'tensor', 'scalar', 'vector'):
            def mk(en):
                def run(eng):
                    for p in self.phases:
                        p.emit(en, eng)
                return run
            getattr(block, engname)(mk(engname))


class RawPhase:
    def __init__(self, emitters):
        self.emitters = emitters

    def emit(self, engname, eng):
        f = self.emitters.get(engname)
        if f is not None:
            f(eng)


MM_WCFG = {}


class MMPhase:
    def __init__(self, S, C, wT_dram, K, N, cols, act_fn, wpanel, psums,
                 evac_fn, evac_engines=('scalar',), c_off=0, part_base=0,
                 extra_tensor_waits=(), wq='gpsimd'):
        self.wq = wq
        self.wc_n = None
        nm = getattr(wT_dram, 'name', None)
        if nm in MM_WCFG:
            self.wq = 'gpsimd'
            self.wc_n = MM_WCFG[nm]
        self.S, self.C = S, C
        self.wT, self.K, self.N = wT_dram, K, N
        self.act_fn, self.wpanel, self.psums = act_fn, wpanel, psums
        self.evac_fn = evac_fn
        self.evac_engines = evac_engines
        self.pb = part_base
        self.kt = ceil_div(K, 128)
        self.supers = [(n0, min(512, N - n0)) for n0 in range(0, N, 512)]
        self.cgs = [(c0 + c_off, min(512, cols - c0))
                    for c0 in range(0, cols, 512)]
        self.groups = []
        for si, (n0, nw) in enumerate(self.supers):
            for nt in range(ceil_div(nw, 128)):
                nr = min(128, nw - nt * 128)
                for (c0, cw) in self.cgs:
                    self.groups.append((si, n0 + nt * 128, nr, c0, cw))
        self.extra_tw = list(extra_tensor_waits)
        self.base_wl = [C.g('wl0'), C.g('wl1')]
        self.base_mm = C.g('mm')
        self.base_ev = C.g('ev')
        ns = len(self.supers)
        C.bump('wl0', 16 * self.kt * ((ns + 1) // 2))
        C.bump('wl1', 16 * self.kt * (ns // 2))
        C.bump('mm', len(self.groups))
        C.bump('ev', len(self.groups))

    def ktile_rows(self, k):
        return min(128, self.K - k * 128)

    def mm_thru_super(self, si):
        return self.base_mm + sum(1 for g in self.groups if g[0] <= si)

    def emit(self, engname, eng):
        if engname == self.wq:
            self._emit_gpsimd(eng)
        if engname == 'tensor':
            self._emit_tensor(eng)
        if engname in self.evac_engines:
            self._emit_evac(engname, eng)

    def _emit_gpsimd(self, g):
        S = self.S
        if self.wc_n is not None:
            g.wait_ge(S['wc'], self.wc_n)
        g.wait_ge(S['mm'], self.base_mm)
        for si, (n0, nw) in enumerate(self.supers):
            if si >= 2:
                g.wait_ge(S['mm'], self.mm_thru_super(si - 2))
            for k in range(self.kt):
                kr = self.ktile_rows(k)
                g.dma_start(
                    out=self.wpanel[si % 2][self.pb:self.pb + kr,
                                            k * 512:k * 512 + nw],
                    in_=self.wT[k * 128:k * 128 + kr, n0:n0 + nw],
                ).then_inc(S['wl0' if si % 2 == 0 else 'wl1'], 16)

    def _emit_tensor(self, t):
        S = self.S
        t.wait_ge(S['ev'], self.base_ev)
        for sem, val in self.extra_tw:
            t.wait_ge(S[sem], val)
        nps = len(self.psums)
        cur_si = -1
        for gi, (si, n_abs, nr, c0, cw) in enumerate(self.groups):
            if si != cur_si:
                par = si % 2
                t.wait_ge(S['wl0' if par == 0 else 'wl1'],
                          self.base_wl[par] + 16 * self.kt * (si // 2 + 1))
                cur_si = si
            ps = self.psums[gi % nps]
            if gi >= nps:
                t.wait_ge(S['ev'], self.base_ev + gi - nps + 1)
            panel = self.wpanel[si % 2]
            n_loc = n_abs - self.supers[si][0]
            for k in range(self.kt):
                kr = self.ktile_rows(k)
                ins = t.matmul(
                    ps[0:nr, 0:cw],
                    panel[self.pb:self.pb + kr,
                          k * 512 + n_loc:k * 512 + n_loc + nr],
                    self.act_fn(k)[:, c0:c0 + cw],
                    start=(k == 0), stop=(k == self.kt - 1),
                )
                if k == self.kt - 1:
                    ins.then_inc(S['mm'], 1)

    def _emit_evac(self, engname, eng):
        S = self.S
        nps = len(self.psums)
        for gi, (si, n_abs, nr, c0, cw) in enumerate(self.groups):
            eng.wait_ge(S['mm'], self.base_mm + gi + 1)
            self.evac_fn(engname, eng, self.psums[gi % nps],
                         n_abs, nr, c0, cw, gi)


class LNPhase:
    def __init__(self, S, C, s_bf_fn, s_fn, out_fn, cw, aux, pre=None):
        self.S, self.C = S, C
        self.s_bf_fn, self.s_fn, self.out_fn = s_bf_fn, s_fn, out_fn
        self.cw, self.aux = cw, aux
        self.pre = pre or {}
        self.base_sq = C.g('sq'); C.bump('sq', 8)
        self.base_lnm = C.g('lnm'); C.bump('lnm', 1)
        self.base_lnv = C.g('lnv'); C.bump('lnv', 1)
        self.base_lns = C.g('lns'); C.bump('lns', 1)
        self.base_lnb = C.g('lnb'); C.bump('lnb', 2)
        self.base_lne = C.g('lne'); C.bump('lne', 8)
        self.base_squ = C.g('squ'); C.bump('squ', 8)

    def emit(self, engname, eng):
        S, A, cw = self.S, self.aux, self.cw
        for sem, val in self.pre.get(engname, ()):
            eng.wait_ge(S[sem], val)
        if engname == 'scalar':
            for k in range(8):
                if k >= 2:
                    eng.wait_ge(S['squ'], self.base_squ + k - 1)
                eng.activation(A['sq_bf'][k % 2][:, 0:cw],
                               self.s_bf_fn(k)[:, 0:cw],
                               AF.Square).then_inc(S['sq'], 1)
            eng.wait_ge(S['lnv'], self.base_lnv + 1)
            eng.activation(A['row_t'][0:1, 0:cw], A['row_v'][0:1, 0:cw],
                           AF.Ln, bias=A['eps'][0:1, 0:1])
            eng.activation(A['row_r'][0:1, 0:cw], A['row_t'][0:1, 0:cw],
                           AF.Exp, bias=0.0, scale=A['neghalf'][0:1, 0:1]
                           ).then_inc(S['lns'], 1)
        elif engname == 'tensor':
            eng.wait_ge(S['lnv'], self.base_lnv)
            eng.wait_ge(S['lne'], self.base_lne)
            for k in range(8):
                ins = eng.matmul(A['ps_m'][0:1, 0:cw], A['ones128_bf'][:, 0:1],
                                 self.s_bf_fn(k)[:, 0:cw],
                                 start=(k == 0), stop=(k == 7))
                if k == 7:
                    ins.then_inc(S['lnm'], 1)
            for k in range(8):
                eng.wait_ge(S['sq'], self.base_sq + k + 1)
                ins = eng.matmul(A['ps_q'][0:1, 0:cw], A['ones128_bf'][:, 0:1],
                                 A['sq_bf'][k % 2][:, 0:cw],
                                 start=(k == 0), stop=(k == 7))
                ins.then_inc(S['squ'], 1)
            eng.wait_ge(S['lns'], self.base_lns + 1)
            eng.matmul(A['ps_mb'][:, 0:cw], A['ones1_f32'][0:1, :],
                       A['row_m'][0:1, 0:cw], start=True, stop=True
                       ).then_inc(S['lnb'], 1)
            eng.matmul(A['ps_rb'][:, 0:cw], A['ones1_f32'][0:1, :],
                       A['row_r'][0:1, 0:cw], start=True, stop=True
                       ).then_inc(S['lnb'], 1)
        elif engname == 'vector':
            eng.wait_ge(S['lnm'], self.base_lnm + 1)
            eng.wait_ge(S['squ'], self.base_squ + 8)
            eng.tensor_scalar_mul(A['row_m'][0:1, 0:cw], A['ps_m'][0:1, 0:cw],
                                  1.0 / 1024.0)
            eng.tensor_scalar_mul(A['row_v'][0:1, 0:cw], A['ps_q'][0:1, 0:cw],
                                  1.0 / 1024.0)
            eng.tensor_mul(A['row_t'][0:1, 0:cw], A['row_m'][0:1, 0:cw],
                           A['row_m'][0:1, 0:cw])
            eng.tensor_sub(A['row_v'][0:1, 0:cw], A['row_v'][0:1, 0:cw],
                           A['row_t'][0:1, 0:cw]).then_inc(S['lnv'], 1)
            eng.wait_ge(S['lnb'], self.base_lnb + 2)
            for k in range(8):
                eng.tensor_sub(A['nrm'][k % 2][:, 0:cw], self.s_fn(k)[:, 0:cw],
                               A['ps_mb'][:, 0:cw])
                eng.tensor_mul(self.out_fn(k)[:, 0:cw],
                               A['nrm'][k % 2][:, 0:cw],
                               A['ps_rb'][:, 0:cw]).then_inc(S['lne'], 1)


A1_SEMS = ('wl0', 'wl1', 'mm', 'ev', 'xl', 'cl', 'cv', 'xs', 'dv',
           'sb', 'sbr', 'sbc', 'st', 'sm0', 'sm1')


def _emit_A1(nc, e, S, C, PL, io):
    xw = io['xw']; w_in = io['w_in']; wx_main = io['wx_main']
    wx_small = io['wx_small']; w_dt = io['w_dt']; convw_d = io['convw']
    convb_d = io['convb']; bdt_d = io['bdt']; selb_d = io['selb']
    G_o = io['G']; P_o = io['P']; M1_o = io['M1']
    if True:
        x_bf = [e(nc.sbuf_tensor(f"xbf{k}", [128, 516], BF16)) for k in range(8)]
        xin = [e(nc.sbuf_tensor(f"xin{j}", [128, 516], F32)) for j in range(16)]
        sres = [e(nc.sbuf_tensor(f"sres{j}", [128, 516], F32)) for j in range(16)]
        xc32 = [e(nc.sbuf_tensor(f"xc32{j}", [128, 512], F32)) for j in range(16)]
        xcbf = [e(nc.sbuf_tensor(f"xcbf{j}", [128, 512], BF16)) for j in range(16)]
        wpanel = [e(nc.sbuf_tensor(f"wp{i}", [128, 16 * 512], BF16)) for i in range(2)]
        small32 = e(nc.sbuf_tensor("small32", [32, 512], F32))
        small32_bf = e(nc.sbuf_tensor("small32bf", [32, 512], BF16))
        sb_row = e(nc.sbuf_tensor("sbrow", [1, 512], F32))
        ctap = [e(nc.sbuf_tensor(f"ctap{i}", [128, 512], F32)) for i in range(2)]
        m1buf = [e(nc.sbuf_tensor(f"m1b{i}", [128, 512], F32)) for i in range(2)]
        tA = [e(nc.sbuf_tensor(f"tA{i}", [128, 512], F32)) for i in range(2)]
        tB = [e(nc.sbuf_tensor(f"tB{i}", [128, 512], F32)) for i in range(2)]
        dbuf = [e(nc.sbuf_tensor(f"dbuf{i}", [128, 512], F32)) for i in range(2)]
        ctbuf = [e(nc.sbuf_tensor(f"ctb{i}", [128, 512], F32)) for i in range(2)]
        gbuf = [e(nc.sbuf_tensor(f"gbuf{i}", [128, 512], F32)) for i in range(2)]
        pbuf = [e(nc.sbuf_tensor(f"pbuf{i}", [128, 512], F32)) for i in range(2)]
        gcarry = e(nc.sbuf_tensor("gcarry", [128, 16], F32))
        pcarry = e(nc.sbuf_tensor("pcarry", [128, 16], F32))
        zeros512 = e(nc.sbuf_tensor("zeros512", [128, 512], F32))
        convw_t = e(nc.sbuf_tensor("convwt", [128, 64], F32))
        convb_t = e(nc.sbuf_tensor("convbt", [128, 16], F32))
        bdt_t = e(nc.sbuf_tensor("bdtt", [128, 16], F32))
        selb_bf = e(nc.sbuf_tensor("selbbf", [32, 1], BF16))
        ones1 = e(nc.sbuf_tensor("ones1", [1, 128], F32))
        psA = [e(nc.psum_tensor(f"psA{i}", [128, 512], F32)) for i in range(2)]
        ps_sb = e(nc.psum_tensor("pssb", [1, 512], F32))
        ps_sbc = e(nc.psum_tensor("pssbc", [128, 512], F32))

        def consts_g(g):
            g.dma_start(out=convw_t[:, :], in_=convw_d).then_inc(S['cl'], 16)
            g.dma_start(out=convb_t[:, :], in_=convb_d).then_inc(S['cl'], 16)
            g.dma_start(out=bdt_t[:, :], in_=bdt_d).then_inc(S['cl'], 16)
            g.dma_start(out=selb_bf[:, :], in_=selb_d).then_inc(S['cl'], 16)

        def consts_v(v):
            v.memset(ones1[0:1, :], 1.0)
            v.memset(zeros512[:, :], 0.0)
            v.memset(gcarry[:, :], 0.0)
            v.memset(pcarry[:, :], 1.0)
            v.wait_ge(S['cl'], 64)

        def consts_s(s):
            s.wait_ge(S['cl'], 64)

        def consts_t(t):
            t.wait_ge(S['cl'], 64)
        C.bump('cl', 64)
        PL.add(RawPhase({'gpsimd': consts_g, 'vector': consts_v,
                         'scalar': consts_s, 'tensor': consts_t}))

        for ci in range(4):
            c0d = ci * 512

            prev_mm = C.g('mm')

            def loadx(g, ci=ci, prev_mm=prev_mm):
                if ci > 0:
                    g.wait_ge(S['mm'], prev_mm)
                for k in range(8):
                    g.dma_start(out=x_bf[k][:, :],
                                in_=xw[k * 128:(k + 1) * 128,
                                       ci * 512:ci * 512 + 516]
                                ).then_inc(S['xl'], 16)
            C.bump('xl', 128)
            xl_after = C.g('xl')
            PL.add(RawPhase({'gpsimd': loadx}))

            st_mm1 = {'conv_j': 0, 'silu_j': 0}
            base_cv = C.g('cv')
            base_xs = C.g('xs')

            def mk_mm1_evac(base_cv=base_cv, base_xs=base_xs, st=st_mm1,
                            ph_holder=[]):
                def evac(engname, eng, ps, n_abs, nr, c0, cw, gi):
                    ph = ph_holder[0]
                    if n_abs < DI:
                        j = n_abs // 128
                        if engname == 'scalar':
                            eng.activation(xin[j][:, c0:c0 + cw], ps[0:nr, 0:cw],
                                           AF.Copy).then_inc(S['ev'], 1)
                            if c0 == 512:
                                if j >= 1:
                                    eng.wait_ge(S['cv'], base_cv + j)
                                    eng.activation(xc32[j - 1][:, :],
                                                   ctap[(j - 1) % 2][:, :],
                                                   AF.Silu,
                                                   bias=convb_t[:, j - 1:j])
                                    eng.activation(xcbf[j - 1][:, :],
                                                   xc32[j - 1][:, :],
                                                   AF.Copy).then_inc(S['xs'], 1)
                        elif engname == 'vector':
                            if c0 == 512:
                                j = n_abs // 128
                                eng.wait_ge(S['ev'], ph.base_ev + gi + 1)
                                if j >= 2:
                                    eng.wait_ge(S['xs'], base_xs + j - 1)
                                eng.tensor_scalar_mul(
                                    ctap[j % 2][:, :], xin[j][:, 1:513],
                                    convw_t[:, 4 * j:4 * j + 1])
                                for k in range(1, 4):
                                    ins = eng.scalar_tensor_tensor(
                                        ctap[j % 2][:, :],
                                        xin[j][:, 1 + k:513 + k],
                                        convw_t[:, 4 * j + k:4 * j + k + 1],
                                        ctap[j % 2][:, :], ALU.mult, ALU.add)
                                ins.then_inc(S['cv'], 1)
                    else:
                        jr = (n_abs - DI) // 128
                        if engname == 'scalar':
                            eng.activation(sres[jr][:, c0:c0 + cw],
                                           ps[0:nr, 0:cw],
                                           AF.Silu).then_inc(S['ev'], 1)
                            if gi == 32:
                                eng.wait_ge(S['cv'], base_cv + 16)
                                eng.activation(xc32[15][:, :], ctap[15 % 2][:, :],
                                               AF.Silu, bias=convb_t[:, 15:16])
                                eng.activation(xcbf[15][:, :], xc32[15][:, :],
                                               AF.Copy).then_inc(S['xs'], 1)
                return evac, ph_holder

            evac1, holder1 = mk_mm1_evac()
            mm1 = MMPhase(S, C, w_in, 1024, 4096, 516, lambda k: x_bf[k][:, :],
                          wpanel, psA, evac1, evac_engines=('scalar', 'vector'),
                          extra_tensor_waits=[('xl', xl_after)])
            holder1.append(mm1)
            C.bump('cv', 16)
            C.bump('xs', 16)
            PL.add(mm1)

            base_st2 = C.g('st')

            base_sm = [C.g('sm0'), C.g('sm1')]

            def mk_mm2_evac(base_sm=base_sm, c0d=c0d, ph_holder=[]):
                def evac(engname, eng, ps, n_abs, nr, c0, cw, gi):
                    j = n_abs // 128
                    if engname == 'vector':
                        par = gi % 2
                        eng.wait_ge(S['sm0' if par == 0 else 'sm1'],
                                    base_sm[par] + 16 * (gi // 2))
                        eng.tensor_mul(m1buf[gi % 2][:, 0:cw], ps[0:nr, 0:cw],
                                       sres[j][:, 4:4 + cw]).then_inc(S['ev'], 1)
                    elif engname == 'sync':
                        eng.wait_ge(S['ev'], ph_holder[0].base_ev + gi + 1)
                        eng.dma_start(out=M1_o[j * 128:(j + 1) * 128,
                                               c0d:c0d + cw],
                                      in_=m1buf[gi % 2][:, 0:cw]
                                      ).then_inc(S['sm0' if gi % 2 == 0 else 'sm1'], 16)
                return evac, ph_holder

            evac2, holder2 = mk_mm2_evac()
            mm2 = MMPhase(S, C, wx_main, DI, DI, 512,
                          lambda k: xcbf[k][:, :], wpanel, psA, evac2,
                          evac_engines=('vector', 'sync'),
                          extra_tensor_waits=[('xs', C.g('xs'))])
            holder2.append(mm2)
            C.bump('sm0', 16 * 8)
            C.bump('sm1', 16 * 8)
            PL.add(mm2)

            def evac3(engname, eng, ps, n_abs, nr, c0, cw, gi):
                eng.activation(small32[0:32, 0:cw], ps[0:32, 0:cw], AF.Copy)
                eng.activation(small32_bf[0:32, 0:cw], ps[0:32, 0:cw],
                               AF.Copy).then_inc(S['ev'], 1)

            mm3 = MMPhase(S, C, wx_small, DI, 32, 512,
                          lambda k: xcbf[k][:, :], wpanel, psA, evac3,
                          evac_engines=('scalar',))
            PL.add(mm3)

            base_dv = C.g('dv')
            sb_n = C.g('sb')

            def sb_t(t, after_ev=C.g('ev'), base_dv=base_dv, sb_n=sb_n):
                t.wait_ge(S['ev'], after_ev)
                t.wait_ge(S['dv'], base_dv)
                t.matmul(ps_sb[0:1, 0:512], selb_bf[0:32, 0:1],
                         small32_bf[0:32, 0:512], start=True, stop=True
                         ).then_inc(S['sb'], 1)
                t.wait_ge(S['sbr'], sb_n + 1)
                t.matmul(ps_sbc[:, 0:512], ones1[0:1, 0:128],
                         sb_row[0:1, 0:512], start=True, stop=True
                         ).then_inc(S['sbc'], 1)

            def sb_s(s, sb_n=sb_n):
                s.wait_ge(S['sb'], sb_n + 1)
                s.activation(sb_row[0:1, 0:512], ps_sb[0:1, 0:512],
                             AF.Copy).then_inc(S['sbr'], 1)
            C.bump('sb', 1)
            C.bump('sbr', 1)
            C.bump('sbc', 1)
            PL.add(RawPhase({'tensor': sb_t, 'scalar': sb_s}))

            base_dv4 = C.g('dv')
            base_st4 = C.g('st')
            sbc_n = C.g('sbc')

            def mk_mm4_evac(base_dv4=base_dv4, base_st4=base_st4, c0d=c0d,
                            sbc_n=sbc_n, ph_holder=[]):
                def evac(engname, eng, ps, n_abs, nr, c0, cw, gi):
                    j = gi
                    if engname == 'scalar':
                        eng.wait_ge(S['dv'], base_dv4 + max(0, j - 1))
                        eng.activation(tA[j % 2][:, 0:cw], ps[0:nr, 0:cw],
                                       AF.Abs, bias=bdt_t[:, j:j + 1])
                        eng.activation(tB[j % 2][:, 0:cw], tA[j % 2][:, 0:cw],
                                       AF.Exp, bias=0.0, scale=-1.0)
                        eng.activation(tA[j % 2][:, 0:cw], tB[j % 2][:, 0:cw],
                                       AF.Ln, bias=1.0)
                        eng.activation(tB[j % 2][:, 0:cw], ps[0:nr, 0:cw],
                                       AF.Relu, bias=bdt_t[:, j:j + 1]
                                       ).then_inc(S['ev'], 1)
                    elif engname == 'vector':
                        eng.wait_ge(S['ev'], ph_holder[0].base_ev + gi + 1)
                        eng.wait_ge(S['sbc'], sbc_n)
                        eng.wait_ge(S['st'], base_st4 + 32 * max(0, j - 1))
                        eng.tensor_add(dbuf[j % 2][:, 0:cw], tA[j % 2][:, 0:cw],
                                       tB[j % 2][:, 0:cw])
                        eng.tensor_mul(ctbuf[j % 2][:, 0:cw], xc32[j][:, 0:cw],
                                       ps_sbc[:, 0:cw])
                        eng.tensor_tensor_scan(
                            gbuf[j % 2][:, 0:cw], dbuf[j % 2][:, 0:cw],
                            ctbuf[j % 2][:, 0:cw], gcarry[:, j:j + 1],
                            ALU.mult, ALU.add)
                        eng.tensor_tensor_scan(
                            pbuf[j % 2][:, 0:cw], dbuf[j % 2][:, 0:cw],
                            zeros512[:, 0:cw], pcarry[:, j:j + 1],
                            ALU.mult, ALU.add)
                        eng.tensor_copy(gcarry[:, j:j + 1],
                                        gbuf[j % 2][:, cw - 1:cw])
                        eng.tensor_copy(pcarry[:, j:j + 1],
                                        pbuf[j % 2][:, cw - 1:cw]
                                        ).then_inc(S['dv'], 1)
                    elif engname == 'sync':
                        eng.wait_ge(S['dv'], base_dv4 + j + 1)
                        eng.dma_start(out=G_o[j * 128:(j + 1) * 128,
                                              c0d:c0d + cw],
                                      in_=gbuf[j % 2][:, 0:cw]
                                      ).then_inc(S['st'], 16)
                        eng.dma_start(out=P_o[j * 128:(j + 1) * 128,
                                              c0d:c0d + cw],
                                      in_=pbuf[j % 2][:, 0:cw]
                                      ).then_inc(S['st'], 16)
                return evac, ph_holder

            evac4, holder4 = mk_mm4_evac()
            mm4 = MMPhase(S, C, w_dt, DS, DI, 512,
                          lambda k: small32_bf[0:16, :], wpanel, psA, evac4,
                          evac_engines=('scalar', 'vector', 'sync'))
            holder4.append(mm4)
            C.bump('dv', 16)
            C.bump('st', 32 * 16)
            PL.add(mm4)

        st_total = C.g('st')
        sm0_t, sm1_t = C.g('sm0'), C.g('sm1')

        def fin(s):
            s.wait_ge(S['st'], st_total)
            s.wait_ge(S['sm0'], sm0_t)
            s.wait_ge(S['sm1'], sm1_t)
        PL.add(RawPhase({'sync': fin}))
        io['_a1'] = dict(gcarry=gcarry)


def build_A1():
    nc = bass.Bass("TRN2", target_bir_lowering=False)
    io = {}
    for nm, shp, knd in (('xw', [D, TLOC + 4], 'ExternalInput'),
                         ('w_in', [D, 2 * DI], 'ExternalInput'),
                         ('wx_main', [DI, DI], 'ExternalInput'),
                         ('wx_small', [DI, 32], 'ExternalInput'),
                         ('w_dt', [DS, DI], 'ExternalInput'),
                         ('convw', [128, 64], 'ExternalInput'),
                         ('convb', [128, 16], 'ExternalInput'),
                         ('bdt', [128, 16], 'ExternalInput'),
                         ('selb', [32, 1], 'ExternalInput'),
                         ('G', [DI, TLOC], 'ExternalOutput'),
                         ('P', [DI, TLOC], 'ExternalOutput'),
                         ('M1', [DI, TLOC], 'ExternalOutput')):
        io[nm] = nc.dram_tensor(nm, shp, F32, kind=knd).ap()
    S = {}
    C = Cnt()
    PL = PhaseList()
    with ExitStack() as cs:
        e = cs.enter_context
        for sname in A1_SEMS:
            S[sname] = e(nc.semaphore(sname))
        _emit_A1(nc, e, S, C, PL, io)
        block = e(nc.Block())
        PL.emit_all(block)
    return nc


def host_prep_A1(inp):
    x = np.asarray(inp["L0_attn_x"], np.float32)
    w_in = f32c(inp["L0_attn_W_in"])
    wx = np.asarray(inp["L0_attn_W_x"], np.float32)
    wx_main = f32c(wx[:, 32:2080])
    wx_small = f32c(np.concatenate([wx[:, 0:16], wx[:, 16:32]], axis=1))
    w_dt = f32c(inp["L0_attn_W_dt"])
    convw = f32c(np.asarray(inp["L0_attn_conv_w"], np.float32)
                 .reshape(DI, 4).reshape(16, 128, 4)
                 .transpose(1, 0, 2).reshape(128, 64))
    convb = cols128(inp["L0_attn_conv_b"])
    bdt = cols128(inp["L0_attn_b_dt"])
    selb = np.zeros((32, 1), np.float32)
    selb[16:32, 0] = 1.0
    maps = []
    for c in range(NCORES):
        b, h = c // 2, c % 2
        t0 = h * TLOC
        xw = np.zeros((D, TLOC + 4), np.float32)
        lo = max(0, t0 - 4)
        xw[:, 4 - (t0 - lo):] = x[b, lo:t0 + TLOC, :].T
        maps.append(dict(xw=f32c(xw), w_in=w_in, wx_main=wx_main,
                         wx_small=wx_small, w_dt=w_dt, convw=convw,
                         convb=convb, bdt=bdt, selb=selb))
    return maps


A2_SEMS = ('wl0', 'wl1', 'mm', 'ev', 'xl0', 'xl1', 'xf', 'cl',
           'yv', 'sv', 'st', 'sq', 'squ', 'lnm', 'lnv', 'lns', 'lnb', 'lne',
           'cc1')


def _emit_A2(nc, e, S, C, PL, io):
    G_i = io['G']; P_i = io['P']; M1_i = io['M1']
    xT_d = io['xT']; wout_d = io['wout']; ln1_o = io['ln1']
    ccol_d = io.get('ccol')
    debug = io.get('debug', False)
    if debug:
        y2_dbg = io['y2dbg']; s_dbg = io['sdbg']; var_dbg = io['vardbg']
    if True:
        grot = [e(nc.sbuf_tensor(f"gr{i}", [128, 512], F32)) for i in range(2)]
        prot = [e(nc.sbuf_tensor(f"pr{i}", [128, 512], F32)) for i in range(2)]
        mrot = [e(nc.sbuf_tensor(f"mr{i}", [128, 512], F32)) for i in range(2)]
        y2t = [e(nc.sbuf_tensor(f"y2t{i}", [128, 512], F32)) for i in range(2)]
        y2bf = [e(nc.sbuf_tensor(f"y2bf{j}", [128, 512], BF16)) for j in range(16)]
        x32 = [e(nc.sbuf_tensor(f"x32{k}", [128, 512], F32)) for k in range(8)]
        s32 = [e(nc.sbuf_tensor(f"s32{k}", [128, 512], F32)) for k in range(8)]
        sbf = [e(nc.sbuf_tensor(f"sbf{k}", [128, 512], BF16)) for k in range(8)]
        lnout = [e(nc.sbuf_tensor(f"lno{k}", [128, 512], F32)) for k in range(8)]
        wpanel = [e(nc.sbuf_tensor(f"wp{i}", [128, 16 * 512], BF16)) for i in range(2)]
        ccol_t = e(nc.sbuf_tensor("ccolt", [128, 16], F32))
        cf_t = e(nc.sbuf_tensor("cft", [128, 1], F32))
        aux = dict(
            ones128_bf=e(nc.sbuf_tensor("o128", [128, 1], BF16)),
            ones1_f32=e(nc.sbuf_tensor("o1f", [1, 128], F32)),
            sq_bf=[e(nc.sbuf_tensor(f"sqbf{i}", [128, 512], BF16)) for i in range(2)],
            row_m=e(nc.sbuf_tensor("row_m", [1, 512], F32)),
            row_v=e(nc.sbuf_tensor("row_v", [1, 512], F32)),
            row_t=e(nc.sbuf_tensor("row_t", [1, 512], F32)),
            row_r=e(nc.sbuf_tensor("row_r", [1, 512], F32)),
            eps=e(nc.sbuf_tensor("cst_eps", [1, 1], F32)),
            neghalf=e(nc.sbuf_tensor("cst_nh", [1, 1], F32)),
            nrm=[e(nc.sbuf_tensor(f"nrm{i}", [128, 512], F32)) for i in range(2)],
            ps_m=e(nc.psum_tensor("psm", [1, 512], F32)),
            ps_q=e(nc.psum_tensor("psq", [1, 512], F32)),
            ps_mb=e(nc.psum_tensor("psmb", [128, 512], F32)),
            ps_rb=e(nc.psum_tensor("psrb", [128, 512], F32)),
        )
        psA = [e(nc.psum_tensor(f"psA{i}", [128, 512], F32)) for i in range(2)]

        base_cl = C.g('cl')
        if ccol_d is not None:
            def consts_g(g, base_cl=base_cl):
                g.dma_start(out=ccol_t[:, :], in_=ccol_d).then_inc(S['cl'], 16)

            def consts_v(v, base_cl=base_cl):
                v.memset(aux['ones1_f32'][0:1, :], 1.0)
                v.memset(aux['ones128_bf'][:, 0:1], 1.0)
                v.memset(aux['eps'][0:1, 0:1], 1e-5)
                v.memset(aux['neghalf'][0:1, 0:1], -0.5)
                v.wait_ge(S['cl'], base_cl + 16)
            C.bump('cl', 16)
            PL.add(RawPhase({'gpsimd': consts_g, 'vector': consts_v}))
        else:
            gout = io['gend_out']; cflag_d = io['cflag']
            cc1_n = io['cc1_n']

            def consts_g(g, base_cl=base_cl):
                g.dma_start(out=cf_t[:, :], in_=cflag_d).then_inc(S['cl'], 16)
                g.wait_ge(S['cc1'], cc1_n)
                g.dma_start(out=ccol_t[:, :], in_=gout[0:128, :]
                            ).then_inc(S['cl'], 16)

            def consts_v(v, base_cl=base_cl):
                v.memset(aux['ones1_f32'][0:1, :], 1.0)
                v.memset(aux['ones128_bf'][:, 0:1], 1.0)
                v.memset(aux['eps'][0:1, 0:1], 1e-5)
                v.memset(aux['neghalf'][0:1, 0:1], -0.5)
                v.wait_ge(S['cl'], base_cl + 32)
                v.tensor_scalar_mul(ccol_t[:, :], ccol_t[:, :], cf_t[:, 0:1]
                                    ).then_inc(S['cl'], 1)
            C.bump('cl', 33)
            PL.add(RawPhase({'gpsimd': consts_g, 'vector': consts_v}))

        for ci in range(4):
            c0d = ci * 512

            base_xl = [C.g('xl0'), C.g('xl1')]
            base_yv = C.g('yv')
            prev_mm = C.g('mm')
            prev_sv = C.g('sv')

            def loads(s, base_yv=base_yv, c0d=c0d, ci=ci, prev_mm=prev_mm,
                      prev_sv=prev_sv):
                if ci > 0:
                    s.wait_ge(S['sv'], prev_sv)
                for j in range(16):
                    if j >= 2:
                        s.wait_ge(S['yv'], base_yv + j - 1)
                    sem = S['xl0' if j % 2 == 0 else 'xl1']
                    s.dma_start(out=grot[j % 2][:, :],
                                in_=G_i[j * 128:(j + 1) * 128, c0d:c0d + 512]
                                ).then_inc(sem, 16)
                    s.dma_start(out=prot[j % 2][:, :],
                                in_=P_i[j * 128:(j + 1) * 128, c0d:c0d + 512]
                                ).then_inc(sem, 16)
                    s.dma_start(out=mrot[j % 2][:, :],
                                in_=M1_i[j * 128:(j + 1) * 128, c0d:c0d + 512]
                                ).then_inc(sem, 16)
                for k in range(8):
                    s.dma_start(out=x32[k][:, :],
                                in_=xT_d[k * 128:(k + 1) * 128, c0d:c0d + 512]
                                ).then_inc(S['xf'], 16)
            C.bump('xl0', 48 * 8)
            C.bump('xl1', 48 * 8)
            C.bump('xf', 16 * 8)
            PL.add(RawPhase({'sync': loads}))

            def y2ph(v, base_xl=base_xl, base_yv=base_yv, prev_mm=prev_mm,
                     ci=ci, cl_now=C.g('cl')):
                if ci == 0:
                    v.wait_ge(S['cl'], cl_now)
                if ci > 0:
                    v.wait_ge(S['mm'], prev_mm)
                for j in range(16):
                    par = j % 2
                    v.wait_ge(S['xl0' if par == 0 else 'xl1'],
                              base_xl[par] + 48 * (j // 2 + 1))
                    v.scalar_tensor_tensor(
                        y2t[j % 2][:, :], prot[j % 2][:, :],
                        ccol_t[:, j:j + 1], grot[j % 2][:, :],
                        ALU.mult, ALU.add)
                    v.tensor_mul(y2bf[j][:, :], y2t[j % 2][:, :],
                                 mrot[j % 2][:, :]).then_inc(S['yv'], 1)
            C.bump('yv', 16)
            PL.add(RawPhase({'vector': y2ph}))

            xl_all = C.g('xf')

            def mk_evac5(ph_holder=[], xl_all=xl_all):
                def evac(engname, eng, ps, n_abs, nr, c0, cw, gi):
                    k = n_abs // 128
                    if engname == 'vector':
                        if gi == 0:
                            eng.wait_ge(S['xf'], xl_all)
                        eng.tensor_add(s32[k][:, 0:cw], ps[0:nr, 0:cw],
                                       x32[k][:, 0:cw]).then_inc(S['ev'], 1)
                        eng.tensor_copy(sbf[k][:, 0:cw], s32[k][:, 0:cw]
                                        ).then_inc(S['sv'], 1)
                return evac, ph_holder

            evac5, holder5 = mk_evac5()
            mm5 = MMPhase(S, C, wout_d, DI, D, 512,
                          lambda k: y2bf[k][:, :], wpanel, psA, evac5,
                          evac_engines=('vector',),
                          extra_tensor_waits=[('yv', C.g('yv'))])
            holder5.append(mm5)
            C.bump('sv', 8)
            PL.add(mm5)

            base_st = C.g('st')
            ln = LNPhase(S, C, lambda k: sbf[k], lambda k: s32[k],
                         lambda k: lnout[k], 512, aux,
                         pre={'tensor': [('sv', C.g('sv'))],
                              'scalar': [('sv', C.g('sv'))],
                              'vector': [('st', base_st)]})
            PL.add(ln)

            def store(s, base=ln.base_lne, c0d=c0d):
                for k in range(8):
                    s.wait_ge(S['lne'], base + k + 1)
                    s.dma_start(out=ln1_o[k * 128:(k + 1) * 128, c0d:c0d + 512],
                                in_=lnout[k][:, :]).then_inc(S['st'], 16)
            C.bump('st', 16 * 8)
            PL.add(RawPhase({'sync': store}))

            if debug:
                def dstore(s, c0d=c0d, lne_end=C.g('lne')):
                    s.wait_ge(S['lne'], lne_end)
                    for j in range(16):
                        s.dma_start(out=y2_dbg[j * 128:(j + 1) * 128,
                                               c0d:c0d + 512],
                                    in_=y2bf[j][:, :]).then_inc(S['st'], 16)
                    for k in range(8):
                        s.dma_start(out=s_dbg[k * 128:(k + 1) * 128,
                                              c0d:c0d + 512],
                                    in_=s32[k][:, :]).then_inc(S['st'], 16)
                    s.dma_start(out=var_dbg[0:1, c0d:c0d + 512],
                                in_=aux['row_m'][0:1, :]).then_inc(S['st'], 16)
                    s.dma_start(out=var_dbg[1:2, c0d:c0d + 512],
                                in_=aux['row_v'][0:1, :]).then_inc(S['st'], 16)
                    s.dma_start(out=var_dbg[2:3, c0d:c0d + 512],
                                in_=aux['row_r'][0:1, :]).then_inc(S['st'], 16)
                C.bump('st', 16 * 27)
                PL.add(RawPhase({'sync': dstore}))

        st_total = C.g('st')
        PL.add(RawPhase({'sync': lambda s: s.wait_ge(S['st'], st_total)}))
        io['_a2'] = dict(lnout=lnout)


def build_A2(debug=False):
    nc = bass.Bass("TRN2", target_bir_lowering=False)
    io = {'debug': debug}
    for nm, shp, knd in (('G', [DI, TLOC], 'ExternalInput'),
                         ('P', [DI, TLOC], 'ExternalInput'),
                         ('M1', [DI, TLOC], 'ExternalInput'),
                         ('ccol', [128, 16], 'ExternalInput'),
                         ('xT', [D, TLOC], 'ExternalInput'),
                         ('wout', [DI, D], 'ExternalInput'),
                         ('ln1', [D, TLOC], 'ExternalOutput')):
        io[nm] = nc.dram_tensor(nm, shp, F32, kind=knd).ap()
    if debug:
        io['y2dbg'] = nc.dram_tensor("y2dbg", [DI, TLOC], BF16, kind="ExternalOutput").ap()
        io['sdbg'] = nc.dram_tensor("sdbg", [D, TLOC], F32, kind="ExternalOutput").ap()
        io['vardbg'] = nc.dram_tensor("vardbg", [4, TLOC], F32, kind="ExternalOutput").ap()
    S = {}
    C = Cnt()
    PL = PhaseList()
    with ExitStack() as cs:
        e = cs.enter_context
        for sname in A2_SEMS:
            S[sname] = e(nc.semaphore(sname))
        _emit_A2(nc, e, S, C, PL, io)
        block = e(nc.Block())
        PL.emit_all(block)
    return nc


def host_prep_A2(inp, a1_results):
    x = np.asarray(inp["L0_attn_x"], np.float32)
    wout = f32c(inp["L0_attn_W_out"])
    maps = []
    for c in range(NCORES):
        b, h = c // 2, c % 2
        t0 = h * TLOC
        if h == 0:
            ccol = np.zeros((128, 16), np.float32)
        else:
            ccol = cols128(a1_results[c - 1]["G"][:, -1])
        maps.append(dict(G=a1_results[c]["G"], P=a1_results[c]["P"],
                         M1=a1_results[c]["M1"], ccol=ccol,
                         xT=f32c(x[b, t0:t0 + TLOC, :].T), wout=wout))
    return maps


B_SEMS = ('wl0', 'wl1', 'mm', 'ev', 'xl', 'cl', 'lp', 'sv', 'st',
          'sq', 'squ', 'lnm', 'lnv', 'lns', 'lnb', 'lne', 'cc2', 'tp')


def _emit_B(nc, e, S, C, PL, io):
    ln1_i = io['ln1']; mur_d = io['mur']; muk_d = io['muk']
    wrT_d = io['wrT']; wkT_d = io['wkT']; wvT_d = io['wvT']
    ln2_o = io['ln2']
    pcol_d = io.get('pcol')
    if True:
        x32 = [e(nc.sbuf_tensor(f"x32{k}", [128, 512], F32)) for k in range(8)]
        xp = [e(nc.sbuf_tensor(f"xp{k}", [128, 512], F32)) for k in range(8)]
        dt_t = [e(nc.sbuf_tensor(f"dt{i}", [128, 512], F32)) for i in range(2)]
        xr32 = [e(nc.sbuf_tensor(f"xr32{i}", [128, 512], F32)) for i in range(2)]
        xrbf = [e(nc.sbuf_tensor(f"xrbf{k}", [128, 512], BF16)) for k in range(8)]
        xkbf = [e(nc.sbuf_tensor(f"xkbf{k}", [128, 512], BF16)) for k in range(8)]
        r32 = [e(nc.sbuf_tensor(f"r32{k}", [128, 512], F32)) for k in range(8)]
        kabf = [e(nc.sbuf_tensor(f"kabf{k}", [128, 512], BF16)) for k in range(8)]
        t32 = [e(nc.sbuf_tensor(f"t32{i}", [128, 512], F32)) for i in range(2)]
        t32v = [e(nc.sbuf_tensor(f"t32v{i}", [128, 512], F32)) for i in range(2)]
        s32 = [e(nc.sbuf_tensor(f"s32{k}", [128, 512], F32)) for k in range(8)]
        sbf = [e(nc.sbuf_tensor(f"sbf{k}", [128, 512], BF16)) for k in range(8)]
        lnout = [e(nc.sbuf_tensor(f"lno{k}", [128, 512], F32)) for k in range(8)]
        wpanel = [e(nc.sbuf_tensor(f"wp{i}", [128, 8 * 512], BF16)) for i in range(2)]
        pcol_t = e(nc.sbuf_tensor("pcolt", [128, 8], F32))
        cf_t = e(nc.sbuf_tensor("cft", [128, 1], F32))
        mur_t = e(nc.sbuf_tensor("murt", [128, 8], F32))
        muk_t = e(nc.sbuf_tensor("mukt", [128, 8], F32))
        aux = dict(
            ones128_bf=e(nc.sbuf_tensor("o128", [128, 1], BF16)),
            ones1_f32=e(nc.sbuf_tensor("o1f", [1, 128], F32)),
            sq_bf=[e(nc.sbuf_tensor(f"sqbf{i}", [128, 512], BF16)) for i in range(2)],
            row_m=e(nc.sbuf_tensor("row_m", [1, 512], F32)),
            row_v=e(nc.sbuf_tensor("row_v", [1, 512], F32)),
            row_t=e(nc.sbuf_tensor("row_t", [1, 512], F32)),
            row_r=e(nc.sbuf_tensor("row_r", [1, 512], F32)),
            eps=e(nc.sbuf_tensor("cst_eps", [1, 1], F32)),
            neghalf=e(nc.sbuf_tensor("cst_nh", [1, 1], F32)),
            nrm=[e(nc.sbuf_tensor(f"nrm{i}", [128, 512], F32)) for i in range(2)],
            ps_m=e(nc.psum_tensor("psm", [1, 512], F32)),
            ps_q=e(nc.psum_tensor("psq", [1, 512], F32)),
            ps_mb=e(nc.psum_tensor("psmb", [128, 512], F32)),
            ps_rb=e(nc.psum_tensor("psrb", [128, 512], F32)),
        )
        psA = [e(nc.psum_tensor(f"psA{i}", [128, 512], F32)) for i in range(2)]

        base_cl = C.g('cl')
        if pcol_d is not None:
            def consts_g(g, base_cl=base_cl):
                g.dma_start(out=pcol_t[:, :], in_=pcol_d).then_inc(S['cl'], 16)
                g.dma_start(out=mur_t[:, :], in_=mur_d).then_inc(S['cl'], 16)
                g.dma_start(out=muk_t[:, :], in_=muk_d).then_inc(S['cl'], 16)

            def consts_v(v, base_cl=base_cl):
                v.memset(aux['ones1_f32'][0:1, :], 1.0)
                v.memset(aux['ones128_bf'][:, 0:1], 1.0)
                v.memset(aux['eps'][0:1, 0:1], 1e-5)
                v.memset(aux['neghalf'][0:1, 0:1], -0.5)
                v.wait_ge(S['cl'], base_cl + 48)
            C.bump('cl', 48)
            PL.add(RawPhase({'gpsimd': consts_g, 'vector': consts_v}))
        else:
            lout = io['l1col_out']; cflag_d = io['cflag']
            cc2_n = io['cc2_n']

            def consts_g(g, base_cl=base_cl):
                g.dma_start(out=cf_t[:, :], in_=cflag_d).then_inc(S['cl'], 16)
                g.dma_start(out=mur_t[:, :], in_=mur_d).then_inc(S['cl'], 16)
                g.dma_start(out=muk_t[:, :], in_=muk_d).then_inc(S['cl'], 16)
                g.wait_ge(S['cc2'], cc2_n)
                g.dma_start(out=pcol_t[:, :], in_=lout[0:128, :]
                            ).then_inc(S['cl'], 16)

            def consts_v(v, base_cl=base_cl):
                v.memset(aux['ones1_f32'][0:1, :], 1.0)
                v.memset(aux['ones128_bf'][:, 0:1], 1.0)
                v.memset(aux['eps'][0:1, 0:1], 1e-5)
                v.memset(aux['neghalf'][0:1, 0:1], -0.5)
                v.wait_ge(S['cl'], base_cl + 64)
                v.tensor_scalar_mul(pcol_t[:, :], pcol_t[:, :], cf_t[:, 0:1]
                                    ).then_inc(S['cl'], 1)
            C.bump('cl', 65)
            PL.add(RawPhase({'gpsimd': consts_g, 'vector': consts_v}))

        for ci in range(4):
            c0d = ci * 512

            base_xl = C.g('xl')
            prev_mm = C.g('mm')

            prev_sv = C.g('sv')
            prev_lne = C.g('lne')

            def loads(s, c0d=c0d, ci=ci, prev_mm=prev_mm, prev_sv=prev_sv,
                      prev_lne=prev_lne):
                if ci > 0:
                    s.wait_ge(S['mm'], prev_mm)
                    s.wait_ge(S['sv'], prev_sv)
                    s.wait_ge(S['lne'], prev_lne)
                for k in range(8):
                    s.dma_start(out=x32[k][:, :],
                                in_=ln1_i[k * 128:(k + 1) * 128, c0d:c0d + 512]
                                ).then_inc(S['xl'], 16)
            C.bump('xl', 16 * 8)
            PL.add(RawPhase({'sync': loads}))

            def lerp(v, base_xl=base_xl, ci=ci, xl_all=C.g('xl'),
                     cl_now=C.g('cl')):
                if ci == 0:
                    v.wait_ge(S['cl'], cl_now)
                v.wait_ge(S['xl'], xl_all)
                for k in range(8):
                    v.tensor_copy(xp[k][:, 1:512], x32[k][:, 0:511])
                    v.tensor_copy(xp[k][:, 0:1], pcol_t[:, k:k + 1])
                    v.tensor_copy(pcol_t[:, k:k + 1], x32[k][:, 511:512])
                    v.tensor_sub(dt_t[k % 2][:, :], x32[k][:, :], xp[k][:, :])
                    v.scalar_tensor_tensor(xr32[k % 2][:, :], dt_t[k % 2][:, :],
                                           mur_t[:, k:k + 1], xp[k][:, :],
                                           ALU.mult, ALU.add)
                    v.tensor_copy(xrbf[k][:, :], xr32[k % 2][:, :])
                    v.scalar_tensor_tensor(xr32[k % 2][:, :], dt_t[k % 2][:, :],
                                           muk_t[:, k:k + 1], xp[k][:, :],
                                           ALU.mult, ALU.add)
                    v.tensor_copy(xkbf[k][:, :], xr32[k % 2][:, :]
                                  ).then_inc(S['lp'], 1)
            C.bump('lp', 8)
            PL.add(RawPhase({'vector': lerp}))

            def evac_r(engname, eng, ps, n_abs, nr, c0, cw, gi):
                eng.activation(r32[n_abs // 128][:, 0:cw], ps[0:nr, 0:cw],
                               AF.Sigmoid).then_inc(S['ev'], 1)

            mmr = MMPhase(S, C, wrT_d, D, D, 512, lambda k: xrbf[k][:, :],
                          wpanel, psA, evac_r, evac_engines=('scalar',),
                          extra_tensor_waits=[('lp', C.g('lp'))])
            PL.add(mmr)

            def evac_k(engname, eng, ps, n_abs, nr, c0, cw, gi):
                k = n_abs // 128
                eng.activation(t32[gi % 2][:, 0:cw], ps[0:nr, 0:cw], AF.Relu)
                eng.activation(kabf[k][:, 0:cw], t32[gi % 2][:, 0:cw],
                               AF.Square).then_inc(S['ev'], 1)

            mmk = MMPhase(S, C, wkT_d, D, D, 512, lambda k: xkbf[k][:, :],
                          wpanel, psA, evac_k, evac_engines=('scalar',))
            PL.add(mmk)

            def evac_v(engname, eng, ps, n_abs, nr, c0, cw, gi):
                k = n_abs // 128
                eng.tensor_mul(t32v[gi % 2][:, 0:cw], ps[0:nr, 0:cw],
                               r32[k][:, 0:cw]).then_inc(S['ev'], 1)
                eng.tensor_add(s32[k][:, 0:cw], t32v[gi % 2][:, 0:cw],
                               x32[k][:, 0:cw])
                eng.tensor_copy(sbf[k][:, 0:cw], s32[k][:, 0:cw]
                                ).then_inc(S['sv'], 1)

            mmv = MMPhase(S, C, wvT_d, D, D, 512, lambda k: kabf[k][:, :],
                          wpanel, psA, evac_v, evac_engines=('vector',))
            C.bump('sv', 8)
            PL.add(mmv)

            base_st = C.g('st')
            ln = LNPhase(S, C, lambda k: sbf[k], lambda k: s32[k],
                         lambda k: lnout[k], 512, aux,
                         pre={'tensor': [('sv', C.g('sv'))],
                              'scalar': [('sv', C.g('sv'))],
                              'vector': [('st', base_st)]})
            PL.add(ln)

            def store(s, base=ln.base_lne, c0d=c0d):
                for k in range(8):
                    s.wait_ge(S['lne'], base + k + 1)
                    s.dma_start(out=ln2_o[k * 128:(k + 1) * 128, c0d:c0d + 512],
                                in_=lnout[k][:, :]).then_inc(S['st'], 16)
            C.bump('st', 16 * 8)
            PL.add(RawPhase({'sync': store}))

        st_total = C.g('st')
        PL.add(RawPhase({'sync': lambda s: s.wait_ge(S['st'], st_total)}))
        io['_b'] = dict(lnout=lnout)


def build_B():
    nc = bass.Bass("TRN2", target_bir_lowering=False)
    io = {}
    for nm, shp, knd in (('ln1', [D, TLOC], 'ExternalInput'),
                         ('pcol', [128, 8], 'ExternalInput'),
                         ('mur', [128, 8], 'ExternalInput'),
                         ('muk', [128, 8], 'ExternalInput'),
                         ('wrT', [D, D], 'ExternalInput'),
                         ('wkT', [D, D], 'ExternalInput'),
                         ('wvT', [D, D], 'ExternalInput'),
                         ('ln2', [D, TLOC], 'ExternalOutput')):
        io[nm] = nc.dram_tensor(nm, shp, F32, kind=knd).ap()
    S = {}
    C = Cnt()
    PL = PhaseList()
    with ExitStack() as cs:
        e = cs.enter_context
        for sname in B_SEMS:
            S[sname] = e(nc.semaphore(sname))
        _emit_B(nc, e, S, C, PL, io)
        block = e(nc.Block())
        PL.emit_all(block)
    return nc


def host_prep_B(inp, a2_results):
    wrT = f32c(np.asarray(inp["L0_ffn_W_r"], np.float32).T)
    wkT = f32c(np.asarray(inp["L0_ffn_W_k"], np.float32).T)
    wvT = f32c(np.asarray(inp["L0_ffn_W_v"], np.float32).T)
    mur = cols128(inp["L0_ffn_mu_r"])
    muk = cols128(inp["L0_ffn_mu_k"])
    maps = []
    for c in range(NCORES):
        h = c % 2
        if h == 0:
            pcol = np.zeros((128, 8), np.float32)
        else:
            pcol = cols128(a2_results[c - 1]["ln1"][:, -1])
        maps.append(dict(ln1=a2_results[c]["ln1"], pcol=pcol, mur=mur,
                         muk=muk, wrT=wrT, wkT=wkT, wvT=wvT))
    return maps


C_SEMS = ('wl0', 'wl1', 'mm', 'ev', 'xl', 'cl', 'cv', 'xs',
          'dv', 'st', 'sd0', 'sd1', 'su0', 'su1', 'sz0', 'sz1', 'cc3', 'gx',
          'gv')


def _emit_C(nc, e, S, C, PL, io):
    xT_d = io.get('xT')
    winT_d = io['winT']; wxT_d = io['wxT']; wdtT_d = io['wdtT']
    convw_d = io['convw']; convb_d = io['convb']; bdt_d = io['bdt']
    u_o = io['u']; sz_o = io['sz']; delta_o = io['delta']; bc_o = io['bc']
    if True:
        x_bf = [e(nc.sbuf_tensor(f"xbf{k}", [128, 516], BF16)) for k in range(8)]
        upre = [e(nc.sbuf_tensor(f"up{j}", [128, 516], F32)) for j in range(16)]
        zbuf = [e(nc.sbuf_tensor(f"zb{i}", [128, 516], BF16)) for i in range(2)]
        u32 = [e(nc.sbuf_tensor(f"u32{i}", [128, 512], F32)) for i in range(2)]
        u_bf = [e(nc.sbuf_tensor(f"ubf{j}", [128, 512], BF16)) for j in range(16)]
        ctap = [e(nc.sbuf_tensor(f"ctap{i}", [128, 512], F32)) for i in range(2)]
        small96 = e(nc.sbuf_tensor("s96", [128, 512], F32))
        small96_bf = e(nc.sbuf_tensor("s96bf", [128, 512], BF16))
        tA = [e(nc.sbuf_tensor(f"tA{i}", [128, 512], F32)) for i in range(2)]
        tB = [e(nc.sbuf_tensor(f"tB{i}", [128, 512], F32)) for i in range(2)]
        dbuf = [e(nc.sbuf_tensor(f"dbuf{i}", [128, 512], F32)) for i in range(2)]
        wpanel = [e(nc.sbuf_tensor(f"wp{i}", [128, 16 * 512], BF16)) for i in range(2)]
        convw_t = e(nc.sbuf_tensor("convwt", [128, 64], F32))
        convb_t = e(nc.sbuf_tensor("convbt", [128, 16], F32))
        bdt_t = e(nc.sbuf_tensor("bdtt", [128, 16], F32))
        psA = [e(nc.psum_tensor(f"psA{i}", [128, 512], F32)) for i in range(2)]
        io['wpanel_out'] = wpanel
        io['psA_out'] = psA
        fused_x = io.get('xT') is None
        if fused_x:
            tokt = [e(nc.sbuf_tensor(f"tok{i}", [128, 1024], BF16))
                    for i in range(4)]
            tokt.append(e(nc.sbuf_tensor("tok4", [4, 1024], BF16)))
            idxc_t = e(nc.sbuf_tensor("idxct", [128, 36], I32))
            ident_t = e(nc.sbuf_tensor("identt", [128, 128], BF16))
            ps_tr = [e(nc.psum_tensor(f"pstr{i}", [128, 128], BF16))
                     for i in range(2)]

        base_cl0 = C.g('cl')

        def consts_g(g, base_cl0=base_cl0):
            g.dma_start(out=convw_t[:, :], in_=convw_d).then_inc(S['cl'], 16)
            g.dma_start(out=convb_t[:, :], in_=convb_d).then_inc(S['cl'], 16)
            g.dma_start(out=bdt_t[:, :], in_=bdt_d).then_inc(S['cl'], 16)
            if fused_x:
                g.dma_start(out=idxc_t[:, 0:33], in_=io['idxC']
                            ).then_inc(S['cl'], 16)
                g.dma_start(out=ident_t[:, :], in_=io['ident']
                            ).then_inc(S['cl'], 16)

        n_cl = 80 if fused_x else 48

        def consts_w(eng, base_cl0=base_cl0, n_cl=n_cl):
            eng.wait_ge(S['cl'], base_cl0 + n_cl)
        C.bump('cl', n_cl)
        io['_c_cl'] = C.g('cl')
        PL.add(RawPhase({'gpsimd': consts_g, 'vector': consts_w,
                         'scalar': consts_w, 'tensor': consts_w}))

        def _c_chunk(ci):
            c0d = ci * 512

            prev_mm = C.g('mm')
            if xT_d is not None:
                def loadx(g, ci=ci, prev_mm=prev_mm):
                    if ci > 0:
                        g.wait_ge(S['mm'], prev_mm)
                    for k in range(8):
                        g.dma_start(out=x_bf[k][:, :],
                                    in_=xT_d[k * 128:(k + 1) * 128,
                                             ci * 512:ci * 512 + 516]
                                    ).then_inc(S['xl'], 16)
                C.bump('xl', 128)
                xl_after = C.g('xl')
                PL.add(RawPhase({'gpsimd': loadx}))
            else:
                src = io['ln2_all']; cc3_n = io['cc3_n']
                base_gx = C.g('gx')
                base_gv = C.g('gv')
                base_xl = C.g('xl')

                def loadg(g, ci=ci, base_gx=base_gx):
                    if ci == 0:
                        g.wait_ge(S['cc3'], cc3_n)
                        g.wait_ge(S['cl'], io['_c_cl'])
                    if ci > 0:
                        g.wait_ge(S['gx'], base_gx)
                    for gI in range(4):
                        g.indirect_dma_start(
                            out=tokt[gI][:, :], out_offset=None, in_=src,
                            in_offset=bass.IndirectOffsetOnAxis(
                                idxc_t[:, 4 * ci + gI:4 * ci + gI + 1], 0),
                        ).then_inc(S['xl'], 16)
                    g.indirect_dma_start(
                        out=tokt[4][0:4, :], out_offset=None, in_=src,
                        in_offset=bass.IndirectOffsetOnAxis(
                            idxc_t[0:4, 4 * ci + 4:4 * ci + 5], 0),
                    ).then_inc(S['xl'], 16)

                def trt(t, ci=ci, base_xl=base_xl, base_gv=base_gv,
                        base_gx=base_gx):
                    t.wait_ge(S['xl'], base_xl + 80)
                    i = 0
                    for gI in range(5):
                        for k in range(8):
                            if i >= 2:
                                t.wait_ge(S['gv'], base_gv + i - 1)
                            if gI < 4:
                                t.transpose(ps_tr[i % 2][:, 0:128],
                                            tokt[gI][:, k * 128:(k + 1) * 128],
                                            ident_t[:, :]).then_inc(S['gx'], 1)
                            else:
                                t.transpose(ps_tr[i % 2][:, 0:4],
                                            tokt[4][0:4, k * 128:(k + 1) * 128],
                                            ident_t[0:4, 0:4]).then_inc(S['gx'], 1)
                            i += 1

                def tre(sc, ci=ci, base_gx=base_gx, base_gv=base_gv):
                    i = 0
                    for gI in range(5):
                        for k in range(8):
                            sc.wait_ge(S['gx'], base_gx + i + 1)
                            if gI < 4:
                                sc.activation(
                                    x_bf[k][:, gI * 128:(gI + 1) * 128],
                                    ps_tr[i % 2][:, 0:128],
                                    AF.Copy).then_inc(S['gv'], 1)
                            else:
                                sc.activation(x_bf[k][:, 512:516],
                                              ps_tr[i % 2][:, 0:4],
                                              AF.Copy).then_inc(S['gv'], 1)
                            i += 1
                C.bump('xl', 80)
                C.bump('gx', 40)
                C.bump('gv', 40)
                xl_after = C.g('xl')
                PL.add(RawPhase({'gpsimd': loadg, 'tensor': trt,
                                 'scalar': tre}))

            base_cv = C.g('cv')
            base_xs = C.g('xs')
            base_su = [C.g('su0'), C.g('su1')]
            base_sz = [C.g('sz0'), C.g('sz1')]

            def mk_evwin(base_cv=base_cv, base_xs=base_xs, base_su=base_su,
                         base_sz=base_sz, c0d=c0d, ph_holder=[]):
                def evac(engname, eng, ps, n_abs, nr, c0, cw, gi):
                    ph = ph_holder[0]
                    if n_abs < DI:
                        j = n_abs // 128
                        if engname == 'scalar':
                            eng.activation(upre[j][:, c0:c0 + cw],
                                           ps[0:nr, 0:cw],
                                           AF.Copy).then_inc(S['ev'], 1)
                            if c0 == 512:
                                if j >= 1:
                                    jj = j - 1
                                    eng.wait_ge(S['cv'], base_cv + j)
                                    par = jj % 2
                                    eng.wait_ge(S['su0' if par == 0 else 'su1'],
                                                base_su[par] + 16 * (jj // 2))
                                    eng.activation(u32[jj % 2][:, :],
                                                   ctap[jj % 2][:, :], AF.Silu,
                                                   bias=convb_t[:, jj:jj + 1])
                                    eng.activation(u_bf[jj][:, :],
                                                   u32[jj % 2][:, :],
                                                   AF.Copy).then_inc(S['xs'], 1)
                        elif engname == 'vector':
                            if c0 == 512:
                                j = n_abs // 128
                                eng.wait_ge(S['ev'], ph.base_ev + gi + 1)
                                if j >= 2:
                                    eng.wait_ge(S['xs'], base_xs + j - 1)
                                eng.tensor_scalar_mul(
                                    ctap[j % 2][:, :], upre[j][:, 1:513],
                                    convw_t[:, 4 * j:4 * j + 1])
                                for k in range(1, 4):
                                    ins = eng.scalar_tensor_tensor(
                                        ctap[j % 2][:, :],
                                        upre[j][:, 1 + k:513 + k],
                                        convw_t[:, 4 * j + k:4 * j + k + 1],
                                        ctap[j % 2][:, :], ALU.mult, ALU.add)
                                ins.then_inc(S['cv'], 1)
                        elif engname == 'sync':
                            if c0 == 512:
                                j = n_abs // 128
                                if j >= 1:
                                    jj = j - 1
                                    eng.wait_ge(S['xs'], base_xs + jj + 1)
                                    eng.dma_start(
                                        out=u_o[jj * 128:(jj + 1) * 128,
                                                c0d:c0d + 512],
                                        in_=u32[jj % 2][:, :]
                                        ).then_inc(S['su0' if jj % 2 == 0
                                                   else 'su1'], 16)
                    else:
                        jr = (n_abs - DI) // 128
                        if engname == 'scalar':
                            if gi == 32:
                                eng.wait_ge(S['cv'], base_cv + 16)
                                eng.wait_ge(S['su1'], base_su[1] + 16 * 7)
                                eng.activation(u32[15 % 2][:, :],
                                               ctap[15 % 2][:, :], AF.Silu,
                                               bias=convb_t[:, 15:16])
                                eng.activation(u_bf[15][:, :],
                                               u32[15 % 2][:, :],
                                               AF.Copy).then_inc(S['xs'], 1)
                            if gi % 2 == 0:
                                par = jr % 2
                                eng.wait_ge(S['sz0' if par == 0 else 'sz1'],
                                            base_sz[par] + 16 * (jr // 2))
                            eng.activation(zbuf[jr % 2][:, c0:c0 + cw],
                                           ps[0:nr, 0:cw],
                                           AF.Silu).then_inc(S['ev'], 1)
                        elif engname == 'sync':
                            if c0 == 512:
                                eng.wait_ge(S['ev'], ph.base_ev + gi + 1)
                                eng.dma_start(
                                    out=sz_o[jr * 128:(jr + 1) * 128,
                                             c0d:c0d + 512],
                                    in_=zbuf[jr % 2][:, 4:516]
                                    ).then_inc(S['sz0' if jr % 2 == 0
                                               else 'sz1'], 16)
                            if gi == 63:
                                eng.wait_ge(S['xs'], base_xs + 16)
                                eng.dma_start(
                                    out=u_o[15 * 128:16 * 128, c0d:c0d + 512],
                                    in_=u32[15 % 2][:, :]
                                    ).then_inc(S['su1'], 16)
                return evac, ph_holder

            evwin, hwin = mk_evwin()
            win_tw = [('xl', xl_after)]
            if fused_x:
                win_tw.append(('gv', C.g('gv')))
            mmwin = MMPhase(S, C, winT_d, D, 2 * DI, 516,
                            lambda k: x_bf[k][:, :], wpanel, psA, evwin,
                            evac_engines=('scalar', 'vector', 'sync'),
                            extra_tensor_waits=win_tw)
            hwin.append(mmwin)
            C.bump('cv', 16)
            C.bump('xs', 16)
            C.bump('su0', 16 * 8)
            C.bump('su1', 16 * 8)
            C.bump('sz0', 16 * 8)
            C.bump('sz1', 16 * 8)
            PL.add(mmwin)

            def evwx(engname, eng, ps, n_abs, nr, c0, cw, gi):
                eng.activation(small96[0:128, 0:cw], ps[0:128, 0:cw], AF.Copy)
                eng.activation(small96_bf[0:128, 0:cw], ps[0:128, 0:cw],
                               AF.Copy).then_inc(S['ev'], 1)

            mmwx = MMPhase(S, C, wxT_d, DI, 128, 512,
                           lambda k: u_bf[k][:, :], wpanel, psA, evwx,
                           evac_engines=('scalar',),
                           extra_tensor_waits=[('xs', C.g('xs'))])
            PL.add(mmwx)

            ev_afterwx = C.g('ev')

            def store_bc(s, ev_afterwx=ev_afterwx, c0d=c0d):
                s.wait_ge(S['ev'], ev_afterwx)
                s.dma_start(out=bc_o[0:32, c0d:c0d + 512],
                            in_=small96[0:32, :]).then_inc(S['st'], 16)
            C.bump('st', 16)
            PL.add(RawPhase({'sync': store_bc}))

            base_dv4 = C.g('dv')
            base_sd = [C.g('sd0'), C.g('sd1')]

            def mk_evdt(base_dv4=base_dv4, base_sd=base_sd, c0d=c0d,
                        ph_holder=[]):
                def evac(engname, eng, ps, n_abs, nr, c0, cw, gi):
                    j = gi
                    if engname == 'scalar':
                        eng.wait_ge(S['dv'], base_dv4 + max(0, j - 1))
                        eng.activation(tA[j % 2][:, 0:cw], ps[0:nr, 0:cw],
                                       AF.Abs, bias=bdt_t[:, j:j + 1])
                        eng.activation(tB[j % 2][:, 0:cw], tA[j % 2][:, 0:cw],
                                       AF.Exp, bias=0.0, scale=-1.0)
                        eng.activation(tA[j % 2][:, 0:cw], tB[j % 2][:, 0:cw],
                                       AF.Ln, bias=1.0)
                        eng.activation(tB[j % 2][:, 0:cw], ps[0:nr, 0:cw],
                                       AF.Relu, bias=bdt_t[:, j:j + 1]
                                       ).then_inc(S['ev'], 1)
                    elif engname == 'vector':
                        eng.wait_ge(S['ev'], ph_holder[0].base_ev + gi + 1)
                        par = j % 2
                        eng.wait_ge(S['sd0' if par == 0 else 'sd1'],
                                    base_sd[par] + 16 * (j // 2))
                        eng.tensor_add(dbuf[j % 2][:, 0:cw], tA[j % 2][:, 0:cw],
                                       tB[j % 2][:, 0:cw]).then_inc(S['dv'], 1)
                    elif engname == 'sync':
                        eng.wait_ge(S['dv'], base_dv4 + j + 1)
                        eng.dma_start(out=delta_o[j * 128:(j + 1) * 128,
                                                  c0d:c0d + 512],
                                      in_=dbuf[j % 2][:, 0:cw]
                                      ).then_inc(S['sd0' if j % 2 == 0
                                                 else 'sd1'], 16)
                return evac, ph_holder

            evdt, hdt = mk_evdt()
            mmdt = MMPhase(S, C, wdtT_d, 64, DI, 512,
                           lambda k: small96_bf[64:128, :], wpanel, psA, evdt,
                           evac_engines=('scalar', 'vector', 'sync'),
                           part_base=64)
            hdt.append(mmdt)
            C.bump('dv', 16)
            C.bump('sd0', 16 * 8)
            C.bump('sd1', 16 * 8)
            PL.add(mmdt)

        io['_c_chunk'] = _c_chunk
        if not io.get('defer_chunks'):
            for ci in range(8):
                _c_chunk(ci)

        if not io.get('defer_chunks'):
            def fin(s, tot={k: C.g(k) for k in ('st', 'sd0', 'sd1', 'su0',
                                                'su1', 'sz0', 'sz1')}):
                for k, v in tot.items():
                    s.wait_ge(S[k], v)
            PL.add(RawPhase({'sync': fin}))


def build_C():
    nc = bass.Bass("TRN2", target_bir_lowering=False)
    io = {}
    for nm, shp, knd in (('xT', [D, T + 4], 'ExternalInput'),
                         ('winT', [D, 2 * DI], 'ExternalInput'),
                         ('wxT', [DI, 128], 'ExternalInput'),
                         ('wdtT', [64, DI], 'ExternalInput'),
                         ('convw', [128, 64], 'ExternalInput'),
                         ('convb', [128, 16], 'ExternalInput'),
                         ('bdt', [128, 16], 'ExternalInput'),
                         ('u', [DI, T], 'ExternalOutput'),
                         ('sz', [DI, T], 'ExternalOutput'),
                         ('delta', [DI, T], 'ExternalOutput'),
                         ('bc', [32, T], 'ExternalOutput')):
        io[nm] = nc.dram_tensor(nm, shp, F32, kind=knd).ap()
    S = {}
    C = Cnt()
    PL = PhaseList()
    with ExitStack() as cs:
        e = cs.enter_context
        for sname in C_SEMS:
            S[sname] = e(nc.semaphore(sname))
        _emit_C(nc, e, S, C, PL, io)
        block = e(nc.Block())
        PL.emit_all(block)
    return nc


def host_prep_C(inp, ln2_by_core):
    maps = []
    for c in range(NCORES):
        b, dirb = c // 2, c % 2
        sfx = 'b' if dirb else 'f'
        full = np.concatenate([ln2_by_core[2 * b], ln2_by_core[2 * b + 1]],
                              axis=1)
        if dirb:
            full = full[:, ::-1]
        xT = np.zeros((D, T + 4), np.float32)
        xT[:, 4:] = full
        wx = np.asarray(inp[f"L1_attn_wx_{sfx}"], np.float32)
        wxT = np.zeros((DI, 128), np.float32)
        wxT[:, 0:32] = wx[64:96].T
        wxT[:, 64:128] = wx[0:64].T
        wxT = f32c(wxT)
        maps.append(dict(
            xT=f32c(xT),
            winT=f32c(np.asarray(inp[f"L1_attn_win_{sfx}"], np.float32).T),
            wxT=wxT,
            wdtT=f32c(np.asarray(inp[f"L1_attn_wdt_{sfx}"], np.float32).T),
            convw=f32c(np.asarray(inp[f"L1_attn_wconv_{sfx}"], np.float32)
                       .reshape(DI, 4).reshape(16, 128, 4)
                       .transpose(1, 0, 2).reshape(128, 64)),
            convb=cols128(inp[f"L1_attn_bconv_{sfx}"]),
            bdt=cols128(inp[f"L1_attn_bdt_{sfx}"]),
        ))
    return maps


D_SEMS = ('wl0', 'wl1', 'mm', 'ev', 'cl', 'bl', 'bm', 'bce',
          'ld0', 'ld1', 'da', 'hv', 'pv', 'ym', 'sc', 'so0', 'so1')


def _emit_D(nc, e, S, C, PL, io):
    u_i = io['u']; sz_i = io['sz']; delta_i = io['delta']; bc_i = io['bc']
    alogc_d = io['alogc']; dpc_d = io['dpc']; woutT_d = io['woutT']
    oh32_d = io['oh32']; i128_d = io['i128']; od_o = io['od']
    io['_d_sc0'] = C.g('sc'); io['_d_da0'] = C.g('da')
    io['_d_pv0'] = C.g('pv'); io['_d_hv0'] = C.g('hv')
    io['_d_ym0'] = C.g('ym')
    if True:
        Bm_w = e(nc.sbuf_tensor("Bmw", [128, 16 * 512], BF16))
        C_w = e(nc.sbuf_tensor("Cw", [128, 16 * 512], BF16))
        bc_bf = e(nc.sbuf_tensor("bcbf", [16, 512], BF16))
        bc_bf2 = e(nc.sbuf_tensor("bcbf2", [16, 512], BF16))
        da_b = [e(nc.sbuf_tensor(f"dab{i}", [128, 512], F32)) for i in range(2)]
        db_b = [e(nc.sbuf_tensor(f"dbb{i}", [128, 512], BF16)) for i in range(2)]
        hb_b = [e(nc.sbuf_tensor(f"hbb{i}", [128, 512], BF16)) for i in range(2)]
        ch_b = [e(nc.sbuf_tensor(f"chb{i}", [128, 512], BF16)) for i in range(2)]
        dl = [e(nc.sbuf_tensor(f"dl{i}", [128, 512], F32)) for i in range(2)]
        ul = [e(nc.sbuf_tensor(f"ul{i}", [128, 512], F32)) for i in range(2)]
        zl = [e(nc.sbuf_tensor(f"zl{i}", [128, 512], BF16)) for i in range(2)]
        duj = [e(nc.sbuf_tensor(f"duj{i}", [128, 512], BF16)) for i in range(2)]
        tp = [e(nc.sbuf_tensor("tp0", [128, 512], F32))] * 2
        ybf = [e(nc.sbuf_tensor(f"ybf{j}", [128, 512], BF16)) for j in range(16)]
        ob = [e(nc.sbuf_tensor(f"ob{i}", [128, 512], io['od'].dtype))
              for i in range(2)]
        hcarry = e(nc.sbuf_tensor("hcarry", [128, 256], F32))
        nA = e(nc.sbuf_tensor("nA", [128, 256], F32))
        dpc_t = e(nc.sbuf_tensor("dpct", [128, 16], F32))
        oh_bf = e(nc.sbuf_tensor("ohbf", [16, 16 * 128], BF16))
        i_bf = e(nc.sbuf_tensor("ibf", [128, 128], BF16))
        if io.get('wpanel') is not None:
            wpanel = io['wpanel']
        else:
            wpanel = [e(nc.sbuf_tensor(f"wp{i}", [128, 16 * 512], BF16))
                      for i in range(2)]
        ps_bc = [e(nc.psum_tensor(f"psbc{i}", [128, 512], F32)) for i in range(2)]
        ps_y = [e(nc.psum_tensor(f"psy{i}", [128, 512], F32)) for i in range(2)]
        if io.get('psA') is not None:
            psA = io['psA']
        else:
            psA = [e(nc.psum_tensor(f"psA{i}", [128, 512], F32))
                   for i in range(2)]

        bcl = C.g('cl')

        atmp = None

        def consts_g(g, bcl=bcl):
            g.dma_start(out=dl[0][:, 0:256], in_=alogc_d).then_inc(S['cl'], 16)
            g.dma_start(out=dpc_t[:, :], in_=dpc_d).then_inc(S['cl'], 16)
            g.dma_start(out=oh_bf[:, :], in_=oh32_d).then_inc(S['cl'], 16)
            g.dma_start(out=i_bf[:, :], in_=i128_d).then_inc(S['cl'], 16)

        def consts_s(s, bcl=bcl):
            s.wait_ge(S['cl'], bcl + 64)
            s.activation(nA[:, :], dl[0][:, 0:256], AF.Exp).then_inc(S['cl'], 1)

        def consts_v(v, bcl=bcl):
            v.memset(hcarry[:, :], 0.0)
            v.wait_ge(S['cl'], bcl + 65)
            v.tensor_scalar_mul(nA[:, :], nA[:, :], -1.0).then_inc(S['cl'], 1)

        def consts_t(t, bcl=bcl):
            t.wait_ge(S['cl'], bcl + 66)
        C.bump('cl', 66)
        io['_d_clnA'] = C.g('cl') - 1
        PL.add(RawPhase({'gpsimd': consts_g, 'scalar': consts_s,
                         'vector': consts_v, 'tensor': consts_t}))

        def _d_chunk(ci):
            c0d = ci * 512
            jj0 = 16 * ci
            gg0 = 256 * ci
            if io.get('defer_chunks'):
                cw = {k: C.g(k) for k in ('sd0', 'sd1', 'su0', 'su1',
                                          'sz0', 'sz1')}
                bc_st = C.g('st')
            else:
                cw = None
                bc_st = None

            def bcload(g, c0d=c0d, bm_prev=C.g('bm'), bc_st=bc_st):
                if bc_st is not None:
                    g.wait_ge(S['st'], bc_st)
                g.wait_ge(S['bm'], bm_prev)
                g.dma_start(out=bc_bf[:, :], in_=bc_i[0:16, c0d:c0d + 512]
                            ).then_inc(S['bl'], 16)
                g.dma_start(out=bc_bf2[:, :], in_=bc_i[16:32, c0d:c0d + 512]
                            ).then_inc(S['bl'], 16)
            C.bump('bl', 32)
            bl_now = C.g('bl')
            PL.add(RawPhase({'gpsimd': bcload}))

            base_bm = C.g('bm')
            base_bce = C.g('bce')

            def bct(t, base_bm=base_bm, base_bce=base_bce, bl_now=bl_now):
                t.wait_ge(S['bl'], bl_now)
                t.wait_ge(S['bce'], base_bce)
                for n in range(32):
                    if n >= 2:
                        t.wait_ge(S['bce'], base_bce + n - 1)
                    src = bc_bf if n < 16 else bc_bf2
                    t.matmul(ps_bc[n % 2][:, :],
                             oh_bf[0:16, (n % 16) * 128:(n % 16 + 1) * 128],
                             src[0:16, :], start=True, stop=True
                             ).then_inc(S['bm'], 1)

            def bcs(s, base_bm=base_bm, ci=ci, hv_prev=C.g('hv')):
                s.wait_ge(S['hv'], hv_prev)
                for n in range(32):
                    s.wait_ge(S['bm'], base_bm + n + 1)
                    tgt = (Bm_w[:, n * 512:(n + 1) * 512] if n < 16 else
                           C_w[:, (n - 16) * 512:(n - 15) * 512])
                    s.activation(tgt, ps_bc[n % 2][:, :], AF.Copy
                                 ).then_inc(S['bce'], 1)
            C.bump('bm', 32)
            C.bump('bce', 32)
            PL.add(RawPhase({'tensor': bct, 'scalar': bcs}))
            bce_now = C.g('bce')

            base_ld = [C.g('ld0'), C.g('ld1')]

            def loads(s, base_sc=C.g('sc') - 16 * ci, c0d=c0d, jj0=jj0,
                      sc0=io['_d_sc0'], cw=cw, ci_=ci, cl_nA=io['_d_clnA']):
                if ci_ == 0:
                    s.wait_ge(S['cl'], cl_nA)
                if cw is not None:
                    for k, v in cw.items():
                        s.wait_ge(S[k], v)
                for j in range(16):
                    if jj0 + j >= 2:
                        s.wait_ge(S['sc'], sc0 + jj0 + j - 1)
                    sem = S['ld0' if j % 2 == 0 else 'ld1']
                    s.dma_start(out=dl[j % 2][:, :],
                                in_=delta_i[j * 128:(j + 1) * 128,
                                            c0d:c0d + 512]).then_inc(sem, 16)
                    s.dma_start(out=ul[j % 2][:, :],
                                in_=u_i[j * 128:(j + 1) * 128,
                                        c0d:c0d + 512]).then_inc(sem, 16)
                    s.dma_start(out=zl[j % 2][:, :],
                                in_=sz_i[j * 128:(j + 1) * 128,
                                         c0d:c0d + 512]).then_inc(sem, 16)
            C.bump('ld0', 48 * 8)
            C.bump('ld1', 48 * 8)
            PL.add(RawPhase({'sync': loads}))

            def scal(s, jj0=jj0, gg0=gg0, base_ld=base_ld, bce_now=bce_now,
                     cl_now=C.g('cl'), hv0=io['_d_hv0']):
                s.wait_ge(S['cl'], cl_now)
                for j in range(16):
                    par = j % 2
                    s.wait_ge(S['ld0' if par == 0 else 'ld1'],
                              base_ld[par] + 48 * (j // 2 + 1))
                    for n in range(16):
                        g = gg0 + j * 16 + n
                        s.wait_ge(S['hv'], hv0 + max(0, g - 1))
                        s.activation(da_b[n % 2][:, :], dl[j % 2][:, :],
                                     AF.Exp, bias=0.0,
                                     scale=nA[:, j * 16 + n:j * 16 + n + 1]
                                     ).then_inc(S['da'], 1)
            C.bump('da', 256)
            PL.add(RawPhase({'scalar': scal}))

            def vec(v, jj0=jj0, gg0=gg0, base_ld=base_ld, bce_now=bce_now,
                    sc0=io['_d_sc0'], da0=io['_d_da0'], pv0=io['_d_pv0'],
                    hv0=io['_d_hv0'], ym0=io['_d_ym0']):
                for j in range(16):
                    par = j % 2
                    v.wait_ge(S['ld0' if par == 0 else 'ld1'],
                              base_ld[par] + 48 * (j // 2 + 1))
                    if j == 0:
                        v.wait_ge(S['bce'], bce_now)
                    v.tensor_mul(duj[j % 2][:, :], dl[j % 2][:, :],
                                 ul[j % 2][:, :])
                    for n in range(16):
                        g = gg0 + j * 16 + n
                        cidx = j * 16 + n
                        v.wait_ge(S['da'], da0 + g + 1)
                        v.wait_ge(S['ym'], ym0 + max(0, g - 1))
                        v.tensor_mul(db_b[n % 2][:, :], duj[j % 2][:, :],
                                     Bm_w[:, n * 512:(n + 1) * 512])
                        v.tensor_tensor_scan(
                            hb_b[n % 2][:, :], da_b[n % 2][:, :],
                            db_b[n % 2][:, :], hcarry[:, cidx:cidx + 1],
                            ALU.mult, ALU.add)
                        v.tensor_copy(hcarry[:, cidx:cidx + 1],
                                      hb_b[n % 2][:, 511:512])
                        v.tensor_mul(ch_b[n % 2][:, :], hb_b[n % 2][:, :],
                                     C_w[:, n * 512:(n + 1) * 512]
                                     ).then_inc(S['hv'], 1)
                    v.wait_ge(S['ym'], ym0 + gg0 + j * 16 + 16)
                    v.scalar_tensor_tensor(tp[j % 2][:, :], ul[j % 2][:, :],
                                           dpc_t[:, j:j + 1], ps_y[j % 2][:, :],
                                           ALU.mult, ALU.add)
                    v.tensor_mul(ybf[j][:, :], tp[j % 2][:, :],
                                 zl[j % 2][:, :]).then_inc(S['sc'], 1)
            C.bump('hv', 256)
            C.bump('sc', 16)
            PL.add(RawPhase({'vector': vec}))

            def tens(t, jj0=jj0, gg0=gg0, sc0=io['_d_sc0'],
                     hv0=io['_d_hv0']):
                for j in range(16):
                    if jj0 + j >= 2:
                        t.wait_ge(S['sc'], sc0 + jj0 + j - 1)
                    for n in range(16):
                        g = gg0 + j * 16 + n
                        t.wait_ge(S['hv'], hv0 + g + 1)
                        t.matmul(ps_y[j % 2][:, :], i_bf[:, :],
                                 ch_b[n % 2][0:128, :],
                                 start=(n == 0), stop=(n == 15)
                                 ).then_inc(S['ym'], 1)
            C.bump('ym', 256)
            PL.add(RawPhase({'tensor': tens}))

            base_so = [C.g('so0'), C.g('so1')]

            def mk_evo(base_so=base_so, c0d=c0d, ph_holder=[]):
                def evac(engname, eng, ps, n_abs, nr, c0, cw, gi):
                    k = n_abs // 128
                    if engname == 'scalar':
                        par = gi % 2
                        eng.wait_ge(S['so0' if par == 0 else 'so1'],
                                    base_so[par] + 16 * (gi // 2))
                        eng.activation(ob[gi % 2][:, 0:cw], ps[0:nr, 0:cw],
                                       AF.Copy).then_inc(S['ev'], 1)
                    elif engname == 'sync':
                        eng.wait_ge(S['ev'], ph_holder[0].base_ev + gi + 1)
                        eng.dma_start(out=od_o[k * 128:(k + 1) * 128,
                                               c0d:c0d + 512],
                                      in_=ob[gi % 2][:, 0:cw]
                                      ).then_inc(S['so0' if gi % 2 == 0
                                                 else 'so1'], 16)
                return evac, ph_holder

            evo, ho = mk_evo()
            mmo = MMPhase(S, C, woutT_d, DI, D, 512,
                          lambda k: ybf[k][:, :], wpanel, psA, evo,
                          evac_engines=('scalar', 'sync'),
                          extra_tensor_waits=[('sc', C.g('sc'))])
            ho.append(mmo)
            C.bump('so0', 16 * 4)
            C.bump('so1', 16 * 4)
            PL.add(mmo)

        io['_d_chunk'] = _d_chunk
        if not io.get('defer_chunks'):
            for ci in range(8):
                _d_chunk(ci)

        def fin_factory():
            def fin(s, t0=C.g('so0'), t1=C.g('so1')):
                s.wait_ge(S['so0'], t0)
                s.wait_ge(S['so1'], t1)
            return fin
        io['_d_fin'] = fin_factory
        if not io.get('defer_chunks'):
            PL.add(RawPhase({'sync': fin_factory()}))


def build_D():
    nc = bass.Bass("TRN2", target_bir_lowering=False)
    io = {}
    for nm, shp, knd in (('u', [DI, T], 'ExternalInput'),
                         ('sz', [DI, T], 'ExternalInput'),
                         ('delta', [DI, T], 'ExternalInput'),
                         ('bc', [32, T], 'ExternalInput'),
                         ('alogc', [128, 256], 'ExternalInput'),
                         ('dpc', [128, 16], 'ExternalInput'),
                         ('woutT', [DI, D], 'ExternalInput'),
                         ('oh32', [32, 32 * 128], 'ExternalInput'),
                         ('i128', [128, 128], 'ExternalInput'),
                         ('od', [D, T], 'ExternalOutput')):
        io[nm] = nc.dram_tensor(nm, shp, F32, kind=knd).ap()
    S = {}
    C = Cnt()
    PL = PhaseList()
    with ExitStack() as cs:
        e = cs.enter_context
        for sname in D_SEMS:
            S[sname] = e(nc.semaphore(sname))
        _emit_D(nc, e, S, C, PL, io)
        block = e(nc.Block())
        PL.emit_all(block)
    return nc


def host_prep_D(inp, c_results):
    eye = np.eye(128, dtype=np.float32)
    oh = np.zeros((16, 16 * 128), np.float32)
    for n in range(16):
        oh[n, n * 128:(n + 1) * 128] = 1.0
    maps = []
    for c in range(NCORES):
        dirb = c % 2
        sfx = 'b' if dirb else 'f'
        alog = np.asarray(inp[f"L1_attn_Alog_{sfx}"], np.float32)
        alogc = f32c(alog.reshape(16, 128, 16).transpose(1, 0, 2)
                     .reshape(128, 256))
        maps.append(dict(
            u=c_results[c]["u"], sz=c_results[c]["sz"],
            delta=c_results[c]["delta"], bc=c_results[c]["bc"],
            alogc=alogc, dpc=cols128(inp[f"L1_attn_Dp_{sfx}"]),
            woutT=f32c(np.asarray(inp[f"L1_attn_wout_{sfx}"], np.float32).T),
            oh32=oh, i128=eye,
        ))
    return maps


E_SEMS = ('wl0', 'wl1', 'mm', 'ev', 'xl', 'xf', 'cl', 'sv', 'st',
          'sq', 'squ', 'lnm', 'lnv', 'lns', 'lnb', 'lne', 'cc4', 'gx', 'gv')


def _emit_E(nc, e, S, C, PL, io):
    ln2_i = io['ln2']; wprojT_d = io['wprojT']; bprojc_d = io['bprojc']
    ln3_o = io['ln3']
    fwd_i = io.get('fwdh'); bwd_i = io.get('bwdh')
    if True:
        catbf = [e(nc.sbuf_tensor(f"cat{k}", [128, 512], BF16)) for k in range(16)]
        x32 = [e(nc.sbuf_tensor(f"x32{k}", [128, 512], F32)) for k in range(8)]
        t32 = [e(nc.sbuf_tensor(f"t32{i}", [128, 512], F32)) for i in range(2)]
        s32 = [e(nc.sbuf_tensor(f"s32{k}", [128, 512], F32)) for k in range(8)]
        sbf = [e(nc.sbuf_tensor(f"sbf{k}", [128, 512], BF16)) for k in range(8)]
        lnout = [e(nc.sbuf_tensor(f"lno{k}", [128, 512], F32)) for k in range(8)]
        wpanel = [e(nc.sbuf_tensor(f"wp{i}", [128, 16 * 512], BF16)) for i in range(2)]
        bprojc_t = e(nc.sbuf_tensor("bpc", [128, 8], F32))
        if io.get('fwdh') is None:
            tokt = [e(nc.sbuf_tensor(f"tok{i}", [128, 1024], BF16))
                    for i in range(4)]
            idxef_t = e(nc.sbuf_tensor("idxef", [128, 16], I32))
            idxeb_t = e(nc.sbuf_tensor("idxeb", [128, 16], I32))
            ident_t = e(nc.sbuf_tensor("identt", [128, 128], BF16))
            ps_tr = [e(nc.psum_tensor(f"pstr{i}", [128, 128], BF16))
                     for i in range(2)]
        aux = dict(
            ones128_bf=e(nc.sbuf_tensor("o128", [128, 1], BF16)),
            ones1_f32=e(nc.sbuf_tensor("o1f", [1, 128], F32)),
            sq_bf=[e(nc.sbuf_tensor(f"sqbf{i}", [128, 512], BF16)) for i in range(2)],
            row_m=e(nc.sbuf_tensor("row_m", [1, 512], F32)),
            row_v=e(nc.sbuf_tensor("row_v", [1, 512], F32)),
            row_t=e(nc.sbuf_tensor("row_t", [1, 512], F32)),
            row_r=e(nc.sbuf_tensor("row_r", [1, 512], F32)),
            eps=e(nc.sbuf_tensor("cst_eps", [1, 1], F32)),
            neghalf=e(nc.sbuf_tensor("cst_nh", [1, 1], F32)),
            nrm=[e(nc.sbuf_tensor(f"nrm{i}", [128, 512], F32)) for i in range(2)],
            ps_m=e(nc.psum_tensor("psm", [1, 512], F32)),
            ps_q=e(nc.psum_tensor("psq", [1, 512], F32)),
            ps_mb=e(nc.psum_tensor("psmb", [128, 512], F32)),
            ps_rb=e(nc.psum_tensor("psrb", [128, 512], F32)),
        )
        psA = [e(nc.psum_tensor(f"psA{i}", [128, 512], F32)) for i in range(2)]

        bcl = C.g('cl')
        fused_x = fwd_i is None
        n_cl = 64 if fused_x else 16

        def consts_g(g, bcl=bcl):
            g.dma_start(out=bprojc_t[:, :], in_=bprojc_d).then_inc(S['cl'], 16)
            if fused_x:
                g.dma_start(out=idxef_t[:, :], in_=io['idxEf']
                            ).then_inc(S['cl'], 16)
                g.dma_start(out=idxeb_t[:, :], in_=io['idxEb']
                            ).then_inc(S['cl'], 16)
                g.dma_start(out=ident_t[:, :], in_=io['ident']
                            ).then_inc(S['cl'], 16)

        def consts_v(v, bcl=bcl, n_cl=n_cl):
            v.memset(aux['ones1_f32'][0:1, :], 1.0)
            v.memset(aux['ones128_bf'][:, 0:1], 1.0)
            v.memset(aux['eps'][0:1, 0:1], 1e-5)
            v.memset(aux['neghalf'][0:1, 0:1], -0.5)
            v.wait_ge(S['cl'], bcl + n_cl)

        def consts_s(s, bcl=bcl, n_cl=n_cl):
            s.wait_ge(S['cl'], bcl + n_cl)
        C.bump('cl', n_cl)
        io['_e_cl'] = C.g('cl')
        PL.add(RawPhase({'gpsimd': consts_g, 'vector': consts_v,
                         'scalar': consts_s}))

        for ci in range(4):
            c0d = ci * 512
            prev_mm = C.g('mm')
            prev_sv = C.g('sv')

            if fwd_i is not None:
                def loads_g(g, c0d=c0d, ci=ci, prev_mm=prev_mm):
                    if ci > 0:
                        g.wait_ge(S['mm'], prev_mm)
                    for k in range(8):
                        g.dma_start(out=catbf[k][:, :],
                                    in_=fwd_i[k * 128:(k + 1) * 128,
                                              c0d:c0d + 512]
                                    ).then_inc(S['xl'], 16)
                    for k in range(8):
                        g.dma_start(out=catbf[8 + k][:, :],
                                    in_=bwd_i[k * 128:(k + 1) * 128,
                                              c0d:c0d + 512]
                                    ).then_inc(S['xl'], 16)
                C.bump('xl', 16 * 16)
                gv_now = None
            else:
                src = io['odall']
                base_gx = C.g('gx')
                base_gv = C.g('gv')
                base_xl = C.g('xl')

                def loads_g(g, ci=ci, base_gx=base_gx, cc4_n=io['cc4_n']):
                    if ci == 0:
                        g.wait_ge(S['cc4'], cc4_n)
                        g.wait_ge(S['cl'], io['_e_cl'])
                    if ci > 0:
                        g.wait_ge(S['gx'], base_gx)
                    for gI in range(4):
                        g.indirect_dma_start(
                            out=tokt[gI][:, :], out_offset=None, in_=src,
                            in_offset=bass.IndirectOffsetOnAxis(
                                idxef_t[:, 4 * ci + gI:4 * ci + gI + 1], 0),
                        ).then_inc(S['xl'], 16)
                    g.wait_ge(S['gx'], base_gx + 32)
                    for gI in range(4):
                        g.indirect_dma_start(
                            out=tokt[gI][:, :], out_offset=None, in_=src,
                            in_offset=bass.IndirectOffsetOnAxis(
                                idxeb_t[:, 4 * ci + gI:4 * ci + gI + 1], 0),
                        ).then_inc(S['xl'], 16)

                def trt(t, base_xl=base_xl, base_gv=base_gv, base_gx=base_gx):
                    i = 0
                    for half in range(2):
                        t.wait_ge(S['xl'], base_xl + 64 * (half + 1))
                        for gI in range(4):
                            for k in range(8):
                                if i >= 2:
                                    t.wait_ge(S['gv'], base_gv + i - 1)
                                t.transpose(ps_tr[i % 2][:, 0:128],
                                            tokt[gI][:, k * 128:(k + 1) * 128],
                                            ident_t[:, :]).then_inc(S['gx'], 1)
                                i += 1

                def tre(sc, base_gx=base_gx, base_gv=base_gv):
                    i = 0
                    for half in range(2):
                        for gI in range(4):
                            for k in range(8):
                                sc.wait_ge(S['gx'], base_gx + i + 1)
                                sc.activation(
                                    catbf[half * 8 + k][:, gI * 128:(gI + 1) * 128],
                                    ps_tr[i % 2][:, 0:128],
                                    AF.Copy).then_inc(S['gv'], 1)
                                i += 1
                C.bump('xl', 16 * 8)
                C.bump('gx', 64)
                C.bump('gv', 64)
                gv_now = C.g('gv')

            def loads_s(s, c0d=c0d, ci=ci, prev_sv=prev_sv):
                if ci > 0:
                    s.wait_ge(S['sv'], prev_sv)
                for k in range(8):
                    s.dma_start(out=x32[k][:, :],
                                in_=ln2_i[k * 128:(k + 1) * 128, c0d:c0d + 512]
                                ).then_inc(S['xf'], 16)
            C.bump('xf', 16 * 8)
            xl_now = C.g('xl')
            xf_now = C.g('xf')
            if fwd_i is not None:
                PL.add(RawPhase({'gpsimd': loads_g, 'sync': loads_s}))
            else:
                PL.add(RawPhase({'gpsimd': loads_g, 'tensor': trt,
                                 'scalar': tre, 'sync': loads_s}))

            def mk_evp(xf_now=xf_now, ph_holder=[]):
                def evac(engname, eng, ps, n_abs, nr, c0, cw, gi):
                    k = n_abs // 128
                    if engname == 'vector':
                        if gi == 0:
                            eng.wait_ge(S['xf'], xf_now)
                        eng.scalar_tensor_tensor(
                            t32[gi % 2][:, 0:cw], ps[0:nr, 0:cw],
                            bprojc_t[:, k:k + 1], x32[k][:, 0:cw],
                            ALU.add, ALU.add).then_inc(S['ev'], 1)
                        eng.tensor_copy(s32[k][:, 0:cw], t32[gi % 2][:, 0:cw])
                        eng.tensor_copy(sbf[k][:, 0:cw], t32[gi % 2][:, 0:cw]
                                        ).then_inc(S['sv'], 1)
                return evac, ph_holder

            evp, hp = mk_evp()
            e_tw = [('xl', xl_now)]
            if fwd_i is None:
                e_tw.append(('gv', gv_now))
            mmp = MMPhase(S, C, wprojT_d, DI, D, 512,
                          lambda k: catbf[k][:, :], wpanel, psA, evp,
                          evac_engines=('vector',),
                          extra_tensor_waits=e_tw)
            hp.append(mmp)
            C.bump('sv', 8)
            PL.add(mmp)

            base_st = C.g('st')
            ln = LNPhase(S, C, lambda k: sbf[k], lambda k: s32[k],
                         lambda k: lnout[k], 512, aux,
                         pre={'tensor': [('sv', C.g('sv'))],
                              'scalar': [('sv', C.g('sv'))],
                              'vector': [('st', base_st)]})
            PL.add(ln)

            def store(s, base=ln.base_lne, c0d=c0d):
                for k in range(8):
                    s.wait_ge(S['lne'], base + k + 1)
                    s.dma_start(out=ln3_o[k * 128:(k + 1) * 128, c0d:c0d + 512],
                                in_=lnout[k][:, :]).then_inc(S['st'], 16)
            C.bump('st', 16 * 8)
            PL.add(RawPhase({'sync': store}))

        st_total = C.g('st')
        PL.add(RawPhase({'sync': lambda s: s.wait_ge(S['st'], st_total)}))


def build_E():
    nc = bass.Bass("TRN2", target_bir_lowering=False)
    io = {}
    for nm, shp, knd in (('fwdh', [D, TLOC], 'ExternalInput'),
                         ('bwdh', [D, TLOC], 'ExternalInput'),
                         ('ln2', [D, TLOC], 'ExternalInput'),
                         ('wprojT', [DI, D], 'ExternalInput'),
                         ('bprojc', [128, 8], 'ExternalInput'),
                         ('ln3', [D, TLOC], 'ExternalOutput')):
        io[nm] = nc.dram_tensor(nm, shp, F32, kind=knd).ap()
    S = {}
    C = Cnt()
    PL = PhaseList()
    with ExitStack() as cs:
        e = cs.enter_context
        for sname in E_SEMS:
            S[sname] = e(nc.semaphore(sname))
        _emit_E(nc, e, S, C, PL, io)
        block = e(nc.Block())
        PL.emit_all(block)
    return nc


def host_prep_E(inp, d_results, ln2_by_core):
    wprojT = f32c(np.asarray(inp["L1_attn_wproj"], np.float32).T)
    bprojc = cols128(inp["L1_attn_bproj"])
    maps = []
    for c in range(NCORES):
        b, h = c // 2, c % 2
        fwd_full = d_results[2 * b]["od"]
        bwd_full = d_results[2 * b + 1]["od"][:, ::-1]
        maps.append(dict(
            fwdh=f32c(fwd_full[:, h * TLOC:(h + 1) * TLOC]),
            bwdh=f32c(bwd_full[:, h * TLOC:(h + 1) * TLOC]),
            ln2=ln2_by_core[c], wprojT=wprojT, bprojc=bprojc))
    return maps


F_SEMS = ('wl0', 'wl1', 'mm', 'ev', 'xl', 'xf', 'cl', 'sv', 'st',
          'sh0', 'sh1', 'sq', 'squ', 'lnm', 'lnv', 'lns', 'lnb', 'lne')


def _emit_F(nc, e, S, C, PL, io):
    ln3_i = io['ln3']; w1T_d = io['w1T']; w2T_d = io['w2T']
    h1T_d = io['h1T']; h2T_d = io['h2T']; w3T_d = io['w3T']
    s1c_d = io['s1c']; s2c_d = io['s2c']; sh1c_d = io['sh1c']
    sh2c_d = io['sh2c']; s3c_d = io['s3c']; y_o = io['y']
    h0_d = io['h0d']; h1_d = io['h1d']; h2_d = io['h2d']
    if True:
        a32 = [e(nc.sbuf_tensor(f"a32{j}", [128, 512], F32)) for j in range(32)]
        hbf = [a32[j][:, :].bitcast(BF16)[:, 0:512] for j in range(32)]
        xbf = [e(nc.sbuf_tensor(f"xbf{k}", [128, 512], BF16)) for k in range(8)]
        hobuf = [e(nc.sbuf_tensor(f"hob{i}", [128, 512], BF16)) for i in range(2)]
        x32 = [e(nc.sbuf_tensor(f"x32{k}", [128, 512], F32)) for k in range(8)]
        s32 = [e(nc.sbuf_tensor(f"s32{k}", [128, 512], F32)) for k in range(8)]
        sbf = [e(nc.sbuf_tensor(f"sbf{k}", [128, 512], BF16)) for k in range(8)]
        wpanel = [e(nc.sbuf_tensor(f"wp{i}", [128, 32 * 512], BF16)) for i in range(2)]
        s1c = e(nc.sbuf_tensor("s1ct", [128, 32], F32))
        s2c = e(nc.sbuf_tensor("s2ct", [128, 32], F32))
        sh1c = e(nc.sbuf_tensor("sh1ct", [128, 32], F32))
        sh2c = e(nc.sbuf_tensor("sh2ct", [128, 32], F32))
        s3c = e(nc.sbuf_tensor("s3ct", [128, 8], F32))
        aux = dict(
            ones128_bf=e(nc.sbuf_tensor("o128", [128, 1], BF16)),
            ones1_f32=e(nc.sbuf_tensor("o1f", [1, 128], F32)),
            sq_bf=[e(nc.sbuf_tensor(f"sqbf{i}", [128, 512], BF16)) for i in range(2)],
            row_m=e(nc.sbuf_tensor("row_m", [1, 512], F32)),
            row_v=e(nc.sbuf_tensor("row_v", [1, 512], F32)),
            row_t=e(nc.sbuf_tensor("row_t", [1, 512], F32)),
            row_r=e(nc.sbuf_tensor("row_r", [1, 512], F32)),
            eps=e(nc.sbuf_tensor("cst_eps", [1, 1], F32)),
            neghalf=e(nc.sbuf_tensor("cst_nh", [1, 1], F32)),
            nrm=[e(nc.sbuf_tensor(f"nrm{i}", [128, 512], F32)) for i in range(2)],
            ps_m=e(nc.psum_tensor("psm", [1, 512], F32)),
            ps_q=e(nc.psum_tensor("psq", [1, 512], F32)),
            ps_mb=e(nc.psum_tensor("psmb", [128, 512], F32)),
            ps_rb=e(nc.psum_tensor("psrb", [128, 512], F32)),
        )
        psA = [e(nc.psum_tensor(f"psA{i}", [128, 512], F32)) for i in range(2)]

        bcl = C.g('cl')

        def consts_g(g, bcl=bcl):
            for t, d_ in ((s1c, s1c_d), (s2c, s2c_d), (sh1c, sh1c_d),
                          (sh2c, sh2c_d), (s3c, s3c_d)):
                g.dma_start(out=t[:, :], in_=d_).then_inc(S['cl'], 16)

        def consts_v(v, bcl=bcl):
            v.memset(aux['ones1_f32'][0:1, :], 1.0)
            v.memset(aux['ones128_bf'][:, 0:1], 1.0)
            v.memset(aux['eps'][0:1, 0:1], 1e-5)
            v.memset(aux['neghalf'][0:1, 0:1], -0.5)
            v.wait_ge(S['cl'], bcl + 80)

        def consts_s(s, bcl=bcl):
            s.wait_ge(S['cl'], bcl + 80)
        C.bump('cl', 80)
        PL.add(RawPhase({'gpsimd': consts_g, 'vector': consts_v,
                         'scalar': consts_s}))

        for ci in range(4):
            c0d = ci * 512
            prev_mm = C.g('mm')

            def loadx(g, c0d=c0d, ci=ci, prev_mm=prev_mm):
                if ci > 0:
                    g.wait_ge(S['mm'], prev_mm)
                for k in range(8):
                    g.dma_start(out=xbf[k][:, :],
                                in_=ln3_i[k * 128:(k + 1) * 128, c0d:c0d + 512]
                                ).then_inc(S['xl'], 16)
            C.bump('xl', 128)
            xl_now = C.g('xl')
            PL.add(RawPhase({'gpsimd': loadx}))

            def ev1(engname, eng, ps, n_abs, nr, c0, cw, gi):
                j = n_abs // 128
                eng.activation(a32[j][:, 0:cw], ps[0:nr, 0:cw], AF.Silu,
                               bias=0.0, scale=s1c[:, j:j + 1]
                               ).then_inc(S['ev'], 1)

            mm1 = MMPhase(S, C, w1T_d, D, 4096, 512, lambda k: xbf[k][:, :],
                          wpanel, psA, ev1, evac_engines=('scalar',),
                          extra_tensor_waits=[('xl', xl_now)])
            PL.add(mm1)

            base_sh = [C.g('sh0'), C.g('sh1')]

            def mk_ev2(base_sh=base_sh, c0d=c0d, ph_holder=[]):
                def evac(engname, eng, ps, n_abs, nr, c0, cw, gi):
                    j = n_abs // 128
                    if engname == 'vector':
                        par = gi % 2
                        eng.wait_ge(S['sh0' if par == 0 else 'sh1'],
                                    base_sh[par] + 16 * (gi // 2))
                        eng.scalar_tensor_tensor(
                            hobuf[gi % 2][:, 0:cw], ps[0:nr, 0:cw],
                            s2c[:, j:j + 1], a32[j][:, 0:cw],
                            ALU.mult, ALU.mult).then_inc(S['ev'], 1)
                    elif engname == 'sync':
                        eng.wait_ge(S['ev'], ph_holder[0].base_ev + gi + 1)
                        eng.dma_start(out=h0_d[j * 128:(j + 1) * 128,
                                               c0d:c0d + cw],
                                      in_=hobuf[gi % 2][:, 0:cw]
                                      ).then_inc(S['sh0' if gi % 2 == 0
                                                 else 'sh1'], 16)
                return evac, ph_holder

            ev2, h2h = mk_ev2()
            mm2 = MMPhase(S, C, w2T_d, D, 4096, 512, lambda k: xbf[k][:, :],
                          wpanel, psA, ev2, evac_engines=('vector', 'sync'))
            h2h.append(mm2)
            C.bump('sh0', 16 * 16)
            C.bump('sh1', 16 * 16)
            PL.add(mm2)

        for (src_d, dst_d, wT, scol) in ((h0_d, h1_d, h1T_d, sh1c),
                                         (h1_d, h2_d, h2T_d, sh2c)):
            for ci in range(4):
                c0d = ci * 512
                prev_mm = C.g('mm')

                def loadh(s, c0d=c0d, prev_mm=prev_mm, src_d=src_d,
                          prev_ev=C.g('ev')):
                    s.wait_ge(S['mm'], prev_mm)
                    s.wait_ge(S['ev'], prev_ev)
                    for j in range(32):
                        s.dma_start(out=hbf[j],
                                    in_=src_d[j * 128:(j + 1) * 128,
                                              c0d:c0d + 512]
                                    ).then_inc(S['xf'], 16)
                C.bump('xf', 16 * 32)
                xf_now = C.g('xf')
                PL.add(RawPhase({'sync': loadh}))

                base_sh = [C.g('sh0'), C.g('sh1')]

                def mk_evh(base_sh=base_sh, c0d=c0d, dst_d=dst_d, scol=scol,
                           ph_holder=[]):
                    def evac(engname, eng, ps, n_abs, nr, c0, cw, gi):
                        j = n_abs // 128
                        if engname == 'scalar':
                            par = gi % 2
                            eng.wait_ge(S['sh0' if par == 0 else 'sh1'],
                                        base_sh[par] + 16 * (gi // 2))
                            eng.activation(hobuf[gi % 2][:, 0:cw],
                                           ps[0:nr, 0:cw], AF.Gelu,
                                           bias=0.0, scale=scol[:, j:j + 1]
                                           ).then_inc(S['ev'], 1)
                        elif engname == 'sync':
                            eng.wait_ge(S['ev'],
                                        ph_holder[0].base_ev + gi + 1)
                            eng.dma_start(out=dst_d[j * 128:(j + 1) * 128,
                                                    c0d:c0d + cw],
                                          in_=hobuf[gi % 2][:, 0:cw]
                                          ).then_inc(S['sh0' if gi % 2 == 0
                                                     else 'sh1'], 16)
                    return evac, ph_holder

                evh, hh = mk_evh()
                mmh = MMPhase(S, C, wT, 4096, 4096, 512, lambda k: hbf[k],
                              wpanel, psA, evh, evac_engines=('scalar', 'sync'),
                              extra_tensor_waits=[('xf', xf_now)])
                hh.append(mmh)
                C.bump('sh0', 16 * 16)
                C.bump('sh1', 16 * 16)
                PL.add(mmh)

        for ci in range(4):
            c0d = ci * 512
            prev_mm = C.g('mm')
            prev_sv = C.g('sv')

            def loadh2(s, c0d=c0d, prev_mm=prev_mm, prev_sv=prev_sv, ci=ci,
                       prev_ev=C.g('ev')):
                s.wait_ge(S['mm'], prev_mm)
                s.wait_ge(S['ev'], prev_ev)
                if ci > 0:
                    s.wait_ge(S['sv'], prev_sv)
                for j in range(32):
                    s.dma_start(out=hbf[j],
                                in_=h2_d[j * 128:(j + 1) * 128, c0d:c0d + 512]
                                ).then_inc(S['xf'], 16)
                for k in range(8):
                    s.dma_start(out=x32[k][:, :],
                                in_=ln3_i[k * 128:(k + 1) * 128, c0d:c0d + 512]
                                ).then_inc(S['xf'], 16)
            C.bump('xf', 16 * 40)
            xf_now = C.g('xf')
            PL.add(RawPhase({'sync': loadh2}))

            def mk_ev4(xf_now=xf_now, ph_holder=[]):
                def evac(engname, eng, ps, n_abs, nr, c0, cw, gi,
                         st_prev=C.g('st')):
                    k = n_abs // 128
                    if engname == 'vector':
                        if gi == 0:
                            eng.wait_ge(S['xf'], xf_now)
                            eng.wait_ge(S['st'], st_prev)
                        eng.scalar_tensor_tensor(
                            s32[k][:, 0:cw], ps[0:nr, 0:cw], s3c[:, k:k + 1],
                            x32[k][:, 0:cw], ALU.mult, ALU.add
                            ).then_inc(S['ev'], 1)
                        eng.tensor_copy(sbf[k][:, 0:cw], s32[k][:, 0:cw]
                                        ).then_inc(S['sv'], 1)
                return evac, ph_holder

            ev4, h4 = mk_ev4()
            mm4 = MMPhase(S, C, w3T_d, 4096, D, 512, lambda k: hbf[k],
                          wpanel, psA, ev4, evac_engines=('vector',),
                          extra_tensor_waits=[('xf', xf_now)])
            h4.append(mm4)
            C.bump('sv', 8)
            PL.add(mm4)

            base_st = C.g('st')
            ln = LNPhase(S, C, lambda k: sbf[k], lambda k: s32[k],
                         lambda k: s32[k], 512, aux,
                         pre={'tensor': [('sv', C.g('sv'))],
                              'scalar': [('sv', C.g('sv'))],
                              'vector': [('st', base_st)]})
            PL.add(ln)

            def store(s, base=ln.base_lne, c0d=c0d):
                for k in range(8):
                    s.wait_ge(S['lne'], base + k + 1)
                    s.dma_start(out=y_o[k * 128:(k + 1) * 128, c0d:c0d + 512],
                                in_=s32[k][:, :]).then_inc(S['st'], 16)
            C.bump('st', 16 * 8)
            PL.add(RawPhase({'sync': store}))

        st_total = C.g('st')
        PL.add(RawPhase({'sync': lambda s: s.wait_ge(S['st'], st_total)}))


def build_F():
    nc = bass.Bass("TRN2", target_bir_lowering=False)
    io = {}
    for nm, shp, knd in (('ln3', [D, TLOC], 'ExternalInput'),
                         ('w1T', [D, 4096], 'ExternalInput'),
                         ('w2T', [D, 4096], 'ExternalInput'),
                         ('h1T', [4096, 4096], 'ExternalInput'),
                         ('h2T', [4096, 4096], 'ExternalInput'),
                         ('w3T', [4096, D], 'ExternalInput'),
                         ('s1c', [128, 32], 'ExternalInput'),
                         ('s2c', [128, 32], 'ExternalInput'),
                         ('sh1c', [128, 32], 'ExternalInput'),
                         ('sh2c', [128, 32], 'ExternalInput'),
                         ('s3c', [128, 8], 'ExternalInput'),
                         ('y', [D, TLOC], 'ExternalOutput')):
        io[nm] = nc.dram_tensor(nm, shp, F32, kind=knd).ap()
    io['h0d'] = nc.dram_tensor("h0d", [4096, TLOC], BF16).ap()
    io['h1d'] = nc.dram_tensor("h1d", [4096, TLOC], BF16).ap()
    io['h2d'] = nc.dram_tensor("h2d", [4096, TLOC], BF16).ap()
    S = {}
    C = Cnt()
    PL = PhaseList()
    with ExitStack() as cs:
        e = cs.enter_context
        for sname in F_SEMS:
            S[sname] = e(nc.semaphore(sname))
        _emit_F(nc, e, S, C, PL, io)
        block = e(nc.Block())
        PL.emit_all(block)
    return nc


def host_prep_F(inp, e_results):
    def colsN(v):
        return f32c(np.asarray(v, np.float32).reshape(-1, 128).T)
    m = dict(
        w1T=f32c(np.asarray(inp["L1_ffn_w1_w"], np.float32).T),
        w2T=f32c(np.asarray(inp["L1_ffn_w2_w"], np.float32).T),
        h1T=f32c(np.asarray(inp["L1_ffn_h1_w"], np.float32).T),
        h2T=f32c(np.asarray(inp["L1_ffn_h2_w"], np.float32).T),
        w3T=f32c(np.asarray(inp["L1_ffn_w3_w"], np.float32).T),
        s1c=colsN(inp["L1_ffn_w1_s"]), s2c=colsN(inp["L1_ffn_w2_s"]),
        sh1c=colsN(inp["L1_ffn_h1_s"]), sh2c=colsN(inp["L1_ffn_h2_s"]),
        s3c=colsN(inp["L1_ffn_w3_s"]),
    )
    return [dict(m, ln3=e_results[c]["ln3"]) for c in range(NCORES)]


def kernel_unfused(**inputs):
    cores = list(range(NCORES))
    r1 = run_bass_kernel_spmd(build_A1(), host_prep_A1(inputs), core_ids=cores).results
    r2 = run_bass_kernel_spmd(build_A2(), host_prep_A2(inputs, r1), core_ids=cores).results
    r3 = run_bass_kernel_spmd(build_B(), host_prep_B(inputs, r2), core_ids=cores).results
    ln2_by_core = [np.asarray(r3[c]["ln2"], np.float32) for c in range(NCORES)]
    rC = run_bass_kernel_spmd(build_C(), host_prep_C(inputs, ln2_by_core), core_ids=cores).results
    rD = run_bass_kernel_spmd(build_D(), host_prep_D(inputs, rC), core_ids=cores).results
    rE = run_bass_kernel_spmd(build_E(), host_prep_E(inputs, rD, ln2_by_core), core_ids=cores).results
    rF = run_bass_kernel_spmd(build_F(), host_prep_F(inputs, rE), core_ids=cores).results
    out = np.empty((B, T, D), np.float32)
    for c in range(NCORES):
        b, h = c // 2, c % 2
        out[b, h * TLOC:(h + 1) * TLOC, :] = rF[c]["y"].T
    return out


def kernel(**inputs):
    return kernel_fused(**inputs)


def _emit_fm_to_tm(nc, e, S, C, PL, src, dst, ident_ap, nrows, ncols,
                   pre_sync=(), dt_=F32, ldq='sync'):
    kt = nrows // 128
    tbn = ncols // 128
    lt = [e(nc.sbuf_tensor(f"tml{C.g('tmuid')}_{k}", [128, 128], dt_))
          for k in range(kt)]
    tmrow = [e(nc.sbuf_tensor(f"tmr{C.g('tmuid')}_{i}", [128, nrows], dt_))
             for i in range(2)]
    idt = e(nc.sbuf_tensor(f"tmi{C.g('tmuid')}", [128, 128], dt_))
    ps_tr = [e(nc.psum_tensor(f"tmp{C.g('tmuid')}_{i}", [128, 128], dt_))
             for i in range(2)]
    C.bump('tmuid', 1)
    b_tma = C.g('tma'); b_tmb = C.g('tmb'); b_tmc = C.g('tmc')
    b_tms = C.g('tms'); b_tmi = C.g('tmi')

    def loads(s):
        for sem, val in pre_sync:
            s.wait_ge(S[sem], val)
        s.dma_start(out=idt[:, :], in_=ident_ap).then_inc(S['tmi'], 16)
        for tb in range(tbn):
            if tb > 0:
                s.wait_ge(S['tmb'], b_tmb + kt * tb)
            for k in range(kt):
                s.dma_start(out=lt[k][:, :],
                            in_=src[k * 128:(k + 1) * 128,
                                    tb * 128:(tb + 1) * 128]
                            ).then_inc(S['tma'], 16)

    def trans(t):
        t.wait_ge(S['tmi'], b_tmi + 16)
        i = 0
        for tb in range(tbn):
            t.wait_ge(S['tma'], b_tma + 16 * kt * (tb + 1))
            for k in range(kt):
                if i >= 2:
                    t.wait_ge(S['tmc'], b_tmc + i - 1)
                t.transpose(ps_tr[i % 2][:, :], lt[k][:, :], idt[:, :]
                            ).then_inc(S['tmb'], 1)
                i += 1

    def evac(sc):
        i = 0
        for tb in range(tbn):
            for k in range(kt):
                sc.wait_ge(S['tmb'], b_tmb + i + 1)
                if k == 0 and tb >= 2:
                    sc.wait_ge(S['tms'], b_tms + 16 * (tb - 1))
                sc.activation(tmrow[tb % 2][:, k * 128:(k + 1) * 128],
                              ps_tr[i % 2][:, :], AF.Copy
                              ).then_inc(S['tmc'], 1)
                i += 1

    def stores(s):
        for tb in range(tbn):
            s.wait_ge(S['tmc'], b_tmc + kt * (tb + 1))
            s.dma_start(out=dst[tb * 128:(tb + 1) * 128, :],
                        in_=tmrow[tb % 2][:, :]).then_inc(S['tms'], 16)
    C.bump('tmi', 16)
    C.bump('tma', 16 * kt * tbn)
    C.bump('tmb', kt * tbn)
    C.bump('tmc', kt * tbn)
    C.bump('tms', 16 * tbn)
    stq = 'gpsimd' if ldq == 'sync' else 'sync'
    PL.add(RawPhase({ldq: loads, 'tensor': trans, 'scalar': evac,
                     stq: stores}))
    return C.g('tms')


ALL_SEMS = sorted(set(A1_SEMS) | set(A2_SEMS) | set(B_SEMS) | set(C_SEMS) |
                  set(D_SEMS) | set(E_SEMS) | set(F_SEMS) |
                  {'cc1', 'cc2', 'cc3', 'cc4', 'tm',
                   'tma', 'tmb', 'tmc', 'tms', 'tmi', 'wc'})


def build_fused():
    nc = bass.Bass("TRN2", target_bir_lowering=False)
    IN, INT = 'ExternalInput', None

    def dt(nm, shp, kind=IN, dtype=F32):
        if kind is None:
            return nc.dram_tensor(nm, shp, dtype).ap()
        return nc.dram_tensor(nm, shp, dtype, kind=kind).ap()

    dr = {}
    for nm, shp in (('xw', [D, TLOC + 4]), ('w_in', [D, 2 * DI]),
                    ('wx_main', [DI, DI]), ('wx_small', [DI, 32]),
                    ('w_dt', [DS, DI]), ('convw', [128, 64]),
                    ('convb', [128, 16]), ('bdt', [128, 16]),
                    ('selb', [32, 1]), ('xT', [D, TLOC]), ('wout', [DI, D]),
                    ('cflag', [128, 1]), ('mur', [128, 8]), ('muk', [128, 8]),
                    ('wrT', [D, D]), ('wkT', [D, D]), ('wvT', [D, D]),
                    ('winT', [D, 2 * DI]), ('wxT', [DI, 128]),
                    ('wdtT', [64, DI]), ('convwc', [128, 64]),
                    ('convbc', [128, 16]), ('bdtc', [128, 16]),
                    ('ident', [128, 128]), ('alogc', [128, 256]),
                    ('dpc', [128, 16]), ('woutT', [DI, D]),
                    ('oh32', [16, 16 * 128]), ('wprojT', [DI, D]),
                    ('bprojc', [128, 8]), ('w1T', [D, 4096]),
                    ('w2T', [D, 4096]), ('h1T', [4096, 4096]),
                    ('h2T', [4096, 4096]), ('w3T', [4096, D]),
                    ('s1c', [128, 32]), ('s2c', [128, 32]),
                    ('sh1c', [128, 32]), ('sh2c', [128, 32]),
                    ('s3c', [128, 8])):
        dr[nm] = dt(nm, shp)
    for nm, shp in (('idxC', [128, 33]), ('idxEf', [128, 16]),
                    ('idxEb', [128, 16])):
        dr[nm] = dt(nm, shp, IN, I32)
    y_o = dt('y', [D, TLOC], 'ExternalOutput')
    for nm, shp in (('G', [DI, TLOC]), ('P', [DI, TLOC]), ('M1', [DI, TLOC]),
                    ('ln1', [D, TLOC]), ('ln2', [D, TLOC]),
                    ('ln3', [D, TLOC]), ('u', [DI, T]),
                    ('delta', [DI, T]), ('bc', [32, T]),
                    ('_keep_f32_marker', [1, 1]),
                    ('gend_in', [128, 16]), ('gend_out', [256, 16]),
                    ('l1col_in', [128, 8]), ('l1col_out', [256, 8])):
        dr[nm] = dt(nm, shp, INT)
    for nm, shp in (('sz', [DI, T]), ('od', [D, T]), ('ln2tm', [TLOC, D]),
                    ('ln2_all', [T + 1, D]), ('odtm', [T, D]),
                    ('odall', [2 * T, D]), ('ident_b', [128, 128])):
        dr[nm] = dt(nm, shp, INT, BF16)
    for nm in ('h0d', 'h1d', 'h2d'):
        dr[nm] = dt(nm, [4096, TLOC], INT, BF16)
    CAST_ORDER = ('w_in', 'wx_main', 'wx_small', 'w_dt', 'wout',
                  'wrT', 'wkT', 'wvT', 'winT', 'wxT', 'wdtT', 'woutT',
                  'wprojT', 'w1T', 'w2T', 'w3T', 'h1T', 'h2T')
    wshapes = dict(w_in=[D, 2 * DI], wx_main=[DI, DI], wx_small=[DI, 32],
                   w_dt=[DS, DI], wout=[DI, D], wrT=[D, D], wkT=[D, D],
                   wvT=[D, D], winT=[D, 2 * DI], wxT=[DI, 128],
                   wdtT=[64, DI], woutT=[DI, D], wprojT=[DI, D],
                   w1T=[D, 4096], w2T=[D, 4096], w3T=[4096, D],
                   h1T=[4096, 4096], h2T=[4096, 4096])
    for nm in CAST_ORDER:
        dr[nm + '_b'] = dt(nm + '_b', wshapes[nm], INT, BF16)
    wc_counts = {}
    cnt = 16
    for nm in CAST_ORDER:
        cnt += 16 * ceil_div(wshapes[nm][0], 128)
        wc_counts[nm + '_b'] = cnt
    MM_WCFG.clear()
    MM_WCFG.update(wc_counts)

    def wsel(nm):
        return dr[nm + '_b'] if nm in CAST_ORDER else dr[nm]

    S = {}
    C = Cnt()
    with ExitStack() as cs:
        e = cs.enter_context
        for sname in ALL_SEMS:
            S[sname] = e(nc.semaphore(sname))

        stage_ctr = [0]

        def stage(emit_fn, io):
            sid = stage_ctr[0]
            stage_ctr[0] += 1
            orig_sb = bass.Bass.sbuf_tensor
            orig_ps = bass.Bass.psum_tensor

            def sb(name, *a, **kw):
                return orig_sb(nc, f"st{sid}_{name}", *a, **kw)

            def ps(name, *a, **kw):
                return orig_ps(nc, f"st{sid}_{name}", *a, **kw)
            nc.sbuf_tensor = sb
            nc.psum_tensor = ps
            try:
                with ExitStack() as scs:
                    PL = PhaseList()
                    emit_fn(nc, scs.enter_context, S, C, PL, io)
                    blk = scs.enter_context(nc.Block())
                    PL.emit_all(blk)
            finally:
                del nc.sbuf_tensor
                del nc.psum_tensor
            return io

        def emit_a1(nc_, e_, S_, C_, PL_, io_):
            def casts(g):
                g.dma_start(out=dr['ident_b'], in_=dr['ident']
                            ).then_inc(S_['wc'], 16)
                for nm in CAST_ORDER:
                    rows = wshapes[nm][0]
                    for r in range(0, rows, 128):
                        rr = min(128, rows - r)
                        g.dma_start(out=dr[nm + '_b'][r:r + rr, :],
                                    in_=dr[nm][r:r + rr, :]
                                    ).then_inc(S_['wc'], 16)
            C_.bump('wc', 16 + sum(16 * ceil_div(wshapes[nm][0], 128)
                                   for nm in CAST_ORDER))
            PL_.add(RawPhase({'gpsimd': casts}))
            _emit_A1(nc_, e_, S_, C_, PL_, io_)
            gcarry = io_['_a1']['gcarry']
            dv_tot = C_.g('dv')
            st_b = C_.g('st')

            def ag_sync(s):
                s.wait_ge(S_['dv'], dv_tot)
                s.dma_start(out=dr['gend_in'], in_=gcarry[:, :]
                            ).then_inc(S_['st'], 16)

            def ag_g(g):
                g.wait_ge(S_['st'], st_b + 16)
                g.collective_compute(
                    "AllGather", ALU.bypass,
                    replica_groups=[[0, 1], [2, 3], [4, 5], [6, 7]],
                    ins=[dr['gend_in'].opt()],
                    outs=[dr['gend_out'].opt()],
                ).then_inc(S_['cc1'], 1)
            C_.bump('st', 16)
            C_.bump('cc1', 1)
            PL_.add(RawPhase({'sync': ag_sync, 'gpsimd': ag_g}))

        io1 = {k: dr[k] for k in ('xw', 'convw', 'convb', 'bdt', 'selb',
                                  'G', 'P', 'M1')}
        for k in ('w_in', 'wx_main', 'wx_small', 'w_dt'):
            io1[k] = wsel(k)
        stage(emit_a1, io1)

        def emit_a2(nc_, e_, S_, C_, PL_, io_):
            _emit_A2(nc_, e_, S_, C_, PL_, io_)
            lnout = io_['_a2']['lnout']
            lne_tot = C_.g('lne')
            tm_b = C_.g('tm')
            st_b = C_.g('st')
            l1col_sb = e_(nc.sbuf_tensor("l1colsb", [128, 8], F32))

            def col_v(v):
                v.wait_ge(S_['lne'], lne_tot)
                for k in range(8):
                    ins = v.tensor_copy(l1col_sb[:, k:k + 1],
                                        lnout[k][:, 511:512])
                ins.then_inc(S_['tm'], 1)

            def col_sync(s):
                s.wait_ge(S_['tm'], tm_b + 1)
                s.dma_start(out=dr['l1col_in'], in_=l1col_sb[:, :]
                            ).then_inc(S_['st'], 16)

            def col_g(g):
                g.wait_ge(S_['st'], st_b + 16)
                g.collective_compute(
                    "AllGather", ALU.bypass,
                    replica_groups=[[0, 1], [2, 3], [4, 5], [6, 7]],
                    ins=[dr['l1col_in'].opt()],
                    outs=[dr['l1col_out'].opt()],
                ).then_inc(S_['cc2'], 1)
            C_.bump('tm', 1)
            C_.bump('st', 16)
            C_.bump('cc2', 1)
            PL_.add(RawPhase({'vector': col_v, 'sync': col_sync,
                              'gpsimd': col_g}))

        io2 = {k: dr[k] for k in ('G', 'P', 'M1', 'xT', 'ln1',
                                  'gend_out', 'cflag')}
        io2['wout'] = wsel('wout')
        io2['ccol'] = None
        io2['cc1_n'] = 1
        stage(emit_a2, io2)

        io3 = {k: dr[k] for k in ('ln1', 'mur', 'muk',
                                  'ln2', 'l1col_out', 'cflag')}
        for k in ('wrT', 'wkT', 'wvT'):
            io3[k] = wsel(k)
        io3['pcol'] = None
        io3['cc2_n'] = 1
        stage(_emit_B, io3)

        def emit_btm(nc_, e_, S_, C_, PL_, io_):
            tms_tot = _emit_fm_to_tm(nc_, e_, S_, C_, PL_, dr['ln2'],
                                     dr['ln2tm'], dr['ident_b'], D, TLOC,
                                     pre_sync=(('wc', 16),), dt_=BF16,
                                     ldq='gpsimd')
            zr = e_(nc.sbuf_tensor("zrow_b", [1, D], BF16))
            st_b = C_.g('st')

            def zrow_v(v):
                v.memset(zr[0:1, :], 0.0).then_inc(S_['tm'], 1)

            def zrow_s(s, tm_n=C_.g('tm') + 1):
                s.wait_ge(S_['tm'], tm_n)
                s.dma_start(out=dr['ln2_all'][T:T + 1, :], in_=zr[0:1, :]
                            ).then_inc(S_['st'], 16)

            def ag_g(g, tms_tot=tms_tot, st_n=st_b + 16):
                g.wait_ge(S_['tms'], tms_tot)
                g.wait_ge(S_['st'], st_n)
                g.collective_compute(
                    "AllGather", ALU.bypass,
                    replica_groups=[[0, 1], [2, 3], [4, 5], [6, 7]],
                    ins=[dr['ln2tm'].opt()],
                    outs=[dr['ln2_all'][0:T, :].opt()],
                ).then_inc(S_['cc3'], 1)
            C_.bump('tm', 1)
            C_.bump('st', 16)
            C_.bump('cc3', 1)
            PL_.add(RawPhase({'vector': zrow_v, 'sync': zrow_s,
                              'gpsimd': ag_g}))

        stage(emit_btm, {})

        io4 = dict(xT=None, winT=wsel('winT'), wxT=wsel('wxT'),
                   wdtT=wsel('wdtT'),
                   convw=dr['convwc'], convb=dr['convbc'], bdt=dr['bdtc'],
                   u=dr['u'], sz=dr['sz'], delta=dr['delta'], bc=dr['bc'],
                   ln2_all=dr['ln2_all'], idxC=dr['idxC'], ident=dr['ident'],
                   cc3_n=1, defer_chunks=True)
        io5 = {k: dr[k] for k in ('u', 'sz', 'delta', 'bc', 'alogc', 'dpc',
                                  'oh32', 'od')}
        io5['woutT'] = wsel('woutT')
        io5['i128'] = dr['ident']
        io5['defer_chunks'] = True

        def emit_cd(nc_, e_, S_, C_, PL_, io_):
            _emit_C(nc_, e_, S_, C_, PL_, io4)
            io5['wpanel'] = io4['wpanel_out']
            io5['psA'] = io4['psA_out']
            _emit_D(nc_, e_, S_, C_, PL_, io5)
            for ci in range(8):
                io4['_c_chunk'](ci)
                io5['_d_chunk'](ci)
            tot = {k: C_.g(k) for k in ('st', 'sd0', 'sd1', 'su0', 'su1',
                                        'sz0', 'sz1')}
            dfin = io5['_d_fin']()

            def fin(s):
                for k, v in tot.items():
                    s.wait_ge(S_[k], v)
                dfin(s)
            PL_.add(RawPhase({'sync': fin}))

        stage(emit_cd, {})

        def emit_dtm(nc_, e_, S_, C_, PL_, io_):
            tms_tot = _emit_fm_to_tm(nc_, e_, S_, C_, PL_, dr['od'],
                                     dr['odtm'], dr['ident_b'], D, T,
                                     pre_sync=(('wc', 16),), dt_=BF16)

            def ag_g(g, tms_tot=tms_tot):
                g.wait_ge(S_['tms'], tms_tot)
                g.collective_compute(
                    "AllGather", ALU.bypass,
                    replica_groups=[[0, 1], [2, 3], [4, 5], [6, 7]],
                    ins=[dr['odtm'].opt()],
                    outs=[dr['odall'].opt()],
                ).then_inc(S_['cc4'], 1)
            C_.bump('cc4', 1)
            PL_.add(RawPhase({'gpsimd': ag_g}))

        stage(emit_dtm, {})

        io6 = dict(fwdh=None, bwdh=None, ln2=dr['ln2'],
                   wprojT=wsel('wprojT'),
                   bprojc=dr['bprojc'], ln3=dr['ln3'], odall=dr['odall'],
                   idxEf=dr['idxEf'], idxEb=dr['idxEb'], ident=dr['ident'],
                   cc4_n=1)
        stage(_emit_E, io6)

        io7 = {k: dr[k] for k in ('ln3', 's1c', 's2c', 'sh1c', 'sh2c', 's3c',
                                  'h0d', 'h1d', 'h2d')}
        for k in ('w1T', 'w2T', 'h1T', 'h2T', 'w3T'):
            io7[k] = wsel(k)
        io7['y'] = y_o
        stage(_emit_F, io7)
        MM_WCFG.clear()
    return nc


def host_prep_fused(inp):
    m1 = host_prep_A1(inp)
    x = np.asarray(inp["L0_attn_x"], np.float32)
    wout = f32c(inp["L0_attn_W_out"])
    wrT = f32c(np.asarray(inp["L0_ffn_W_r"], np.float32).T)
    wkT = f32c(np.asarray(inp["L0_ffn_W_k"], np.float32).T)
    wvT = f32c(np.asarray(inp["L0_ffn_W_v"], np.float32).T)
    mur = cols128(inp["L0_ffn_mu_r"])
    muk = cols128(inp["L0_ffn_mu_k"])
    wprojT = f32c(np.asarray(inp["L1_attn_wproj"], np.float32).T)
    bprojc = cols128(inp["L1_attn_bproj"])
    eye = np.eye(128, dtype=np.float32)
    oh = np.zeros((16, 16 * 128), np.float32)
    for n in range(16):
        oh[n, n * 128:(n + 1) * 128] = 1.0

    def colsN(v):
        return f32c(np.asarray(v, np.float32).reshape(-1, 128).T)
    fshared = dict(
        w1T=f32c(np.asarray(inp["L1_ffn_w1_w"], np.float32).T),
        w2T=f32c(np.asarray(inp["L1_ffn_w2_w"], np.float32).T),
        h1T=f32c(np.asarray(inp["L1_ffn_h1_w"], np.float32).T),
        h2T=f32c(np.asarray(inp["L1_ffn_h2_w"], np.float32).T),
        w3T=f32c(np.asarray(inp["L1_ffn_w3_w"], np.float32).T),
        s1c=colsN(inp["L1_ffn_w1_s"]), s2c=colsN(inp["L1_ffn_w2_s"]),
        sh1c=colsN(inp["L1_ffn_h1_s"]), sh2c=colsN(inp["L1_ffn_h2_s"]),
        s3c=colsN(inp["L1_ffn_w3_s"]),
    )
    maps = []
    for c in range(NCORES):
        b, h = c // 2, c % 2
        dirb = c % 2
        sfx = 'b' if dirb else 'f'
        t0 = h * TLOC
        idxC = np.empty((128, 33), np.int32)
        for j in range(33):
            pos = j * 128 + np.arange(128)
            tau = pos - 4
            if dirb == 0:
                row = np.where((tau >= 0) & (tau < T), tau, T)
            else:
                row = np.where((tau >= 0) & (tau < T), (T - 1) - tau, T)
            idxC[:, j] = row
        idxEf = np.empty((128, 16), np.int32)
        idxEb = np.empty((128, 16), np.int32)
        for j in range(16):
            t_ = t0 + j * 128 + np.arange(128)
            idxEf[:, j] = t_
            idxEb[:, j] = T + (T - 1) - t_
        alog = np.asarray(inp[f"L1_attn_Alog_{sfx}"], np.float32)
        m = dict(m1[c])
        m.update(dict(
            xT=f32c(x[b, t0:t0 + TLOC, :].T), wout=wout,
            cflag=np.full((128, 1), float(h), np.float32),
            mur=mur, muk=muk, wrT=wrT, wkT=wkT, wvT=wvT,
            winT=f32c(np.asarray(inp[f"L1_attn_win_{sfx}"], np.float32).T),
            wdtT=f32c(np.asarray(inp[f"L1_attn_wdt_{sfx}"], np.float32).T),
            convwc=f32c(np.asarray(inp[f"L1_attn_wconv_{sfx}"], np.float32)
                        .reshape(DI, 4).reshape(16, 128, 4)
                        .transpose(1, 0, 2).reshape(128, 64)),
            convbc=cols128(inp[f"L1_attn_bconv_{sfx}"]),
            bdtc=cols128(inp[f"L1_attn_bdt_{sfx}"]),
            ident=eye, alogc=f32c(alog.reshape(16, 128, 16)
                                  .transpose(1, 0, 2).reshape(128, 256)),
            dpc=cols128(inp[f"L1_attn_Dp_{sfx}"]),
            woutT=f32c(np.asarray(inp[f"L1_attn_wout_{sfx}"], np.float32).T),
            oh32=oh, wprojT=wprojT, bprojc=bprojc,
            idxC=idxC, idxEf=f32c(idxEf).astype(np.int32),
            idxEb=f32c(idxEb).astype(np.int32),
        ))
        m['idxC'] = np.ascontiguousarray(idxC)
        m['idxEf'] = np.ascontiguousarray(idxEf)
        m['idxEb'] = np.ascontiguousarray(idxEb)
        wxf = np.asarray(inp[f"L1_attn_wx_{sfx}"], np.float32)
        wxT = np.zeros((DI, 128), np.float32)
        wxT[:, 0:32] = wxf[64:96].T
        wxT[:, 64:128] = wxf[0:64].T
        m['wxT'] = f32c(wxT)
        m.update(fshared)
        maps.append(m)
    return maps


def kernel_fused(**inputs):
    nc = build_fused()
    maps = host_prep_fused(inputs)
    res = run_bass_kernel_spmd(nc, maps, core_ids=list(range(NCORES)))
    out = np.empty((B, T, D), np.float32)
    for c in range(NCORES):
        b, h = c // 2, c % 2
        out[b, h * TLOC:(h + 1) * TLOC, :] = res.results[c]["y"].T
    return out


# version 13
# speedup vs baseline: 1.1367x; 1.0128x over previous
import numpy as np
import ml_dtypes
from contextlib import ExitStack

import concourse.bass as bass
import concourse.mybir as mybir
from concourse.bass_utils import run_bass_kernel_spmd

F32 = mybir.dt.float32
BF16 = mybir.dt.bfloat16
I32 = mybir.dt.int32
AF = mybir.ActivationFunctionType
ALU = mybir.AluOpType
AX = mybir.AxisListType.X
BF16NP = ml_dtypes.bfloat16

B, T, D = 4, 4096, 1024
DI = 2048
DS = 16
TLOC = 2048
NCORES = 8
TC = 512


def f32c(x):
    return np.ascontiguousarray(x, dtype=np.float32)


def cols128(v):
    return f32c(np.asarray(v, np.float32).reshape(-1, 128).T)


def ceil_div(a, b):
    return (a + b - 1) // b


class Cnt(dict):
    def bump(self, k, n=1):
        self[k] = self.get(k, 0) + n
        return self[k]

    def g(self, k):
        return self.get(k, 0)


class PhaseList:
    def __init__(self):
        self.phases = []

    def add(self, p):
        self.phases.append(p)
        return p

    def emit_all(self, block):
        for engname in ('gpsimd', 'sync', 'tensor', 'scalar', 'vector'):
            def mk(en):
                def run(eng):
                    for p in self.phases:
                        p.emit(en, eng)
                return run
            getattr(block, engname)(mk(engname))


class RawPhase:
    def __init__(self, emitters):
        self.emitters = emitters

    def emit(self, engname, eng):
        f = self.emitters.get(engname)
        if f is not None:
            f(eng)


MM_WCFG = {}


class MMPhase:
    def __init__(self, S, C, wT_dram, K, N, cols, act_fn, wpanel, psums,
                 evac_fn, evac_engines=('scalar',), c_off=0, part_base=0,
                 extra_tensor_waits=(), wq='gpsimd', sems=None):
        self.sm = sems or {'wl0': 'wl0', 'wl1': 'wl1', 'mm': 'mm', 'ev': 'ev'}
        self.wq = wq
        self.wc_n = None
        nm = getattr(wT_dram, 'name', None)
        if nm in MM_WCFG:
            self.wq = 'gpsimd'
            self.wc_n = MM_WCFG[nm]
        self.S, self.C = S, C
        self.wT, self.K, self.N = wT_dram, K, N
        self.act_fn, self.wpanel, self.psums = act_fn, wpanel, psums
        self.evac_fn = evac_fn
        self.evac_engines = evac_engines
        self.pb = part_base
        self.kt = ceil_div(K, 128)
        self.supers = [(n0, min(512, N - n0)) for n0 in range(0, N, 512)]
        self.cgs = [(c0 + c_off, min(512, cols - c0))
                    for c0 in range(0, cols, 512)]
        self.groups = []
        for si, (n0, nw) in enumerate(self.supers):
            for nt in range(ceil_div(nw, 128)):
                nr = min(128, nw - nt * 128)
                for (c0, cw) in self.cgs:
                    self.groups.append((si, n0 + nt * 128, nr, c0, cw))
        self.extra_tw = list(extra_tensor_waits)
        self.single_panel = wpanel[0] is wpanel[1]
        self.base_wl = [C.g(self.sm['wl0']), C.g(self.sm['wl1'])]
        self.base_mm = C.g(self.sm['mm'])
        self.base_ev = C.g(self.sm['ev'])
        ns = len(self.supers)
        C.bump(self.sm['wl0'], 16 * self.kt * ((ns + 1) // 2))
        C.bump(self.sm['wl1'], 16 * self.kt * (ns // 2))
        C.bump(self.sm['mm'], len(self.groups))
        C.bump(self.sm['ev'], len(self.groups))

    def ktile_rows(self, k):
        return min(128, self.K - k * 128)

    def mm_thru_super(self, si):
        return self.base_mm + sum(1 for g in self.groups if g[0] <= si)

    def emit(self, engname, eng):
        if engname == self.wq:
            self._emit_gpsimd(eng)
        if engname == 'tensor':
            self._emit_tensor(eng)
        if engname in self.evac_engines:
            self._emit_evac(engname, eng)

    def _emit_gpsimd(self, g):
        S = self.S
        if self.wc_n is not None:
            g.wait_ge(S['wc'], self.wc_n)
        g.wait_ge(S[self.sm['mm']], self.base_mm)
        for si, (n0, nw) in enumerate(self.supers):
            if self.single_panel:
                if si >= 1:
                    g.wait_ge(S[self.sm['mm']], self.mm_thru_super(si - 1))
            elif si >= 2:
                g.wait_ge(S[self.sm['mm']], self.mm_thru_super(si - 2))
            for k in range(self.kt):
                kr = self.ktile_rows(k)
                g.dma_start(
                    out=self.wpanel[si % 2][self.pb:self.pb + kr,
                                            k * 512:k * 512 + nw],
                    in_=self.wT[k * 128:k * 128 + kr, n0:n0 + nw],
                ).then_inc(S[self.sm['wl0' if si % 2 == 0 else 'wl1']], 16)

    def _emit_tensor(self, t):
        S = self.S
        t.wait_ge(S[self.sm['ev']], self.base_ev)
        for sem, val in self.extra_tw:
            t.wait_ge(S[sem], val)
        nps = len(self.psums)
        cur_si = -1
        for gi, (si, n_abs, nr, c0, cw) in enumerate(self.groups):
            if si != cur_si:
                par = si % 2
                t.wait_ge(S[self.sm['wl0' if par == 0 else 'wl1']],
                          self.base_wl[par] + 16 * self.kt * (si // 2 + 1))
                cur_si = si
            ps = self.psums[gi % nps]
            if gi >= nps:
                t.wait_ge(S[self.sm['ev']], self.base_ev + gi - nps + 1)
            panel = self.wpanel[si % 2]
            n_loc = n_abs - self.supers[si][0]
            for k in range(self.kt):
                kr = self.ktile_rows(k)
                ins = t.matmul(
                    ps[0:nr, 0:cw],
                    panel[self.pb:self.pb + kr,
                          k * 512 + n_loc:k * 512 + n_loc + nr],
                    self.act_fn(k)[:, c0:c0 + cw],
                    start=(k == 0), stop=(k == self.kt - 1),
                )
                if k == self.kt - 1:
                    ins.then_inc(S[self.sm['mm']], 1)

    def _emit_evac(self, engname, eng):
        S = self.S
        nps = len(self.psums)
        for gi, (si, n_abs, nr, c0, cw) in enumerate(self.groups):
            eng.wait_ge(S[self.sm['mm']], self.base_mm + gi + 1)
            self.evac_fn(engname, eng, self.psums[gi % nps],
                         n_abs, nr, c0, cw, gi)


class LNPhase:
    def __init__(self, S, C, s_bf_fn, s_fn, out_fn, cw, aux, pre=None):
        self.S, self.C = S, C
        self.s_bf_fn, self.s_fn, self.out_fn = s_bf_fn, s_fn, out_fn
        self.cw, self.aux = cw, aux
        self.pre = pre or {}
        self.base_sq = C.g('sq'); C.bump('sq', 8)
        self.base_lnm = C.g('lnm'); C.bump('lnm', 1)
        self.base_lnv = C.g('lnv'); C.bump('lnv', 1)
        self.base_lns = C.g('lns'); C.bump('lns', 1)
        self.base_lnb = C.g('lnb'); C.bump('lnb', 2)
        self.base_lne = C.g('lne'); C.bump('lne', 8)
        self.base_squ = C.g('squ'); C.bump('squ', 8)

    def emit(self, engname, eng):
        S, A, cw = self.S, self.aux, self.cw
        for sem, val in self.pre.get(engname, ()):
            eng.wait_ge(S[sem], val)
        if engname == 'scalar':
            for k in range(8):
                if k >= 2:
                    eng.wait_ge(S['squ'], self.base_squ + k - 1)
                eng.activation(A['sq_bf'][k % 2][:, 0:cw],
                               self.s_bf_fn(k)[:, 0:cw],
                               AF.Square).then_inc(S['sq'], 1)
            eng.wait_ge(S['lnv'], self.base_lnv + 1)
            eng.activation(A['row_t'][0:1, 0:cw], A['row_v'][0:1, 0:cw],
                           AF.Ln, bias=A['eps'][0:1, 0:1])
            eng.activation(A['row_r'][0:1, 0:cw], A['row_t'][0:1, 0:cw],
                           AF.Exp, bias=0.0, scale=A['neghalf'][0:1, 0:1]
                           ).then_inc(S['lns'], 1)
        elif engname == 'tensor':
            eng.wait_ge(S['lnv'], self.base_lnv)
            eng.wait_ge(S['lne'], self.base_lne)
            for k in range(8):
                ins = eng.matmul(A['ps_m'][0:1, 0:cw], A['ones128_bf'][:, 0:1],
                                 self.s_bf_fn(k)[:, 0:cw],
                                 start=(k == 0), stop=(k == 7))
                if k == 7:
                    ins.then_inc(S['lnm'], 1)
            for k in range(8):
                eng.wait_ge(S['sq'], self.base_sq + k + 1)
                ins = eng.matmul(A['ps_q'][0:1, 0:cw], A['ones128_bf'][:, 0:1],
                                 A['sq_bf'][k % 2][:, 0:cw],
                                 start=(k == 0), stop=(k == 7))
                ins.then_inc(S['squ'], 1)
            eng.wait_ge(S['lns'], self.base_lns + 1)
            eng.matmul(A['ps_mb'][:, 0:cw], A['ones1_f32'][0:1, :],
                       A['row_m'][0:1, 0:cw], start=True, stop=True
                       ).then_inc(S['lnb'], 1)
            eng.matmul(A['ps_rb'][:, 0:cw], A['ones1_f32'][0:1, :],
                       A['row_r'][0:1, 0:cw], start=True, stop=True
                       ).then_inc(S['lnb'], 1)
        elif engname == 'vector':
            eng.wait_ge(S['lnm'], self.base_lnm + 1)
            eng.wait_ge(S['squ'], self.base_squ + 8)
            eng.tensor_scalar_mul(A['row_m'][0:1, 0:cw], A['ps_m'][0:1, 0:cw],
                                  1.0 / 1024.0)
            eng.tensor_scalar_mul(A['row_v'][0:1, 0:cw], A['ps_q'][0:1, 0:cw],
                                  1.0 / 1024.0)
            eng.tensor_mul(A['row_t'][0:1, 0:cw], A['row_m'][0:1, 0:cw],
                           A['row_m'][0:1, 0:cw])
            eng.tensor_sub(A['row_v'][0:1, 0:cw], A['row_v'][0:1, 0:cw],
                           A['row_t'][0:1, 0:cw]).then_inc(S['lnv'], 1)
            eng.wait_ge(S['lnb'], self.base_lnb + 2)
            for k in range(8):
                eng.tensor_sub(A['nrm'][k % 2][:, 0:cw], self.s_fn(k)[:, 0:cw],
                               A['ps_mb'][:, 0:cw])
                eng.tensor_mul(self.out_fn(k)[:, 0:cw],
                               A['nrm'][k % 2][:, 0:cw],
                               A['ps_rb'][:, 0:cw]).then_inc(S['lne'], 1)


A1_SEMS = ('wl0', 'wl1', 'mm', 'ev', 'xl', 'cl', 'cv', 'xs', 'dv',
           'sb', 'sbr', 'sbc', 'st', 'sm0', 'sm1')


def _emit_A1(nc, e, S, C, PL, io):
    xw = io['xw']; w_in = io['w_in']; wx_main = io['wx_main']
    wx_small = io['wx_small']; w_dt = io['w_dt']; convw_d = io['convw']
    convb_d = io['convb']; bdt_d = io['bdt']; selb_d = io['selb']
    G_o = io['G']; P_o = io['P']; M1_o = io['M1']
    if True:
        x_bf = [e(nc.sbuf_tensor(f"xbf{k}", [128, 516], BF16)) for k in range(8)]
        xin = [e(nc.sbuf_tensor(f"xin{j}", [128, 516], F32)) for j in range(16)]
        sres = [e(nc.sbuf_tensor(f"sres{j}", [128, 516], F32)) for j in range(16)]
        xc32 = [e(nc.sbuf_tensor(f"xc32{j}", [128, 512], F32)) for j in range(16)]
        xcbf = [e(nc.sbuf_tensor(f"xcbf{j}", [128, 512], BF16)) for j in range(16)]
        wpanel = [e(nc.sbuf_tensor(f"wp{i}", [128, 16 * 512], BF16)) for i in range(2)]
        small32 = e(nc.sbuf_tensor("small32", [32, 512], F32))
        small32_bf = e(nc.sbuf_tensor("small32bf", [32, 512], BF16))
        sb_row = e(nc.sbuf_tensor("sbrow", [1, 512], F32))
        ctap = [e(nc.sbuf_tensor(f"ctap{i}", [128, 512], F32)) for i in range(2)]
        m1buf = [e(nc.sbuf_tensor(f"m1b{i}", [128, 512], F32)) for i in range(2)]
        tA = [e(nc.sbuf_tensor(f"tA{i}", [128, 512], F32)) for i in range(2)]
        tB = [e(nc.sbuf_tensor(f"tB{i}", [128, 512], F32)) for i in range(2)]
        dbuf = [e(nc.sbuf_tensor(f"dbuf{i}", [128, 512], F32)) for i in range(2)]
        ctbuf = [e(nc.sbuf_tensor(f"ctb{i}", [128, 512], F32)) for i in range(2)]
        gbuf = [e(nc.sbuf_tensor(f"gbuf{i}", [128, 512], F32)) for i in range(2)]
        pbuf = [e(nc.sbuf_tensor(f"pbuf{i}", [128, 512], F32)) for i in range(2)]
        gcarry = e(nc.sbuf_tensor("gcarry", [128, 16], F32))
        pcarry = e(nc.sbuf_tensor("pcarry", [128, 16], F32))
        zeros512 = e(nc.sbuf_tensor("zeros512", [128, 512], F32))
        convw_t = e(nc.sbuf_tensor("convwt", [128, 64], F32))
        convb_t = e(nc.sbuf_tensor("convbt", [128, 16], F32))
        bdt_t = e(nc.sbuf_tensor("bdtt", [128, 16], F32))
        selb_bf = e(nc.sbuf_tensor("selbbf", [32, 1], BF16))
        ones1 = e(nc.sbuf_tensor("ones1", [1, 128], F32))
        psA = [e(nc.psum_tensor(f"psA{i}", [128, 512], F32)) for i in range(2)]
        ps_sb = e(nc.psum_tensor("pssb", [1, 512], F32))
        ps_sbc = e(nc.psum_tensor("pssbc", [128, 512], F32))

        def consts_g(g):
            g.dma_start(out=convw_t[:, :], in_=convw_d).then_inc(S['cl'], 16)
            g.dma_start(out=convb_t[:, :], in_=convb_d).then_inc(S['cl'], 16)
            g.dma_start(out=bdt_t[:, :], in_=bdt_d).then_inc(S['cl'], 16)
            g.dma_start(out=selb_bf[:, :], in_=selb_d).then_inc(S['cl'], 16)

        def consts_v(v):
            v.memset(ones1[0:1, :], 1.0)
            v.memset(zeros512[:, :], 0.0)
            v.memset(gcarry[:, :], 0.0)
            v.memset(pcarry[:, :], 1.0)
            v.wait_ge(S['cl'], 64)

        def consts_s(s):
            s.wait_ge(S['cl'], 64)

        def consts_t(t):
            t.wait_ge(S['cl'], 64)
        C.bump('cl', 64)
        PL.add(RawPhase({'gpsimd': consts_g, 'vector': consts_v,
                         'scalar': consts_s, 'tensor': consts_t}))

        for ci in range(4):
            c0d = ci * 512

            prev_mm = C.g('mm')

            def loadx(g, ci=ci, prev_mm=prev_mm):
                if ci > 0:
                    g.wait_ge(S['mm'], prev_mm)
                for k in range(8):
                    g.dma_start(out=x_bf[k][:, :],
                                in_=xw[k * 128:(k + 1) * 128,
                                       ci * 512:ci * 512 + 516]
                                ).then_inc(S['xl'], 16)
            C.bump('xl', 128)
            xl_after = C.g('xl')
            PL.add(RawPhase({'gpsimd': loadx}))

            st_mm1 = {'conv_j': 0, 'silu_j': 0}
            base_cv = C.g('cv')
            base_xs = C.g('xs')

            def mk_mm1_evac(base_cv=base_cv, base_xs=base_xs, st=st_mm1,
                            ph_holder=[]):
                def evac(engname, eng, ps, n_abs, nr, c0, cw, gi):
                    ph = ph_holder[0]
                    if n_abs < DI:
                        j = n_abs // 128
                        if engname == 'scalar':
                            eng.activation(xin[j][:, c0:c0 + cw], ps[0:nr, 0:cw],
                                           AF.Copy).then_inc(S['ev'], 1)
                            if c0 == 512:
                                if j >= 1:
                                    eng.wait_ge(S['cv'], base_cv + j)
                                    eng.activation(xc32[j - 1][:, :],
                                                   ctap[(j - 1) % 2][:, :],
                                                   AF.Silu,
                                                   bias=convb_t[:, j - 1:j])
                                    eng.activation(xcbf[j - 1][:, :],
                                                   xc32[j - 1][:, :],
                                                   AF.Copy).then_inc(S['xs'], 1)
                        elif engname == 'vector':
                            if c0 == 512:
                                j = n_abs // 128
                                eng.wait_ge(S['ev'], ph.base_ev + gi + 1)
                                if j >= 2:
                                    eng.wait_ge(S['xs'], base_xs + j - 1)
                                eng.tensor_scalar_mul(
                                    ctap[j % 2][:, :], xin[j][:, 1:513],
                                    convw_t[:, 4 * j:4 * j + 1])
                                for k in range(1, 4):
                                    ins = eng.scalar_tensor_tensor(
                                        ctap[j % 2][:, :],
                                        xin[j][:, 1 + k:513 + k],
                                        convw_t[:, 4 * j + k:4 * j + k + 1],
                                        ctap[j % 2][:, :], ALU.mult, ALU.add)
                                ins.then_inc(S['cv'], 1)
                    else:
                        jr = (n_abs - DI) // 128
                        if engname == 'scalar':
                            eng.activation(sres[jr][:, c0:c0 + cw],
                                           ps[0:nr, 0:cw],
                                           AF.Silu).then_inc(S['ev'], 1)
                            if gi == 32:
                                eng.wait_ge(S['cv'], base_cv + 16)
                                eng.activation(xc32[15][:, :], ctap[15 % 2][:, :],
                                               AF.Silu, bias=convb_t[:, 15:16])
                                eng.activation(xcbf[15][:, :], xc32[15][:, :],
                                               AF.Copy).then_inc(S['xs'], 1)
                return evac, ph_holder

            evac1, holder1 = mk_mm1_evac()
            mm1 = MMPhase(S, C, w_in, 1024, 4096, 516, lambda k: x_bf[k][:, :],
                          wpanel, psA, evac1, evac_engines=('scalar', 'vector'),
                          extra_tensor_waits=[('xl', xl_after)])
            holder1.append(mm1)
            C.bump('cv', 16)
            C.bump('xs', 16)
            PL.add(mm1)

            base_st2 = C.g('st')

            base_sm = [C.g('sm0'), C.g('sm1')]

            def mk_mm2_evac(base_sm=base_sm, c0d=c0d, ph_holder=[]):
                def evac(engname, eng, ps, n_abs, nr, c0, cw, gi):
                    j = n_abs // 128
                    if engname == 'vector':
                        par = gi % 2
                        eng.wait_ge(S['sm0' if par == 0 else 'sm1'],
                                    base_sm[par] + 16 * (gi // 2))
                        eng.tensor_mul(m1buf[gi % 2][:, 0:cw], ps[0:nr, 0:cw],
                                       sres[j][:, 4:4 + cw]).then_inc(S['ev'], 1)
                    elif engname == 'sync':
                        eng.wait_ge(S['ev'], ph_holder[0].base_ev + gi + 1)
                        eng.dma_start(out=M1_o[j * 128:(j + 1) * 128,
                                               c0d:c0d + cw],
                                      in_=m1buf[gi % 2][:, 0:cw]
                                      ).then_inc(S['sm0' if gi % 2 == 0 else 'sm1'], 16)
                return evac, ph_holder

            evac2, holder2 = mk_mm2_evac()
            mm2 = MMPhase(S, C, wx_main, DI, DI, 512,
                          lambda k: xcbf[k][:, :], wpanel, psA, evac2,
                          evac_engines=('vector', 'sync'),
                          extra_tensor_waits=[('xs', C.g('xs'))])
            holder2.append(mm2)
            C.bump('sm0', 16 * 8)
            C.bump('sm1', 16 * 8)
            PL.add(mm2)

            def evac3(engname, eng, ps, n_abs, nr, c0, cw, gi):
                eng.activation(small32[0:32, 0:cw], ps[0:32, 0:cw], AF.Copy)
                eng.activation(small32_bf[0:32, 0:cw], ps[0:32, 0:cw],
                               AF.Copy).then_inc(S['ev'], 1)

            mm3 = MMPhase(S, C, wx_small, DI, 32, 512,
                          lambda k: xcbf[k][:, :], wpanel, psA, evac3,
                          evac_engines=('scalar',))
            PL.add(mm3)

            base_dv = C.g('dv')
            sb_n = C.g('sb')

            def sb_t(t, after_ev=C.g('ev'), base_dv=base_dv, sb_n=sb_n):
                t.wait_ge(S['ev'], after_ev)
                t.wait_ge(S['dv'], base_dv)
                t.matmul(ps_sb[0:1, 0:512], selb_bf[0:32, 0:1],
                         small32_bf[0:32, 0:512], start=True, stop=True
                         ).then_inc(S['sb'], 1)
                t.wait_ge(S['sbr'], sb_n + 1)
                t.matmul(ps_sbc[:, 0:512], ones1[0:1, 0:128],
                         sb_row[0:1, 0:512], start=True, stop=True
                         ).then_inc(S['sbc'], 1)

            def sb_s(s, sb_n=sb_n):
                s.wait_ge(S['sb'], sb_n + 1)
                s.activation(sb_row[0:1, 0:512], ps_sb[0:1, 0:512],
                             AF.Copy).then_inc(S['sbr'], 1)
            C.bump('sb', 1)
            C.bump('sbr', 1)
            C.bump('sbc', 1)
            PL.add(RawPhase({'tensor': sb_t, 'scalar': sb_s}))

            base_dv4 = C.g('dv')
            base_st4 = C.g('st')
            sbc_n = C.g('sbc')

            def mk_mm4_evac(base_dv4=base_dv4, base_st4=base_st4, c0d=c0d,
                            sbc_n=sbc_n, ph_holder=[]):
                def evac(engname, eng, ps, n_abs, nr, c0, cw, gi):
                    j = gi
                    if engname == 'scalar':
                        eng.wait_ge(S['dv'], base_dv4 + max(0, j - 1))
                        eng.activation(tA[j % 2][:, 0:cw], ps[0:nr, 0:cw],
                                       AF.Abs, bias=bdt_t[:, j:j + 1])
                        eng.activation(tB[j % 2][:, 0:cw], tA[j % 2][:, 0:cw],
                                       AF.Exp, bias=0.0, scale=-1.0)
                        eng.activation(tA[j % 2][:, 0:cw], tB[j % 2][:, 0:cw],
                                       AF.Ln, bias=1.0)
                        eng.activation(tB[j % 2][:, 0:cw], ps[0:nr, 0:cw],
                                       AF.Relu, bias=bdt_t[:, j:j + 1]
                                       ).then_inc(S['ev'], 1)
                    elif engname == 'vector':
                        eng.wait_ge(S['ev'], ph_holder[0].base_ev + gi + 1)
                        eng.wait_ge(S['sbc'], sbc_n)
                        eng.wait_ge(S['st'], base_st4 + 32 * max(0, j - 1))
                        eng.tensor_add(dbuf[j % 2][:, 0:cw], tA[j % 2][:, 0:cw],
                                       tB[j % 2][:, 0:cw])
                        eng.tensor_mul(ctbuf[j % 2][:, 0:cw], xc32[j][:, 0:cw],
                                       ps_sbc[:, 0:cw])
                        eng.tensor_tensor_scan(
                            gbuf[j % 2][:, 0:cw], dbuf[j % 2][:, 0:cw],
                            ctbuf[j % 2][:, 0:cw], gcarry[:, j:j + 1],
                            ALU.mult, ALU.add)
                        eng.tensor_tensor_scan(
                            pbuf[j % 2][:, 0:cw], dbuf[j % 2][:, 0:cw],
                            zeros512[:, 0:cw], pcarry[:, j:j + 1],
                            ALU.mult, ALU.add)
                        eng.tensor_copy(gcarry[:, j:j + 1],
                                        gbuf[j % 2][:, cw - 1:cw])
                        eng.tensor_copy(pcarry[:, j:j + 1],
                                        pbuf[j % 2][:, cw - 1:cw]
                                        ).then_inc(S['dv'], 1)
                    elif engname == 'sync':
                        eng.wait_ge(S['dv'], base_dv4 + j + 1)
                        eng.dma_start(out=G_o[j * 128:(j + 1) * 128,
                                              c0d:c0d + cw],
                                      in_=gbuf[j % 2][:, 0:cw]
                                      ).then_inc(S['st'], 16)
                        eng.dma_start(out=P_o[j * 128:(j + 1) * 128,
                                              c0d:c0d + cw],
                                      in_=pbuf[j % 2][:, 0:cw]
                                      ).then_inc(S['st'], 16)
                return evac, ph_holder

            evac4, holder4 = mk_mm4_evac()
            mm4 = MMPhase(S, C, w_dt, DS, DI, 512,
                          lambda k: small32_bf[0:16, :], wpanel, psA, evac4,
                          evac_engines=('scalar', 'vector', 'sync'))
            holder4.append(mm4)
            C.bump('dv', 16)
            C.bump('st', 32 * 16)
            PL.add(mm4)

        st_total = C.g('st')
        sm0_t, sm1_t = C.g('sm0'), C.g('sm1')

        def fin(s):
            s.wait_ge(S['st'], st_total)
            s.wait_ge(S['sm0'], sm0_t)
            s.wait_ge(S['sm1'], sm1_t)
        PL.add(RawPhase({'sync': fin}))
        io['_a1'] = dict(gcarry=gcarry)


def build_A1():
    nc = bass.Bass("TRN2", target_bir_lowering=False)
    io = {}
    for nm, shp, knd in (('xw', [D, TLOC + 4], 'ExternalInput'),
                         ('w_in', [D, 2 * DI], 'ExternalInput'),
                         ('wx_main', [DI, DI], 'ExternalInput'),
                         ('wx_small', [DI, 32], 'ExternalInput'),
                         ('w_dt', [DS, DI], 'ExternalInput'),
                         ('convw', [128, 64], 'ExternalInput'),
                         ('convb', [128, 16], 'ExternalInput'),
                         ('bdt', [128, 16], 'ExternalInput'),
                         ('selb', [32, 1], 'ExternalInput'),
                         ('G', [DI, TLOC], 'ExternalOutput'),
                         ('P', [DI, TLOC], 'ExternalOutput'),
                         ('M1', [DI, TLOC], 'ExternalOutput')):
        io[nm] = nc.dram_tensor(nm, shp, F32, kind=knd).ap()
    S = {}
    C = Cnt()
    PL = PhaseList()
    with ExitStack() as cs:
        e = cs.enter_context
        for sname in A1_SEMS:
            S[sname] = e(nc.semaphore(sname))
        _emit_A1(nc, e, S, C, PL, io)
        block = e(nc.Block())
        PL.emit_all(block)
    return nc


def host_prep_A1(inp):
    x = np.asarray(inp["L0_attn_x"], np.float32)
    w_in = f32c(inp["L0_attn_W_in"])
    wx = np.asarray(inp["L0_attn_W_x"], np.float32)
    wx_main = f32c(wx[:, 32:2080])
    wx_small = f32c(np.concatenate([wx[:, 0:16], wx[:, 16:32]], axis=1))
    w_dt = f32c(inp["L0_attn_W_dt"])
    convw = f32c(np.asarray(inp["L0_attn_conv_w"], np.float32)
                 .reshape(DI, 4).reshape(16, 128, 4)
                 .transpose(1, 0, 2).reshape(128, 64))
    convb = cols128(inp["L0_attn_conv_b"])
    bdt = cols128(inp["L0_attn_b_dt"])
    selb = np.zeros((32, 1), np.float32)
    selb[16:32, 0] = 1.0
    maps = []
    for c in range(NCORES):
        b, h = c // 2, c % 2
        t0 = h * TLOC
        xw = np.zeros((D, TLOC + 4), np.float32)
        lo = max(0, t0 - 4)
        xw[:, 4 - (t0 - lo):] = x[b, lo:t0 + TLOC, :].T
        maps.append(dict(xw=f32c(xw), w_in=w_in, wx_main=wx_main,
                         wx_small=wx_small, w_dt=w_dt, convw=convw,
                         convb=convb, bdt=bdt, selb=selb))
    return maps


A2_SEMS = ('wl0', 'wl1', 'mm', 'ev', 'xl0', 'xl1', 'xf', 'cl',
           'yv', 'sv', 'st', 'sq', 'squ', 'lnm', 'lnv', 'lns', 'lnb', 'lne',
           'cc1')


def _emit_A2(nc, e, S, C, PL, io):
    G_i = io['G']; P_i = io['P']; M1_i = io['M1']
    xT_d = io['xT']; wout_d = io['wout']; ln1_o = io['ln1']
    ccol_d = io.get('ccol')
    debug = io.get('debug', False)
    if debug:
        y2_dbg = io['y2dbg']; s_dbg = io['sdbg']; var_dbg = io['vardbg']
    if True:
        grot = [e(nc.sbuf_tensor(f"gr{i}", [128, 512], F32)) for i in range(2)]
        prot = [e(nc.sbuf_tensor(f"pr{i}", [128, 512], F32)) for i in range(2)]
        mrot = [e(nc.sbuf_tensor(f"mr{i}", [128, 512], F32)) for i in range(2)]
        y2t = [e(nc.sbuf_tensor(f"y2t{i}", [128, 512], F32)) for i in range(2)]
        y2bf = [e(nc.sbuf_tensor(f"y2bf{j}", [128, 512], BF16)) for j in range(16)]
        x32 = [e(nc.sbuf_tensor(f"x32{k}", [128, 512], F32)) for k in range(8)]
        s32 = [e(nc.sbuf_tensor(f"s32{k}", [128, 512], F32)) for k in range(8)]
        sbf = [e(nc.sbuf_tensor(f"sbf{k}", [128, 512], BF16)) for k in range(8)]
        lnout = [e(nc.sbuf_tensor(f"lno{k}", [128, 512], F32)) for k in range(8)]
        wpanel = [e(nc.sbuf_tensor(f"wp{i}", [128, 16 * 512], BF16)) for i in range(2)]
        ccol_t = e(nc.sbuf_tensor("ccolt", [128, 16], F32))
        cf_t = e(nc.sbuf_tensor("cft", [128, 1], F32))
        aux = dict(
            ones128_bf=e(nc.sbuf_tensor("o128", [128, 1], BF16)),
            ones1_f32=e(nc.sbuf_tensor("o1f", [1, 128], F32)),
            sq_bf=[e(nc.sbuf_tensor(f"sqbf{i}", [128, 512], BF16)) for i in range(2)],
            row_m=e(nc.sbuf_tensor("row_m", [1, 512], F32)),
            row_v=e(nc.sbuf_tensor("row_v", [1, 512], F32)),
            row_t=e(nc.sbuf_tensor("row_t", [1, 512], F32)),
            row_r=e(nc.sbuf_tensor("row_r", [1, 512], F32)),
            eps=e(nc.sbuf_tensor("cst_eps", [1, 1], F32)),
            neghalf=e(nc.sbuf_tensor("cst_nh", [1, 1], F32)),
            nrm=[e(nc.sbuf_tensor(f"nrm{i}", [128, 512], F32)) for i in range(2)],
            ps_m=e(nc.psum_tensor("psm", [1, 512], F32)),
            ps_q=e(nc.psum_tensor("psq", [1, 512], F32)),
            ps_mb=e(nc.psum_tensor("psmb", [128, 512], F32)),
            ps_rb=e(nc.psum_tensor("psrb", [128, 512], F32)),
        )
        psA = [e(nc.psum_tensor(f"psA{i}", [128, 512], F32)) for i in range(2)]

        base_cl = C.g('cl')
        if ccol_d is not None:
            def consts_g(g, base_cl=base_cl):
                g.dma_start(out=ccol_t[:, :], in_=ccol_d).then_inc(S['cl'], 16)

            def consts_v(v, base_cl=base_cl):
                v.memset(aux['ones1_f32'][0:1, :], 1.0)
                v.memset(aux['ones128_bf'][:, 0:1], 1.0)
                v.memset(aux['eps'][0:1, 0:1], 1e-5)
                v.memset(aux['neghalf'][0:1, 0:1], -0.5)
                v.wait_ge(S['cl'], base_cl + 16)
            C.bump('cl', 16)
            PL.add(RawPhase({'gpsimd': consts_g, 'vector': consts_v}))
        else:
            gout = io['gend_out']; cflag_d = io['cflag']
            cc1_n = io['cc1_n']

            def consts_g(g, base_cl=base_cl):
                g.dma_start(out=cf_t[:, :], in_=cflag_d).then_inc(S['cl'], 16)
                g.wait_ge(S['cc1'], cc1_n)
                g.dma_start(out=ccol_t[:, :], in_=gout[0:128, :]
                            ).then_inc(S['cl'], 16)

            def consts_v(v, base_cl=base_cl):
                v.memset(aux['ones1_f32'][0:1, :], 1.0)
                v.memset(aux['ones128_bf'][:, 0:1], 1.0)
                v.memset(aux['eps'][0:1, 0:1], 1e-5)
                v.memset(aux['neghalf'][0:1, 0:1], -0.5)
                v.wait_ge(S['cl'], base_cl + 32)
                v.tensor_scalar_mul(ccol_t[:, :], ccol_t[:, :], cf_t[:, 0:1]
                                    ).then_inc(S['cl'], 1)
            C.bump('cl', 33)
            PL.add(RawPhase({'gpsimd': consts_g, 'vector': consts_v}))

        for ci in range(4):
            c0d = ci * 512

            base_xl = [C.g('xl0'), C.g('xl1')]
            base_yv = C.g('yv')
            prev_mm = C.g('mm')
            prev_sv = C.g('sv')

            def loads(s, base_yv=base_yv, c0d=c0d, ci=ci, prev_mm=prev_mm,
                      prev_sv=prev_sv):
                if ci > 0:
                    s.wait_ge(S['sv'], prev_sv)
                for j in range(16):
                    if j >= 2:
                        s.wait_ge(S['yv'], base_yv + j - 1)
                    sem = S['xl0' if j % 2 == 0 else 'xl1']
                    s.dma_start(out=grot[j % 2][:, :],
                                in_=G_i[j * 128:(j + 1) * 128, c0d:c0d + 512]
                                ).then_inc(sem, 16)
                    s.dma_start(out=prot[j % 2][:, :],
                                in_=P_i[j * 128:(j + 1) * 128, c0d:c0d + 512]
                                ).then_inc(sem, 16)
                    s.dma_start(out=mrot[j % 2][:, :],
                                in_=M1_i[j * 128:(j + 1) * 128, c0d:c0d + 512]
                                ).then_inc(sem, 16)
                for k in range(8):
                    s.dma_start(out=x32[k][:, :],
                                in_=xT_d[k * 128:(k + 1) * 128, c0d:c0d + 512]
                                ).then_inc(S['xf'], 16)
            C.bump('xl0', 48 * 8)
            C.bump('xl1', 48 * 8)
            C.bump('xf', 16 * 8)
            PL.add(RawPhase({'sync': loads}))

            def y2ph(v, base_xl=base_xl, base_yv=base_yv, prev_mm=prev_mm,
                     ci=ci, cl_now=C.g('cl')):
                if ci == 0:
                    v.wait_ge(S['cl'], cl_now)
                if ci > 0:
                    v.wait_ge(S['mm'], prev_mm)
                for j in range(16):
                    par = j % 2
                    v.wait_ge(S['xl0' if par == 0 else 'xl1'],
                              base_xl[par] + 48 * (j // 2 + 1))
                    v.scalar_tensor_tensor(
                        y2t[j % 2][:, :], prot[j % 2][:, :],
                        ccol_t[:, j:j + 1], grot[j % 2][:, :],
                        ALU.mult, ALU.add)
                    v.tensor_mul(y2bf[j][:, :], y2t[j % 2][:, :],
                                 mrot[j % 2][:, :]).then_inc(S['yv'], 1)
            C.bump('yv', 16)
            PL.add(RawPhase({'vector': y2ph}))

            xl_all = C.g('xf')

            def mk_evac5(ph_holder=[], xl_all=xl_all):
                def evac(engname, eng, ps, n_abs, nr, c0, cw, gi):
                    k = n_abs // 128
                    if engname == 'vector':
                        if gi == 0:
                            eng.wait_ge(S['xf'], xl_all)
                        eng.tensor_add(s32[k][:, 0:cw], ps[0:nr, 0:cw],
                                       x32[k][:, 0:cw]).then_inc(S['ev'], 1)
                        eng.tensor_copy(sbf[k][:, 0:cw], s32[k][:, 0:cw]
                                        ).then_inc(S['sv'], 1)
                return evac, ph_holder

            evac5, holder5 = mk_evac5()
            mm5 = MMPhase(S, C, wout_d, DI, D, 512,
                          lambda k: y2bf[k][:, :], wpanel, psA, evac5,
                          evac_engines=('vector',),
                          extra_tensor_waits=[('yv', C.g('yv'))])
            holder5.append(mm5)
            C.bump('sv', 8)
            PL.add(mm5)

            base_st = C.g('st')
            ln = LNPhase(S, C, lambda k: sbf[k], lambda k: s32[k],
                         lambda k: lnout[k], 512, aux,
                         pre={'tensor': [('sv', C.g('sv'))],
                              'scalar': [('sv', C.g('sv'))],
                              'vector': [('st', base_st)]})
            PL.add(ln)

            def store(s, base=ln.base_lne, c0d=c0d):
                for k in range(8):
                    s.wait_ge(S['lne'], base + k + 1)
                    s.dma_start(out=ln1_o[k * 128:(k + 1) * 128, c0d:c0d + 512],
                                in_=lnout[k][:, :]).then_inc(S['st'], 16)
            C.bump('st', 16 * 8)
            PL.add(RawPhase({'sync': store}))

            if debug:
                def dstore(s, c0d=c0d, lne_end=C.g('lne')):
                    s.wait_ge(S['lne'], lne_end)
                    for j in range(16):
                        s.dma_start(out=y2_dbg[j * 128:(j + 1) * 128,
                                               c0d:c0d + 512],
                                    in_=y2bf[j][:, :]).then_inc(S['st'], 16)
                    for k in range(8):
                        s.dma_start(out=s_dbg[k * 128:(k + 1) * 128,
                                              c0d:c0d + 512],
                                    in_=s32[k][:, :]).then_inc(S['st'], 16)
                    s.dma_start(out=var_dbg[0:1, c0d:c0d + 512],
                                in_=aux['row_m'][0:1, :]).then_inc(S['st'], 16)
                    s.dma_start(out=var_dbg[1:2, c0d:c0d + 512],
                                in_=aux['row_v'][0:1, :]).then_inc(S['st'], 16)
                    s.dma_start(out=var_dbg[2:3, c0d:c0d + 512],
                                in_=aux['row_r'][0:1, :]).then_inc(S['st'], 16)
                C.bump('st', 16 * 27)
                PL.add(RawPhase({'sync': dstore}))

        st_total = C.g('st')
        PL.add(RawPhase({'sync': lambda s: s.wait_ge(S['st'], st_total)}))
        io['_a2'] = dict(lnout=lnout)


def build_A2(debug=False):
    nc = bass.Bass("TRN2", target_bir_lowering=False)
    io = {'debug': debug}
    for nm, shp, knd in (('G', [DI, TLOC], 'ExternalInput'),
                         ('P', [DI, TLOC], 'ExternalInput'),
                         ('M1', [DI, TLOC], 'ExternalInput'),
                         ('ccol', [128, 16], 'ExternalInput'),
                         ('xT', [D, TLOC], 'ExternalInput'),
                         ('wout', [DI, D], 'ExternalInput'),
                         ('ln1', [D, TLOC], 'ExternalOutput')):
        io[nm] = nc.dram_tensor(nm, shp, F32, kind=knd).ap()
    if debug:
        io['y2dbg'] = nc.dram_tensor("y2dbg", [DI, TLOC], BF16, kind="ExternalOutput").ap()
        io['sdbg'] = nc.dram_tensor("sdbg", [D, TLOC], F32, kind="ExternalOutput").ap()
        io['vardbg'] = nc.dram_tensor("vardbg", [4, TLOC], F32, kind="ExternalOutput").ap()
    S = {}
    C = Cnt()
    PL = PhaseList()
    with ExitStack() as cs:
        e = cs.enter_context
        for sname in A2_SEMS:
            S[sname] = e(nc.semaphore(sname))
        _emit_A2(nc, e, S, C, PL, io)
        block = e(nc.Block())
        PL.emit_all(block)
    return nc


def host_prep_A2(inp, a1_results):
    x = np.asarray(inp["L0_attn_x"], np.float32)
    wout = f32c(inp["L0_attn_W_out"])
    maps = []
    for c in range(NCORES):
        b, h = c // 2, c % 2
        t0 = h * TLOC
        if h == 0:
            ccol = np.zeros((128, 16), np.float32)
        else:
            ccol = cols128(a1_results[c - 1]["G"][:, -1])
        maps.append(dict(G=a1_results[c]["G"], P=a1_results[c]["P"],
                         M1=a1_results[c]["M1"], ccol=ccol,
                         xT=f32c(x[b, t0:t0 + TLOC, :].T), wout=wout))
    return maps


B_SEMS = ('wl0', 'wl1', 'mm', 'ev', 'xl', 'cl', 'lp', 'sv', 'st',
          'sq', 'squ', 'lnm', 'lnv', 'lns', 'lnb', 'lne', 'cc2', 'tp')


def _emit_B(nc, e, S, C, PL, io):
    ln1_i = io['ln1']; mur_d = io['mur']; muk_d = io['muk']
    wrT_d = io['wrT']; wkT_d = io['wkT']; wvT_d = io['wvT']
    ln2_o = io['ln2']
    pcol_d = io.get('pcol')
    if True:
        x32 = [e(nc.sbuf_tensor(f"x32{k}", [128, 512], F32)) for k in range(8)]
        xp = [e(nc.sbuf_tensor(f"xp{k}", [128, 512], F32)) for k in range(8)]
        dt_t = [e(nc.sbuf_tensor(f"dt{i}", [128, 512], F32)) for i in range(2)]
        xr32 = [e(nc.sbuf_tensor(f"xr32{i}", [128, 512], F32)) for i in range(2)]
        xrbf = [e(nc.sbuf_tensor(f"xrbf{k}", [128, 512], BF16)) for k in range(8)]
        xkbf = [e(nc.sbuf_tensor(f"xkbf{k}", [128, 512], BF16)) for k in range(8)]
        r32 = [e(nc.sbuf_tensor(f"r32{k}", [128, 512], F32)) for k in range(8)]
        kabf = [e(nc.sbuf_tensor(f"kabf{k}", [128, 512], BF16)) for k in range(8)]
        t32 = [e(nc.sbuf_tensor(f"t32{i}", [128, 512], F32)) for i in range(2)]
        t32v = [e(nc.sbuf_tensor(f"t32v{i}", [128, 512], F32)) for i in range(2)]
        s32 = [e(nc.sbuf_tensor(f"s32{k}", [128, 512], F32)) for k in range(8)]
        sbf = [e(nc.sbuf_tensor(f"sbf{k}", [128, 512], BF16)) for k in range(8)]
        lnout = [e(nc.sbuf_tensor(f"lno{k}", [128, 512], F32)) for k in range(8)]
        wpanel = [e(nc.sbuf_tensor(f"wp{i}", [128, 8 * 512], BF16)) for i in range(2)]
        pcol_t = e(nc.sbuf_tensor("pcolt", [128, 8], F32))
        cf_t = e(nc.sbuf_tensor("cft", [128, 1], F32))
        mur_t = e(nc.sbuf_tensor("murt", [128, 8], F32))
        muk_t = e(nc.sbuf_tensor("mukt", [128, 8], F32))
        aux = dict(
            ones128_bf=e(nc.sbuf_tensor("o128", [128, 1], BF16)),
            ones1_f32=e(nc.sbuf_tensor("o1f", [1, 128], F32)),
            sq_bf=[e(nc.sbuf_tensor(f"sqbf{i}", [128, 512], BF16)) for i in range(2)],
            row_m=e(nc.sbuf_tensor("row_m", [1, 512], F32)),
            row_v=e(nc.sbuf_tensor("row_v", [1, 512], F32)),
            row_t=e(nc.sbuf_tensor("row_t", [1, 512], F32)),
            row_r=e(nc.sbuf_tensor("row_r", [1, 512], F32)),
            eps=e(nc.sbuf_tensor("cst_eps", [1, 1], F32)),
            neghalf=e(nc.sbuf_tensor("cst_nh", [1, 1], F32)),
            nrm=[e(nc.sbuf_tensor(f"nrm{i}", [128, 512], F32)) for i in range(2)],
            ps_m=e(nc.psum_tensor("psm", [1, 512], F32)),
            ps_q=e(nc.psum_tensor("psq", [1, 512], F32)),
            ps_mb=e(nc.psum_tensor("psmb", [128, 512], F32)),
            ps_rb=e(nc.psum_tensor("psrb", [128, 512], F32)),
        )
        psA = [e(nc.psum_tensor(f"psA{i}", [128, 512], F32)) for i in range(2)]

        base_cl = C.g('cl')
        if pcol_d is not None:
            def consts_g(g, base_cl=base_cl):
                g.dma_start(out=pcol_t[:, :], in_=pcol_d).then_inc(S['cl'], 16)
                g.dma_start(out=mur_t[:, :], in_=mur_d).then_inc(S['cl'], 16)
                g.dma_start(out=muk_t[:, :], in_=muk_d).then_inc(S['cl'], 16)

            def consts_v(v, base_cl=base_cl):
                v.memset(aux['ones1_f32'][0:1, :], 1.0)
                v.memset(aux['ones128_bf'][:, 0:1], 1.0)
                v.memset(aux['eps'][0:1, 0:1], 1e-5)
                v.memset(aux['neghalf'][0:1, 0:1], -0.5)
                v.wait_ge(S['cl'], base_cl + 48)
            C.bump('cl', 48)
            PL.add(RawPhase({'gpsimd': consts_g, 'vector': consts_v}))
        else:
            lout = io['l1col_out']; cflag_d = io['cflag']
            cc2_n = io['cc2_n']

            def consts_g(g, base_cl=base_cl):
                g.dma_start(out=cf_t[:, :], in_=cflag_d).then_inc(S['cl'], 16)
                g.dma_start(out=mur_t[:, :], in_=mur_d).then_inc(S['cl'], 16)
                g.dma_start(out=muk_t[:, :], in_=muk_d).then_inc(S['cl'], 16)
                g.wait_ge(S['cc2'], cc2_n)
                g.dma_start(out=pcol_t[:, :], in_=lout[0:128, :]
                            ).then_inc(S['cl'], 16)

            def consts_v(v, base_cl=base_cl):
                v.memset(aux['ones1_f32'][0:1, :], 1.0)
                v.memset(aux['ones128_bf'][:, 0:1], 1.0)
                v.memset(aux['eps'][0:1, 0:1], 1e-5)
                v.memset(aux['neghalf'][0:1, 0:1], -0.5)
                v.wait_ge(S['cl'], base_cl + 64)
                v.tensor_scalar_mul(pcol_t[:, :], pcol_t[:, :], cf_t[:, 0:1]
                                    ).then_inc(S['cl'], 1)
            C.bump('cl', 65)
            PL.add(RawPhase({'gpsimd': consts_g, 'vector': consts_v}))

        for ci in range(4):
            c0d = ci * 512

            base_xl = C.g('xl')
            prev_mm = C.g('mm')

            prev_sv = C.g('sv')
            prev_lne = C.g('lne')

            def loads(s, c0d=c0d, ci=ci, prev_mm=prev_mm, prev_sv=prev_sv,
                      prev_lne=prev_lne):
                if ci > 0:
                    s.wait_ge(S['mm'], prev_mm)
                    s.wait_ge(S['sv'], prev_sv)
                    s.wait_ge(S['lne'], prev_lne)
                for k in range(8):
                    s.dma_start(out=x32[k][:, :],
                                in_=ln1_i[k * 128:(k + 1) * 128, c0d:c0d + 512]
                                ).then_inc(S['xl'], 16)
            C.bump('xl', 16 * 8)
            PL.add(RawPhase({'sync': loads}))

            def lerp(v, base_xl=base_xl, ci=ci, xl_all=C.g('xl'),
                     cl_now=C.g('cl')):
                if ci == 0:
                    v.wait_ge(S['cl'], cl_now)
                v.wait_ge(S['xl'], xl_all)
                for k in range(8):
                    v.tensor_copy(xp[k][:, 1:512], x32[k][:, 0:511])
                    v.tensor_copy(xp[k][:, 0:1], pcol_t[:, k:k + 1])
                    v.tensor_copy(pcol_t[:, k:k + 1], x32[k][:, 511:512])
                    v.tensor_sub(dt_t[k % 2][:, :], x32[k][:, :], xp[k][:, :])
                    v.scalar_tensor_tensor(xr32[k % 2][:, :], dt_t[k % 2][:, :],
                                           mur_t[:, k:k + 1], xp[k][:, :],
                                           ALU.mult, ALU.add)
                    v.tensor_copy(xrbf[k][:, :], xr32[k % 2][:, :])
                    v.scalar_tensor_tensor(xr32[k % 2][:, :], dt_t[k % 2][:, :],
                                           muk_t[:, k:k + 1], xp[k][:, :],
                                           ALU.mult, ALU.add)
                    v.tensor_copy(xkbf[k][:, :], xr32[k % 2][:, :]
                                  ).then_inc(S['lp'], 1)
            C.bump('lp', 8)
            PL.add(RawPhase({'vector': lerp}))

            def evac_r(engname, eng, ps, n_abs, nr, c0, cw, gi):
                eng.activation(r32[n_abs // 128][:, 0:cw], ps[0:nr, 0:cw],
                               AF.Sigmoid).then_inc(S['ev'], 1)

            mmr = MMPhase(S, C, wrT_d, D, D, 512, lambda k: xrbf[k][:, :],
                          wpanel, psA, evac_r, evac_engines=('scalar',),
                          extra_tensor_waits=[('lp', C.g('lp'))])
            PL.add(mmr)

            def evac_k(engname, eng, ps, n_abs, nr, c0, cw, gi):
                k = n_abs // 128
                eng.activation(t32[gi % 2][:, 0:cw], ps[0:nr, 0:cw], AF.Relu)
                eng.activation(kabf[k][:, 0:cw], t32[gi % 2][:, 0:cw],
                               AF.Square).then_inc(S['ev'], 1)

            mmk = MMPhase(S, C, wkT_d, D, D, 512, lambda k: xkbf[k][:, :],
                          wpanel, psA, evac_k, evac_engines=('scalar',))
            PL.add(mmk)

            def evac_v(engname, eng, ps, n_abs, nr, c0, cw, gi):
                k = n_abs // 128
                eng.tensor_mul(t32v[gi % 2][:, 0:cw], ps[0:nr, 0:cw],
                               r32[k][:, 0:cw]).then_inc(S['ev'], 1)
                eng.tensor_add(s32[k][:, 0:cw], t32v[gi % 2][:, 0:cw],
                               x32[k][:, 0:cw])
                eng.tensor_copy(sbf[k][:, 0:cw], s32[k][:, 0:cw]
                                ).then_inc(S['sv'], 1)

            mmv = MMPhase(S, C, wvT_d, D, D, 512, lambda k: kabf[k][:, :],
                          wpanel, psA, evac_v, evac_engines=('vector',))
            C.bump('sv', 8)
            PL.add(mmv)

            base_st = C.g('st')
            ln = LNPhase(S, C, lambda k: sbf[k], lambda k: s32[k],
                         lambda k: lnout[k], 512, aux,
                         pre={'tensor': [('sv', C.g('sv'))],
                              'scalar': [('sv', C.g('sv'))],
                              'vector': [('st', base_st)]})
            PL.add(ln)

            def store(s, base=ln.base_lne, c0d=c0d):
                for k in range(8):
                    s.wait_ge(S['lne'], base + k + 1)
                    s.dma_start(out=ln2_o[k * 128:(k + 1) * 128, c0d:c0d + 512],
                                in_=lnout[k][:, :]).then_inc(S['st'], 16)
            C.bump('st', 16 * 8)
            PL.add(RawPhase({'sync': store}))

        st_total = C.g('st')
        PL.add(RawPhase({'sync': lambda s: s.wait_ge(S['st'], st_total)}))
        io['_b'] = dict(lnout=lnout)


def build_B():
    nc = bass.Bass("TRN2", target_bir_lowering=False)
    io = {}
    for nm, shp, knd in (('ln1', [D, TLOC], 'ExternalInput'),
                         ('pcol', [128, 8], 'ExternalInput'),
                         ('mur', [128, 8], 'ExternalInput'),
                         ('muk', [128, 8], 'ExternalInput'),
                         ('wrT', [D, D], 'ExternalInput'),
                         ('wkT', [D, D], 'ExternalInput'),
                         ('wvT', [D, D], 'ExternalInput'),
                         ('ln2', [D, TLOC], 'ExternalOutput')):
        io[nm] = nc.dram_tensor(nm, shp, F32, kind=knd).ap()
    S = {}
    C = Cnt()
    PL = PhaseList()
    with ExitStack() as cs:
        e = cs.enter_context
        for sname in B_SEMS:
            S[sname] = e(nc.semaphore(sname))
        _emit_B(nc, e, S, C, PL, io)
        block = e(nc.Block())
        PL.emit_all(block)
    return nc


def host_prep_B(inp, a2_results):
    wrT = f32c(np.asarray(inp["L0_ffn_W_r"], np.float32).T)
    wkT = f32c(np.asarray(inp["L0_ffn_W_k"], np.float32).T)
    wvT = f32c(np.asarray(inp["L0_ffn_W_v"], np.float32).T)
    mur = cols128(inp["L0_ffn_mu_r"])
    muk = cols128(inp["L0_ffn_mu_k"])
    maps = []
    for c in range(NCORES):
        h = c % 2
        if h == 0:
            pcol = np.zeros((128, 8), np.float32)
        else:
            pcol = cols128(a2_results[c - 1]["ln1"][:, -1])
        maps.append(dict(ln1=a2_results[c]["ln1"], pcol=pcol, mur=mur,
                         muk=muk, wrT=wrT, wkT=wkT, wvT=wvT))
    return maps


C_SEMS = ('wl0', 'wl1', 'mm', 'ev', 'xl', 'cl', 'cv', 'xs',
          'dv', 'st', 'sd0', 'sd1', 'su0', 'su1', 'sz0', 'sz1', 'cc3', 'gx',
          'gv')


def _emit_C(nc, e, S, C, PL, io):
    EV = io.get('ev_sem', 'ev')
    MMS = io.get('mm_sem', 'mm')
    WLS = io.get('wl_sems', ('wl0', 'wl1'))
    msems = {'wl0': WLS[0], 'wl1': WLS[1], 'mm': MMS, 'ev': EV}
    xT_d = io.get('xT')
    winT_d = io['winT']; wxT_d = io['wxT']; wdtT_d = io['wdtT']
    convw_d = io['convw']; convb_d = io['convb']; bdt_d = io['bdt']
    u_o = io['u']; sz_o = io['sz']; delta_o = io['delta']; bc_o = io['bc']
    if True:
        x_bf = [e(nc.sbuf_tensor(f"xbf{k}", [128, 516], BF16)) for k in range(8)]
        upre = [e(nc.sbuf_tensor(f"up{j}", [128, 516], BF16)) for j in range(16)]
        zbuf = [e(nc.sbuf_tensor(f"zb{i}", [128, 516], BF16)) for i in range(2)]
        u_bf = [e(nc.sbuf_tensor(f"ubf{j}", [128, 512], BF16)) for j in range(16)]
        ctap = [e(nc.sbuf_tensor(f"ctap{i}", [128, 512], F32)) for i in range(2)]
        small96 = e(nc.sbuf_tensor("s96", [128, 512], F32))
        small96_bf = e(nc.sbuf_tensor("s96bf", [128, 512], BF16))
        tA = [e(nc.sbuf_tensor(f"tA{i}", [128, 512], F32)) for i in range(2)]
        tB = [e(nc.sbuf_tensor(f"tB{i}", [128, 512], F32)) for i in range(2)]
        dbuf = [e(nc.sbuf_tensor(f"dbuf{i}", [128, 512], F32)) for i in range(2)]
        wpanel = [e(nc.sbuf_tensor(f"wp{i}", [128, 16 * 512], BF16)) for i in range(2)]
        convw_t = e(nc.sbuf_tensor("convwt", [128, 64], F32))
        convb_t = e(nc.sbuf_tensor("convbt", [128, 16], F32))
        bdt_t = e(nc.sbuf_tensor("bdtt", [128, 16], F32))
        psA = [e(nc.psum_tensor(f"psA{i}", [128, 512], F32))
               for i in range(io.get('npsA', 2))]
        io['wpanel_out'] = wpanel
        io['psA_out'] = psA
        fused_x = io.get('xT') is None
        if fused_x:
            tokt = [e(nc.sbuf_tensor(f"tok{i}", [128, 1024], BF16))
                    for i in range(4)]
            tokt.append(e(nc.sbuf_tensor("tok4", [4, 1024], BF16)))
            idxc_t = e(nc.sbuf_tensor("idxct", [128, 36], I32))
            ident_t = e(nc.sbuf_tensor("identt", [128, 128], BF16))
            if io.get('pstr_single'):
                _pst = e(nc.psum_tensor("pstr", [128, 256], BF16))
                ps_tr = [_pst[:, 0:128], _pst[:, 128:256]]
            else:
                ps_tr = [e(nc.psum_tensor(f"pstr{i}", [128, 128], BF16))
                         for i in range(2)]

        base_cl0 = C.g('cl')

        def consts_g(g, base_cl0=base_cl0):
            g.dma_start(out=convw_t[:, :], in_=convw_d).then_inc(S['cl'], 16)
            g.dma_start(out=convb_t[:, :], in_=convb_d).then_inc(S['cl'], 16)
            g.dma_start(out=bdt_t[:, :], in_=bdt_d).then_inc(S['cl'], 16)
            if fused_x:
                g.dma_start(out=idxc_t[:, 0:33], in_=io['idxC']
                            ).then_inc(S['cl'], 16)
                g.dma_start(out=ident_t[:, :], in_=io['ident']
                            ).then_inc(S['cl'], 16)

        n_cl = 80 if fused_x else 48

        def consts_w(eng, base_cl0=base_cl0, n_cl=n_cl):
            eng.wait_ge(S['cl'], base_cl0 + n_cl)
        C.bump('cl', n_cl)
        io['_c_cl'] = C.g('cl')
        PL.add(RawPhase({'gpsimd': consts_g, 'vector': consts_w,
                         'scalar': consts_w, 'tensor': consts_w}))

        def _c_chunk(ci):
            c0d = ci * 512

            prev_mm = C.g('mm')
            if xT_d is not None:
                def loadx(g, ci=ci, prev_mm=prev_mm):
                    if ci > 0:
                        g.wait_ge(S[MMS], prev_mm)
                    for k in range(8):
                        g.dma_start(out=x_bf[k][:, :],
                                    in_=xT_d[k * 128:(k + 1) * 128,
                                             ci * 512:ci * 512 + 516]
                                    ).then_inc(S['xl'], 16)
                C.bump('xl', 128)
                xl_after = C.g('xl')
                PL.add(RawPhase({'gpsimd': loadx}))
            else:
                src = io['ln2_all']; cc3_n = io['cc3_n']
                base_gx = C.g('gx')
                base_gv = C.g('gv')
                base_xl = C.g('xl')

                def loadg(g, ci=ci, base_gx=base_gx):
                    if ci == 0:
                        g.wait_ge(S['cc3'], cc3_n)
                        g.wait_ge(S['cl'], io['_c_cl'])
                    if ci > 0:
                        g.wait_ge(S['gx'], base_gx)
                    for gI in range(4):
                        g.indirect_dma_start(
                            out=tokt[gI][:, :], out_offset=None, in_=src,
                            in_offset=bass.IndirectOffsetOnAxis(
                                idxc_t[:, 4 * ci + gI:4 * ci + gI + 1], 0),
                        ).then_inc(S['xl'], 16)
                    g.indirect_dma_start(
                        out=tokt[4][0:4, :], out_offset=None, in_=src,
                        in_offset=bass.IndirectOffsetOnAxis(
                            idxc_t[0:4, 4 * ci + 4:4 * ci + 5], 0),
                    ).then_inc(S['xl'], 16)

                def trt(t, ci=ci, base_xl=base_xl, base_gv=base_gv,
                        base_gx=base_gx):
                    t.wait_ge(S['xl'], base_xl + 80)
                    i = 0
                    for gI in range(5):
                        for k in range(8):
                            if i >= 2:
                                t.wait_ge(S['gv'], base_gv + i - 1)
                            if gI < 4:
                                t.transpose(ps_tr[i % 2][:, 0:128],
                                            tokt[gI][:, k * 128:(k + 1) * 128],
                                            ident_t[:, :]).then_inc(S['gx'], 1)
                            else:
                                t.transpose(ps_tr[i % 2][:, 0:4],
                                            tokt[4][0:4, k * 128:(k + 1) * 128],
                                            ident_t[0:4, 0:4]).then_inc(S['gx'], 1)
                            i += 1

                def tre(sc, ci=ci, base_gx=base_gx, base_gv=base_gv):
                    i = 0
                    for gI in range(5):
                        for k in range(8):
                            sc.wait_ge(S['gx'], base_gx + i + 1)
                            if gI < 4:
                                sc.activation(
                                    x_bf[k][:, gI * 128:(gI + 1) * 128],
                                    ps_tr[i % 2][:, 0:128],
                                    AF.Copy).then_inc(S['gv'], 1)
                            else:
                                sc.activation(x_bf[k][:, 512:516],
                                              ps_tr[i % 2][:, 0:4],
                                              AF.Copy).then_inc(S['gv'], 1)
                            i += 1
                C.bump('xl', 80)
                C.bump('gx', 40)
                C.bump('gv', 40)
                xl_after = C.g('xl')
                PL.add(RawPhase({'gpsimd': loadg, 'tensor': trt,
                                 'scalar': tre}))

            base_cv = C.g('cv')
            base_xs = C.g('xs')
            base_su = [C.g('su0'), C.g('su1')]
            base_sz = [C.g('sz0'), C.g('sz1')]

            def mk_evwin(base_cv=base_cv, base_xs=base_xs, base_su=base_su,
                         base_sz=base_sz, c0d=c0d, ph_holder=[]):
                def evac(engname, eng, ps, n_abs, nr, c0, cw, gi):
                    ph = ph_holder[0]
                    if n_abs < DI:
                        j = n_abs // 128
                        if engname == 'scalar':
                            eng.activation(upre[j][:, c0:c0 + cw],
                                           ps[0:nr, 0:cw],
                                           AF.Copy).then_inc(S[EV], 1)
                            if c0 == 512:
                                if j >= 1:
                                    jj = j - 1
                                    eng.wait_ge(S['cv'], base_cv + j)
                                    par = jj % 2
                                    eng.wait_ge(S['su0' if par == 0 else 'su1'],
                                                base_su[par] + 16 * (jj // 2))
                                    eng.activation(u_bf[jj][:, :],
                                                   ctap[jj % 2][:, :], AF.Silu,
                                                   bias=convb_t[:, jj:jj + 1]
                                                   ).then_inc(S['xs'], 1)
                        elif engname == 'vector':
                            if c0 == 512:
                                j = n_abs // 128
                                eng.wait_ge(S[EV], ph.base_ev + gi + 1)
                                if j >= 2:
                                    eng.wait_ge(S['xs'], base_xs + j - 1)
                                eng.tensor_scalar_mul(
                                    ctap[j % 2][:, :], upre[j][:, 1:513],
                                    convw_t[:, 4 * j:4 * j + 1])
                                for k in range(1, 4):
                                    ins = eng.scalar_tensor_tensor(
                                        ctap[j % 2][:, :],
                                        upre[j][:, 1 + k:513 + k],
                                        convw_t[:, 4 * j + k:4 * j + k + 1],
                                        ctap[j % 2][:, :], ALU.mult, ALU.add)
                                ins.then_inc(S['cv'], 1)
                        elif engname == 'sync':
                            if c0 == 512:
                                j = n_abs // 128
                                if j >= 1:
                                    jj = j - 1
                                    eng.wait_ge(S['xs'], base_xs + jj + 1)
                                    eng.dma_start(
                                        out=u_o[jj * 128:(jj + 1) * 128,
                                                c0d:c0d + 512],
                                        in_=u_bf[jj][:, :]
                                        ).then_inc(S['su0' if jj % 2 == 0
                                                   else 'su1'], 16)
                    else:
                        jr = (n_abs - DI) // 128
                        if engname == 'scalar':
                            if gi == 32:
                                eng.wait_ge(S['cv'], base_cv + 16)
                                eng.wait_ge(S['su1'], base_su[1] + 16 * 7)
                                eng.activation(u_bf[15][:, :],
                                               ctap[15 % 2][:, :], AF.Silu,
                                               bias=convb_t[:, 15:16]
                                               ).then_inc(S['xs'], 1)
                            if gi % 2 == 0:
                                par = jr % 2
                                eng.wait_ge(S['sz0' if par == 0 else 'sz1'],
                                            base_sz[par] + 16 * (jr // 2))
                            eng.activation(zbuf[jr % 2][:, c0:c0 + cw],
                                           ps[0:nr, 0:cw],
                                           AF.Silu).then_inc(S[EV], 1)
                        elif engname == 'sync':
                            if c0 == 512:
                                eng.wait_ge(S[EV], ph.base_ev + gi + 1)
                                eng.dma_start(
                                    out=sz_o[jr * 128:(jr + 1) * 128,
                                             c0d:c0d + 512],
                                    in_=zbuf[jr % 2][:, 4:516]
                                    ).then_inc(S['sz0' if jr % 2 == 0
                                               else 'sz1'], 16)
                            if gi == 63:
                                eng.wait_ge(S['xs'], base_xs + 16)
                                eng.dma_start(
                                    out=u_o[15 * 128:16 * 128, c0d:c0d + 512],
                                    in_=u_bf[15][:, :]
                                    ).then_inc(S['su1'], 16)
                return evac, ph_holder

            evwin, hwin = mk_evwin()
            win_tw = [('xl', xl_after)]
            if fused_x:
                win_tw.append(('gv', C.g('gv')))
            mmwin = MMPhase(S, C, winT_d, D, 2 * DI, 516,
                            lambda k: x_bf[k][:, :], wpanel, psA, evwin,
                            evac_engines=('scalar', 'vector', 'sync'),
                            extra_tensor_waits=win_tw, sems=msems)
            hwin.append(mmwin)
            C.bump('cv', 16)
            C.bump('xs', 16)
            C.bump('su0', 16 * 8)
            C.bump('su1', 16 * 8)
            C.bump('sz0', 16 * 8)
            C.bump('sz1', 16 * 8)
            PL.add(mmwin)

            def evwx(engname, eng, ps, n_abs, nr, c0, cw, gi):
                eng.activation(small96[0:128, 0:cw], ps[0:128, 0:cw], AF.Copy)
                eng.activation(small96_bf[0:128, 0:cw], ps[0:128, 0:cw],
                               AF.Copy).then_inc(S[EV], 1)

            mmwx = MMPhase(S, C, wxT_d, DI, 128, 512,
                           lambda k: u_bf[k][:, :], wpanel, psA, evwx,
                           evac_engines=('scalar',),
                           extra_tensor_waits=[('xs', C.g('xs'))], sems=msems)
            PL.add(mmwx)

            ev_afterwx = C.g('ev')

            def store_bc(s, ev_afterwx=ev_afterwx, c0d=c0d):
                s.wait_ge(S[EV], ev_afterwx)
                s.dma_start(out=bc_o[0:32, c0d:c0d + 512],
                            in_=small96[0:32, :]).then_inc(S['st'], 16)
            C.bump('st', 16)
            PL.add(RawPhase({'sync': store_bc}))

            base_dv4 = C.g('dv')
            base_sd = [C.g('sd0'), C.g('sd1')]

            def mk_evdt(base_dv4=base_dv4, base_sd=base_sd, c0d=c0d,
                        ph_holder=[]):
                def evac(engname, eng, ps, n_abs, nr, c0, cw, gi):
                    j = gi
                    if engname == 'scalar':
                        eng.wait_ge(S['dv'], base_dv4 + max(0, j - 1))
                        eng.activation(tA[j % 2][:, 0:cw], ps[0:nr, 0:cw],
                                       AF.Abs, bias=bdt_t[:, j:j + 1])
                        eng.activation(tB[j % 2][:, 0:cw], tA[j % 2][:, 0:cw],
                                       AF.Exp, bias=0.0, scale=-1.0)
                        eng.activation(tA[j % 2][:, 0:cw], tB[j % 2][:, 0:cw],
                                       AF.Ln, bias=1.0)
                        eng.activation(tB[j % 2][:, 0:cw], ps[0:nr, 0:cw],
                                       AF.Relu, bias=bdt_t[:, j:j + 1]
                                       ).then_inc(S[EV], 1)
                    elif engname == 'vector':
                        eng.wait_ge(S[EV], ph_holder[0].base_ev + gi + 1)
                        par = j % 2
                        eng.wait_ge(S['sd0' if par == 0 else 'sd1'],
                                    base_sd[par] + 16 * (j // 2))
                        eng.tensor_add(dbuf[j % 2][:, 0:cw], tA[j % 2][:, 0:cw],
                                       tB[j % 2][:, 0:cw]).then_inc(S['dv'], 1)
                    elif engname == 'sync':
                        eng.wait_ge(S['dv'], base_dv4 + j + 1)
                        eng.dma_start(out=delta_o[j * 128:(j + 1) * 128,
                                                  c0d:c0d + 512],
                                      in_=dbuf[j % 2][:, 0:cw]
                                      ).then_inc(S['sd0' if j % 2 == 0
                                                 else 'sd1'], 16)
                return evac, ph_holder

            evdt, hdt = mk_evdt()
            mmdt = MMPhase(S, C, wdtT_d, 64, DI, 512,
                           lambda k: small96_bf[64:128, :], wpanel, psA, evdt,
                           evac_engines=('scalar', 'vector', 'sync'),
                           part_base=64, sems=msems)
            hdt.append(mmdt)
            C.bump('dv', 16)
            C.bump('sd0', 16 * 8)
            C.bump('sd1', 16 * 8)
            PL.add(mmdt)

        io['_c_chunk'] = _c_chunk
        if not io.get('defer_chunks'):
            for ci in range(8):
                _c_chunk(ci)

        if not io.get('defer_chunks'):
            def fin(s, tot={k: C.g(k) for k in ('st', 'sd0', 'sd1', 'su0',
                                                'su1', 'sz0', 'sz1')}):
                for k, v in tot.items():
                    s.wait_ge(S[k], v)
            PL.add(RawPhase({'sync': fin}))


def build_C():
    nc = bass.Bass("TRN2", target_bir_lowering=False)
    io = {}
    for nm, shp, knd in (('xT', [D, T + 4], 'ExternalInput'),
                         ('winT', [D, 2 * DI], 'ExternalInput'),
                         ('wxT', [DI, 128], 'ExternalInput'),
                         ('wdtT', [64, DI], 'ExternalInput'),
                         ('convw', [128, 64], 'ExternalInput'),
                         ('convb', [128, 16], 'ExternalInput'),
                         ('bdt', [128, 16], 'ExternalInput'),
                         ('delta', [DI, T], 'ExternalOutput'),
                         ('bc', [32, T], 'ExternalOutput')):
        io[nm] = nc.dram_tensor(nm, shp, F32, kind=knd).ap()
    io['u'] = nc.dram_tensor('u', [DI, T], BF16, kind='ExternalOutput').ap()
    io['sz'] = nc.dram_tensor('sz', [DI, T], BF16, kind='ExternalOutput').ap()
    S = {}
    C = Cnt()
    PL = PhaseList()
    with ExitStack() as cs:
        e = cs.enter_context
        for sname in C_SEMS:
            S[sname] = e(nc.semaphore(sname))
        _emit_C(nc, e, S, C, PL, io)
        block = e(nc.Block())
        PL.emit_all(block)
    return nc


def host_prep_C(inp, ln2_by_core):
    maps = []
    for c in range(NCORES):
        b, dirb = c // 2, c % 2
        sfx = 'b' if dirb else 'f'
        full = np.concatenate([ln2_by_core[2 * b], ln2_by_core[2 * b + 1]],
                              axis=1)
        if dirb:
            full = full[:, ::-1]
        xT = np.zeros((D, T + 4), np.float32)
        xT[:, 4:] = full
        wx = np.asarray(inp[f"L1_attn_wx_{sfx}"], np.float32)
        wxT = np.zeros((DI, 128), np.float32)
        wxT[:, 0:32] = wx[64:96].T
        wxT[:, 64:128] = wx[0:64].T
        wxT = f32c(wxT)
        maps.append(dict(
            xT=f32c(xT),
            winT=f32c(np.asarray(inp[f"L1_attn_win_{sfx}"], np.float32).T),
            wxT=wxT,
            wdtT=f32c(np.asarray(inp[f"L1_attn_wdt_{sfx}"], np.float32).T),
            convw=f32c(np.asarray(inp[f"L1_attn_wconv_{sfx}"], np.float32)
                       .reshape(DI, 4).reshape(16, 128, 4)
                       .transpose(1, 0, 2).reshape(128, 64)),
            convb=cols128(inp[f"L1_attn_bconv_{sfx}"]),
            bdt=cols128(inp[f"L1_attn_bdt_{sfx}"]),
        ))
    return maps


D_SEMS = ('wl0', 'wl1', 'mm', 'ev', 'cl', 'bl', 'bm', 'bce',
          'ld0', 'ld1', 'da', 'hv', 'pv', 'ym', 'sc', 'so0', 'so1')


def _emit_D(nc, e, S, C, PL, io):
    u_i = io['u']; sz_i = io['sz']; delta_i = io['delta']; bc_i = io['bc']
    alogc_d = io['alogc']; dpc_d = io['dpc']; woutT_d = io['woutT']
    oh32_d = io['oh32']; i128_d = io['i128']; od_o = io['od']
    io['_d_sc0'] = C.g('sc'); io['_d_da0'] = C.g('da')
    io['_d_pv0'] = C.g('pv'); io['_d_hv0'] = C.g('hv')
    io['_d_ym0'] = C.g('ym')
    if True:
        Bm_w = e(nc.sbuf_tensor("Bmw", [128, 16 * 512], BF16))
        C_w = e(nc.sbuf_tensor("Cw", [128, 16 * 512], BF16))
        bc_bf = e(nc.sbuf_tensor("bcbf", [16, 512], BF16))
        bc_bf2 = e(nc.sbuf_tensor("bcbf2", [16, 512], BF16))
        da_b = [e(nc.sbuf_tensor(f"dab{i}", [128, 512], F32)) for i in range(2)]
        db_b = [e(nc.sbuf_tensor(f"dbb{i}", [128, 512], BF16)) for i in range(2)]
        hb_b = [e(nc.sbuf_tensor(f"hbb{i}", [128, 512], BF16)) for i in range(2)]
        ch_b = [e(nc.sbuf_tensor(f"chb{i}", [128, 512], BF16)) for i in range(2)]
        dl = [e(nc.sbuf_tensor(f"dl{i}", [128, 512], F32)) for i in range(2)]
        ul = [e(nc.sbuf_tensor(f"ul{i}", [128, 512], BF16)) for i in range(2)]
        zl = [e(nc.sbuf_tensor(f"zl{i}", [128, 512], BF16)) for i in range(2)]
        duj = [e(nc.sbuf_tensor(f"duj{i}", [128, 512], BF16)) for i in range(2)]
        tp = [e(nc.sbuf_tensor("tp0", [128, 512], F32))] * 2
        ybf = [e(nc.sbuf_tensor(f"ybf{j}", [128, 512], BF16)) for j in range(16)]
        ob = [e(nc.sbuf_tensor(f"ob{i}", [128, 512], io['od'].dtype))
              for i in range(2)]
        hcarry = e(nc.sbuf_tensor("hcarry", [128, 256], F32))
        nA = e(nc.sbuf_tensor("nA", [128, 256], F32))
        dpc_t = e(nc.sbuf_tensor("dpct", [128, 16], F32))
        oh_bf = e(nc.sbuf_tensor("ohbf", [16, 16 * 128], BF16))
        i_bf = e(nc.sbuf_tensor("ibf", [128, 128], BF16))
        if io.get('wpanel') is not None:
            wpanel = io['wpanel']
        elif io.get('single_panel'):
            _wp = e(nc.sbuf_tensor("wpd0", [128, 16 * 512], BF16))
            wpanel = [_wp, _wp]
        else:
            wpanel = [e(nc.sbuf_tensor(f"wp{i}", [128, 16 * 512], BF16))
                      for i in range(2)]
        _psbc = e(nc.psum_tensor("psbc", [128, 512], F32))
        ps_bc = [_psbc, _psbc]
        ps_y = [e(nc.psum_tensor(f"psy{i}", [128, 512], F32)) for i in range(2)]
        if io.get('psA') is not None:
            psA = io['psA']
        else:
            psA = [e(nc.psum_tensor(f"psAd{i}", [128, 512], F32))
                   for i in range(2)]

        bcl = C.g('cl')

        atmp = None

        def consts_g(g, bcl=bcl):
            g.dma_start(out=dl[0][:, 0:256], in_=alogc_d).then_inc(S['cl'], 16)
            g.dma_start(out=dpc_t[:, :], in_=dpc_d).then_inc(S['cl'], 16)
            g.dma_start(out=oh_bf[:, :], in_=oh32_d).then_inc(S['cl'], 16)
            g.dma_start(out=i_bf[:, :], in_=i128_d).then_inc(S['cl'], 16)

        def consts_s(s, bcl=bcl):
            s.wait_ge(S['cl'], bcl + 64)
            s.activation(nA[:, :], dl[0][:, 0:256], AF.Exp).then_inc(S['cl'], 1)

        def consts_v(v, bcl=bcl):
            v.memset(hcarry[:, :], 0.0)
            v.wait_ge(S['cl'], bcl + 65)
            v.tensor_scalar_mul(nA[:, :], nA[:, :], -1.0).then_inc(S['cl'], 1)

        def consts_t(t, bcl=bcl):
            t.wait_ge(S['cl'], bcl + 66)
        C.bump('cl', 66)
        io['_d_clnA'] = C.g('cl') - 1
        PL.add(RawPhase({'gpsimd': consts_g, 'scalar': consts_s,
                         'vector': consts_v, 'tensor': consts_t}))

        def _d_chunk(ci):
            c0d = ci * 512
            jj0 = 16 * ci
            gg0 = 256 * ci
            if io.get('defer_chunks'):
                cwrec = io['_cw_by_chunk'][ci]
                cw = cwrec['cw']
                bc_st = cwrec['bc_st']
            else:
                cw = None
                bc_st = None

            def bcload(g, c0d=c0d, bm_prev=C.g('bm'), bc_st=bc_st):
                if bc_st is not None:
                    g.wait_ge(S['st'], bc_st)
                g.wait_ge(S['bm'], bm_prev)
                g.dma_start(out=bc_bf[:, :], in_=bc_i[0:16, c0d:c0d + 512]
                            ).then_inc(S['bl'], 16)
                g.dma_start(out=bc_bf2[:, :], in_=bc_i[16:32, c0d:c0d + 512]
                            ).then_inc(S['bl'], 16)
            C.bump('bl', 32)
            bl_now = C.g('bl')
            PL.add(RawPhase({'gpsimd': bcload}))

            base_bm = C.g('bm')
            base_bce = C.g('bce')

            def bct(t, base_bm=base_bm, base_bce=base_bce, bl_now=bl_now):
                t.wait_ge(S['bl'], bl_now)
                t.wait_ge(S['bce'], base_bce)
                for n in range(32):
                    if n >= 1:
                        t.wait_ge(S['bce'], base_bce + n)
                    src = bc_bf if n < 16 else bc_bf2
                    t.matmul(ps_bc[n % 2][:, :],
                             oh_bf[0:16, (n % 16) * 128:(n % 16 + 1) * 128],
                             src[0:16, :], start=True, stop=True
                             ).then_inc(S['bm'], 1)

            def bcs(s, base_bm=base_bm, ci=ci, hv_prev=C.g('hv')):
                s.wait_ge(S['hv'], hv_prev)
                for n in range(32):
                    s.wait_ge(S['bm'], base_bm + n + 1)
                    tgt = (Bm_w[:, n * 512:(n + 1) * 512] if n < 16 else
                           C_w[:, (n - 16) * 512:(n - 15) * 512])
                    s.activation(tgt, ps_bc[n % 2][:, :], AF.Copy
                                 ).then_inc(S['bce'], 1)
            C.bump('bm', 32)
            C.bump('bce', 32)
            PL.add(RawPhase({'tensor': bct, 'scalar': bcs}))
            bce_now = C.g('bce')

            base_ld = [C.g('ld0'), C.g('ld1')]

            def loads(s, base_sc=C.g('sc') - 16 * ci, c0d=c0d, jj0=jj0,
                      sc0=io['_d_sc0'], cw=cw, ci_=ci, cl_nA=io['_d_clnA']):
                if ci_ == 0:
                    s.wait_ge(S['cl'], cl_nA)
                if cw is not None:
                    for k, v in cw.items():
                        s.wait_ge(S[k], v)
                for j in range(16):
                    if jj0 + j >= 2:
                        s.wait_ge(S['sc'], sc0 + jj0 + j - 1)
                    sem = S['ld0' if j % 2 == 0 else 'ld1']
                    s.dma_start(out=dl[j % 2][:, :],
                                in_=delta_i[j * 128:(j + 1) * 128,
                                            c0d:c0d + 512]).then_inc(sem, 16)
                    s.dma_start(out=ul[j % 2][:, :],
                                in_=u_i[j * 128:(j + 1) * 128,
                                        c0d:c0d + 512]).then_inc(sem, 16)
                    s.dma_start(out=zl[j % 2][:, :],
                                in_=sz_i[j * 128:(j + 1) * 128,
                                         c0d:c0d + 512]).then_inc(sem, 16)
            C.bump('ld0', 48 * 8)
            C.bump('ld1', 48 * 8)
            PL.add(RawPhase({'sync': loads}))

            def scal(s, jj0=jj0, gg0=gg0, base_ld=base_ld, bce_now=bce_now,
                     cl_now=C.g('cl'), hv0=io['_d_hv0']):
                s.wait_ge(S['cl'], cl_now)
                for j in range(16):
                    par = j % 2
                    s.wait_ge(S['ld0' if par == 0 else 'ld1'],
                              base_ld[par] + 48 * (j // 2 + 1))
                    for n in range(16):
                        g = gg0 + j * 16 + n
                        s.wait_ge(S['hv'], hv0 + max(0, g - 1))
                        s.activation(da_b[n % 2][:, :], dl[j % 2][:, :],
                                     AF.Exp, bias=0.0,
                                     scale=nA[:, j * 16 + n:j * 16 + n + 1]
                                     ).then_inc(S['da'], 1)
            C.bump('da', 256)
            PL.add(RawPhase({'scalar': scal}))

            def vec(v, jj0=jj0, gg0=gg0, base_ld=base_ld, bce_now=bce_now,
                    sc0=io['_d_sc0'], da0=io['_d_da0'], pv0=io['_d_pv0'],
                    hv0=io['_d_hv0'], ym0=io['_d_ym0']):
                for j in range(16):
                    par = j % 2
                    v.wait_ge(S['ld0' if par == 0 else 'ld1'],
                              base_ld[par] + 48 * (j // 2 + 1))
                    if j == 0:
                        v.wait_ge(S['bce'], bce_now)
                    v.tensor_mul(duj[j % 2][:, :], dl[j % 2][:, :],
                                 ul[j % 2][:, :])
                    for n in range(16):
                        g = gg0 + j * 16 + n
                        cidx = j * 16 + n
                        v.wait_ge(S['da'], da0 + g + 1)
                        v.wait_ge(S['ym'], ym0 + max(0, g - 1))
                        v.tensor_mul(db_b[n % 2][:, :], duj[j % 2][:, :],
                                     Bm_w[:, n * 512:(n + 1) * 512])
                        v.tensor_tensor_scan(
                            hb_b[n % 2][:, :], da_b[n % 2][:, :],
                            db_b[n % 2][:, :], hcarry[:, cidx:cidx + 1],
                            ALU.mult, ALU.add)
                        v.tensor_copy(hcarry[:, cidx:cidx + 1],
                                      hb_b[n % 2][:, 511:512])
                        v.tensor_mul(ch_b[n % 2][:, :], hb_b[n % 2][:, :],
                                     C_w[:, n * 512:(n + 1) * 512]
                                     ).then_inc(S['hv'], 1)
                    v.wait_ge(S['ym'], ym0 + gg0 + j * 16 + 16)
                    v.scalar_tensor_tensor(tp[j % 2][:, :], ul[j % 2][:, :],
                                           dpc_t[:, j:j + 1], ps_y[j % 2][:, :],
                                           ALU.mult, ALU.add)
                    v.tensor_mul(ybf[j][:, :], tp[j % 2][:, :],
                                 zl[j % 2][:, :]).then_inc(S['sc'], 1)
            C.bump('hv', 256)
            C.bump('sc', 16)
            PL.add(RawPhase({'vector': vec}))

            def tens(t, jj0=jj0, gg0=gg0, sc0=io['_d_sc0'],
                     hv0=io['_d_hv0']):
                for j in range(16):
                    if jj0 + j >= 2:
                        t.wait_ge(S['sc'], sc0 + jj0 + j - 1)
                    for n in range(16):
                        g = gg0 + j * 16 + n
                        t.wait_ge(S['hv'], hv0 + g + 1)
                        t.matmul(ps_y[j % 2][:, :], i_bf[:, :],
                                 ch_b[n % 2][0:128, :],
                                 start=(n == 0), stop=(n == 15)
                                 ).then_inc(S['ym'], 1)
            C.bump('ym', 256)
            PL.add(RawPhase({'tensor': tens}))

            base_so = [C.g('so0'), C.g('so1')]

            def mk_evo(base_so=base_so, c0d=c0d, ph_holder=[]):
                def evac(engname, eng, ps, n_abs, nr, c0, cw, gi):
                    k = n_abs // 128
                    if engname == 'scalar':
                        par = gi % 2
                        eng.wait_ge(S['so0' if par == 0 else 'so1'],
                                    base_so[par] + 16 * (gi // 2))
                        eng.activation(ob[gi % 2][:, 0:cw], ps[0:nr, 0:cw],
                                       AF.Copy).then_inc(S['ev'], 1)
                    elif engname == 'sync':
                        eng.wait_ge(S['ev'], ph_holder[0].base_ev + gi + 1)
                        eng.dma_start(out=od_o[k * 128:(k + 1) * 128,
                                               c0d:c0d + 512],
                                      in_=ob[gi % 2][:, 0:cw]
                                      ).then_inc(S['so0' if gi % 2 == 0
                                                 else 'so1'], 16)
                return evac, ph_holder

            evo, ho = mk_evo()
            mmo = MMPhase(S, C, woutT_d, DI, D, 512,
                          lambda k: ybf[k][:, :], wpanel, psA, evo,
                          evac_engines=('scalar', 'sync'),
                          extra_tensor_waits=[('sc', C.g('sc'))])
            ho.append(mmo)
            C.bump('so0', 16 * 4)
            C.bump('so1', 16 * 4)
            PL.add(mmo)

        io['_d_chunk'] = _d_chunk
        if not io.get('defer_chunks'):
            for ci in range(8):
                _d_chunk(ci)

        def fin_factory():
            def fin(s, t0=C.g('so0'), t1=C.g('so1')):
                s.wait_ge(S['so0'], t0)
                s.wait_ge(S['so1'], t1)
            return fin
        io['_d_fin'] = fin_factory
        if not io.get('defer_chunks'):
            PL.add(RawPhase({'sync': fin_factory()}))


def build_D():
    nc = bass.Bass("TRN2", target_bir_lowering=False)
    io = {}
    for nm, shp, knd in (('delta', [DI, T], 'ExternalInput'),
                         ('bc', [32, T], 'ExternalInput'),
                         ('alogc', [128, 256], 'ExternalInput'),
                         ('dpc', [128, 16], 'ExternalInput'),
                         ('woutT', [DI, D], 'ExternalInput'),
                         ('oh32', [16, 16 * 128], 'ExternalInput'),
                         ('i128', [128, 128], 'ExternalInput'),
                         ('od', [D, T], 'ExternalOutput')):
        io[nm] = nc.dram_tensor(nm, shp, F32, kind=knd).ap()
    io['u'] = nc.dram_tensor('u', [DI, T], BF16, kind='ExternalInput').ap()
    io['sz'] = nc.dram_tensor('sz', [DI, T], BF16, kind='ExternalInput').ap()
    S = {}
    C = Cnt()
    PL = PhaseList()
    with ExitStack() as cs:
        e = cs.enter_context
        for sname in D_SEMS:
            S[sname] = e(nc.semaphore(sname))
        _emit_D(nc, e, S, C, PL, io)
        block = e(nc.Block())
        PL.emit_all(block)
    return nc


def host_prep_D(inp, c_results):
    eye = np.eye(128, dtype=np.float32)
    oh = np.zeros((16, 16 * 128), np.float32)
    for n in range(16):
        oh[n, n * 128:(n + 1) * 128] = 1.0
    maps = []
    for c in range(NCORES):
        dirb = c % 2
        sfx = 'b' if dirb else 'f'
        alog = np.asarray(inp[f"L1_attn_Alog_{sfx}"], np.float32)
        alogc = f32c(alog.reshape(16, 128, 16).transpose(1, 0, 2)
                     .reshape(128, 256))
        maps.append(dict(
            u=c_results[c]["u"], sz=c_results[c]["sz"],
            delta=c_results[c]["delta"], bc=c_results[c]["bc"],
            alogc=alogc, dpc=cols128(inp[f"L1_attn_Dp_{sfx}"]),
            woutT=f32c(np.asarray(inp[f"L1_attn_wout_{sfx}"], np.float32).T),
            oh32=oh, i128=eye,
        ))
    return maps


E_SEMS = ('wl0', 'wl1', 'mm', 'ev', 'xl', 'xf', 'cl', 'sv', 'st',
          'sq', 'squ', 'lnm', 'lnv', 'lns', 'lnb', 'lne', 'cc4', 'gx', 'gv')


def _emit_E(nc, e, S, C, PL, io):
    ln2_i = io['ln2']; wprojT_d = io['wprojT']; bprojc_d = io['bprojc']
    ln3_o = io['ln3']
    fwd_i = io.get('fwdh'); bwd_i = io.get('bwdh')
    if True:
        catbf = [e(nc.sbuf_tensor(f"cat{k}", [128, 512], BF16)) for k in range(16)]
        x32 = [e(nc.sbuf_tensor(f"x32{k}", [128, 512], F32)) for k in range(8)]
        t32 = [e(nc.sbuf_tensor(f"t32{i}", [128, 512], F32)) for i in range(2)]
        s32 = [e(nc.sbuf_tensor(f"s32{k}", [128, 512], F32)) for k in range(8)]
        sbf = [e(nc.sbuf_tensor(f"sbf{k}", [128, 512], BF16)) for k in range(8)]
        lnout = [e(nc.sbuf_tensor(f"lno{k}", [128, 512], F32)) for k in range(8)]
        wpanel = [e(nc.sbuf_tensor(f"wp{i}", [128, 16 * 512], BF16)) for i in range(2)]
        bprojc_t = e(nc.sbuf_tensor("bpc", [128, 8], F32))
        if io.get('fwdh') is None:
            tokt = [e(nc.sbuf_tensor(f"tok{i}", [128, 1024], BF16))
                    for i in range(4)]
            idxef_t = e(nc.sbuf_tensor("idxef", [128, 16], I32))
            idxeb_t = e(nc.sbuf_tensor("idxeb", [128, 16], I32))
            ident_t = e(nc.sbuf_tensor("identt", [128, 128], BF16))
            ps_tr = [e(nc.psum_tensor(f"pstr{i}", [128, 128], BF16))
                     for i in range(2)]
        aux = dict(
            ones128_bf=e(nc.sbuf_tensor("o128", [128, 1], BF16)),
            ones1_f32=e(nc.sbuf_tensor("o1f", [1, 128], F32)),
            sq_bf=[e(nc.sbuf_tensor(f"sqbf{i}", [128, 512], BF16)) for i in range(2)],
            row_m=e(nc.sbuf_tensor("row_m", [1, 512], F32)),
            row_v=e(nc.sbuf_tensor("row_v", [1, 512], F32)),
            row_t=e(nc.sbuf_tensor("row_t", [1, 512], F32)),
            row_r=e(nc.sbuf_tensor("row_r", [1, 512], F32)),
            eps=e(nc.sbuf_tensor("cst_eps", [1, 1], F32)),
            neghalf=e(nc.sbuf_tensor("cst_nh", [1, 1], F32)),
            nrm=[e(nc.sbuf_tensor(f"nrm{i}", [128, 512], F32)) for i in range(2)],
            ps_m=e(nc.psum_tensor("psm", [1, 512], F32)),
            ps_q=e(nc.psum_tensor("psq", [1, 512], F32)),
            ps_mb=e(nc.psum_tensor("psmb", [128, 512], F32)),
            ps_rb=e(nc.psum_tensor("psrb", [128, 512], F32)),
        )
        psA = [e(nc.psum_tensor(f"psA{i}", [128, 512], F32)) for i in range(2)]

        bcl = C.g('cl')
        fused_x = fwd_i is None
        n_cl = 64 if fused_x else 16

        def consts_g(g, bcl=bcl):
            g.dma_start(out=bprojc_t[:, :], in_=bprojc_d).then_inc(S['cl'], 16)
            if fused_x:
                g.dma_start(out=idxef_t[:, :], in_=io['idxEf']
                            ).then_inc(S['cl'], 16)
                g.dma_start(out=idxeb_t[:, :], in_=io['idxEb']
                            ).then_inc(S['cl'], 16)
                g.dma_start(out=ident_t[:, :], in_=io['ident']
                            ).then_inc(S['cl'], 16)

        def consts_v(v, bcl=bcl, n_cl=n_cl):
            v.memset(aux['ones1_f32'][0:1, :], 1.0)
            v.memset(aux['ones128_bf'][:, 0:1], 1.0)
            v.memset(aux['eps'][0:1, 0:1], 1e-5)
            v.memset(aux['neghalf'][0:1, 0:1], -0.5)
            v.wait_ge(S['cl'], bcl + n_cl)

        def consts_s(s, bcl=bcl, n_cl=n_cl):
            s.wait_ge(S['cl'], bcl + n_cl)
        C.bump('cl', n_cl)
        io['_e_cl'] = C.g('cl')
        PL.add(RawPhase({'gpsimd': consts_g, 'vector': consts_v,
                         'scalar': consts_s}))

        for ci in range(4):
            c0d = ci * 512
            prev_mm = C.g('mm')
            prev_sv = C.g('sv')

            if fwd_i is not None:
                def loads_g(g, c0d=c0d, ci=ci, prev_mm=prev_mm):
                    if ci > 0:
                        g.wait_ge(S['mm'], prev_mm)
                    for k in range(8):
                        g.dma_start(out=catbf[k][:, :],
                                    in_=fwd_i[k * 128:(k + 1) * 128,
                                              c0d:c0d + 512]
                                    ).then_inc(S['xl'], 16)
                    for k in range(8):
                        g.dma_start(out=catbf[8 + k][:, :],
                                    in_=bwd_i[k * 128:(k + 1) * 128,
                                              c0d:c0d + 512]
                                    ).then_inc(S['xl'], 16)
                C.bump('xl', 16 * 16)
                gv_now = None
            else:
                src = io['odall']
                base_gx = C.g('gx')
                base_gv = C.g('gv')
                base_xl = C.g('xl')

                def loads_g(g, ci=ci, base_gx=base_gx, cc4_n=io['cc4_n']):
                    if ci == 0:
                        g.wait_ge(S['cc4'], cc4_n)
                        g.wait_ge(S['cl'], io['_e_cl'])
                    if ci > 0:
                        g.wait_ge(S['gx'], base_gx)
                    for gI in range(4):
                        g.indirect_dma_start(
                            out=tokt[gI][:, :], out_offset=None, in_=src,
                            in_offset=bass.IndirectOffsetOnAxis(
                                idxef_t[:, 4 * ci + gI:4 * ci + gI + 1], 0),
                        ).then_inc(S['xl'], 16)
                    g.wait_ge(S['gx'], base_gx + 32)
                    for gI in range(4):
                        g.indirect_dma_start(
                            out=tokt[gI][:, :], out_offset=None, in_=src,
                            in_offset=bass.IndirectOffsetOnAxis(
                                idxeb_t[:, 4 * ci + gI:4 * ci + gI + 1], 0),
                        ).then_inc(S['xl'], 16)

                def trt(t, base_xl=base_xl, base_gv=base_gv, base_gx=base_gx):
                    i = 0
                    for half in range(2):
                        t.wait_ge(S['xl'], base_xl + 64 * (half + 1))
                        for gI in range(4):
                            for k in range(8):
                                if i >= 2:
                                    t.wait_ge(S['gv'], base_gv + i - 1)
                                t.transpose(ps_tr[i % 2][:, 0:128],
                                            tokt[gI][:, k * 128:(k + 1) * 128],
                                            ident_t[:, :]).then_inc(S['gx'], 1)
                                i += 1

                def tre(sc, base_gx=base_gx, base_gv=base_gv):
                    i = 0
                    for half in range(2):
                        for gI in range(4):
                            for k in range(8):
                                sc.wait_ge(S['gx'], base_gx + i + 1)
                                sc.activation(
                                    catbf[half * 8 + k][:, gI * 128:(gI + 1) * 128],
                                    ps_tr[i % 2][:, 0:128],
                                    AF.Copy).then_inc(S['gv'], 1)
                                i += 1
                C.bump('xl', 16 * 8)
                C.bump('gx', 64)
                C.bump('gv', 64)
                gv_now = C.g('gv')

            def loads_s(s, c0d=c0d, ci=ci, prev_sv=prev_sv):
                if ci > 0:
                    s.wait_ge(S['sv'], prev_sv)
                for k in range(8):
                    s.dma_start(out=x32[k][:, :],
                                in_=ln2_i[k * 128:(k + 1) * 128, c0d:c0d + 512]
                                ).then_inc(S['xf'], 16)
            C.bump('xf', 16 * 8)
            xl_now = C.g('xl')
            xf_now = C.g('xf')
            if fwd_i is not None:
                PL.add(RawPhase({'gpsimd': loads_g, 'sync': loads_s}))
            else:
                PL.add(RawPhase({'gpsimd': loads_g, 'tensor': trt,
                                 'scalar': tre, 'sync': loads_s}))

            def mk_evp(xf_now=xf_now, ph_holder=[]):
                def evac(engname, eng, ps, n_abs, nr, c0, cw, gi):
                    k = n_abs // 128
                    if engname == 'vector':
                        if gi == 0:
                            eng.wait_ge(S['xf'], xf_now)
                        eng.scalar_tensor_tensor(
                            t32[gi % 2][:, 0:cw], ps[0:nr, 0:cw],
                            bprojc_t[:, k:k + 1], x32[k][:, 0:cw],
                            ALU.add, ALU.add).then_inc(S['ev'], 1)
                        eng.tensor_copy(s32[k][:, 0:cw], t32[gi % 2][:, 0:cw])
                        eng.tensor_copy(sbf[k][:, 0:cw], t32[gi % 2][:, 0:cw]
                                        ).then_inc(S['sv'], 1)
                return evac, ph_holder

            evp, hp = mk_evp()
            e_tw = [('xl', xl_now)]
            if fwd_i is None:
                e_tw.append(('gv', gv_now))
            mmp = MMPhase(S, C, wprojT_d, DI, D, 512,
                          lambda k: catbf[k][:, :], wpanel, psA, evp,
                          evac_engines=('vector',),
                          extra_tensor_waits=e_tw)
            hp.append(mmp)
            C.bump('sv', 8)
            PL.add(mmp)

            base_st = C.g('st')
            ln = LNPhase(S, C, lambda k: sbf[k], lambda k: s32[k],
                         lambda k: lnout[k], 512, aux,
                         pre={'tensor': [('sv', C.g('sv'))],
                              'scalar': [('sv', C.g('sv'))],
                              'vector': [('st', base_st)]})
            PL.add(ln)

            def store(s, base=ln.base_lne, c0d=c0d):
                for k in range(8):
                    s.wait_ge(S['lne'], base + k + 1)
                    s.dma_start(out=ln3_o[k * 128:(k + 1) * 128, c0d:c0d + 512],
                                in_=lnout[k][:, :]).then_inc(S['st'], 16)
            C.bump('st', 16 * 8)
            PL.add(RawPhase({'sync': store}))

        st_total = C.g('st')
        PL.add(RawPhase({'sync': lambda s: s.wait_ge(S['st'], st_total)}))


def build_E():
    nc = bass.Bass("TRN2", target_bir_lowering=False)
    io = {}
    for nm, shp, knd in (('fwdh', [D, TLOC], 'ExternalInput'),
                         ('bwdh', [D, TLOC], 'ExternalInput'),
                         ('ln2', [D, TLOC], 'ExternalInput'),
                         ('wprojT', [DI, D], 'ExternalInput'),
                         ('bprojc', [128, 8], 'ExternalInput'),
                         ('ln3', [D, TLOC], 'ExternalOutput')):
        io[nm] = nc.dram_tensor(nm, shp, F32, kind=knd).ap()
    S = {}
    C = Cnt()
    PL = PhaseList()
    with ExitStack() as cs:
        e = cs.enter_context
        for sname in E_SEMS:
            S[sname] = e(nc.semaphore(sname))
        _emit_E(nc, e, S, C, PL, io)
        block = e(nc.Block())
        PL.emit_all(block)
    return nc


def host_prep_E(inp, d_results, ln2_by_core):
    wprojT = f32c(np.asarray(inp["L1_attn_wproj"], np.float32).T)
    bprojc = cols128(inp["L1_attn_bproj"])
    maps = []
    for c in range(NCORES):
        b, h = c // 2, c % 2
        fwd_full = d_results[2 * b]["od"]
        bwd_full = d_results[2 * b + 1]["od"][:, ::-1]
        maps.append(dict(
            fwdh=f32c(fwd_full[:, h * TLOC:(h + 1) * TLOC]),
            bwdh=f32c(bwd_full[:, h * TLOC:(h + 1) * TLOC]),
            ln2=ln2_by_core[c], wprojT=wprojT, bprojc=bprojc))
    return maps


F_SEMS = ('wl0', 'wl1', 'mm', 'ev', 'xl', 'xf', 'cl', 'sv', 'st',
          'sh0', 'sh1', 'sq', 'squ', 'lnm', 'lnv', 'lns', 'lnb', 'lne')


def _emit_F(nc, e, S, C, PL, io):
    ln3_i = io['ln3']; w1T_d = io['w1T']; w2T_d = io['w2T']
    h1T_d = io['h1T']; h2T_d = io['h2T']; w3T_d = io['w3T']
    s1c_d = io['s1c']; s2c_d = io['s2c']; sh1c_d = io['sh1c']
    sh2c_d = io['sh2c']; s3c_d = io['s3c']; y_o = io['y']
    h0_d = io['h0d']; h1_d = io['h1d']; h2_d = io['h2d']
    if True:
        a32 = [e(nc.sbuf_tensor(f"a32{j}", [128, 512], F32)) for j in range(32)]
        hbf = [a32[j][:, :].bitcast(BF16)[:, 0:512] for j in range(32)]
        xbf = [e(nc.sbuf_tensor(f"xbf{k}", [128, 512], BF16)) for k in range(8)]
        hobuf = [e(nc.sbuf_tensor(f"hob{i}", [128, 512], BF16)) for i in range(2)]
        x32 = [e(nc.sbuf_tensor(f"x32{k}", [128, 512], F32)) for k in range(8)]
        s32 = [e(nc.sbuf_tensor(f"s32{k}", [128, 512], F32)) for k in range(8)]
        sbf = [e(nc.sbuf_tensor(f"sbf{k}", [128, 512], BF16)) for k in range(8)]
        wpanel = [e(nc.sbuf_tensor(f"wp{i}", [128, 32 * 512], BF16)) for i in range(2)]
        s1c = e(nc.sbuf_tensor("s1ct", [128, 32], F32))
        s2c = e(nc.sbuf_tensor("s2ct", [128, 32], F32))
        sh1c = e(nc.sbuf_tensor("sh1ct", [128, 32], F32))
        sh2c = e(nc.sbuf_tensor("sh2ct", [128, 32], F32))
        s3c = e(nc.sbuf_tensor("s3ct", [128, 8], F32))
        aux = dict(
            ones128_bf=e(nc.sbuf_tensor("o128", [128, 1], BF16)),
            ones1_f32=e(nc.sbuf_tensor("o1f", [1, 128], F32)),
            sq_bf=[e(nc.sbuf_tensor(f"sqbf{i}", [128, 512], BF16)) for i in range(2)],
            row_m=e(nc.sbuf_tensor("row_m", [1, 512], F32)),
            row_v=e(nc.sbuf_tensor("row_v", [1, 512], F32)),
            row_t=e(nc.sbuf_tensor("row_t", [1, 512], F32)),
            row_r=e(nc.sbuf_tensor("row_r", [1, 512], F32)),
            eps=e(nc.sbuf_tensor("cst_eps", [1, 1], F32)),
            neghalf=e(nc.sbuf_tensor("cst_nh", [1, 1], F32)),
            nrm=[e(nc.sbuf_tensor(f"nrm{i}", [128, 512], F32)) for i in range(2)],
            ps_m=e(nc.psum_tensor("psm", [1, 512], F32)),
            ps_q=e(nc.psum_tensor("psq", [1, 512], F32)),
            ps_mb=e(nc.psum_tensor("psmb", [128, 512], F32)),
            ps_rb=e(nc.psum_tensor("psrb", [128, 512], F32)),
        )
        psA = [e(nc.psum_tensor(f"psA{i}", [128, 512], F32)) for i in range(2)]

        bcl = C.g('cl')

        def consts_g(g, bcl=bcl):
            for t, d_ in ((s1c, s1c_d), (s2c, s2c_d), (sh1c, sh1c_d),
                          (sh2c, sh2c_d), (s3c, s3c_d)):
                g.dma_start(out=t[:, :], in_=d_).then_inc(S['cl'], 16)

        def consts_v(v, bcl=bcl):
            v.memset(aux['ones1_f32'][0:1, :], 1.0)
            v.memset(aux['ones128_bf'][:, 0:1], 1.0)
            v.memset(aux['eps'][0:1, 0:1], 1e-5)
            v.memset(aux['neghalf'][0:1, 0:1], -0.5)
            v.wait_ge(S['cl'], bcl + 80)

        def consts_s(s, bcl=bcl):
            s.wait_ge(S['cl'], bcl + 80)
        C.bump('cl', 80)
        PL.add(RawPhase({'gpsimd': consts_g, 'vector': consts_v,
                         'scalar': consts_s}))

        for ci in range(4):
            c0d = ci * 512
            prev_mm = C.g('mm')

            def loadx(g, c0d=c0d, ci=ci, prev_mm=prev_mm):
                if ci > 0:
                    g.wait_ge(S['mm'], prev_mm)
                for k in range(8):
                    g.dma_start(out=xbf[k][:, :],
                                in_=ln3_i[k * 128:(k + 1) * 128, c0d:c0d + 512]
                                ).then_inc(S['xl'], 16)
            C.bump('xl', 128)
            xl_now = C.g('xl')
            PL.add(RawPhase({'gpsimd': loadx}))

            def ev1(engname, eng, ps, n_abs, nr, c0, cw, gi):
                j = n_abs // 128
                eng.activation(a32[j][:, 0:cw], ps[0:nr, 0:cw], AF.Silu,
                               bias=0.0, scale=s1c[:, j:j + 1]
                               ).then_inc(S['ev'], 1)

            mm1 = MMPhase(S, C, w1T_d, D, 4096, 512, lambda k: xbf[k][:, :],
                          wpanel, psA, ev1, evac_engines=('scalar',),
                          extra_tensor_waits=[('xl', xl_now)])
            PL.add(mm1)

            base_sh = [C.g('sh0'), C.g('sh1')]

            def mk_ev2(base_sh=base_sh, c0d=c0d, ph_holder=[]):
                def evac(engname, eng, ps, n_abs, nr, c0, cw, gi):
                    j = n_abs // 128
                    if engname == 'vector':
                        par = gi % 2
                        eng.wait_ge(S['sh0' if par == 0 else 'sh1'],
                                    base_sh[par] + 16 * (gi // 2))
                        eng.scalar_tensor_tensor(
                            hobuf[gi % 2][:, 0:cw], ps[0:nr, 0:cw],
                            s2c[:, j:j + 1], a32[j][:, 0:cw],
                            ALU.mult, ALU.mult).then_inc(S['ev'], 1)
                    elif engname == 'sync':
                        eng.wait_ge(S['ev'], ph_holder[0].base_ev + gi + 1)
                        eng.dma_start(out=h0_d[j * 128:(j + 1) * 128,
                                               c0d:c0d + cw],
                                      in_=hobuf[gi % 2][:, 0:cw]
                                      ).then_inc(S['sh0' if gi % 2 == 0
                                                 else 'sh1'], 16)
                return evac, ph_holder

            ev2, h2h = mk_ev2()
            mm2 = MMPhase(S, C, w2T_d, D, 4096, 512, lambda k: xbf[k][:, :],
                          wpanel, psA, ev2, evac_engines=('vector', 'sync'))
            h2h.append(mm2)
            C.bump('sh0', 16 * 16)
            C.bump('sh1', 16 * 16)
            PL.add(mm2)

        for (src_d, dst_d, wT, scol) in ((h0_d, h1_d, h1T_d, sh1c),
                                         (h1_d, h2_d, h2T_d, sh2c)):
            for ci in range(4):
                c0d = ci * 512
                prev_mm = C.g('mm')

                def loadh(s, c0d=c0d, prev_mm=prev_mm, src_d=src_d,
                          prev_ev=C.g('ev')):
                    s.wait_ge(S['mm'], prev_mm)
                    s.wait_ge(S['ev'], prev_ev)
                    for j in range(32):
                        s.dma_start(out=hbf[j],
                                    in_=src_d[j * 128:(j + 1) * 128,
                                              c0d:c0d + 512]
                                    ).then_inc(S['xf'], 16)
                C.bump('xf', 16 * 32)
                xf_now = C.g('xf')
                PL.add(RawPhase({'sync': loadh}))

                base_sh = [C.g('sh0'), C.g('sh1')]

                def mk_evh(base_sh=base_sh, c0d=c0d, dst_d=dst_d, scol=scol,
                           ph_holder=[]):
                    def evac(engname, eng, ps, n_abs, nr, c0, cw, gi):
                        j = n_abs // 128
                        if engname == 'scalar':
                            par = gi % 2
                            eng.wait_ge(S['sh0' if par == 0 else 'sh1'],
                                        base_sh[par] + 16 * (gi // 2))
                            eng.activation(hobuf[gi % 2][:, 0:cw],
                                           ps[0:nr, 0:cw], AF.Gelu,
                                           bias=0.0, scale=scol[:, j:j + 1]
                                           ).then_inc(S['ev'], 1)
                        elif engname == 'sync':
                            eng.wait_ge(S['ev'],
                                        ph_holder[0].base_ev + gi + 1)
                            eng.dma_start(out=dst_d[j * 128:(j + 1) * 128,
                                                    c0d:c0d + cw],
                                          in_=hobuf[gi % 2][:, 0:cw]
                                          ).then_inc(S['sh0' if gi % 2 == 0
                                                     else 'sh1'], 16)
                    return evac, ph_holder

                evh, hh = mk_evh()
                mmh = MMPhase(S, C, wT, 4096, 4096, 512, lambda k: hbf[k],
                              wpanel, psA, evh, evac_engines=('scalar', 'sync'),
                              extra_tensor_waits=[('xf', xf_now)])
                hh.append(mmh)
                C.bump('sh0', 16 * 16)
                C.bump('sh1', 16 * 16)
                PL.add(mmh)

        for ci in range(4):
            c0d = ci * 512
            prev_mm = C.g('mm')
            prev_sv = C.g('sv')

            def loadh2(s, c0d=c0d, prev_mm=prev_mm, prev_sv=prev_sv, ci=ci,
                       prev_ev=C.g('ev')):
                s.wait_ge(S['mm'], prev_mm)
                s.wait_ge(S['ev'], prev_ev)
                if ci > 0:
                    s.wait_ge(S['sv'], prev_sv)
                for j in range(32):
                    s.dma_start(out=hbf[j],
                                in_=h2_d[j * 128:(j + 1) * 128, c0d:c0d + 512]
                                ).then_inc(S['xf'], 16)
                for k in range(8):
                    s.dma_start(out=x32[k][:, :],
                                in_=ln3_i[k * 128:(k + 1) * 128, c0d:c0d + 512]
                                ).then_inc(S['xf'], 16)
            C.bump('xf', 16 * 40)
            xf_now = C.g('xf')
            PL.add(RawPhase({'sync': loadh2}))

            def mk_ev4(xf_now=xf_now, ph_holder=[]):
                def evac(engname, eng, ps, n_abs, nr, c0, cw, gi,
                         st_prev=C.g('st')):
                    k = n_abs // 128
                    if engname == 'vector':
                        if gi == 0:
                            eng.wait_ge(S['xf'], xf_now)
                            eng.wait_ge(S['st'], st_prev)
                        eng.scalar_tensor_tensor(
                            s32[k][:, 0:cw], ps[0:nr, 0:cw], s3c[:, k:k + 1],
                            x32[k][:, 0:cw], ALU.mult, ALU.add
                            ).then_inc(S['ev'], 1)
                        eng.tensor_copy(sbf[k][:, 0:cw], s32[k][:, 0:cw]
                                        ).then_inc(S['sv'], 1)
                return evac, ph_holder

            ev4, h4 = mk_ev4()
            mm4 = MMPhase(S, C, w3T_d, 4096, D, 512, lambda k: hbf[k],
                          wpanel, psA, ev4, evac_engines=('vector',),
                          extra_tensor_waits=[('xf', xf_now)])
            h4.append(mm4)
            C.bump('sv', 8)
            PL.add(mm4)

            base_st = C.g('st')
            ln = LNPhase(S, C, lambda k: sbf[k], lambda k: s32[k],
                         lambda k: s32[k], 512, aux,
                         pre={'tensor': [('sv', C.g('sv'))],
                              'scalar': [('sv', C.g('sv'))],
                              'vector': [('st', base_st)]})
            PL.add(ln)

            def store(s, base=ln.base_lne, c0d=c0d):
                for k in range(8):
                    s.wait_ge(S['lne'], base + k + 1)
                    s.dma_start(out=y_o[k * 128:(k + 1) * 128, c0d:c0d + 512],
                                in_=s32[k][:, :]).then_inc(S['st'], 16)
            C.bump('st', 16 * 8)
            PL.add(RawPhase({'sync': store}))

        st_total = C.g('st')
        PL.add(RawPhase({'sync': lambda s: s.wait_ge(S['st'], st_total)}))


def build_F():
    nc = bass.Bass("TRN2", target_bir_lowering=False)
    io = {}
    for nm, shp, knd in (('ln3', [D, TLOC], 'ExternalInput'),
                         ('w1T', [D, 4096], 'ExternalInput'),
                         ('w2T', [D, 4096], 'ExternalInput'),
                         ('h1T', [4096, 4096], 'ExternalInput'),
                         ('h2T', [4096, 4096], 'ExternalInput'),
                         ('w3T', [4096, D], 'ExternalInput'),
                         ('s1c', [128, 32], 'ExternalInput'),
                         ('s2c', [128, 32], 'ExternalInput'),
                         ('sh1c', [128, 32], 'ExternalInput'),
                         ('sh2c', [128, 32], 'ExternalInput'),
                         ('s3c', [128, 8], 'ExternalInput'),
                         ('y', [D, TLOC], 'ExternalOutput')):
        io[nm] = nc.dram_tensor(nm, shp, F32, kind=knd).ap()
    io['h0d'] = nc.dram_tensor("h0d", [4096, TLOC], BF16).ap()
    io['h1d'] = nc.dram_tensor("h1d", [4096, TLOC], BF16).ap()
    io['h2d'] = nc.dram_tensor("h2d", [4096, TLOC], BF16).ap()
    S = {}
    C = Cnt()
    PL = PhaseList()
    with ExitStack() as cs:
        e = cs.enter_context
        for sname in F_SEMS:
            S[sname] = e(nc.semaphore(sname))
        _emit_F(nc, e, S, C, PL, io)
        block = e(nc.Block())
        PL.emit_all(block)
    return nc


def host_prep_F(inp, e_results):
    def colsN(v):
        return f32c(np.asarray(v, np.float32).reshape(-1, 128).T)
    m = dict(
        w1T=f32c(np.asarray(inp["L1_ffn_w1_w"], np.float32).T),
        w2T=f32c(np.asarray(inp["L1_ffn_w2_w"], np.float32).T),
        h1T=f32c(np.asarray(inp["L1_ffn_h1_w"], np.float32).T),
        h2T=f32c(np.asarray(inp["L1_ffn_h2_w"], np.float32).T),
        w3T=f32c(np.asarray(inp["L1_ffn_w3_w"], np.float32).T),
        s1c=colsN(inp["L1_ffn_w1_s"]), s2c=colsN(inp["L1_ffn_w2_s"]),
        sh1c=colsN(inp["L1_ffn_h1_s"]), sh2c=colsN(inp["L1_ffn_h2_s"]),
        s3c=colsN(inp["L1_ffn_w3_s"]),
    )
    return [dict(m, ln3=e_results[c]["ln3"]) for c in range(NCORES)]


def kernel_unfused(**inputs):
    cores = list(range(NCORES))
    r1 = run_bass_kernel_spmd(build_A1(), host_prep_A1(inputs), core_ids=cores).results
    r2 = run_bass_kernel_spmd(build_A2(), host_prep_A2(inputs, r1), core_ids=cores).results
    r3 = run_bass_kernel_spmd(build_B(), host_prep_B(inputs, r2), core_ids=cores).results
    ln2_by_core = [np.asarray(r3[c]["ln2"], np.float32) for c in range(NCORES)]
    rC = run_bass_kernel_spmd(build_C(), host_prep_C(inputs, ln2_by_core), core_ids=cores).results
    rD = run_bass_kernel_spmd(build_D(), host_prep_D(inputs, rC), core_ids=cores).results
    rE = run_bass_kernel_spmd(build_E(), host_prep_E(inputs, rD, ln2_by_core), core_ids=cores).results
    rF = run_bass_kernel_spmd(build_F(), host_prep_F(inputs, rE), core_ids=cores).results
    out = np.empty((B, T, D), np.float32)
    for c in range(NCORES):
        b, h = c // 2, c % 2
        out[b, h * TLOC:(h + 1) * TLOC, :] = rF[c]["y"].T
    return out


def kernel(**inputs):
    return kernel_fused(**inputs)


def _emit_fm_to_tm(nc, e, S, C, PL, src, dst, ident_ap, nrows, ncols,
                   pre_sync=(), dt_=F32, ldq='sync'):
    kt = nrows // 128
    tbn = ncols // 128
    lt = [e(nc.sbuf_tensor(f"tml{C.g('tmuid')}_{k}", [128, 128], dt_))
          for k in range(kt)]
    tmrow = [e(nc.sbuf_tensor(f"tmr{C.g('tmuid')}_{i}", [128, nrows], dt_))
             for i in range(2)]
    idt = e(nc.sbuf_tensor(f"tmi{C.g('tmuid')}", [128, 128], dt_))
    ps_tr = [e(nc.psum_tensor(f"tmp{C.g('tmuid')}_{i}", [128, 128], dt_))
             for i in range(2)]
    C.bump('tmuid', 1)
    b_tma = C.g('tma'); b_tmb = C.g('tmb'); b_tmc = C.g('tmc')
    b_tms = C.g('tms'); b_tmi = C.g('tmi')

    def loads(s):
        for sem, val in pre_sync:
            s.wait_ge(S[sem], val)
        s.dma_start(out=idt[:, :], in_=ident_ap).then_inc(S['tmi'], 16)
        for tb in range(tbn):
            if tb > 0:
                s.wait_ge(S['tmb'], b_tmb + kt * tb)
            for k in range(kt):
                s.dma_start(out=lt[k][:, :],
                            in_=src[k * 128:(k + 1) * 128,
                                    tb * 128:(tb + 1) * 128]
                            ).then_inc(S['tma'], 16)

    def trans(t):
        t.wait_ge(S['tmi'], b_tmi + 16)
        i = 0
        for tb in range(tbn):
            t.wait_ge(S['tma'], b_tma + 16 * kt * (tb + 1))
            for k in range(kt):
                if i >= 2:
                    t.wait_ge(S['tmc'], b_tmc + i - 1)
                t.transpose(ps_tr[i % 2][:, :], lt[k][:, :], idt[:, :]
                            ).then_inc(S['tmb'], 1)
                i += 1

    def evac(sc):
        i = 0
        for tb in range(tbn):
            for k in range(kt):
                sc.wait_ge(S['tmb'], b_tmb + i + 1)
                if k == 0 and tb >= 2:
                    sc.wait_ge(S['tms'], b_tms + 16 * (tb - 1))
                sc.activation(tmrow[tb % 2][:, k * 128:(k + 1) * 128],
                              ps_tr[i % 2][:, :], AF.Copy
                              ).then_inc(S['tmc'], 1)
                i += 1

    def stores(s):
        for tb in range(tbn):
            s.wait_ge(S['tmc'], b_tmc + kt * (tb + 1))
            s.dma_start(out=dst[tb * 128:(tb + 1) * 128, :],
                        in_=tmrow[tb % 2][:, :]).then_inc(S['tms'], 16)
    C.bump('tmi', 16)
    C.bump('tma', 16 * kt * tbn)
    C.bump('tmb', kt * tbn)
    C.bump('tmc', kt * tbn)
    C.bump('tms', 16 * tbn)
    stq = 'gpsimd' if ldq == 'sync' else 'sync'
    PL.add(RawPhase({ldq: loads, 'tensor': trans, 'scalar': evac,
                     stq: stores}))
    return C.g('tms')


CD_DECOUPLE = False

ALL_SEMS = sorted(set(A1_SEMS) | set(A2_SEMS) | set(B_SEMS) | set(C_SEMS) |
                  set(D_SEMS) | set(E_SEMS) | set(F_SEMS) |
                  {'cc1', 'cc2', 'cc3', 'cc4', 'tm',
                   'tma', 'tmb', 'tmc', 'tms', 'tmi', 'wc'})


def build_fused(n_stages=99, cd_mode='both'):
    nc = bass.Bass("TRN2", target_bir_lowering=False)
    IN, INT = 'ExternalInput', None

    def dt(nm, shp, kind=IN, dtype=F32):
        if kind is None:
            return nc.dram_tensor(nm, shp, dtype).ap()
        return nc.dram_tensor(nm, shp, dtype, kind=kind).ap()

    dr = {}
    for nm, shp in (('xw', [D, TLOC + 4]), ('w_in', [D, 2 * DI]),
                    ('wx_main', [DI, DI]), ('wx_small', [DI, 32]),
                    ('w_dt', [DS, DI]), ('convw', [128, 64]),
                    ('convb', [128, 16]), ('bdt', [128, 16]),
                    ('selb', [32, 1]), ('xT', [D, TLOC]), ('wout', [DI, D]),
                    ('cflag', [128, 1]), ('mur', [128, 8]), ('muk', [128, 8]),
                    ('wrT', [D, D]), ('wkT', [D, D]), ('wvT', [D, D]),
                    ('winT', [D, 2 * DI]), ('wxT', [DI, 128]),
                    ('wdtT', [64, DI]), ('convwc', [128, 64]),
                    ('convbc', [128, 16]), ('bdtc', [128, 16]),
                    ('ident', [128, 128]), ('alogc', [128, 256]),
                    ('dpc', [128, 16]), ('woutT', [DI, D]),
                    ('oh32', [16, 16 * 128]), ('wprojT', [DI, D]),
                    ('bprojc', [128, 8]), ('w1T', [D, 4096]),
                    ('w2T', [D, 4096]), ('h1T', [4096, 4096]),
                    ('h2T', [4096, 4096]), ('w3T', [4096, D]),
                    ('s1c', [128, 32]), ('s2c', [128, 32]),
                    ('sh1c', [128, 32]), ('sh2c', [128, 32]),
                    ('s3c', [128, 8])):
        dr[nm] = dt(nm, shp)
    for nm, shp in (('idxC', [128, 33]), ('idxEf', [128, 16]),
                    ('idxEb', [128, 16])):
        dr[nm] = dt(nm, shp, IN, I32)
    y_o = dt('y', [D, TLOC], 'ExternalOutput')
    for nm, shp in (('G', [DI, TLOC]), ('P', [DI, TLOC]), ('M1', [DI, TLOC]),
                    ('ln1', [D, TLOC]), ('ln2', [D, TLOC]),
                    ('ln3', [D, TLOC]),
                    ('delta', [DI, T]), ('bc', [32, T]),
                    ('_keep_f32_marker', [1, 1]),
                    ('gend_in', [128, 16]), ('gend_out', [256, 16]),
                    ('l1col_in', [128, 8]), ('l1col_out', [256, 8])):
        dr[nm] = dt(nm, shp, INT)
    for nm, shp in (('u', [DI, T]), ('sz', [DI, T]), ('od', [D, T]),
                    ('ln2tm', [TLOC, D]),
                    ('ln2_all', [T + 1, D]), ('odtm', [T, D]),
                    ('odall', [2 * T, D]), ('ident_b', [128, 128])):
        dr[nm] = dt(nm, shp, INT, BF16)
    for nm in ('h0d', 'h1d', 'h2d'):
        dr[nm] = dt(nm, [4096, TLOC], INT, BF16)
    CAST_ORDER = ('w_in', 'wx_main', 'wx_small', 'w_dt', 'wout',
                  'wrT', 'wkT', 'wvT', 'winT', 'wxT', 'wdtT', 'woutT',
                  'wprojT', 'w1T', 'w2T', 'w3T', 'h1T', 'h2T')
    wshapes = dict(w_in=[D, 2 * DI], wx_main=[DI, DI], wx_small=[DI, 32],
                   w_dt=[DS, DI], wout=[DI, D], wrT=[D, D], wkT=[D, D],
                   wvT=[D, D], winT=[D, 2 * DI], wxT=[DI, 128],
                   wdtT=[64, DI], woutT=[DI, D], wprojT=[DI, D],
                   w1T=[D, 4096], w2T=[D, 4096], w3T=[4096, D],
                   h1T=[4096, 4096], h2T=[4096, 4096])
    for nm in CAST_ORDER:
        dr[nm + '_b'] = dt(nm + '_b', wshapes[nm], INT, BF16)
    wc_counts = {}
    cnt = 16
    for nm in CAST_ORDER:
        cnt += 16 * ceil_div(wshapes[nm][0], 128)
        wc_counts[nm + '_b'] = cnt
    MM_WCFG.clear()
    MM_WCFG.update(wc_counts)

    def wsel(nm):
        return dr[nm + '_b'] if nm in CAST_ORDER else dr[nm]

    S = {}
    C = Cnt()
    with ExitStack() as cs:
        e = cs.enter_context
        for sname in ALL_SEMS:
            S[sname] = e(nc.semaphore(sname))

        stage_ctr = [0]

        def stage(emit_fn, io):
            sid = stage_ctr[0]
            stage_ctr[0] += 1
            orig_sb = bass.Bass.sbuf_tensor
            orig_ps = bass.Bass.psum_tensor

            def sb(name, *a, **kw):
                return orig_sb(nc, f"st{sid}_{name}", *a, **kw)

            def ps(name, *a, **kw):
                return orig_ps(nc, f"st{sid}_{name}", *a, **kw)
            nc.sbuf_tensor = sb
            nc.psum_tensor = ps
            try:
                with ExitStack() as scs:
                    PL = PhaseList()
                    emit_fn(nc, scs.enter_context, S, C, PL, io)
                    blk = scs.enter_context(nc.Block())
                    PL.emit_all(blk)
            finally:
                del nc.sbuf_tensor
                del nc.psum_tensor
            return io

        def emit_a1(nc_, e_, S_, C_, PL_, io_):
            def casts(g):
                g.dma_start(out=dr['ident_b'], in_=dr['ident']
                            ).then_inc(S_['wc'], 16)
                for nm in CAST_ORDER:
                    rows = wshapes[nm][0]
                    for r in range(0, rows, 128):
                        rr = min(128, rows - r)
                        g.dma_start(out=dr[nm + '_b'][r:r + rr, :],
                                    in_=dr[nm][r:r + rr, :]
                                    ).then_inc(S_['wc'], 16)
            C_.bump('wc', 16 + sum(16 * ceil_div(wshapes[nm][0], 128)
                                   for nm in CAST_ORDER))
            PL_.add(RawPhase({'gpsimd': casts}))
            _emit_A1(nc_, e_, S_, C_, PL_, io_)
            gcarry = io_['_a1']['gcarry']
            dv_tot = C_.g('dv')
            st_b = C_.g('st')

            def ag_sync(s):
                s.wait_ge(S_['dv'], dv_tot)
                s.dma_start(out=dr['gend_in'], in_=gcarry[:, :]
                            ).then_inc(S_['st'], 16)

            def ag_g(g):
                g.wait_ge(S_['st'], st_b + 16)
                g.collective_compute(
                    "AllGather", ALU.bypass,
                    replica_groups=[[0, 1], [2, 3], [4, 5], [6, 7]],
                    ins=[dr['gend_in'].opt()],
                    outs=[dr['gend_out'].opt()],
                ).then_inc(S_['cc1'], 1)
            C_.bump('st', 16)
            C_.bump('cc1', 1)
            PL_.add(RawPhase({'sync': ag_sync, 'gpsimd': ag_g}))

        io1 = {k: dr[k] for k in ('xw', 'convw', 'convb', 'bdt', 'selb',
                                  'G', 'P', 'M1')}
        for k in ('w_in', 'wx_main', 'wx_small', 'w_dt'):
            io1[k] = wsel(k)
        if n_stages >= 1:
            stage(emit_a1, io1)

        def emit_a2(nc_, e_, S_, C_, PL_, io_):
            _emit_A2(nc_, e_, S_, C_, PL_, io_)
            lnout = io_['_a2']['lnout']
            lne_tot = C_.g('lne')
            tm_b = C_.g('tm')
            st_b = C_.g('st')
            l1col_sb = e_(nc.sbuf_tensor("l1colsb", [128, 8], F32))

            def col_v(v):
                v.wait_ge(S_['lne'], lne_tot)
                for k in range(8):
                    ins = v.tensor_copy(l1col_sb[:, k:k + 1],
                                        lnout[k][:, 511:512])
                ins.then_inc(S_['tm'], 1)

            def col_sync(s):
                s.wait_ge(S_['tm'], tm_b + 1)
                s.dma_start(out=dr['l1col_in'], in_=l1col_sb[:, :]
                            ).then_inc(S_['st'], 16)

            def col_g(g):
                g.wait_ge(S_['st'], st_b + 16)
                g.collective_compute(
                    "AllGather", ALU.bypass,
                    replica_groups=[[0, 1], [2, 3], [4, 5], [6, 7]],
                    ins=[dr['l1col_in'].opt()],
                    outs=[dr['l1col_out'].opt()],
                ).then_inc(S_['cc2'], 1)
            C_.bump('tm', 1)
            C_.bump('st', 16)
            C_.bump('cc2', 1)
            PL_.add(RawPhase({'vector': col_v, 'sync': col_sync,
                              'gpsimd': col_g}))

        io2 = {k: dr[k] for k in ('G', 'P', 'M1', 'xT', 'ln1',
                                  'gend_out', 'cflag')}
        io2['wout'] = wsel('wout')
        io2['ccol'] = None
        io2['cc1_n'] = 1
        if n_stages >= 2:
            stage(emit_a2, io2)

        io3 = {k: dr[k] for k in ('ln1', 'mur', 'muk',
                                  'ln2', 'l1col_out', 'cflag')}
        for k in ('wrT', 'wkT', 'wvT'):
            io3[k] = wsel(k)
        io3['pcol'] = None
        io3['cc2_n'] = 1
        if n_stages >= 3:
            stage(_emit_B, io3)

        def emit_btm(nc_, e_, S_, C_, PL_, io_):
            tms_tot = _emit_fm_to_tm(nc_, e_, S_, C_, PL_, dr['ln2'],
                                     dr['ln2tm'], dr['ident_b'], D, TLOC,
                                     pre_sync=(('wc', 16),), dt_=BF16,
                                     ldq='gpsimd')
            zr = e_(nc.sbuf_tensor("zrow_b", [1, D], BF16))
            st_b = C_.g('st')

            def zrow_v(v):
                v.memset(zr[0:1, :], 0.0).then_inc(S_['tm'], 1)

            def zrow_s(s, tm_n=C_.g('tm') + 1):
                s.wait_ge(S_['tm'], tm_n)
                s.dma_start(out=dr['ln2_all'][T:T + 1, :], in_=zr[0:1, :]
                            ).then_inc(S_['st'], 16)

            def ag_g(g, tms_tot=tms_tot, st_n=st_b + 16):
                g.wait_ge(S_['tms'], tms_tot)
                g.wait_ge(S_['st'], st_n)
                g.collective_compute(
                    "AllGather", ALU.bypass,
                    replica_groups=[[0, 1], [2, 3], [4, 5], [6, 7]],
                    ins=[dr['ln2tm'].opt()],
                    outs=[dr['ln2_all'][0:T, :].opt()],
                ).then_inc(S_['cc3'], 1)
            C_.bump('tm', 1)
            C_.bump('st', 16)
            C_.bump('cc3', 1)
            PL_.add(RawPhase({'vector': zrow_v, 'sync': zrow_s,
                              'gpsimd': ag_g}))

        if n_stages >= 4:
            stage(emit_btm, {})

        io4 = dict(xT=None, winT=wsel('winT'), wxT=wsel('wxT'),
                   wdtT=wsel('wdtT'),
                   convw=dr['convwc'], convb=dr['convbc'], bdt=dr['bdtc'],
                   u=dr['u'], sz=dr['sz'], delta=dr['delta'], bc=dr['bc'],
                   ln2_all=dr['ln2_all'], idxC=dr['idxC'], ident=dr['ident'],
                   cc3_n=1, defer_chunks=True)
        if CD_DECOUPLE:
            io4.update(ev_sem='yv', mm_sem='pv', wl_sems=('xl0', 'xl1'),
                       npsA=1)
        io5 = {k: dr[k] for k in ('u', 'sz', 'delta', 'bc', 'alogc', 'dpc',
                                  'oh32', 'od')}
        io5['woutT'] = wsel('woutT')
        io5['i128'] = dr['ident']
        io5['defer_chunks'] = True

        def emit_cd(nc_, e_, S_, C_, PL_, io_):
            if cd_mode == 'd':
                io5['_cw_by_chunk'] = {
                    ci: dict(cw={k: 0 for k in ('sd0', 'sd1', 'su0', 'su1',
                                                'sz0', 'sz1')}, bc_st=0)
                    for ci in range(8)}
                _emit_D(nc_, e_, S_, C_, PL_, io5)
                for ci in range(8):
                    io5['_d_chunk'](ci)
                dfin = io5['_d_fin']()
                PL_.add(RawPhase({'sync': dfin}))
                return
            _emit_C(nc_, e_, S_, C_, PL_, io4)
            if cd_mode == 'c':
                for ci in range(8):
                    io4['_c_chunk'](ci)
                tot = {k: C_.g(k) for k in ('st', 'sd0', 'sd1', 'su0',
                                            'su1', 'sz0', 'sz1')}

                def cfin(s):
                    for k, v in tot.items():
                        s.wait_ge(S_[k], v)
                PL_.add(RawPhase({'sync': cfin}))
                return
            if CD_DECOUPLE:
                io5['single_panel'] = True
            else:
                io5['wpanel'] = io4['wpanel_out']
                io5['psA'] = io4['psA_out']
            io5['_cw_by_chunk'] = {}
            _emit_D(nc_, e_, S_, C_, PL_, io5)

            def rec(ci):
                io5['_cw_by_chunk'][ci] = dict(
                    cw={k: C_.g(k) for k in ('sd0', 'sd1', 'su0', 'su1',
                                             'sz0', 'sz1')},
                    bc_st=C_.g('st'))
            if CD_DECOUPLE:
                io4['_c_chunk'](0)
                rec(0)
                for ci in range(8):
                    if ci + 1 < 8:
                        io4['_c_chunk'](ci + 1)
                        rec(ci + 1)
                    io5['_d_chunk'](ci)
            else:
                for ci in range(8):
                    io4['_c_chunk'](ci)
                    rec(ci)
                    io5['_d_chunk'](ci)
            tot = {k: C_.g(k) for k in ('st', 'sd0', 'sd1', 'su0', 'su1',
                                        'sz0', 'sz1')}
            dfin = io5['_d_fin']()

            def fin(s):
                for k, v in tot.items():
                    s.wait_ge(S_[k], v)
                dfin(s)
            PL_.add(RawPhase({'sync': fin}))

        if n_stages >= 5:
            stage(emit_cd, {})

        def emit_dtm(nc_, e_, S_, C_, PL_, io_):
            tms_tot = _emit_fm_to_tm(nc_, e_, S_, C_, PL_, dr['od'],
                                     dr['odtm'], dr['ident_b'], D, T,
                                     pre_sync=(('wc', 16),), dt_=BF16)

            def ag_g(g, tms_tot=tms_tot):
                g.wait_ge(S_['tms'], tms_tot)
                g.collective_compute(
                    "AllGather", ALU.bypass,
                    replica_groups=[[0, 1], [2, 3], [4, 5], [6, 7]],
                    ins=[dr['odtm'].opt()],
                    outs=[dr['odall'].opt()],
                ).then_inc(S_['cc4'], 1)
            C_.bump('cc4', 1)
            PL_.add(RawPhase({'gpsimd': ag_g}))

        if n_stages >= 6:
            stage(emit_dtm, {})

        io6 = dict(fwdh=None, bwdh=None, ln2=dr['ln2'],
                   wprojT=wsel('wprojT'),
                   bprojc=dr['bprojc'], ln3=dr['ln3'], odall=dr['odall'],
                   idxEf=dr['idxEf'], idxEb=dr['idxEb'], ident=dr['ident'],
                   cc4_n=1)
        if n_stages >= 7:
            stage(_emit_E, io6)

        io7 = {k: dr[k] for k in ('ln3', 's1c', 's2c', 'sh1c', 'sh2c', 's3c',
                                  'h0d', 'h1d', 'h2d')}
        for k in ('w1T', 'w2T', 'h1T', 'h2T', 'w3T'):
            io7[k] = wsel(k)
        io7['y'] = y_o
        if n_stages >= 8:
            stage(_emit_F, io7)
        MM_WCFG.clear()
        build_fused.last_counts = dict(C)
    return nc


def host_prep_fused(inp):
    m1 = host_prep_A1(inp)
    x = np.asarray(inp["L0_attn_x"], np.float32)
    wout = f32c(inp["L0_attn_W_out"])
    wrT = f32c(np.asarray(inp["L0_ffn_W_r"], np.float32).T)
    wkT = f32c(np.asarray(inp["L0_ffn_W_k"], np.float32).T)
    wvT = f32c(np.asarray(inp["L0_ffn_W_v"], np.float32).T)
    mur = cols128(inp["L0_ffn_mu_r"])
    muk = cols128(inp["L0_ffn_mu_k"])
    wprojT = f32c(np.asarray(inp["L1_attn_wproj"], np.float32).T)
    bprojc = cols128(inp["L1_attn_bproj"])
    eye = np.eye(128, dtype=np.float32)
    oh = np.zeros((16, 16 * 128), np.float32)
    for n in range(16):
        oh[n, n * 128:(n + 1) * 128] = 1.0

    def colsN(v):
        return f32c(np.asarray(v, np.float32).reshape(-1, 128).T)
    fshared = dict(
        w1T=f32c(np.asarray(inp["L1_ffn_w1_w"], np.float32).T),
        w2T=f32c(np.asarray(inp["L1_ffn_w2_w"], np.float32).T),
        h1T=f32c(np.asarray(inp["L1_ffn_h1_w"], np.float32).T),
        h2T=f32c(np.asarray(inp["L1_ffn_h2_w"], np.float32).T),
        w3T=f32c(np.asarray(inp["L1_ffn_w3_w"], np.float32).T),
        s1c=colsN(inp["L1_ffn_w1_s"]), s2c=colsN(inp["L1_ffn_w2_s"]),
        sh1c=colsN(inp["L1_ffn_h1_s"]), sh2c=colsN(inp["L1_ffn_h2_s"]),
        s3c=colsN(inp["L1_ffn_w3_s"]),
    )
    maps = []
    for c in range(NCORES):
        b, h = c // 2, c % 2
        dirb = c % 2
        sfx = 'b' if dirb else 'f'
        t0 = h * TLOC
        idxC = np.empty((128, 33), np.int32)
        for j in range(33):
            pos = j * 128 + np.arange(128)
            tau = pos - 4
            if dirb == 0:
                row = np.where((tau >= 0) & (tau < T), tau, T)
            else:
                row = np.where((tau >= 0) & (tau < T), (T - 1) - tau, T)
            idxC[:, j] = row
        idxEf = np.empty((128, 16), np.int32)
        idxEb = np.empty((128, 16), np.int32)
        for j in range(16):
            t_ = t0 + j * 128 + np.arange(128)
            idxEf[:, j] = t_
            idxEb[:, j] = T + (T - 1) - t_
        alog = np.asarray(inp[f"L1_attn_Alog_{sfx}"], np.float32)
        m = dict(m1[c])
        m.update(dict(
            xT=f32c(x[b, t0:t0 + TLOC, :].T), wout=wout,
            cflag=np.full((128, 1), float(h), np.float32),
            mur=mur, muk=muk, wrT=wrT, wkT=wkT, wvT=wvT,
            winT=f32c(np.asarray(inp[f"L1_attn_win_{sfx}"], np.float32).T),
            wdtT=f32c(np.asarray(inp[f"L1_attn_wdt_{sfx}"], np.float32).T),
            convwc=f32c(np.asarray(inp[f"L1_attn_wconv_{sfx}"], np.float32)
                        .reshape(DI, 4).reshape(16, 128, 4)
                        .transpose(1, 0, 2).reshape(128, 64)),
            convbc=cols128(inp[f"L1_attn_bconv_{sfx}"]),
            bdtc=cols128(inp[f"L1_attn_bdt_{sfx}"]),
            ident=eye, alogc=f32c(alog.reshape(16, 128, 16)
                                  .transpose(1, 0, 2).reshape(128, 256)),
            dpc=cols128(inp[f"L1_attn_Dp_{sfx}"]),
            woutT=f32c(np.asarray(inp[f"L1_attn_wout_{sfx}"], np.float32).T),
            oh32=oh, wprojT=wprojT, bprojc=bprojc,
            idxC=idxC, idxEf=f32c(idxEf).astype(np.int32),
            idxEb=f32c(idxEb).astype(np.int32),
        ))
        m['idxC'] = np.ascontiguousarray(idxC)
        m['idxEf'] = np.ascontiguousarray(idxEf)
        m['idxEb'] = np.ascontiguousarray(idxEb)
        wxf = np.asarray(inp[f"L1_attn_wx_{sfx}"], np.float32)
        wxT = np.zeros((DI, 128), np.float32)
        wxT[:, 0:32] = wxf[64:96].T
        wxT[:, 64:128] = wxf[0:64].T
        m['wxT'] = f32c(wxT)
        m.update(fshared)
        maps.append(m)
    return maps


def kernel_fused(**inputs):
    nc = build_fused()
    maps = host_prep_fused(inputs)
    res = run_bass_kernel_spmd(nc, maps, core_ids=list(range(NCORES)))
    out = np.empty((B, T, D), np.float32)
    for c in range(NCORES):
        b, h = c // 2, c % 2
        out[b, h * TLOC:(h + 1) * TLOC, :] = res.results[c]["y"].T
    return out
